# version 13
# baseline (speedup 1.0000x reference)
import math
import numpy as np
import ml_dtypes
import concourse.bass as bass
import concourse.mybir as mybir
from concourse.bass_utils import run_bass_kernel_spmd

F32 = mybir.dt.float32
BF16 = mybir.dt.bfloat16
I32 = mybir.dt.int32
AF = mybir.ActivationFunctionType
ALU = mybir.AluOpType
AX = mybir.AxisListType
ENG = mybir.EngineType

T = 2048
D = 1024
NTB = T // 128
NFC = D // 128

DEV_STAGE = "full"
DEV_PROBE = None


def cdiv(a, b):
    return -(-a // b)


class RK:
    __slots__ = ("name",)

    def __init__(self, name):
        self.name = name


class KB:
    def __init__(self, nc):
        self.nc = nc
        self.eng = {"t": nc.tensor, "v": nc.vector, "s": nc.scalar,
                    "g": nc.gpsimd, "q": nc.sync}
        self.dmaeng = {"q": nc.sync, "s2": nc.scalar, "g2": nc.gpsimd}
        self.esem = {"t": "tS", "v": "vS", "s": "sS", "g": "gS"}
        self.sem = {n: nc.alloc_semaphore(n)
                    for n in ("tS", "vS", "sS", "gS", "cc")}
        self.cnt = {k: 0 for k in self.sem}
        self.stream_maxwait = {}
        self.stream_boundary = {}
        self.seq = {e: 0 for e in ("t", "v", "s", "g", "q", "s2", "g2")}
        self.events = {e: [] for e in self.seq}
        self.wr = {}
        self.rd = {}
        self.last_drain = {"v": 0, "s": 0, "g": 0}
        self.hwm = {e: {} for e in self.seq}

    def _event_after(self, ename, seq):
        ev = self.events[ename]
        lo, hi = 0, len(ev)
        while lo < hi:
            mid = (lo + hi) // 2
            if ev[mid][0] >= seq:
                hi = mid
            else:
                lo = mid + 1
        return ev[lo][1:] if lo < len(ev) else None

    def _signal(self, ename):
        sem = self.esem[ename]
        inst = self.eng[ename].drain()
        if ename != "t":
            self.last_drain[ename] = self.seq[ename]
        inst.then_inc(self.sem[sem], 1)
        self.cnt[sem] += 1
        self.seq[ename] += 1
        self.events[ename].append((self.seq[ename], sem, self.cnt[sem]))
        return (sem, self.cnt[sem])

    def dstream(self, name):
        key = "D_" + name
        if key not in self.sem:
            self.sem[key] = self.nc.alloc_semaphore(key)
            self.cnt[key] = 0
            self.stream_maxwait[key] = 0
            self.stream_boundary[key] = 0
        return key

    def _wait(self, consumer, sem, val):
        h = self.hwm[consumer]
        if sem.startswith("D_"):
            self.stream_maxwait[sem] = max(self.stream_maxwait[sem], val)
        if h.get(sem, 0) >= val:
            return
        e = self.dmaeng[consumer] if consumer in self.dmaeng else self.eng[consumer]
        e.wait_ge(self.sem[sem], val)
        h[sem] = val

    def _need(self, consumer, producer, pseq):
        if producer == consumer and producer not in ("q", "s2", "g2"):
            if producer == "t":
                return
            if self.last_drain.get(producer, 0) < pseq:
                self.eng[producer].drain()
                self.seq[producer] += 1
                self.last_drain[producer] = self.seq[producer]
            return
        ev = self._event_after(producer, pseq)
        if ev is None:
            ev = self._signal(producer)
        sem, c = ev
        if sem.startswith("D_"):
            c = self.cnt[sem]
        self._wait(consumer, sem, c)

    def op(self, ename, fn, reads=(), writes=(), inc=False):
        rk = [t.name for t in reads]
        wk = [t.name for t in writes]
        for k in rk:
            if k in self.wr:
                self._need(ename, *self.wr[k])
        for k in wk:
            for ev in self.rd.get(k, ()):
                self._need(ename, *ev)
            if k in self.wr:
                self._need(ename, *self.wr[k])
        inst = fn(self.eng[ename])
        self.seq[ename] += 1
        myseq = self.seq[ename]
        for k in rk:
            self.rd.setdefault(k, []).append((ename, myseq))
        for k in wk:
            self.wr[k] = (ename, myseq)
            self.rd[k] = []
        if inc:
            sem = self.esem[ename]
            inst.then_inc(self.sem[sem], 1)
            self.cnt[sem] += 1
            self.events[ename].append((myseq, sem, self.cnt[sem]))
        return inst

    def dma(self, qname, out_ap, in_ap, reads=(), writes=(), stream="main", **kw):
        sem = self.dstream(("g" if qname == "g2" else "h") + stream)
        phys = {"q": None, "s2": "s", "g2": "g"}[qname]
        if self.stream_maxwait[sem] > self.stream_boundary[sem]:
            self._wait(qname, sem, self.cnt[sem])
            self.stream_boundary[sem] = self.cnt[sem]
        for t in reads:
            k = t.name
            if k in self.wr:
                p, ps = self.wr[k]
                if phys is not None and p == phys:
                    if self.last_drain[p] < ps:
                        self.eng[p].drain()
                        self.seq[p] += 1
                        self.last_drain[p] = self.seq[p]
                else:
                    self._need(qname, p, ps)
        for t in writes:
            k = t.name
            for ev in self.rd.get(k, ()):
                p = ev[0]
                if phys is not None and p == phys:
                    if self.last_drain[p] < ev[1]:
                        self.eng[p].drain()
                        self.seq[p] += 1
                        self.last_drain[p] = self.seq[p]
                else:
                    self._need(qname, *ev)
            if k in self.wr:
                p, ps = self.wr[k]
                if phys is not None and p == phys:
                    if self.last_drain[p] < ps:
                        self.eng[p].drain()
                        self.seq[p] += 1
                        self.last_drain[p] = self.seq[p]
                else:
                    self._need(qname, p, ps)
        e = self.dmaeng[qname]
        inst = e.dma_start(out_ap, in_ap, **kw)
        inst.then_inc(self.sem[sem], 16)
        self.cnt[sem] += 16
        self.seq[qname] += 1
        myseq = self.seq[qname]
        self.events[qname].append((myseq, sem, self.cnt[sem]))
        for t in reads:
            self.rd.setdefault(t.name, []).append((qname, myseq))
        for t in writes:
            self.wr[t.name] = (qname, myseq)
            self.rd[t.name] = []
        return inst

    def sync_read(self, ename, tensors):
        for t in tensors:
            if t.name in self.wr:
                self._need(ename, *self.wr[t.name])

    def finish(self, ename="q"):
        for sname, c in self.cnt.items():
            if c > 0:
                self._wait(ename, sname, c)

    def barrier(self):
        for e in ("t", "v", "s", "g", "q"):
            for sname, c in self.cnt.items():
                if c > 0:
                    self._wait(e, sname, c)
        self.wr.clear()
        self.rd.clear()


class Builder:
    def __init__(self):
        nc = bass.Bass("TRN2", target_bir_lowering=False)
        self.nc = nc
        self.kb = KB(nc)
        self.dram = {}
        self.rows_off = 0

    def din(self, name, shape):
        t = self.nc.dram_tensor(name, list(shape), F32, kind="ExternalInput")
        self.dram[name] = t
        return t

    def rows(self, n):
        o = self.rows_off
        assert o + n <= 5632, (o, n)
        self.rows_off += n
        return o

    def rows_mark(self):
        return self.rows_off

    def rows_reset(self, mark):
        self.rows_off = mark

    def row_ap(self, off, n):
        return bass.AP(self.ROWS, off, [[5632, 1], [1, n]])

    def declare(self):
        nc = self.nc
        self.x_in = self.din("L0_attn_x", (T, D))
        self.g0 = self.din("L0_attn_gamma", (D,))
        self.Wq0 = self.din("L0_attn_Wq", (D, 1024))
        self.Wkv0 = self.din("L0_attn_Wkv", (D, 128))
        self.qs0 = self.din("L0_attn_q_scale", (64,))
        self.ks0 = self.din("L0_attn_k_scale", (64,))
        self.nkv0 = self.din("L0_attn_null_kv", (2, 2, 64))
        self.Wout0 = self.din("L0_attn_Wout", (1024, D))
        self.wg0 = self.din("L0_ffn_w_gate", (3072, 1024))
        self.wu0 = self.din("L0_ffn_w_up", (3072, 1024))
        self.wd0 = self.din("L0_ffn_w_down", (1024, 3072))
        self.wlqkv = self.din("L1_attn_w_lqkv", (512, 1536))
        self.blqkv = self.din("L1_attn_b_lqkv", (1536,))
        self.wgqkv = self.din("L1_attn_w_gqkv", (512, 1536))
        self.bgqkv = self.din("L1_attn_b_gqkv", (1536,))
        self.wcomp = self.din("L1_attn_w_comp", (4096, 1024))
        self.bcomp = self.din("L1_attn_b_comp", (1024,))
        self.wlproj = self.din("L1_attn_w_lproj", (512, 512))
        self.blproj = self.din("L1_attn_b_lproj", (512,))
        self.wgproj = self.din("L1_attn_w_gproj", (512, 512))
        self.bgproj = self.din("L1_attn_b_gproj", (512,))
        self.wgate1 = self.din("L1_attn_w_gate", (1024, 2))
        self.bgate1 = self.din("L1_attn_b_gate", (2,))
        self.wq1 = self.din("L1_ffn_wq", (1024, 1024))
        self.wk1 = self.din("L1_ffn_wk", (1024, 1024))
        self.wv1 = self.din("L1_ffn_wv", (1024, 1024))
        self.wp1 = self.din("L1_ffn_wp", (1024, 1024))
        self.wo1 = self.din("L1_ffn_wo", (1024, 1024))
        self.ln1g = self.din("L1_ffn_ln1_g", (1024,))
        self.ln1b = self.din("L1_ffn_ln1_b", (1024,))
        self.ln2g = self.din("L1_ffn_ln2_g", (1024,))
        self.ln2b = self.din("L1_ffn_ln2_b", (1024,))
        self.selw = self.din("L1_ffn_sel_w", (1024, 32))
        self.keys1 = self.din("L1_ffn_keys", (32, 1024, 128))
        self.vals1 = self.din("L1_ffn_values", (32, 128, 1024))

        self.out_t = nc.dram_tensor("out", [T, D], F32, kind="ExternalOutput")
        if DEV_PROBE is not None:
            self.dbg_t = nc.dram_tensor("dbg", [128, 8192], BF16,
                                        kind="ExternalOutput")
            self.dbgf_t = nc.dram_tensor("dbgf", [128, 2048], F32,
                                         kind="ExternalOutput")
        _spk = "ExternalOutput" if DEV_PROBE in ("ups", "gts") else "Internal"
        self.UPS = nc.dram_tensor("UPS", [3072, T], F32, kind=_spk)
        self.GTS = nc.dram_tensor("GTS", [3072, T], F32, kind=_spk)
        self.WDNT = nc.dram_tensor("WDNT", [3072, 1024], BF16,
                                   kind="ExternalOutput" if DEV_PROBE == "wdnt"
                                   else "Internal")
        self.CCI = nc.dram_tensor("CCI", [1, 8], F32)
        self.CCO = nc.dram_tensor("CCO", [1, 8], F32)
        self.SCR = nc.dram_tensor("SCR", [1, 8], F32)
        self.X3S = nc.dram_tensor("X3S", [T, D], F32)
        self.BDWp = [nc.dram_tensor(f"BDWa{i}", [2 * T * 2304 + 8192], BF16)
                     for i in range(2)]
        self.OHS = nc.dram_tensor("OHS", [1024, T], BF16)
        self.GATES = nc.dram_tensor("GATES", [32, T], F32)

        self.identbf_d = nc.inline_tensor(np.eye(128).astype(ml_dtypes.bfloat16),
                                          "identbf")
        self.identf_d = nc.inline_tensor(np.eye(128, dtype=np.float32), "identf")
        jp = np.arange(128)[:, None]
        ic = np.arange(512)[None, :]
        ml = [(jp <= ic + 2 - 128 * k) for k in range(5)]
        ml.append((jp <= ic + 2) | ((ic < 64) & (jp < 66)))
        ml += [(jp <= ic - 128 * k) for k in range(5)]
        masks_np = np.concatenate([m.astype(ml_dtypes.bfloat16) for m in ml], axis=1)
        self.masks_d = nc.inline_tensor(masks_np, "masksc")
        m_idx = np.arange(4095, dtype=np.float64)
        pos = m_idx - 2047.0
        invf = 1.0 / (10000.0 ** (np.arange(0, 1024, 2, dtype=np.float64) / 1024.0))
        ang = pos[:, None] * invf[None, :]
        pe2 = np.concatenate([np.sin(ang), np.cos(ang)], axis=1)
        pe2T = np.zeros((1024, 4096), dtype=np.float64)
        pe2T[:, 0:4095] = pe2.T
        pe2T_bf = pe2T.astype(np.float32).astype(ml_dtypes.bfloat16)
        pe2_cat = np.concatenate([pe2T_bf[c * 128:(c + 1) * 128] for c in range(8)],
                                 axis=1)
        self.pe2_d = nc.inline_tensor(pe2_cat, "pe2t")
        sh = np.zeros((64, 128), dtype=np.float32)
        sh[np.arange(64), 64 + np.arange(64)] = 1.0
        self.sh64_d = nc.inline_tensor(sh.astype(ml_dtypes.bfloat16), "sh64")
        urow = np.zeros((1, 128), dtype=np.float32)
        urow[0, 64] = 1.0
        A = nc.alloc_sbuf_tensor
        self.XRES = A("XRES", [128, NTB * D], F32)
        self.XF = A("XF", [128, NFC * T], BF16)
        self.QT = A("QT", [128, NFC * T], BF16)
        self.AF2 = A("AF2", [128, 1536], F32)
        self.WB0 = A("WB0", [128, 8192], BF16)
        self.WB1 = A("WB1", [128, 8192], BF16)
        self.SBbf0 = A("SBbf0", [128, 512], BF16)
        self.SBbf1 = A("SBbf1", [128, 512], BF16)
        self.MSK = A("MSK", [128, 6 * 512], BF16)
        self.IDB = A("IDB", [128, 128], BF16)
        self.SH64 = A("SH64", [64, 128], BF16)
        self.IDF = A("IDF", [128, 128], F32)
        self.ROWS = A("ROWS", [1, 5632], F32)
        self.COLS = A("COLS", [128, 96], F32)
        self.ONESF = A("ONESF", [128, 1], F32)
        self.ONESB = A("ONESB", [128, 1], BF16)
        self.GAM = A("GAM", [128, NFC], F32)
        self.SCC = A("SCC", [128, 8], F32)
        self.QS = A("QS", [128, 1], F32)
        self.KS = A("KS", [128, 1], F32)
        self.ONER = A("ONER", [128, 128], F32)
        self.VTL = A("VTL", [128, 4224], BF16)
        self.BL1 = A("BL1", [128, 40], F32)
        P = nc.alloc_psum_tensor
        self.PA = P("PA", [128, 512], F32)
        self.PB = P("PB", [128, 512], F32)
        self.PC = P("PC", [128, 512], F32)
        self.PD = P("PD", [128, 512], F32)
        self.P4 = P("P4", [128, 512], F32)
        self.P5 = P("P5", [128, 512], F32)
        self.P6 = P("P6", [128, 512], F32)
        self.P7 = P("P7", [128, 512], F32)
        self.PTB = self.P7.ap().bitcast(BF16)[:, 0:128]
        self.PTF = self.P7[:, 128:512]
        self.NRM = self.AF2[:, 512:1024]
        self.KVS = self.QT.ap().bitcast(F32)[:, 0:2052]
        self.KH = self.VTL[:, 0:2052]
        self.VT = self.VTL[:, 2052:3174]
        self.AF0 = self.AF2[:, 0:512]
        self.XCB = self.AF2.ap().bitcast(BF16)[:, 2048:3072]
        self.QSQv = self.AF2[:, 1024:1536]

    def load_w_chunks(self, dst, dst_col0, w, K, M, row0=0, col0=0, nsplit=2):
        kb = self.kb
        nk = K // 128
        W = w.shape[1]
        per = cdiv(nk, nsplit)
        for sp in range(nsplit):
            k0, k1 = sp * per, min(nk, (sp + 1) * per)
            if k0 >= k1:
                break
            src = bass.AP(w, (row0 + k0 * 128) * W + col0,
                          [[W, 128], [128 * W, k1 - k0], [1, M]])
            dap = bass.AP(dst, dst_col0 + k0 * M,
                          [[dst.shape[1], 128], [M, k1 - k0], [1, M]])
            kb.dma("g2", dap, src, writes=(dst,))

    def row_rsqrt(self, off, n, floor=1e-24):
        kb = self.kb
        if not hasattr(self, "_rs_scr"):
            self._rs_scr = self.rows(1024)
        so = self._rs_scr
        R = self.ROWS
        for c0 in range(0, n, 512):
            m = min(512, n - c0)
            v_ = self.row_ap(off + c0, m)
            t0 = self.row_ap(so, m)
            t1 = self.row_ap(so + 512, m)
            kb.op("v", lambda e, v_=v_: e.tensor_scalar(v_, v_, floor, None, ALU.max),
                  reads=(R,), writes=(R,))
            kb.op("v", lambda e, v_=v_, t0=t0: e.reciprocal(t0, v_), reads=(R,), writes=(R,))
            kb.op("s", lambda e, t0=t0, t1=t1: e.activation(t1, t0, AF.Sqrt),
                  reads=(R,), writes=(R,))
            kb.op("v", lambda e, t0=t0, t1=t1: e.tensor_tensor(t0, t1, t1, ALU.mult),
                  reads=(R,), writes=(R,))
            kb.op("v", lambda e, t0=t0, v_=v_: e.tensor_tensor(t0, t0, v_, ALU.mult),
                  reads=(R,), writes=(R,))
            kb.op("v", lambda e, t0=t0: e.tensor_scalar(t0, t0, -0.5, 1.5, ALU.mult, ALU.add),
                  reads=(R,), writes=(R,))
            kb.op("v", lambda e, v_=v_, t0=t0, t1=t1: e.tensor_tensor(v_, t1, t0, ALU.mult),
                  reads=(R,), writes=(R,))
        return self.row_ap(off, n)

    def col_rsqrt(self, vcol, out, tmp1, tmp2, eps):
        kb = self.kb
        C = self.COLS
        kb.op("v", lambda e: e.tensor_scalar(out, vcol, eps, None, ALU.add),
              reads=(C,), writes=(C,))
        kb.op("v", lambda e: e.reciprocal(tmp1, out), reads=(C,), writes=(C,))
        kb.op("s", lambda e: e.activation(tmp2, tmp1, AF.Sqrt), reads=(C,), writes=(C,))
        kb.op("v", lambda e: e.tensor_tensor(tmp1, tmp2, tmp2, ALU.mult),
              reads=(C,), writes=(C,))
        kb.op("v", lambda e: e.tensor_tensor(tmp1, tmp1, out, ALU.mult),
              reads=(C,), writes=(C,))
        kb.op("v", lambda e: e.tensor_scalar(tmp1, tmp1, -0.5, 1.5, ALU.mult, ALU.add),
              reads=(C,), writes=(C,))
        kb.op("v", lambda e: e.tensor_tensor(out, tmp2, tmp1, ALU.mult),
              reads=(C,), writes=(C,))

    def ln_stats(self, src_ap, src_t, mcol, rcol, eps=1e-5):
        kb = self.kb
        C = self.COLS
        n = src_ap.shape[1]
        sq = self.AF2[:, 0:512]
        t2 = C[:, 59:60]
        kb.op("v", lambda e: e.reduce_sum(mcol, src_ap, axis=AX.X),
              reads=(src_t,), writes=(C,))
        kb.op("v", lambda e: e.tensor_scalar(mcol, mcol, 1.0 / n, None, ALU.mult),
              reads=(C,), writes=(C,))
        kb.op("v", lambda e: e.memset(rcol, 0.0), writes=(C,))
        for c0 in range(0, n, 512):
            m = min(512, n - c0)
            kb.op("s", lambda e, c0=c0, m=m: e.activation(
                sq[:, 0:m], src_ap[:, c0:c0 + m], AF.Square),
                reads=(src_t,), writes=(self.AF2,))
            kb.op("v", lambda e, m=m: e.reduce_sum(t2, sq[:, 0:m], axis=AX.X),
                  reads=(self.AF2,), writes=(C,))
            kb.op("v", lambda e: e.tensor_add(rcol, rcol, t2),
                  reads=(C,), writes=(C,))
        kb.op("v", lambda e: e.tensor_scalar(rcol, rcol, 1.0 / n, None, ALU.mult),
              reads=(C,), writes=(C,))
        t0 = C[:, 60:61]
        kb.op("v", lambda e: e.tensor_tensor(t0, mcol, mcol, ALU.mult),
              reads=(C,), writes=(C,))
        kb.op("v", lambda e: e.tensor_sub(rcol, rcol, t0), reads=(C,), writes=(C,))
        self.col_rsqrt(rcol, rcol, C[:, 60:61], C[:, 61:62], eps)

    def tm_tile_to_fm(self, src_fn, dst, dst_col_of_fc, tb, nfc=NFC):
        kb = self.kb
        for fc in range(nfc):
            sap = src_fn(fc)
            kb.op("t", lambda e, sap=sap: e.transpose(self.PTB[:], sap, self.IDB[:]),
                  reads=(sap.tensor, self.IDB), writes=(self.PTB,), inc=True)
            c0 = dst_col_of_fc(fc) + tb * 128
            kb.op("s", lambda e, c0=c0: e.activation(dst[:, c0:c0 + 128],
                                                     self.PTB[:], AF.Copy),
                  reads=(self.PTB,), writes=(dst,), inc=True)

    def phase_l0a(self):
        kb, nc = self.kb, self.nc
        XRES, XF, QT = self.XRES, self.XF, self.QT
        ATTC2 = lambda c, hh: XF[64 * hh:64 * hh + 64,
                                 8192 + c * 512: 8192 + (c + 1) * 512]
        ATTTMP = self.SBbf0
        PA, PB, PC, PD = self.PA, self.PB, self.PC, self.PD
        KVS, KH, VT = self.KVS, self.KH, self.VT
        ROWS, COLS = self.ROWS, self.COLS

        for tb in range(NTB):
            kb.dma("q", XRES[:, tb * D:(tb + 1) * D],
                   bass.AP(self.x_in, tb * 128 * D, [[D, 128], [1, D]]),
                   writes=(XRES,))
        kb.dma("q", self.IDB[:], self.identbf_d.ap(), writes=(self.IDB,))
        kb.dma("q", self.IDF[:], self.identf_d.ap(), writes=(self.IDF,))
        kb.dma("q", self.SH64[:], self.sh64_d.ap(), writes=(self.SH64,))
        kb.dma("q", self.MSK[:], bass.AP(self.masks_d, 0, [[11 * 512, 128], [1, 6 * 512]]), writes=(self.MSK,))
        kb.op("v", lambda e: e.memset(self.ONESF[:], 1.0), writes=(self.ONESF,))
        kb.op("v", lambda e: e.memset(self.ONESB[:], 1.0), writes=(self.ONESB,))
        kb.op("v", lambda e: e.memset(self.ONER[:], 1.0), writes=(self.ONER,))
        self.load_w_chunks(self.WB0, 0, self.Wkv0, 1024, 128, nsplit=1)
        kb.dma("q", self.GAM[:], bass.AP(self.g0, 0, [[1, 128], [128, NFC]]),
               writes=(self.GAM,), allow_slow_non_contiguous=True)
        for half in range(2):
            kb.dma("q", self.QS[64 * half:64 * half + 64, :],
                   bass.AP(self.qs0, 0, [[1, 64], [1, 1]]), writes=(self.QS,))
            kb.dma("q", self.KS[64 * half:64 * half + 64, :],
                   bass.AP(self.ks0, 0, [[1, 64], [1, 1]]), writes=(self.KS,))
        kb.op("v", lambda e: e.scalar_tensor_tensor(
            self.QS[:], self.QS[:], 8.0, self.KS[:], ALU.mult, ALU.mult),
            reads=(self.QS, self.KS), writes=(self.QS,))

        for tb in range(NTB):
            kb.op("v", lambda e, tb=tb: e.tensor_copy(
                self.XCB[:], XRES[:, tb * D:(tb + 1) * D]),
                reads=(XRES,), writes=(self.AF2,), inc=True)
            self.tm_tile_to_fm(lambda fc: self.XCB[:, fc * 128:(fc + 1) * 128],
                               XF, lambda fc: fc * T, tb)

        for seg in range(4):
            for dc in range(NFC):
                l = self.WB0[:, dc * 128:(dc + 1) * 128]
                r = XF[:, dc * T + seg * 512: dc * T + seg * 512 + 512]
                kb.op("t", lambda e, l=l, r=r, dc=dc: e.matmul(
                    PA[:], l, r, start=(dc == 0), stop=(dc == NFC - 1)),
                    reads=(self.WB0, XF),
                    writes=(PA,), inc=(dc == NFC - 1))
            kb.op("s", lambda e, seg=seg: e.activation(
                KVS[:, 2 + seg * 512: 2 + seg * 512 + 512], PA[:], AF.Copy),
                reads=(PA,), writes=(KVS,), inc=True)
        kb.dma("q", KVS[0:64, 0:2], bass.AP(self.nkv0, 0, [[1, 64], [64, 2]]),
               writes=(KVS,), allow_slow_non_contiguous=True)
        kb.dma("q", KVS[64:128, 0:2], bass.AP(self.nkv0, 128, [[1, 64], [64, 2]]),
               writes=(KVS,), allow_slow_non_contiguous=True)

        for tt in range(17):
            c0 = tt * 128
            n = min(128, 2050 - c0)
            kb.op("t", lambda e, c0=c0, n=n: e.matmul(
                self.PTF[0:n, 0:64], KVS[64:128, c0:c0 + n], self.IDF[64:128, 64:128],
                start=True, stop=True),
                reads=(KVS, self.IDF), writes=(self.PTF,), inc=True)
            kb.op("s", lambda e, tt=tt, n=n: e.activation(
                VT[0:n, tt * 66: tt * 66 + 64], self.PTF[0:n, 0:64], AF.Copy),
                reads=(self.PTF,), writes=(VT,), inc=True)
            kb.op("v", lambda e, tt=tt, n=n: e.memset(
                VT[0:n, tt * 66 + 64: tt * 66 + 65], 1.0), writes=(VT,))

        kb.op("s", lambda e: e.activation(KVS[64:128, 0:2050], KVS[0:64, 0:2050],
                                          AF.Square),
              reads=(KVS,), writes=(KVS,), inc=True)
        _rmark = self.rows_mark()
        k2o = self.rows(2050)
        for seg in range(5):
            c0 = seg * 512
            c1 = min(2050, c0 + 512)
            kb.op("t", lambda e, c0=c0, c1=c1: e.matmul(
                PB[0:1, 0:c1 - c0], self.ONESF[64:128, :], KVS[64:128, c0:c1],
                start=True, stop=True),
                reads=(self.ONESF, KVS), writes=(PB,), inc=True)
            kb.op("v", lambda e, c0=c0, c1=c1: e.tensor_copy(
                self.row_ap(k2o + c0, c1 - c0), PB[0:1, 0:c1 - c0]),
                reads=(PB,), writes=(ROWS,), inc=True)
        kinv = self.row_rsqrt(k2o, 2050)
        for seg in range(5):
            c0 = seg * 512
            c1 = min(2050, c0 + 512)
            kb.op("t", lambda e, c0=c0, c1=c1: e.matmul(
                PB[0:64, 0:c1 - c0], self._ind_row(0)[0:1, 0:64],
                self.row_ap(k2o + c0, c1 - c0), start=True, stop=True),
                reads=(ROWS, self.INDR), writes=(PB,), inc=True)
            kb.op("v", lambda e, c0=c0, c1=c1: e.tensor_tensor(
                KH[0:64, c0:c1], KVS[0:64, c0:c1], PB[0:64, 0:c1 - c0], ALU.mult),
                reads=(KVS, PB), writes=(KH,), inc=True)
            kb.op("t", lambda e, c0=c0, c1=c1: e.matmul(
                PC[:, 0:c1 - c0], self.SH64[:], KH[0:64, c0:c1],
                start=True, stop=True),
                reads=(self.SH64, KH), writes=(PC,), inc=True)
            kb.op("s", lambda e, c0=c0, c1=c1: e.activation(
                KH[64:128, c0:c1], PC[64:128, 0:c1 - c0], AF.Copy),
                reads=(PC,), writes=(KH,), inc=True)

        self.rows_reset(_rmark)
        self._rs_scr_del = self.__dict__.pop('_rs_scr', None)
        self.load_w_chunks(self.WB1, 0, self.Wq0, 1024, 1024, nsplit=4)
        for dc in range(NFC):
            kb.op("s", lambda e, dc=dc: e.activation(
                self.WB1[:, dc * 1024:(dc + 1) * 1024],
                self.WB1[:, dc * 1024:(dc + 1) * 1024], AF.Copy,
                scale=self.GAM[:, dc:dc + 1]),
                reads=(self.WB1, self.GAM), writes=(self.WB1,), inc=True)
        gq_o = self.rows(1024)
        for seg in range(2):
            for dc in range(NFC):
                kb.op("t", lambda e, dc=dc, seg=seg: e.matmul(
                    PB[0:1, :], self.ONESB[:],
                    self.WB1[:, dc * 1024 + seg * 512: dc * 1024 + seg * 512 + 512],
                    start=(dc == 0), stop=(dc == NFC - 1)),
                    reads=(self.ONESB, self.WB1), writes=(PB,), inc=(dc == NFC - 1))
            kb.op("v", lambda e, seg=seg: e.tensor_copy(
                self.row_ap(gq_o + seg * 512, 512), PB[0:1, :]),
                reads=(PB,), writes=(ROWS,), inc=True)

        nmr_o = self.rows(T)
        for tb in range(NTB):
            mcol, rcol = COLS[:, 0:1], COLS[:, 1:2]
            self.ln_stats(XRES[:, tb * D:(tb + 1) * D], XRES, mcol, rcol)
            kb.op("s", lambda e, tb=tb: e.activation(
                self.XCB[:], XRES[:, tb * D:(tb + 1) * D], AF.Copy, scale=rcol),
                reads=(XRES, COLS), writes=(self.AF2,), inc=True)
            self.tm_tile_to_fm(lambda fc: self.XCB[:, fc * 128:(fc + 1) * 128],
                               XF, lambda fc: fc * T, tb)
            nm = COLS[:, 2:3]
            kb.op("v", lambda e: e.tensor_tensor(nm, mcol, rcol, ALU.mult),
                  reads=(COLS,), writes=(COLS,))
            kb.op("v", lambda e: e.tensor_scalar(nm, nm, -1.0, None, ALU.mult),
                  reads=(COLS,), writes=(COLS,))
            kb.op("t", lambda e: e.matmul(self.PTF[0:1, 0:128], nm, self.IDF[:],
                                          start=True, stop=True),
                  reads=(COLS, self.IDF), writes=(self.PTF,), inc=True)
            kb.op("v", lambda e, tb=tb: e.tensor_copy(
                self.row_ap(nmr_o + tb * 128, 128), self.PTF[0:1, 0:128]),
                reads=(self.PTF,), writes=(ROWS,), inc=True)

        QSB = self.AF0
        for dc in range(NFC):
            for seg in range(4):
                t0 = seg * 512
                for fc in range(NFC):
                    l = self.WB1[:, fc * 1024 + dc * 128: fc * 1024 + dc * 128 + 128]
                    r = XF[:, fc * T + t0: fc * T + t0 + 512]
                    kb.op("t", lambda e, l=l, r=r, fc=fc: e.matmul(
                        PA[:], l, r, start=(fc == 0), stop=False),
                        reads=(self.WB1, XF), writes=(PA,))
                kb.op("t", lambda e, dc=dc, t0=t0: e.matmul(
                    PA[:], self.row_ap(gq_o + dc * 128, 128),
                    self.row_ap(nmr_o + t0, 512), start=False, stop=True),
                    reads=(ROWS,), writes=(PA,), inc=True)
                kb.op("s", lambda e: e.activation(QSB[:, 0:512], PA[:], AF.Copy),
                      reads=(PA,), writes=(QSB,), inc=True)
                kb.op("s", lambda e: e.activation(self.QSQv, QSB[:, 0:512],
                                                  AF.Square),
                      reads=(QSB,), writes=(self.AF2,), inc=True)
                ro = self.rows(1024) if (dc == 0 and seg == 0) else self._qrow
                self._qrow = ro
                for hh in range(2):
                    kb.op("t", lambda e, hh=hh: e.matmul(
                        PB[0:1, :], self.ONESF[64 * hh:64 * hh + 64, :],
                        self.QSQv[64 * hh:64 * hh + 64, :], start=True, stop=True),
                        reads=(self.ONESF, self.AF2), writes=(PB,), inc=True)
                    kb.op("v", lambda e, hh=hh: e.tensor_copy(
                        self.row_ap(ro + hh * 512, 512), PB[0:1, :]),
                        reads=(PB,), writes=(ROWS,), inc=True)
                for hh in range(2):
                    self.row_rsqrt(ro + hh * 512, 512)
                for hh in range(2):
                    kb.op("t", lambda e, hh=hh, ro=ro: e.matmul(
                        PC[:], self._ind_row(hh), self.row_ap(ro + hh * 512, 512),
                        start=(hh == 0), stop=(hh == 1)),
                        reads=(self.INDR, ROWS),
                        writes=(PC,) if hh == 1 else (PC,), inc=(hh == 1))
                c0 = dc * T + t0
                kb.op("v", lambda e, c0=c0: e.scalar_tensor_tensor(
                    QT[:, c0:c0 + 512], QSB[:, 0:512], self.QS[:], PC[:],
                    ALU.mult, ALU.mult),
                    reads=(QSB, self.QS, PC), writes=(QT,), inc=True)

        WOUT = self.WB1
        self.load_w_chunks(WOUT, 0, self.Wout0, 1024, 1024, nsplit=4)
        for ib in range(4):
            i0 = ib * 512
            for h in range(16):
                dc, hh = h // 2, h % 2
                qap = lambda: QT[64 * hh:64 * hh + 64,
                                 dc * T + i0: dc * T + i0 + 512]
                njt = min(17, 4 * ib + 4 + 1)
                first = True
                for jt in range(njt):
                    jc0 = jt * 128
                    n = min(128, 2050 - jc0)
                    pp = self.P4 if jt % 2 == 0 else self.P5
                    kb.op("t", lambda e, jc0=jc0, n=n, hh=hh, pp=pp: e.matmul(
                        pp[0:n, :], KH[64 * hh:64 * hh + 64, jc0:jc0 + n], qap(),
                        start=True, stop=True),
                        reads=(KH, QT), writes=(pp,), inc=True)
                    pt = self.SBbf0[:, 0:512] if (jt % 2 == 0) else self.SBbf1[:, 0:512]
                    ptt = self.SBbf0 if (jt % 2 == 0) else self.SBbf1
                    kb.op("s", lambda e, pt=pt, n=n, pp=pp: e.activation(
                        pt[0:n, :] if n < 128 else pt, pp[0:n, :], AF.Exp),
                        reads=(pp,), writes=(ptt,), inc=True)
                    kk = jt - 4 * ib
                    if 0 <= kk <= 4 and jt < 17:
                        mi = 5 if (ib == 0 and jt == 0) else kk
                        kb.op("v", lambda e, pt=pt, mi=mi, n=n: e.tensor_tensor(
                            pt[0:n, :] if n < 128 else pt,
                            pt[0:n, :] if n < 128 else pt,
                            self.MSK[0:n, mi * 512:(mi + 1) * 512] if n < 128
                            else self.MSK[:, mi * 512:(mi + 1) * 512], ALU.mult),
                            reads=(ptt, self.MSK), writes=(ptt,), inc=True)
                    kb.op("t", lambda e, jt=jt, n=n, first=first, pt=pt: e.matmul(
                        PB[0:65, :], VT[0:n, jt * 66: jt * 66 + 65],
                        pt[0:n, :] if n < 128 else pt,
                        start=first, stop=(jt == njt - 1)),
                        reads=(VT, ptt), writes=(PB,),
                        inc=(jt == njt - 1))
                    first = False
                kb.op("v", lambda e: e.reciprocal(self.NRM[64:65, :], PB[64:65, :]),
                      reads=(PB,), writes=(self.AF2,), inc=True)
                kb.op("t", lambda e: e.matmul(
                    PC[0:64, :], self.ONER[64:65, 0:64], self.NRM[64:65, :],
                    start=True, stop=True),
                    reads=(self.AF2, self.ONER), writes=(PC,), inc=True)
                kb.op("s", lambda e: e.activation(self.NRM[0:64, :], PC[0:64, :],
                                                  AF.Copy),
                      reads=(PC,), writes=(self.AF2,), inc=True)
                if hh == 0:
                    kb.op("v", lambda e, dc=dc: e.tensor_tensor(
                        ATTC2(dc, 0), PB[0:64, :],
                        self.NRM[0:64, :], ALU.mult),
                        reads=(PB, self.AF2), writes=(XF,), inc=True)
                else:
                    kb.op("v", lambda e: e.tensor_tensor(
                        ATTTMP[:, 0:512][0:64, :], PB[0:64, :],
                        self.NRM[0:64, :], ALU.mult),
                        reads=(PB, self.AF2), writes=(ATTTMP,), inc=True)
                    kb.op("t", lambda e: e.matmul(
                        PC[:], self.SH64[:], ATTTMP[0:64, 0:512],
                        start=True, stop=True),
                        reads=(self.SH64, ATTTMP), writes=(PC,), inc=True)
                    kb.op("s", lambda e, dc=dc: e.activation(
                        ATTC2(dc, 1), PC[64:128, :], AF.Copy),
                        reads=(PC,), writes=(XF,), inc=True)
            for tl in range(4):
                tb = ib * 4 + tl
                for seg in range(2):
                    for c in range(8):
                        l = XF[:, 8192 + c * 512 + tl * 128:
                               8192 + c * 512 + tl * 128 + 128]
                        r = WOUT[:, c * 1024 + seg * 512: c * 1024 + seg * 512 + 512]
                        kb.op("t", lambda e, l=l, r=r, c=c: e.matmul(
                            PD[:], l, r, start=(c == 0), stop=(c == 7)),
                            reads=(XF, WOUT), writes=(PD,), inc=(c == 7))
                    xap = XRES[:, tb * D + seg * 512: tb * D + seg * 512 + 512]
                    kb.op("v", lambda e, xap=xap: e.tensor_add(xap, xap, PD[:]),
                          reads=(XRES, PD), writes=(XRES,), inc=True)
        for tb in range(NTB):
            mcol, rcol = COLS[:, 0:1], COLS[:, 1:2]
            self.ln_stats(XRES[:, tb * D:(tb + 1) * D], XRES, mcol, rcol)
            nb = COLS[:, 3:4]
            kb.op("v", lambda e: e.tensor_tensor(nb, mcol, rcol, ALU.mult),
                  reads=(COLS,), writes=(COLS,))
            kb.op("v", lambda e: e.tensor_scalar(nb, nb, -1.0, None, ALU.mult),
                  reads=(COLS,), writes=(COLS,))
            kb.op("v", lambda e, tb=tb: e.scalar_tensor_tensor(
                XRES[:, tb * D:(tb + 1) * D], XRES[:, tb * D:(tb + 1) * D],
                rcol, nb.broadcast_to([128, D]), ALU.mult, ALU.add),
                reads=(XRES, COLS), writes=(XRES,), inc=True)


    def _ind_row(self, hh):
        return self.row_ap(self._indr_off + hh * 128, 128)


    def allreduce_max_row(self, row_off, n):
        kb = self.kb
        if not hasattr(self, "_cc_stage"):
            self._cc_stage = self.rows(8)
        st = self._cc_stage
        kb.op("v", lambda e: e.memset(self.row_ap(st, 8), 0.0), writes=(self.ROWS,))
        kb.op("v", lambda e: e.tensor_copy(self.row_ap(st, n), self.row_ap(row_off, n)),
              reads=(self.ROWS,), writes=(self.ROWS,))
        kb.dma("q", bass.AP(self.CCI, 0, [[8, 1], [1, 8]]),
               self.row_ap(st, 8), reads=(self.ROWS,), writes=(self.CCI,),
               stream="cc")
        kb.op("g", lambda e: e.collective_compute(
            "AllReduce", ALU.max, replica_groups=[list(range(8))],
            ins=[self.CCI.ap().opt()], outs=[self.CCO.ap().opt()]),
            reads=(self.CCI,), writes=(self.CCO,), inc=True)
        kb.dma("q", self.row_ap(row_off, n), bass.AP(self.CCO, 0, [[8, 1], [1, n]]),
               reads=(self.CCO,), writes=(self.ROWS,), stream="cc")

    def col_max_to_row(self, col_ap, row_off):
        kb = self.kb
        kb.op("t", lambda e: e.matmul(self.PTF[0:1, 0:128], col_ap, self.IDF[:],
                                      start=True, stop=True),
              reads=(self.COLS, self.IDF), writes=(self.P7,), inc=True)
        kb.op("v", lambda e: e.tensor_copy(self.row_ap(row_off + 1, 128),
                                           self.PTF[0:1, 0:128]),
              reads=(self.P7,), writes=(self.ROWS,), inc=True)
        kb.op("v", lambda e: e.reduce_max(self.row_ap(row_off, 1),
                                          self.row_ap(row_off + 1, 128), axis=AX.X),
              reads=(self.ROWS,), writes=(self.ROWS,))

    def bcast_scalar_col(self, scr_idx, col_ap, tgt):
        self.kb.dma("q", col_ap, bass.AP(self.SCR, scr_idx, [[0, 128], [1, 1]]),
                    reads=(self.SCR,), writes=(tgt,), stream="cc")

    def phase_l0f(self):
        kb, nc = self.kb, self.nc
        XRES, XF = self.XRES, self.XF
        PA, PB, PC, PD = self.PA, self.PB, self.PC, self.PD
        P4, P5, P6, P7 = self.P4, self.P5, self.P6, self.P7
        ROWS, COLS = self.ROWS, self.COLS
        QF = self.QT.ap().bitcast(F32)
        QI = self.QT.ap().bitcast(I32)
        QB = self.QT

        self.rows_reset(self._l0f_rmark if hasattr(self, "_l0f_rmark") else self.rows_mark())

        mx = COLS[:, 0:1]
        t = COLS[:, 1:2]
        kb.op("v", lambda e: e.memset(mx, 0.0), writes=(COLS,))
        for tb in range(NTB):
            kb.op("v", lambda e, tb=tb: e.reduce_max(
                t, XRES[:, tb * D:(tb + 1) * D], axis=AX.X,
                apply_absolute_value=True),
                reads=(XRES,), writes=(COLS,))
            kb.op("v", lambda e: e.tensor_max(mx, mx, t), reads=(COLS,), writes=(COLS,))
        ro = self.rows(160)
        self.col_max_to_row(mx, ro)
        self.allreduce_max_row(ro, 1)
        r1 = self.row_ap(ro, 1)
        r2 = self.row_ap(ro + 1, 1)
        kb.op("v", lambda e: e.tensor_scalar(r1, r1, 1.0 / 32767.0, 1e-12,
                                             ALU.mult, ALU.max),
              reads=(ROWS,), writes=(ROWS,))
        kb.op("v", lambda e: e.reciprocal(r2, r1), reads=(ROWS,), writes=(ROWS,))
        kb.dma("q", bass.AP(self.SCR, 0, [[8, 1], [1, 2]]), self.row_ap(ro, 2),
               reads=(ROWS,), writes=(self.SCR,), stream="cc")
        SXc, ISXc = self.SCC[:, 0:1], self.SCC[:, 1:2]
        self.bcast_scalar_col(0, SXc, self.SCC)
        self.bcast_scalar_col(1, ISXc, self.SCC)

        def fround(dst_ap, dst_t, src_ap, src_t, scale, Xi, Xf, Yf):
            sc = (self.SCC,) if isinstance(scale, bass.AP) else ()
            s0 = 1.0 if scale is None else scale
            kb.op("v", lambda e: e.tensor_scalar(Xi, src_ap, s0, 0.5,
                                                 ALU.mult, ALU.add),
                  reads=(src_t,) + sc, writes=(self.QT,))
            kb.op("v", lambda e: e.tensor_copy(Yf, Xi),
                  reads=(self.QT,), writes=(self.QT,))
            if scale is None:
                kb.op("v", lambda e: e.tensor_tensor(Xf, src_ap, Yf, ALU.subtract),
                      reads=(src_t, self.QT), writes=(self.QT,))
            else:
                kb.op("v", lambda e: e.scalar_tensor_tensor(
                    Xf, src_ap, scale, Yf, ALU.mult, ALU.subtract),
                    reads=(src_t, self.QT) + sc, writes=(self.QT,))
            kb.op("v", lambda e: e.tensor_scalar(Xf, Xf, -0.5, None, ALU.is_lt),
                  reads=(self.QT,), writes=(self.QT,))
            kb.op("v", lambda e: e.tensor_tensor(dst_ap, Yf, Xf, ALU.subtract),
                  reads=(self.QT,), writes=(dst_t,), inc=True)
        self._fround = fround

        for tb in range(NTB):
            fround(self.XCB[:], self.AF2, XRES[:, tb * D:(tb + 1) * D], XRES,
                   ISXc, QI[:, 0:1024], QF[:, 0:1024], QF[:, 1024:2048])
            self.tm_tile_to_fm(lambda fc: self.XCB[:, fc * 128:(fc + 1) * 128],
                               XF, lambda fc: fc * T, tb)
        if DEV_PROBE == "xq":
            kb.dma("q", self.dbg_t.ap(), XF[:, 0:8192], reads=(XF,),
                   writes=(self.dbg_t,), stream="probe")

        def dequant_chunk(wraw_ap, wraw_t, wdq_ap, wdq_t, NI):
            ng = NI // 32
            bmx = COLS[:, 0:ng] if ng <= 32 else None
            assert ng <= 32 or NI == 3072
            if NI == 3072:
                bmx = self.SCC
            bmx = COLS[:, 0:ng] if ng <= 32 else self.AF2[:, 0:ng]
            bmx_t = COLS if ng <= 32 else self.AF2
            inv = (COLS[:, 32:32 + ng] if ng <= 32 else self.AF2[:, 96:96 + ng])
            kb.op("v", lambda e: e.reduce_max(
                bmx, wraw_ap.rearrange("p (g k) -> p g k", k=32),
                axis=AX.X, apply_absolute_value=True),
                reads=(wraw_t,), writes=(bmx_t,))
            kb.op("v", lambda e: e.tensor_scalar(bmx, bmx, 1.0 / 7.0, 1e-12,
                                                 ALU.mult, ALU.max),
                  reads=(bmx_t,), writes=(bmx_t,))
            kb.op("v", lambda e: e.reciprocal(inv, bmx), reads=(bmx_t,), writes=(bmx_t,))
            tw = QF[:, 5120:5120 + NI]
            ti = QI[:, 5120 + NI:5120 + 2 * NI]
            tf2 = QF[:, 5120 + 2 * NI:5120 + 3 * NI]
            kb.op("v", lambda e: e.tensor_tensor(
                tw.rearrange("p (g k) -> p g k", k=32),
                wraw_ap.rearrange("p (g k) -> p g k", k=32),
                inv.unsqueeze(2).broadcast_to([128, ng, 32]), ALU.mult),
                reads=(wraw_t, bmx_t), writes=(self.QT,))
            tiF = QF[:, 5120 + NI:5120 + 2 * NI]
            kb.op("v", lambda e: e.tensor_scalar(ti, tw, 1.0, 0.5, ALU.mult, ALU.add),
                  reads=(self.QT,), writes=(self.QT,))
            kb.op("v", lambda e: e.tensor_copy(tf2, ti),
                  reads=(self.QT,), writes=(self.QT,))
            kb.op("v", lambda e: e.tensor_tensor(tiF, tw, tf2, ALU.subtract),
                  reads=(self.QT,), writes=(self.QT,))
            kb.op("v", lambda e: e.tensor_scalar(tiF, tiF, -0.5, None, ALU.is_lt),
                  reads=(self.QT,), writes=(self.QT,))
            kb.op("v", lambda e: e.tensor_tensor(tf2, tf2, tiF, ALU.subtract),
                  reads=(self.QT,), writes=(self.QT,))
            kb.op("v", lambda e: e.tensor_tensor(
                wdq_ap.rearrange("p (g k) -> p g k", k=32),
                tf2.rearrange("p (g k) -> p g k", k=32),
                bmx.unsqueeze(2).broadcast_to([128, ng, 32]), ALU.mult),
                reads=(self.QT, bmx_t), writes=(wdq_t,))

        WF = self.WB0.ap().bitcast(F32)
        mu_c, mg_c = COLS[:, 64:65], COLS[:, 65:66]
        kb.op("v", lambda e: e.memset(COLS[:, 64:66], 0.0), writes=(COLS,))
        for wi, (wsrc, spill, mxc) in enumerate(
                [(self.wu0, self.UPS, mu_c), (self.wg0, self.GTS, mg_c)]):
            for oc in range(24):
                par = oc % 2
                wraw = WF[:, par * 1024:(par + 1) * 1024]
                kb.dma("q", wraw,
                       bass.AP(wsrc, oc * 128 * 1024, [[1024, 128], [1, 1024]]),
                       writes=(self.WB0,), stream="wraw")
                wdq = self.WB0[:, 4096 + par * 1024: 4096 + (par + 1) * 1024]
                dequant_chunk(wraw, self.WB0, wdq, self.WB0, 1024)
                for it in range(NFC):
                    kb.op("t", lambda e, it=it: e.transpose(
                        self.PTB, wdq[:, it * 128:(it + 1) * 128], self.IDB[:]),
                        reads=(self.WB0, self.IDB), writes=(self.P7,), inc=True)
                    kb.op("s", lambda e, it=it, par=par: e.activation(
                        self.WB1[:, par * 1024 + it * 128: par * 1024 + it * 128 + 128],
                        self.PTB, AF.Copy),
                        reads=(self.P7,), writes=(self.WB1,), inc=True)
                for seg in range(4):
                    ps = (PA, PB, PC, PD)[seg]
                    for ic in range(NFC):
                        l = self.WB1[:, par * 1024 + ic * 128: par * 1024 + ic * 128 + 128]
                        r = XF[:, ic * T + seg * 512: ic * T + seg * 512 + 512]
                        kb.op("t", lambda e, l=l, r=r, ic=ic, ps=ps: e.matmul(
                            ps[:], l, r, start=(ic == 0), stop=(ic == NFC - 1)),
                            reads=(self.WB1, XF), writes=(ps,), inc=(ic == NFC - 1))
                    st = QF[:, 3072 + seg * 512: 3072 + (seg + 1) * 512]
                    kb.op("s", lambda e, st=st, ps=ps: e.activation(
                        st, ps[:], AF.Copy, scale=SXc),
                        reads=(ps, self.SCC), writes=(self.QT,), inc=True)
                    kb.op("v", lambda e, st=st: e.reduce_max(
                        COLS[:, 66:67], st, axis=AX.X, apply_absolute_value=True),
                        reads=(self.QT,), writes=(COLS,))
                    kb.op("v", lambda e, mxc=mxc: e.tensor_max(mxc, mxc, COLS[:, 66:67]),
                          reads=(COLS,), writes=(COLS,))
                    kb.dma("s2", bass.AP(spill, oc * 128 * T + seg * 512,
                                         [[T, 128], [1, 512]]), st,
                           reads=(self.QT,), writes=(spill,), stream="spill")
        self.col_max_to_row(mu_c, ro)
        kb.op("v", lambda e: e.tensor_copy(self.row_ap(ro + 140, 1), self.row_ap(ro, 1)),
              reads=(ROWS,), writes=(ROWS,))
        self.col_max_to_row(mg_c, ro)
        kb.op("v", lambda e: e.tensor_copy(self.row_ap(ro + 1, 1), self.row_ap(ro + 140, 1)),
              reads=(ROWS,), writes=(ROWS,))
        self.allreduce_max_row(ro, 2)
        rMg = self.row_ap(ro, 1)
        rMu = self.row_ap(ro + 1, 1)
        CONST_SO = 32767.0 * 65535.0 / 65536.0
        rsg = self.row_ap(ro + 2, 1); rsu = self.row_ap(ro + 3, 1)
        risg = self.row_ap(ro + 4, 1); risu = self.row_ap(ro + 5, 1)
        rso = self.row_ap(ro + 6, 1)
        kb.op("v", lambda e: e.tensor_scalar(rsg, rMg, 1.0 / 32767.0, 1e-12,
                                             ALU.mult, ALU.max), reads=(ROWS,), writes=(ROWS,))
        kb.op("v", lambda e: e.tensor_scalar(rsu, rMu, 1.0 / 32767.0, 1e-12,
                                             ALU.mult, ALU.max), reads=(ROWS,), writes=(ROWS,))
        kb.op("v", lambda e: e.reciprocal(risg, rsg), reads=(ROWS,), writes=(ROWS,))
        kb.op("v", lambda e: e.reciprocal(risu, rsu), reads=(ROWS,), writes=(ROWS,))
        kb.op("v", lambda e: e.tensor_tensor(rso, rsg, rsu, ALU.mult),
              reads=(ROWS,), writes=(ROWS,))
        kb.op("v", lambda e: e.tensor_scalar(rso, rso, CONST_SO, None, ALU.mult),
              reads=(ROWS,), writes=(ROWS,))
        kb.op("v", lambda e: e.tensor_copy(self.row_ap(ro + 8, 1), rsg), reads=(ROWS,), writes=(ROWS,))
        kb.op("v", lambda e: e.tensor_copy(self.row_ap(ro + 9, 1), risg), reads=(ROWS,), writes=(ROWS,))
        kb.op("v", lambda e: e.tensor_copy(self.row_ap(ro + 10, 1), risu), reads=(ROWS,), writes=(ROWS,))
        kb.op("v", lambda e: e.tensor_copy(self.row_ap(ro + 11, 1), rso), reads=(ROWS,), writes=(ROWS,))
        kb.op("v", lambda e: e.tensor_scalar(self.row_ap(ro + 12, 1), rsg,
                                             -32768.0, None, ALU.mult),
              reads=(ROWS,), writes=(ROWS,))
        kb.dma("q", bass.AP(self.SCR, 2, [[8, 1], [1, 5]]), self.row_ap(ro + 8, 5),
               reads=(ROWS,), writes=(self.SCR,), stream="cc")
        SGc, ISGc = self.SCC[:, 2:3], self.SCC[:, 3:4]
        ISUc, SOc, NSGc = self.SCC[:, 4:5], self.SCC[:, 5:6], self.SCC[:, 6:7]
        self.bcast_scalar_col(2, SGc, self.SCC)
        self.bcast_scalar_col(3, ISGc, self.SCC)
        self.bcast_scalar_col(4, ISUc, self.SCC)
        self.bcast_scalar_col(5, SOc, self.SCC)
        self.bcast_scalar_col(6, NSGc, self.SCC)
        if DEV_PROBE == "scc":
            kb.dma("q", bass.AP(self.dbgf_t, 0, [[2048, 128], [1, 7]]),
                   self.SCC[:, 0:7], reads=(self.SCC,), writes=(self.dbgf_t,),
                   stream="probe")
        C1 = 1.0 / 65535.0
        C2 = 1.0 / 32767.0

        for hc in range(8):
            par = hc % 2
            for sub in range(3):
                wraw = WF[:, par * 1024:(par + 1) * 1024]
                kb.dma("q", wraw,
                       bass.AP(self.wd0, hc * 128 * 3072 + sub * 1024,
                               [[3072, 128], [1, 1024]]),
                       writes=(self.WB0,), stream="wraw")
                wdq = self.WB0[:, 4096 + par * 1024: 4096 + (par + 1) * 1024]
                dequant_chunk(wraw, self.WB0, wdq, self.WB0, 1024)
                for it in range(NFC):
                    ic = sub * 8 + it
                    kb.op("t", lambda e, it=it: e.transpose(
                        self.PTB, wdq[:, it * 128:(it + 1) * 128], self.IDB[:]),
                        reads=(self.WB0, self.IDB), writes=(self.P7,), inc=True)
                    dst = self.SBbf0[:, 0:128] if ic % 2 == 0 else self.SBbf1[:, 0:128]
                    dst_t = self.SBbf0 if ic % 2 == 0 else self.SBbf1
                    kb.op("s", lambda e, dst=dst: e.activation(dst, self.PTB, AF.Copy),
                          reads=(self.P7,), writes=(dst_t,), inc=True)
                    kb.dma("s2", bass.AP(self.WDNT, ic * 128 * 1024 + hc * 128,
                                         [[1024, 128], [1, 128]]), dst,
                           reads=(dst_t,), writes=(self.WDNT,), stream="wdnt")
        kb.barrier()
        NGF = 3
        NWS = 4
        GFs = [QF[:, i * 512:(i + 1) * 512] for i in range(NGF)]
        UFs = [QF[:, 1536 + i * 512: 1536 + (i + 1) * 512] for i in range(NGF)]
        GFK = [RK(f"gf{i}") for i in range(NGF)]
        UFK = [RK(f"uf{i}") for i in range(NGF)]
        A_, B_, C_ = QF[:, 3072:3584], QF[:, 3584:4096], QF[:, 4096:4608]
        Xv, XvI = QF[:, 4608:5120], QI[:, 4608:5120]
        Yv = QF[:, 5120:5632]
        RU = QF[:, 5632:6144]
        Xg, XgI = QF[:, 6144:6656], QI[:, 6144:6656]
        Yg = QF[:, 6656:7168]
        OBs = (QB[:, 14336:14848], QB[:, 14848:15360])
        CHK, CHG = RK("chainV"), RK("chainG")
        OBK = (RK("ob0"), RK("ob1"))
        WSK = [RK(f"ws{i}") for i in range(NWS)]
        SCCK, XRK = RK("sccS3"), RK("xresS3")
        psl = (PA, PB, PC, PD, P4, P5, P6, P7)
        PSK3 = {id(p): RK(f"p5k{i}") for i, p in enumerate(psl)}

        def fround3(eng, dst_ap, dst_k, src_ap, src_k, scale, Xi, Xf, Yf, tmp_k):
            sc = (SCCK,) if isinstance(scale, bass.AP) else ()
            s0 = 1.0 if scale is None else scale
            kb.op(eng, lambda e: e.tensor_scalar(Xi, src_ap, s0, 0.5,
                                                 ALU.mult, ALU.add),
                  reads=(src_k,) + sc, writes=(tmp_k,))
            kb.op(eng, lambda e: e.tensor_copy(Yf, Xi),
                  reads=(tmp_k,), writes=(tmp_k,))
            if scale is None:
                kb.op(eng, lambda e: e.tensor_tensor(Xf, src_ap, Yf, ALU.subtract),
                      reads=(src_k, tmp_k), writes=(tmp_k,))
            elif eng == "g":
                Zg = QF[:, 7680:8192]
                kb.op(eng, lambda e: e.tensor_scalar(Zg, src_ap, s0, None,
                                                     ALU.mult),
                      reads=(src_k,), writes=(tmp_k,))
                kb.op(eng, lambda e: e.tensor_tensor(Xf, Zg, Yf, ALU.subtract),
                      reads=(tmp_k,), writes=(tmp_k,))
            else:
                kb.op(eng, lambda e: e.scalar_tensor_tensor(
                    Xf, src_ap, scale, Yf, ALU.mult, ALU.subtract),
                    reads=(src_k, tmp_k) + sc, writes=(tmp_k,))
            kb.op(eng, lambda e: e.tensor_scalar(Xf, Xf, -0.5, None, ALU.is_lt),
                  reads=(tmp_k,), writes=(tmp_k,))
            kb.op(eng, lambda e: e.tensor_tensor(dst_ap, Yf, Xf, ALU.subtract),
                  reads=(tmp_k,), writes=(dst_k,), inc=True)

        def ld_gu(n):
            tq, ic = divmod(n, 24)
            sl = n % NGF
            kb.dma("q", GFs[sl], bass.AP(self.GTS, ic * 128 * T + tq * 512,
                                         [[T, 128], [1, 512]]),
                   reads=(self.GTS,), writes=(GFK[sl],), stream=f"gf{sl}")
            kb.dma("s2", UFs[sl], bass.AP(self.UPS, ic * 128 * T + tq * 512,
                                          [[T, 128], [1, 512]]),
                   reads=(self.UPS,), writes=(UFK[sl],), stream=f"uf{sl}")

        def ld_w(n):
            tq, ic = divmod(n, 24)
            sl = n % NWS
            kb.dma("q", self.WB1[:, sl * 1024:(sl + 1) * 1024],
                   bass.AP(self.WDNT, ic * 128 * 1024, [[1024, 128], [1, 1024]]),
                   reads=(self.WDNT,), writes=(WSK[sl],), stream=f"wds{sl}")

        for n in range(NGF):
            ld_gu(n)
        for n in range(NWS):
            ld_w(n)
        for tq in range(4):
            for ic in range(24):
                n = tq * 24 + ic
                par = ic % 2
                gsl, wsl = n % NGF, n % NWS
                ob = OBs[par]
                fround3("v", A_, CHK, GFs[gsl], GFK[gsl], ISGc, XvI, Xv, Yv, CHK)
                kb.op("s", lambda e: e.activation(C_, A_, AF.Sigmoid, scale=SGc),
                      reads=(CHK, SCCK), writes=(CHK,), inc=True)
                fround3("v", B_, CHG, C_, CHK, 65536.0, XgI, Xg, Yg, CHG)
                fround3("v", RU, CHK, UFs[gsl], UFK[gsl], ISUc, XvI, Xv, Yv, CHK)
                kb.op("v", lambda e: e.tensor_tensor(A_, A_, B_, ALU.mult),
                      reads=(CHK, CHG), writes=(CHK,))
                fround3("v", B_, CHG, A_, CHK, C1, XgI, Xg, Yg, CHG)
                kb.op("v", lambda e: e.tensor_tensor(A_, B_, RU, ALU.mult),
                      reads=(CHK, CHG), writes=(CHK,))
                fround3("v", ob, OBK[par], A_, CHK, C2, XgI, Xg, Yg, CHG)
                for tl in range(4):
                    for hs in range(2):
                        ps = psl[tl * 2 + hs]
                        l = ob[:, tl * 128:(tl + 1) * 128]
                        r = self.WB1[:, wsl * 1024 + hs * 512:
                                     wsl * 1024 + hs * 512 + 512]
                        kb.op("t", lambda e, l=l, r=r, ic=ic, ps=ps: e.matmul(
                            ps[:], l, r, start=(ic == 0), stop=(ic == 23)),
                            reads=(OBK[par], WSK[wsl]), writes=(PSK3[id(ps)],),
                            inc=(ic == 23))
                if n + NGF < 96:
                    ld_gu(n + NGF)
                if n + NWS < 96:
                    ld_w(n + NWS)
            for tl in range(4):
                tb = tq * 4 + tl
                for hs in range(2):
                    ps = psl[tl * 2 + hs]
                    xap = XRES[:, tb * D + hs * 512: tb * D + hs * 512 + 512]
                    kb.op("v", lambda e, xap=xap, ps=ps: e.scalar_tensor_tensor(
                        xap, ps[:], SOc, xap, ALU.mult, ALU.add),
                        reads=(XRK, PSK3[id(ps)], SCCK), writes=(XRK,), inc=True)
        kb.barrier()
        for tb in range(NTB):
            mcol, rcol = COLS[:, 0:1], COLS[:, 1:2]
            self.ln_stats(XRES[:, tb * D:(tb + 1) * D], XRES, mcol, rcol)
            nb = COLS[:, 3:4]
            kb.op("v", lambda e: e.tensor_tensor(nb, mcol, rcol, ALU.mult),
                  reads=(COLS,), writes=(COLS,))
            kb.op("v", lambda e: e.tensor_scalar(nb, nb, -1.0, None, ALU.mult),
                  reads=(COLS,), writes=(COLS,))
            kb.op("v", lambda e, tb=tb: e.scalar_tensor_tensor(
                XRES[:, tb * D:(tb + 1) * D], XRES[:, tb * D:(tb + 1) * D],
                rcol, nb.broadcast_to([128, D]), ALU.mult, ALU.add),
                reads=(XRES, COLS), writes=(XRES,), inc=True)

    def finish_head(self, av_ps, dst_even_ap, dst_odd_ap, hh, dst_t, n_i=512):
        kb = self.kb
        PC = self.PC
        kb.op("v", lambda e: e.reciprocal(self.NRM[64:65, 0:n_i], av_ps[64:65, 0:n_i]),
              reads=(av_ps,), writes=(self.AF2,), inc=True)
        kb.op("t", lambda e: e.matmul(
            PC[0:64, 0:n_i], self.ONER[64:65, 0:64], self.NRM[64:65, 0:n_i],
            start=True, stop=True),
            reads=(self.AF2, self.ONER), writes=(PC,), inc=True)
        kb.op("s", lambda e: e.activation(self.NRM[0:64, 0:n_i], PC[0:64, 0:n_i],
                                          AF.Copy),
              reads=(PC,), writes=(self.AF2,), inc=True)
        if hh == 0:
            kb.op("v", lambda e: e.tensor_tensor(
                dst_even_ap, av_ps[0:64, 0:n_i], self.NRM[0:64, 0:n_i], ALU.mult),
                reads=(av_ps, self.AF2), writes=(dst_t,), inc=True)
        else:
            tmp = self.SBbf0[0:64, 0:n_i]
            kb.op("v", lambda e: e.tensor_tensor(
                tmp, av_ps[0:64, 0:n_i], self.NRM[0:64, 0:n_i], ALU.mult),
                reads=(av_ps, self.AF2), writes=(self.SBbf0,), inc=True)
            kb.op("t", lambda e: e.matmul(
                PC[:, 0:n_i], self.SH64[:], tmp, start=True, stop=True),
                reads=(self.SH64, self.SBbf0), writes=(PC,), inc=True)
            kb.op("s", lambda e: e.activation(dst_odd_ap, PC[64:128, 0:n_i], AF.Copy),
                  reads=(PC,), writes=(dst_t,), inc=True)

    def phase_l1a(self):
        kb, nc = self.kb, self.nc
        XRES, XF, QT, VTL = self.XRES, self.XF, self.QT, self.VTL
        PA, PB, PC, PD = self.PA, self.PB, self.PC, self.PD
        ROWS, COLS = self.ROWS, self.COLS
        self.rows_reset(self._base_rows)

        kb.dma("q", self.MSK[:, 0:5 * 512],
               bass.AP(self.masks_d, 6 * 512, [[11 * 512, 128], [1, 5 * 512]]),
               writes=(self.MSK,), stream="msk")
        kb.dma("q", self.BL1[:, 0:12], bass.AP(self.blqkv, 0, [[1, 128], [128, 12]]),
               writes=(self.BL1,), allow_slow_non_contiguous=True, stream="msk")
        kb.dma("q", self.BL1[:, 12:24], bass.AP(self.bgqkv, 0, [[1, 128], [128, 12]]),
               writes=(self.BL1,), allow_slow_non_contiguous=True, stream="msk")
        kb.dma("q", self.BL1[:, 24:28], bass.AP(self.bcomp, 512, [[1, 128], [128, 4]]),
               writes=(self.BL1,), allow_slow_non_contiguous=True, stream="msk")
        kb.dma("q", self.BL1[:, 28:32], bass.AP(self.blproj, 0, [[1, 128], [128, 4]]),
               writes=(self.BL1,), allow_slow_non_contiguous=True, stream="msk")
        kb.dma("q", self.BL1[:, 32:36], bass.AP(self.bgproj, 0, [[1, 128], [128, 4]]),
               writes=(self.BL1,), allow_slow_non_contiguous=True, stream="msk")
        blv_o = self.rows(512)
        bgv_o = self.rows(512)
        bg_o = self.rows(2)
        kb.dma("q", self.row_ap(blv_o, 512), bass.AP(self.blqkv, 1024, [[512, 1], [1, 512]]),
               writes=(ROWS,), stream="msk")
        kb.dma("q", self.row_ap(bgv_o, 512), bass.AP(self.bgqkv, 1024, [[512, 1], [1, 512]]),
               writes=(ROWS,), stream="msk")
        kb.dma("q", self.row_ap(bg_o, 2), bass.AP(self.bgate1, 0, [[2, 1], [1, 2]]),
               writes=(ROWS,), stream="msk")
        self.load_w_chunks(self.WB0, 0, self.wlqkv, 512, 1536, nsplit=2)
        self.load_w_chunks(self.WB1, 0, self.wgqkv, 512, 1536, nsplit=2)

        for tb in range(NTB):
            kb.op("v", lambda e, tb=tb: e.tensor_copy(
                self.XCB[:], XRES[:, tb * D:(tb + 1) * D]),
                reads=(XRES,), writes=(self.AF2,), inc=True)
            self.tm_tile_to_fm(lambda fc: self.XCB[:, fc * 128:(fc + 1) * 128],
                               XF, lambda fc: fc * T, tb)

        if DEV_PROBE == "xf":
            kb.dma("q", self.dbg_t.ap(), XF[:, 0:8192], reads=(XF,),
                   writes=(self.dbg_t,), stream="probe")
        CGF = QT[:, 8256:10304]
        for ci in range(4):
            n = 0
            for r in range(4):
                for ic in range(NFC):
                    wt = self.SBbf1[:, 0:128] if n % 2 == 0 else self.SBbf1[:, 128:256]
                    kb.dma("g2", wt, bass.AP(self.wcomp,
                           (r * 1024 + ic * 128) * 1024 + 512 + ci * 128,
                           [[1024, 128], [1, 128]]),
                           writes=(self.SBbf1,), stream="wcmp")
                    rhs = XF[:, ic * T + r: ic * T + r + 2045: 4]
                    kb.op("t", lambda e, wt=wt, rhs=rhs, n=n: e.matmul(
                        PA[:], wt, rhs, start=(n == 0), stop=(n == 31)),
                        reads=(self.SBbf1, XF), writes=(PA,), inc=(n == 31))
                    n += 1
            kb.op("s", lambda e, ci=ci: e.activation(
                CGF[:, ci * 512:(ci + 1) * 512], PA[:], AF.Identity,
                bias=self.BL1[:, 24 + ci:25 + ci]),
                reads=(PA, self.BL1), writes=(QT,), inc=True)

        GQ = lambda c: QT[:, 2048 + c * 512: 2048 + (c + 1) * 512]
        GK = lambda c: QT[:, 4096 + c * 512: 4096 + (c + 1) * 512]
        for oc in range(8):
            for ic in range(4):
                l = self.WB1[:, ic * 1536 + oc * 128: ic * 1536 + oc * 128 + 128]
                r = CGF[:, ic * 512:(ic + 1) * 512]
                kb.op("t", lambda e, l=l, r=r, ic=ic: e.matmul(
                    PB[:], l, r, start=(ic == 0), stop=(ic == 3)),
                    reads=(self.WB1, QT), writes=(PB,), inc=(ic == 3))
            dst = GQ(oc) if oc < 4 else GK(oc - 4)
            kb.op("s", lambda e, dst=dst, oc=oc: e.activation(
                dst, PB[:], AF.Identity, bias=self.BL1[:, 12 + oc:13 + oc]),
                reads=(PB, self.BL1), writes=(QT,), inc=True)
        for tt in range(4):
            for ic in range(4):
                l = CGF[:, ic * 512 + tt * 128: ic * 512 + tt * 128 + 128]
                r = self.WB1[:, ic * 1536 + 1024: ic * 1536 + 1536]
                kb.op("t", lambda e, l=l, r=r, ic=ic: e.matmul(
                    PB[:], l, r, start=(ic == 0), stop=False),
                    reads=(self.WB1, QT), writes=(PB,))
            kb.op("t", lambda e: e.matmul(
                PB[:], self.ONER[0:1, 0:128], self.row_ap(bgv_o, 512),
                start=False, stop=True),
                reads=(self.ONER, ROWS), writes=(PB,), inc=True)
            for h in range(8):
                off = 6144 + tt * 528 + h * 66
                kb.op("s", lambda e, off=off, h=h: e.activation(
                    QT[:, off:off + 64], PB[:, h * 64:(h + 1) * 64], AF.Copy),
                    reads=(PB,), writes=(QT,), inc=True)
                kb.op("v", lambda e, off=off: e.memset(QT[:, off + 64:off + 65], 1.0),
                      writes=(QT,))
        ATTG = lambda c, hh: QT[64 * hh:64 * hh + 64, c * 512:(c + 1) * 512]
        for h in range(8):
            dc, hh = h // 2, h % 2
            for jt in range(4):
                pp = self.P4 if jt % 2 == 0 else self.P5
                kb.op("t", lambda e, dc=dc, hh=hh, jt=jt, pp=pp: e.matmul(
                    pp[0:128, :], GK(dc)[64 * hh:64 * hh + 64, jt * 128:(jt + 1) * 128],
                    GQ(dc)[64 * hh:64 * hh + 64, :], start=True, stop=True),
                    reads=(QT,), writes=(pp,), inc=True)
                pt = self.SBbf1[:, 0:512] if jt % 2 == 0 else self.SBbf0[:, 0:512]
                ptt = self.SBbf1 if jt % 2 == 0 else self.SBbf0
                kb.op("s", lambda e, pt=pt, pp=pp: e.activation(pt, pp[:], AF.Exp, scale=0.125),
                      reads=(pp,), writes=(ptt,), inc=True)
                kb.op("v", lambda e, pt=pt, jt=jt: e.tensor_tensor(
                    pt, pt, self.MSK[:, jt * 512:(jt + 1) * 512], ALU.mult),
                    reads=(ptt, self.MSK), writes=(ptt,), inc=True)
                kb.op("t", lambda e, jt=jt, h=h, pt=pt: e.matmul(
                    PB[0:65, :], QT[:, 6144 + jt * 528 + h * 66: 6144 + jt * 528 + h * 66 + 65],
                    pt, start=(jt == 0), stop=(jt == 3)),
                    reads=(QT, ptt), writes=(PB,), inc=(jt == 3))
            self.finish_head(PB, ATTG(dc, 0), ATTG(dc, 1), hh, QT)
        GOF = self.WB1[:, 6144:8192]
        for oc in range(4):
            for ic in range(4):
                l_ap = bass.AP(self.wgproj, ic * 128 * 512 + oc * 128,
                               [[512, 128], [1, 128]])
                wt = self.SBbf1[:, 0:128] if ic % 2 == 0 else self.SBbf1[:, 128:256]
                kb.dma("g2", wt, l_ap, writes=(self.SBbf1,), stream="wcmp")
                r = QT[:, ic * 512:(ic + 1) * 512]
                kb.op("t", lambda e, wt=wt, r=r, ic=ic: e.matmul(
                    PB[:], wt, r, start=(ic == 0), stop=(ic == 3)),
                    reads=(self.SBbf1, QT), writes=(PB,), inc=(ic == 3))
            kb.op("s", lambda e, oc=oc: e.activation(
                GOF[:, oc * 512:(oc + 1) * 512], PB[:], AF.Identity,
                bias=self.BL1[:, 32 + oc:33 + oc]),
                reads=(PB, self.BL1), writes=(self.WB1,), inc=True)

        for oc in range(8):
            for seg in range(4):
                for ic in range(4):
                    l = self.WB0[:, ic * 1536 + oc * 128: ic * 1536 + oc * 128 + 128]
                    r = XF[:, ic * T + seg * 512: ic * T + seg * 512 + 512]
                    kb.op("t", lambda e, l=l, r=r, ic=ic: e.matmul(
                        PA[:], l, r, start=(ic == 0), stop=(ic == 3)),
                        reads=(self.WB0, XF), writes=(PA,), inc=(ic == 3))
                base = (oc % 4) * 2048 + (0 if oc < 4 else 8192)
                kb.op("s", lambda e, base=base, seg=seg, oc=oc: e.activation(
                    QT[:, base + seg * 512: base + seg * 512 + 512], PA[:], AF.Identity,
                    bias=self.BL1[:, oc:oc + 1]),
                    reads=(PA, self.BL1), writes=(QT,), inc=True)
        LQ = lambda c: QT[:, c * 2048: (c + 1) * 2048]
        LK = lambda c: QT[:, 8192 + c * 2048: 8192 + (c + 1) * 2048]
        ATTC2L = lambda ib, c, hh: XF[64 * hh:64 * hh + 64,
                                      8192 + ib * 2048 + c * 512:
                                      8192 + ib * 2048 + (c + 1) * 512]
        for hg in range(2):
            for tt in range(16):
                for ic in range(4):
                    l = XF[:, ic * T + tt * 128: ic * T + tt * 128 + 128]
                    r = self.WB0[:, ic * 1536 + 1024 + hg * 256:
                                  ic * 1536 + 1024 + hg * 256 + 256]
                    kb.op("t", lambda e, l=l, r=r, ic=ic: e.matmul(
                        PB[:, 0:256], l, r, start=(ic == 0), stop=False),
                        reads=(self.WB0, XF), writes=(PB,))
                kb.op("t", lambda e, hg=hg: e.matmul(
                    PB[:, 0:256], self.ONER[0:1, 0:128],
                    self.row_ap(blv_o + hg * 256, 256),
                    start=False, stop=True),
                    reads=(self.ONER, ROWS), writes=(PB,), inc=True)
                for hl in range(4):
                    off = tt * 264 + hl * 66
                    kb.op("s", lambda e, off=off, hl=hl: e.activation(
                        VTL[:, off:off + 64], PB[:, hl * 64:(hl + 1) * 64], AF.Copy),
                        reads=(PB,), writes=(VTL,), inc=True)
                    kb.op("v", lambda e, off=off: e.memset(VTL[:, off + 64:off + 65], 1.0),
                          writes=(VTL,))
            for ib in range(4):
                i0 = ib * 512
                for hl in range(4):
                    h = hg * 4 + hl
                    dc, hh = h // 2, h % 2
                    njt = 4 * ib + 4
                    for jt in range(njt):
                        pp = self.P4 if jt % 2 == 0 else self.P5
                        kb.op("t", lambda e, dc=dc, hh=hh, jt=jt, i0=i0, pp=pp: e.matmul(
                            pp[:], LK(dc)[64 * hh:64 * hh + 64, jt * 128:(jt + 1) * 128],
                            LQ(dc)[64 * hh:64 * hh + 64, i0:i0 + 512],
                            start=True, stop=True),
                            reads=(QT,), writes=(pp,), inc=True)
                        pt = self.SBbf1[:, 0:512] if jt % 2 == 0 else self.SBbf0[:, 0:512]
                        ptt = self.SBbf1 if jt % 2 == 0 else self.SBbf0
                        kb.op("s", lambda e, pt=pt, pp=pp: e.activation(pt, pp[:], AF.Exp,
                                                                 scale=0.125),
                              reads=(pp,), writes=(ptt,), inc=True)
                        kk = jt - 4 * ib
                        if 0 <= kk <= 4:
                            kb.op("v", lambda e, pt=pt, kk=kk: e.tensor_tensor(
                                pt, pt, self.MSK[:, kk * 512:(kk + 1) * 512], ALU.mult),
                                reads=(ptt, self.MSK), writes=(ptt,), inc=True)
                        kb.op("t", lambda e, jt=jt, hl=hl, pt=pt, njt=njt: e.matmul(
                            PB[0:65, :], VTL[:, jt * 264 + hl * 66: jt * 264 + hl * 66 + 65],
                            pt, start=(jt == 0), stop=(jt == njt - 1)),
                            reads=(VTL, ptt), writes=(PB,), inc=(jt == njt - 1))
                    self.finish_head(PB, ATTC2L(ib, dc, 0), ATTC2L(ib, dc, 1), hh, XF)
        for ib in range(4):
            i0 = ib * 512
            for oc in range(4):
                for ci in range(4):
                    l_ap = bass.AP(self.wlproj, ci * 128 * 512 + oc * 128,
                                   [[512, 128], [1, 128]])
                    wt = self.SBbf1[:, 0:128] if ci % 2 == 0 else self.SBbf1[:, 128:256]
                    kb.dma("g2", wt, l_ap, writes=(self.SBbf1,), stream="wcmp")
                    r = XF[:, 8192 + ib * 2048 + ci * 512: 8192 + ib * 2048 + (ci + 1) * 512]
                    kb.op("t", lambda e, wt=wt, r=r, ci=ci: e.matmul(
                        PB[:], wt, r, start=(ci == 0), stop=(ci == 3)),
                        reads=(self.SBbf1, XF), writes=(PB,), inc=(ci == 3))
                kb.op("s", lambda e, oc=oc, i0=i0: e.activation(
                    XF[:, oc * 2048 + i0: oc * 2048 + i0 + 512], PB[:], AF.Identity,
                    bias=self.BL1[:, 28 + oc:29 + oc]),
                    reads=(PB, self.BL1), writes=(XF,), inc=True)

        if DEV_PROBE == "lo":
            kb.dma("q", self.dbg_t.ap(), XF[:, 0:8192], reads=(XF,),
                   writes=(self.dbg_t,), stream="probe")
        if DEV_PROBE == "go":
            kb.dma("q", bass.AP(self.dbg_t, 0, [[8192, 128], [1, 2048]]),
                   GOF[:], reads=(self.WB1,), writes=(self.dbg_t,), stream="probe")
        for ci in range(4):
            src = GOF[:, ci * 512:(ci + 1) * 512]
            dst = QT[:, ci * 2048:(ci + 1) * 2048]
            kb.op("v", lambda e, src=src, dst=dst: e.tensor_copy(
                dst.rearrange("p (a b) -> p a b", b=4),
                src.unsqueeze(2).broadcast_to([128, 512, 4])),
                reads=(self.WB1,), writes=(QT,), inc=True)

        dbr = self.rows(2)
        kb.op("v", lambda e: e.tensor_sub(self.row_ap(dbr, 1), self.row_ap(bg_o, 1),
                                          self.row_ap(bg_o + 1, 1)),
              reads=(ROWS,), writes=(ROWS,))
        kb.dma("q", bass.AP(self.SCR, 7, [[8, 1], [1, 1]]), self.row_ap(dbr, 1),
               reads=(ROWS,), writes=(self.SCR,), stream="cc")
        self.bcast_scalar_col(7, self.SCC[:, 7:8], self.SCC)
        WGT = self.SBbf1[:, 256:288]
        for n in range(8):
            kb.dma("g2", WGT[:, 2 * n:2 * n + 2],
                   bass.AP(self.wgate1, n * 128 * 2, [[2, 128], [1, 2]]),
                   writes=(self.SBbf1,), stream="wcmp")
        for tb in range(NTB):
            n = 0
            for src, srct in ((XF, XF), (QT, QT)):
                for ci in range(4):
                    l = src[:, ci * 2048 + tb * 128: ci * 2048 + tb * 128 + 128]
                    wt = WGT[:, 2 * n:2 * n + 2]
                    kb.op("t", lambda e, l=l, wt=wt, n=n: e.matmul(
                        PD[0:128, 0:2], l, wt, start=(n == 0), stop=(n == 7)),
                        reads=(srct, self.SBbf1), writes=(PD,), inc=(n == 7))
                    n += 1
            g0 = COLS[:, 70:71]
            g1 = COLS[:, 71:72]
            t_ = COLS[:, 72:73]
            kb.op("s", lambda e: e.activation(COLS[:, 78:80], PD[0:128, 0:2], AF.Copy),
                  reads=(PD,), writes=(COLS,), inc=True)
            kb.op("v", lambda e: e.tensor_sub(t_, COLS[:, 78:79], COLS[:, 79:80]),
                  reads=(COLS,), writes=(COLS,), inc=True)
            kb.op("s", lambda e: e.activation(g0, t_, AF.Sigmoid, bias=self.SCC[:, 7:8]),
                  reads=(COLS, self.SCC), writes=(COLS,), inc=True)
            kb.op("v", lambda e: e.tensor_scalar(g1, g0, -1.0, 1.0, ALU.mult, ALU.add),
                  reads=(COLS,), writes=(COLS,))
            if DEV_PROBE == "gate":
                kb.dma("q", bass.AP(self.out_t, tb * 128 * D, [[D, 128], [1, 2]]),
                       COLS[:, 70:72], reads=(COLS,), writes=(self.out_t,),
                       stream="probe")
            for ci in range(8):
                src = XF if ci < 4 else QT
                cc = ci % 4
                sap = src[:, cc * 2048 + tb * 128: cc * 2048 + tb * 128 + 128]
                kb.op("t", lambda e, sap=sap: e.transpose(self.PTB, sap, self.IDB[:]),
                      reads=(src if ci < 4 else QT, self.IDB), writes=(self.P7,), inc=True)
                xap = XRES[:, tb * D + (0 if ci < 4 else 512) + cc * 128:
                           tb * D + (0 if ci < 4 else 512) + cc * 128 + 128]
                gc = g0 if ci < 4 else g1
                kb.op("v", lambda e, xap=xap, gc=gc: e.scalar_tensor_tensor(
                    xap, self.PTB, gc, xap, ALU.mult, ALU.add),
                    reads=(self.P7, COLS, XRES), writes=(XRES,), inc=True)

        for tb in range(NTB):
            mcol, rcol = COLS[:, 0:1], COLS[:, 1:2]
            self.ln_stats(XRES[:, tb * D:(tb + 1) * D], XRES, mcol, rcol)
            nb = COLS[:, 3:4]
            kb.op("v", lambda e: e.tensor_tensor(nb, mcol, rcol, ALU.mult),
                  reads=(COLS,), writes=(COLS,))
            kb.op("v", lambda e: e.tensor_scalar(nb, nb, -1.0, None, ALU.mult),
                  reads=(COLS,), writes=(COLS,))
            kb.op("v", lambda e, tb=tb: e.scalar_tensor_tensor(
                XRES[:, tb * D:(tb + 1) * D], XRES[:, tb * D:(tb + 1) * D],
                rcol, nb.broadcast_to([128, D]), ALU.mult, ALU.add),
                reads=(XRES, COLS), writes=(XRES,), inc=True)

    def xn_materialize(self, gcol_base, bcol_base):
        kb = self.kb
        XRES, XF, COLS = self.XRES, self.XF, self.COLS
        for tb in range(NTB):
            mcol, rcol = COLS[:, 0:1], COLS[:, 1:2]
            self.ln_stats(XRES[:, tb * D:(tb + 1) * D], XRES, mcol, rcol)
            nb = COLS[:, 3:4]
            kb.op("v", lambda e: e.tensor_tensor(nb, mcol, rcol, ALU.mult),
                  reads=(COLS,), writes=(COLS,))
            kb.op("v", lambda e: e.tensor_scalar(nb, nb, -1.0, None, ALU.mult),
                  reads=(COLS,), writes=(COLS,))
            kb.op("s", lambda e, tb=tb: e.activation(
                self.XCB[:], XRES[:, tb * D:(tb + 1) * D], AF.Identity,
                scale=rcol, bias=nb),
                reads=(XRES, COLS), writes=(self.AF2,), inc=True)
            for fc in range(NFC):
                sap = self.XCB[:, fc * 128:(fc + 1) * 128]
                kb.op("t", lambda e, sap=sap: e.transpose(self.PTB, sap, self.IDB[:]),
                      reads=(self.AF2, self.IDB), writes=(self.P7,), inc=True)
                c0 = fc * T + tb * 128
                kb.op("s", lambda e, c0=c0, fc=fc: e.activation(
                    XF[:, c0:c0 + 128], self.PTB, AF.Identity,
                    scale=self.BL1[:, gcol_base + fc:gcol_base + fc + 1],
                    bias=self.BL1[:, bcol_base + fc:bcol_base + fc + 1]),
                    reads=(self.P7, self.BL1), writes=(XF,), inc=True)

    def phase_l1f(self):
        kb, nc = self.kb, self.nc
        XRES, XF, QT = self.XRES, self.XF, self.QT
        PA, PB, PC, PD = self.PA, self.PB, self.PC, self.PD
        P4, P5, P6, P7 = self.P4, self.P5, self.P6, self.P7
        ROWS, COLS = self.ROWS, self.COLS
        self.rows_reset(self._base_rows)
        QTF = self.QT.ap().bitcast(F32)
        XRB = self.XRES.ap().bitcast(BF16)

        for src, base in ((self.ln1g, 40), (self.ln1b, 48),
                          (self.ln2g, 56), (self.ln2b, 64)):
            kb.dma("q", self.BL1[:, base // 8 * 8:base // 8 * 8 + 8] if False else
                   self.BL1[:, base - 40 + 0:base - 40 + 8] if False else
                   self.BL1[:, (base - 40) + 0:(base - 40) + 8],
                   bass.AP(src, 0, [[1, 128], [128, 8]]),
                   writes=(self.BL1,), allow_slow_non_contiguous=True, stream="msk")

        self.xn_materialize(0, 8)

        for tb in range(NTB):
            kb.dma("s2", bass.AP(self.X3S, tb * 128 * D, [[D, 128], [1, D]]),
                   XRES[:, tb * D:(tb + 1) * D], reads=(XRES,), writes=(self.X3S,),
                   stream="x3s")

        self.load_w_chunks(self.WB0, 0, self.wp1, 1024, 1024, nsplit=4)
        psall = (PA, PB, PC, PD, P4, P5, P6, P7)
        for mseg in range(8):
            for ic in range(NFC):
                par = ic % 2
                pesl = self.SBbf0[:, 0:512] if par == 0 else self.SBbf1[:, 0:512]
                peslt = self.SBbf0 if par == 0 else self.SBbf1
                kb.dma("q", pesl, bass.AP(self.pe2_d, ic * 4096 + mseg * 512,
                                          [[32768, 128], [1, 512]]),
                       writes=(peslt,), stream="pe2")
                for pc in range(8):
                    l = self.WB0[:, ic * 1024 + pc * 128: ic * 1024 + pc * 128 + 128]
                    kb.op("t", lambda e, l=l, r=pesl, ic=ic, pc=pc: e.matmul(
                        psall[pc][:], l, r, start=(ic == 0), stop=(ic == NFC - 1)),
                        reads=(self.WB0, peslt), writes=(psall[pc],),
                        inc=(ic == NFC - 1))
            for pc in range(8):
                kb.op("s", lambda e, pc=pc, mseg=mseg: e.activation(
                    XRB[:, pc * 4096 + mseg * 512: pc * 4096 + (mseg + 1) * 512],
                    psall[pc][:], AF.Copy),
                    reads=(psall[pc],), writes=(XRES,), inc=True)

        self.load_w_chunks(self.WB0, 0, self.wo1, 1024, 1024, nsplit=4)
        kb.barrier()
        VTL = self.VTL
        XFK, P2K, NRK = RK("xfH"), RK("p2H"), RK("nrmH")
        PRK = (RK("pair0"), RK("pair1"))
        WHK = (RK("wh0"), RK("wh1"))
        WTK = (RK("wt0"), RK("wt1"))
        OHK = (RK("oh0"), RK("oh1"))
        BDK = tuple(RK(f"bdt{i}") for i in range(6))
        PTK = (RK("pt0"), RK("pt1"))
        OHSK = RK("ohsK")
        PSK = {id(p): RK(f"psl{i}") for i, p in
               enumerate((PA, PB, PC, PD, P4, P5, P6, P7))}
        QHBp = lambda pb: QT[:, pb * 6208: pb * 6208 + 2048]
        KHBp = lambda pb: QT[:, pb * 6208 + 2048: pb * 6208 + 4096]
        vt_off = lambda pb, hh, tt: pb * 6208 + 4096 + hh * 1056 + tt * 66
        WT = (QT[:, 12416:14592], VTL[:, 1536:3712])
        OH2b = (VTL[:, 0:512], VTL[:, 3712:4224])
        A2B = self.AF2.ap().bitcast(BF16)
        BDT = (VTL[:, 512:1024], VTL[:, 1024:1536], A2B[:, 0:512],
               A2B[:, 512:1024], A2B[:, 2048:2560], A2B[:, 2560:3072])
        NBD = len(BDT)

        def loadW(p):
            pb = p % 2
            for wi, wsrc in enumerate((self.wq1, self.wk1, self.wv1)):
                kb.dma("g2",
                       bass.AP(self.WB1, pb * 3072 + wi * 1024,
                               [[8192, 128], [128, 8], [1, 128]]),
                       bass.AP(wsrc, p * 128, [[1024, 128], [128 * 1024, 8],
                                               [1, 128]]),
                       writes=(WHK[pb],), stream=f"wh{pb}")

        def qkv(p):
            pb = p % 2
            WQ = self.WB1[:, pb * 3072: pb * 3072 + 1024]
            WKb = self.WB1[:, pb * 3072 + 1024: pb * 3072 + 2048]
            WV = self.WB1[:, pb * 3072 + 2048: pb * 3072 + 3072]
            for di, dst in enumerate((QHBp(pb), KHBp(pb))):
                Wm = WQ if di == 0 else WKb
                for seg in range(4):
                    pq = PA if seg % 2 == 0 else PB
                    for ic in range(NFC):
                        l = Wm[:, ic * 128:(ic + 1) * 128]
                        r = XF[:, ic * T + seg * 512: ic * T + seg * 512 + 512]
                        kb.op("t", lambda e, l=l, r=r, ic=ic, pq=pq: e.matmul(
                            pq[:], l, r, start=(ic == 0), stop=(ic == NFC - 1)),
                            reads=(WHK[pb], XFK), writes=(PSK[id(pq)],),
                            inc=(ic == NFC - 1))
                    kb.op("s", lambda e, dst=dst, seg=seg, pq=pq: e.activation(
                        dst[:, seg * 512:(seg + 1) * 512], pq[:], AF.Copy),
                        reads=(PSK[id(pq)],), writes=(PRK[pb],), inc=True)
            for tt in range(NTB):
                pv = PC if tt % 2 == 0 else PD
                for ic in range(NFC):
                    l = XF[:, ic * T + tt * 128: ic * T + tt * 128 + 128]
                    r = WV[:, ic * 128:(ic + 1) * 128]
                    kb.op("t", lambda e, l=l, r=r, ic=ic, pv=pv: e.matmul(
                        pv[:, 0:128], l, r, start=(ic == 0), stop=(ic == NFC - 1)),
                        reads=(WHK[pb], XFK), writes=(PSK[id(pv)],),
                        inc=(ic == NFC - 1))
                for hh in range(2):
                    off = vt_off(pb, hh, tt)
                    eng_ = "s" if hh == 0 else "v"
                    if hh == 0:
                        kb.op("s", lambda e, off=off, pv=pv: e.activation(
                            QT[:, off:off + 64], pv[:, 0:64], AF.Copy),
                            reads=(PSK[id(pv)],), writes=(PRK[pb],), inc=True)
                    else:
                        kb.op("v", lambda e, off=off, pv=pv: e.tensor_copy(
                            QT[:, off:off + 64], pv[:, 64:128]),
                            reads=(PSK[id(pv)],), writes=(PRK[pb],), inc=True)
                    kb.op("v", lambda e, off=off: e.memset(
                        QT[:, off + 64:off + 65], 1.0), writes=(PRK[pb],))

        def bd(p):
            pb = p % 2
            psl = (PA, PB, PC, PD, P4)
            for hh in range(2):
                for i128 in range(16):
                    base = 1920 - 128 * i128
                    lq = QHBp(pb)[64 * hh:64 * hh + 64, i128 * 128:(i128 + 1) * 128]
                    for sgi in range(5):
                        n = 512 if sgi < 4 else 128
                        rp = XRB[64 * hh:64 * hh + 64,
                                 p * 4096 + base + sgi * 512:
                                 p * 4096 + base + sgi * 512 + n]
                        kb.op("t", lambda e, lq=lq, rp=rp, n=n, sgi=sgi: e.matmul(
                            psl[sgi][:, 0:n], lq, rp, start=True, stop=True),
                            reads=(PRK[pb], P2K), writes=(PSK[id(psl[sgi])],),
                            inc=True)
                    wi = (hh * 16 + i128) % 2
                    wt = WT[wi]
                    for sgi in range(5):
                        n = 512 if sgi < 4 else 128
                        if sgi % 2 == 0:
                            kb.op("s", lambda e, sgi=sgi, n=n, wt=wt: e.activation(
                                wt[:, sgi * 512: sgi * 512 + n], psl[sgi][:, 0:n],
                                AF.Copy),
                                reads=(PSK[id(psl[sgi])],), writes=(WTK[wi],),
                                inc=True)
                        else:
                            kb.op("v", lambda e, sgi=sgi, n=n, wt=wt: e.tensor_copy(
                                wt[:, sgi * 512: sgi * 512 + n], psl[sgi][:, 0:n]),
                                reads=(PSK[id(psl[sgi])],), writes=(WTK[wi],),
                                inc=True)
                    kb.dma("s2", bass.AP(self.BDWp[pb],
                                         hh * T * 2304 + i128 * 128 * 2304,
                                         [[2305, 128], [1, 2176]]),
                           wt, reads=(WTK[wi],), writes=(self.BDWp[pb],),
                           stream=f"bdw{pb}")

        def flash(p):
            pb = p % 2
            tiles = [(hh, ib, jt) for hh in range(2) for ib in range(4)
                     for jt in range(16)]

            def issue_bdr(n):
                hh, ib, jt = tiles[n]
                s = n % NBD
                kb.dma("q", BDT[s],
                       bass.AP(self.BDWp[pb],
                               hh * T * 2304 + ib * 512 * 2304 + jt * 128 + 127,
                               [[2304, 512], [1, 128]]),
                       reads=(self.BDWp[pb],), writes=(BDK[s],),
                       stream=f"bdr{s}", transpose=True)

            for n in range(NBD):
                issue_bdr(n)
            for hh in range(2):
                h = 2 * p + hh
                for ib in range(4):
                    i0 = ib * 512
                    for jt in range(16):
                        n = (hh * 4 + ib) * 16 + jt
                        sl = n % NBD
                        bdt = BDT[sl]
                        ps = P5 if jt % 2 == 0 else P7
                        kb.op("t", lambda e, jt=jt, i0=i0, hh=hh, ps=ps: e.matmul(
                            ps[:], KHBp(pb)[64 * hh:64 * hh + 64,
                                            jt * 128:(jt + 1) * 128],
                            QHBp(pb)[64 * hh:64 * hh + 64, i0:i0 + 512],
                            start=True, stop=False),
                            reads=(PRK[pb],), writes=(PSK[id(ps)],))
                        kb.op("t", lambda e, ps=ps, bdt=bdt: e.matmul(
                            ps[:], self.IDB[:], bdt, start=False, stop=True),
                            reads=(self.IDB, BDK[sl]),
                            writes=(PSK[id(ps)],), inc=True)
                        if n + NBD < 128:
                            issue_bdr(n + NBD)
                        pt = (self.SBbf0 if jt % 2 == 0 else self.SBbf1)[:, 0:512]
                        kb.op("s", lambda e, pt=pt, ps=ps: e.activation(
                            pt, ps[:], AF.Exp, scale=0.125),
                            reads=(PSK[id(ps)],), writes=(PTK[jt % 2],), inc=True)
                        vt = QT[:, vt_off(pb, hh, jt): vt_off(pb, hh, jt) + 65]
                        kb.op("t", lambda e, jt=jt, pt=pt, vt=vt: e.matmul(
                            P6[0:65, :], vt, pt, start=(jt == 0), stop=(jt == 15)),
                            reads=(PRK[pb], PTK[jt % 2]), writes=(PSK[id(P6)],),
                            inc=(jt == 15))
                    kb.op("v", lambda e: e.reciprocal(self.NRM[64:65, :],
                                                      P6[64:65, :]),
                          reads=(PSK[id(P6)],), writes=(NRK,), inc=True)
                    kb.op("t", lambda e: e.matmul(
                        P7[0:64, :], self.ONER[64:65, 0:64], self.NRM[64:65, :],
                        start=True, stop=True),
                        reads=(NRK, self.ONER), writes=(PSK[id(P7)],), inc=True)
                    kb.op("s", lambda e: e.activation(self.NRM[0:64, :],
                                                      P7[0:64, :], AF.Copy),
                          reads=(PSK[id(P7)],), writes=(NRK,), inc=True)
                    op = (hh * 4 + ib) % 2
                    oh2 = OH2b[op]
                    kb.op("v", lambda e, oh2=oh2: e.tensor_tensor(
                        oh2[0:64, 0:512], P6[0:64, :], self.NRM[0:64, :], ALU.mult),
                        reads=(PSK[id(P6)], NRK), writes=(OHK[op],), inc=True)
                    kb.dma("q", bass.AP(self.OHS, h * 64 * T + i0,
                                        [[T, 64], [1, 512]]),
                           oh2[0:64, 0:512], reads=(OHK[op],), writes=(OHSK,),
                           stream="ohs")

        loadW(0)
        qkv(0)
        bd(0)
        for p in range(8):
            if p < 7:
                loadW(p + 1)
                qkv(p + 1)
                bd(p + 1)
            flash(p)
        kb.barrier()

        OTK = (RK("ot0"), RK("ot1"))
        XSK = (RK("xs0"), RK("xs1"))
        for tb in range(NTB):
            for seg in range(2):
                pso = PA if seg == 0 else PB
                xs = QTF[:, 256 + seg * 512: 768 + seg * 512]
                kb.dma("s2", xs, bass.AP(self.X3S, tb * 128 * D + seg * 512,
                                         [[D, 128], [1, 512]]),
                       reads=(self.X3S,), writes=(XSK[seg],), stream=f"x3r{seg}")
                for c in range(8):
                    ot = QT[:, 14592 + (c % 2) * 128: 14592 + (c % 2) * 128 + 128]
                    kb.dma("q", ot, bass.AP(self.OHS, c * 128 * T + tb * 128,
                                            [[T, 128], [1, 128]]),
                           reads=(self.OHS,), writes=(OTK[c % 2],),
                           stream=f"ohr{c % 2}")
                    wt = self.WB0[:, c * 1024 + seg * 512: c * 1024 + seg * 512 + 512]
                    kb.op("t", lambda e, ot=ot, wt=wt, c=c, pso=pso: e.matmul(
                        pso[:], ot, wt, start=(c == 0), stop=(c == 7)),
                        reads=(OTK[c % 2], self.WB0), writes=(PSK[id(pso)],),
                        inc=(c == 7))
                kb.op("v", lambda e, tb=tb, seg=seg, xs=xs, pso=pso:
                      e.scalar_tensor_tensor(
                          XRES[:, tb * D + seg * 512: tb * D + seg * 512 + 512],
                          xs, 2.0, pso[:], ALU.mult, ALU.add),
                      reads=(XSK[seg], PSK[id(pso)]), writes=(XRES,), inc=True)

        self.xn_materialize(16, 24)

        for ic in range(NFC):
            kb.dma("g2", self.SBbf0[:, ic * 32:(ic + 1) * 32],
                   bass.AP(self.selw, ic * 128 * 32, [[32, 128], [1, 32]]),
                   writes=(self.SBbf0,), stream="whead")
        for tb in range(NTB):
            for ic in range(NFC):
                l = XF[:, ic * T + tb * 128: ic * T + tb * 128 + 128]
                r = self.SBbf0[:, ic * 32:(ic + 1) * 32]
                kb.op("t", lambda e, l=l, r=r, ic=ic: e.matmul(
                    PB[:, 0:32], l, r, start=(ic == 0), stop=(ic == NFC - 1)),
                    reads=(XF, self.SBbf0), writes=(PB,), inc=(ic == NFC - 1))
            sel = self.AF2[:, 0:32]
            kb.op("s", lambda e, sel=sel: e.activation(sel, PB[:, 0:32], AF.Sigmoid),
                  reads=(PB,), writes=(self.AF2,), inc=True)
            m1, m4 = COLS[:, 74:75], COLS[:, 77:78]
            m2, m3 = COLS[:, 75:76], COLS[:, 76:77]
            tmsk = self.AF2[:, 32:64]
            tt_ = self.AF2[:, 64:96]
            kb.op("v", lambda e: e.reduce_max(m1, sel, axis=AX.X),
                  reads=(self.AF2,), writes=(COLS,))
            cur = sel
            for mi, mnext in ((m1, m2), (m2, m3), (m3, m4)):
                kb.op("v", lambda e, cur=cur, mi=mi: e.tensor_scalar(
                    tmsk, cur, mi, None, ALU.is_ge),
                    reads=(self.AF2, COLS), writes=(self.AF2,))
                kb.op("v", lambda e, cur=cur: e.scalar_tensor_tensor(
                    tt_, tmsk, -1e30, cur, ALU.mult, ALU.add),
                    reads=(self.AF2,), writes=(self.AF2,))
                kb.op("v", lambda e, mnext=mnext: e.reduce_max(mnext, tt_, axis=AX.X),
                      reads=(self.AF2,), writes=(COLS,))
                cur = tt_
            kb.op("v", lambda e: e.tensor_scalar(tmsk, sel, m4, None, ALU.is_ge),
                  reads=(self.AF2, COLS), writes=(self.AF2,))
            kb.op("v", lambda e: e.tensor_tensor(tmsk, tmsk, sel, ALU.mult),
                  reads=(self.AF2,), writes=(self.AF2,))
            kb.op("t", lambda e: e.matmul(self.PTF[0:32, 0:128], tmsk, self.IDF[:],
                                          start=True, stop=True),
                  reads=(self.AF2, self.IDF), writes=(self.P7,), inc=True)
            gt = self.AF2[:, 96:224]
            kb.op("v", lambda e, gt=gt: e.tensor_copy(gt[0:32, :],
                                                      self.PTF[0:32, 0:128]),
                  reads=(self.P7,), writes=(self.AF2,), inc=True)
            kb.dma("q", bass.AP(self.GATES, tb * 128, [[T, 32], [1, 128]]),
                   gt[0:32, :], reads=(self.AF2,), writes=(self.GATES,),
                   stream="gat")

        kb.barrier()
        gr_off = (self.rows(256), self.rows(256))
        WK = (RK("moeW0"), RK("moeW1"))
        HK = (RK("moeH0"), RK("moeH1"))
        SK = (RK("moeS0"), RK("moeS1"))
        GK_ = (RK("moeGR0"), RK("moeGR1"))
        XRK = RK("xresMoe")
        XFK = RK("xfMoe")
        PSK = {id(p): RK(f"ps{i}") for i, p in
               enumerate((PA, PB, PC, PD, P4, P5, P6, P7))}

        def load_group(g):
            buf = g % 2
            for k in range(4):
                e_ = g * 4 + k
                ks_off = buf * 8192 + k * 1024
                vs_off = buf * 8192 + 4096 + k * 1024
                kb.dma("g2", bass.AP(QT, ks_off, [[16384, 128], [128, 8], [1, 128]]),
                       bass.AP(self.keys1, e_ * 1024 * 128,
                               [[128, 128], [128 * 128, 8], [1, 128]]),
                       writes=(WK[buf],), stream=f"moe{buf}")
                kb.dma("g2", QT[:, vs_off:vs_off + 1024],
                       bass.AP(self.vals1, e_ * 128 * 1024, [[1024, 128], [1, 1024]]),
                       writes=(WK[buf],), stream=f"moe{buf}")

        load_group(0)
        for g in range(8):
            buf = g % 2
            if g < 7:
                load_group(g + 1)
            for tq in range(8):
                t0 = tq * 256
                for k in range(4):
                    e_ = g * 4 + k
                    par = k % 2
                    ks = QT[:, buf * 8192 + k * 1024: buf * 8192 + k * 1024 + 1024]
                    vs = QT[:, buf * 8192 + 4096 + k * 1024:
                            buf * 8192 + 4096 + k * 1024 + 1024]
                    pk = PA if par == 0 else P5
                    pg = P6 if par == 0 else P7
                    hrel = self.AF2[:, 0:256] if par == 0 else self.AF2[:, 256:512]
                    hs = self.SBbf0[:, 0:256] if par == 0 else self.SBbf1[:, 0:256]
                    hst = self.SBbf0 if par == 0 else self.SBbf1
                    kb.dma("q", self.row_ap(gr_off[par], 256),
                           bass.AP(self.GATES, e_ * T + t0, [[T, 1], [1, 256]]),
                           reads=(self.GATES,), writes=(GK_[par],),
                           stream=f"mgr{par}")
                    kb.op("t", lambda e, pg=pg, par=par: e.matmul(
                        pg[:, 0:256], self.ONER[0:1, 0:128],
                        self.row_ap(gr_off[par], 256), start=True, stop=True),
                        reads=(GK_[par], self.ONER), writes=(PSK[id(pg)],), inc=True)
                    for ic in range(NFC):
                        l = ks[:, ic * 128:(ic + 1) * 128]
                        r = XF[:, ic * T + t0: ic * T + t0 + 256]
                        kb.op("t", lambda e, l=l, r=r, ic=ic, pk=pk: e.matmul(
                            pk[:, 0:256], l, r, start=(ic == 0), stop=(ic == NFC - 1)),
                            reads=(WK[buf], XFK), writes=(PSK[id(pk)],),
                            inc=(ic == NFC - 1))
                    kb.op("s", lambda e, hrel=hrel, pk=pk: e.activation(
                        hrel, pk[:, 0:256], AF.Relu),
                        reads=(PSK[id(pk)],), writes=(HK[par],), inc=True)
                    kb.op("v", lambda e, hs=hs, hrel=hrel, pg=pg: e.tensor_tensor(
                        hs, hrel, pg[:, 0:256], ALU.mult),
                        reads=(HK[par], PSK[id(pg)]), writes=(SK[par],), inc=True)
                    for tl in range(2):
                        for oseg in range(2):
                            ps = (PB, PC, PD, P4)[tl * 2 + oseg]
                            l = hs[:, tl * 128:(tl + 1) * 128]
                            r = vs[:, oseg * 512:(oseg + 1) * 512]
                            kb.op("t", lambda e, l=l, r=r, ps=ps, k=k: e.matmul(
                                ps[:], l, r, start=(k == 0), stop=(k == 3)),
                                reads=(SK[par], WK[buf]), writes=(PSK[id(ps)],),
                                inc=(k == 3))
                for tl in range(2):
                    tb = (t0 + tl * 128) // 128
                    for oseg in range(2):
                        ps = (PB, PC, PD, P4)[tl * 2 + oseg]
                        xap = XRES[:, tb * D + oseg * 512: tb * D + oseg * 512 + 512]
                        kb.op("v", lambda e, xap=xap, ps=ps: e.tensor_add(
                            xap, xap, ps[:]),
                            reads=(XRK, PSK[id(ps)]), writes=(XRK,), inc=True)
        kb.barrier()

        for tb in range(NTB):
            mcol, rcol = COLS[:, 0:1], COLS[:, 1:2]
            self.ln_stats(XRES[:, tb * D:(tb + 1) * D], XRES, mcol, rcol)
            nb = COLS[:, 3:4]
            kb.op("v", lambda e: e.tensor_tensor(nb, mcol, rcol, ALU.mult),
                  reads=(COLS,), writes=(COLS,))
            kb.op("v", lambda e: e.tensor_scalar(nb, nb, -1.0, None, ALU.mult),
                  reads=(COLS,), writes=(COLS,))
            kb.op("v", lambda e, tb=tb: e.scalar_tensor_tensor(
                XRES[:, tb * D:(tb + 1) * D], XRES[:, tb * D:(tb + 1) * D],
                rcol, nb.broadcast_to([128, D]), ALU.mult, ALU.add),
                reads=(XRES, COLS), writes=(XRES,), inc=True)

    def emit_output(self):
        kb = self.kb
        if DEV_PROBE is not None:
            kb.finish("q")
            return
        for tb in range(NTB):
            kb.dma("q", bass.AP(self.out_t, tb * 128 * D, [[D, 128], [1, D]]),
                   self.XRES[:, tb * D:(tb + 1) * D], reads=(self.XRES,))
        kb.finish("q")

    def build(self):
        self.declare()
        ind = np.zeros((1, 256), dtype=np.float32)
        ind[0, 0:64] = 1.0
        ind[0, 128 + 64:128 + 128] = 1.0
        ind_d = self.nc.inline_tensor(ind, "indrows")
        self._indr_off = self.rows(256)
        self.INDR = self.ROWS
        ctx = self.nc.body("main", valid_engines=[ENG.Pool, ENG.Activation,
                                                  ENG.PE, ENG.DVE, ENG.SP])
        bb = ctx.__enter__()
        self.kb.dma("q", self.row_ap(self._indr_off, 256), ind_d.ap(),
                    writes=(self.ROWS,))
        self._base_rows = self.rows_off
        self.phase_l0a()
        if DEV_STAGE != "l0a":
            self.phase_l0f()
        if DEV_STAGE in ("l1a", "l1f", "full"):
            self.phase_l1a()
        if DEV_STAGE in ("l1f", "full"):
            self.phase_l1f()
        self.emit_output()
        ctx.__exit__(None, None, None)
        return self.nc


def kernel(**inputs):
    b = Builder()
    nc = b.build()
    x = np.ascontiguousarray(inputs["L0_attn_x"])
    in_maps = []
    for c in range(8):
        m = {"L0_attn_x": x[c]}
        for k, v in inputs.items():
            if k != "L0_attn_x":
                m[k] = np.ascontiguousarray(v)
        in_maps.append(m)
    declared = set(b.dram.keys())
    in_maps = [{k: v for k, v in m.items() if k in declared} for m in in_maps]
    res = run_bass_kernel_spmd(nc, in_maps, core_ids=list(range(8)))
    return np.stack([r["out"] for r in res.results], axis=0)


# version 14 (final)
# speedup vs baseline: 1.0454x; 1.0454x over previous
import math
import numpy as np
import ml_dtypes
import concourse.bass as bass
import concourse.mybir as mybir
from concourse.bass_utils import run_bass_kernel_spmd

F32 = mybir.dt.float32
BF16 = mybir.dt.bfloat16
I32 = mybir.dt.int32
AF = mybir.ActivationFunctionType
ALU = mybir.AluOpType
AX = mybir.AxisListType
ENG = mybir.EngineType

T = 2048
D = 1024
NTB = T // 128
NFC = D // 128

DEV_STAGE = "full"
DEV_PROBE = None


def cdiv(a, b):
    return -(-a // b)


class RK:
    __slots__ = ("name",)

    def __init__(self, name):
        self.name = name


class KB:
    def __init__(self, nc):
        self.nc = nc
        self.eng = {"t": nc.tensor, "v": nc.vector, "s": nc.scalar,
                    "g": nc.gpsimd, "q": nc.sync}
        self.dmaeng = {"q": nc.sync, "s2": nc.scalar, "g2": nc.gpsimd}
        self.esem = {"t": "tS", "v": "vS", "s": "sS", "g": "gS"}
        self.sem = {n: nc.alloc_semaphore(n)
                    for n in ("tS", "vS", "sS", "gS", "cc")}
        self.cnt = {k: 0 for k in self.sem}
        self.stream_maxwait = {}
        self.stream_boundary = {}
        self.seq = {e: 0 for e in ("t", "v", "s", "g", "q", "s2", "g2")}
        self.events = {e: [] for e in self.seq}
        self.wr = {}
        self.rd = {}
        self.last_drain = {"v": 0, "s": 0, "g": 0}
        self.hwm = {e: {} for e in self.seq}

    def _event_after(self, ename, seq):
        ev = self.events[ename]
        lo, hi = 0, len(ev)
        while lo < hi:
            mid = (lo + hi) // 2
            if ev[mid][0] >= seq:
                hi = mid
            else:
                lo = mid + 1
        return ev[lo][1:] if lo < len(ev) else None

    def _signal(self, ename):
        sem = self.esem[ename]
        inst = self.eng[ename].drain()
        if ename != "t":
            self.last_drain[ename] = self.seq[ename]
        inst.then_inc(self.sem[sem], 1)
        self.cnt[sem] += 1
        self.seq[ename] += 1
        self.events[ename].append((self.seq[ename], sem, self.cnt[sem]))
        return (sem, self.cnt[sem])

    def dstream(self, name):
        key = "D_" + name
        if key not in self.sem:
            self.sem[key] = self.nc.alloc_semaphore(key)
            self.cnt[key] = 0
            self.stream_maxwait[key] = 0
            self.stream_boundary[key] = 0
        return key

    def _wait(self, consumer, sem, val):
        h = self.hwm[consumer]
        if sem.startswith("D_"):
            self.stream_maxwait[sem] = max(self.stream_maxwait[sem], val)
        if h.get(sem, 0) >= val:
            return
        e = self.dmaeng[consumer] if consumer in self.dmaeng else self.eng[consumer]
        e.wait_ge(self.sem[sem], val)
        h[sem] = val

    def _need(self, consumer, producer, pseq):
        if producer == consumer and producer not in ("q", "s2", "g2"):
            if producer == "t":
                return
            if self.last_drain.get(producer, 0) < pseq:
                self.eng[producer].drain()
                self.seq[producer] += 1
                self.last_drain[producer] = self.seq[producer]
            return
        ev = self._event_after(producer, pseq)
        if ev is None:
            ev = self._signal(producer)
        sem, c = ev
        if sem.startswith("D_"):
            c = self.cnt[sem]
        self._wait(consumer, sem, c)

    def op(self, ename, fn, reads=(), writes=(), inc=False):
        rk = [t.name for t in reads]
        wk = [t.name for t in writes]
        for k in rk:
            if k in self.wr:
                self._need(ename, *self.wr[k])
        for k in wk:
            for ev in self.rd.get(k, ()):
                self._need(ename, *ev)
            if k in self.wr:
                self._need(ename, *self.wr[k])
        inst = fn(self.eng[ename])
        self.seq[ename] += 1
        myseq = self.seq[ename]
        for k in rk:
            self.rd.setdefault(k, []).append((ename, myseq))
        for k in wk:
            self.wr[k] = (ename, myseq)
            self.rd[k] = []
        if inc:
            sem = self.esem[ename]
            inst.then_inc(self.sem[sem], 1)
            self.cnt[sem] += 1
            self.events[ename].append((myseq, sem, self.cnt[sem]))
        return inst

    def dma(self, qname, out_ap, in_ap, reads=(), writes=(), stream="main", **kw):
        sem = self.dstream(("g" if qname == "g2" else "h") + stream)
        phys = {"q": None, "s2": "s", "g2": "g"}[qname]
        if self.stream_maxwait[sem] > self.stream_boundary[sem]:
            self._wait(qname, sem, self.cnt[sem])
            self.stream_boundary[sem] = self.cnt[sem]
        for t in reads:
            k = t.name
            if k in self.wr:
                p, ps = self.wr[k]
                if phys is not None and p == phys:
                    if self.last_drain[p] < ps:
                        self.eng[p].drain()
                        self.seq[p] += 1
                        self.last_drain[p] = self.seq[p]
                else:
                    self._need(qname, p, ps)
        for t in writes:
            k = t.name
            for ev in self.rd.get(k, ()):
                p = ev[0]
                if phys is not None and p == phys:
                    if self.last_drain[p] < ev[1]:
                        self.eng[p].drain()
                        self.seq[p] += 1
                        self.last_drain[p] = self.seq[p]
                else:
                    self._need(qname, *ev)
            if k in self.wr:
                p, ps = self.wr[k]
                if phys is not None and p == phys:
                    if self.last_drain[p] < ps:
                        self.eng[p].drain()
                        self.seq[p] += 1
                        self.last_drain[p] = self.seq[p]
                else:
                    self._need(qname, p, ps)
        e = self.dmaeng[qname]
        inst = e.dma_start(out_ap, in_ap, **kw)
        inst.then_inc(self.sem[sem], 16)
        self.cnt[sem] += 16
        self.seq[qname] += 1
        myseq = self.seq[qname]
        self.events[qname].append((myseq, sem, self.cnt[sem]))
        for t in reads:
            self.rd.setdefault(t.name, []).append((qname, myseq))
        for t in writes:
            self.wr[t.name] = (qname, myseq)
            self.rd[t.name] = []
        return inst

    def sync_read(self, ename, tensors):
        for t in tensors:
            if t.name in self.wr:
                self._need(ename, *self.wr[t.name])

    def finish(self, ename="q"):
        for sname, c in self.cnt.items():
            if c > 0:
                self._wait(ename, sname, c)

    def barrier(self):
        for e in ("t", "v", "s", "g", "q"):
            for sname, c in self.cnt.items():
                if c > 0:
                    self._wait(e, sname, c)
        self.wr.clear()
        self.rd.clear()


class Builder:
    def __init__(self):
        nc = bass.Bass("TRN2", target_bir_lowering=False)
        self.nc = nc
        self.kb = KB(nc)
        self.dram = {}
        self.rows_off = 0

    def din(self, name, shape):
        t = self.nc.dram_tensor(name, list(shape), F32, kind="ExternalInput")
        self.dram[name] = t
        return t

    def rows(self, n):
        o = self.rows_off
        assert o + n <= 5632, (o, n)
        self.rows_off += n
        return o

    def rows_mark(self):
        return self.rows_off

    def rows_reset(self, mark):
        self.rows_off = mark

    def row_ap(self, off, n):
        return bass.AP(self.ROWS, off, [[5632, 1], [1, n]])

    def declare(self):
        nc = self.nc
        self.x_in = self.din("L0_attn_x", (T, D))
        self.g0 = self.din("L0_attn_gamma", (D,))
        self.Wq0 = self.din("L0_attn_Wq", (D, 1024))
        self.Wkv0 = self.din("L0_attn_Wkv", (D, 128))
        self.qs0 = self.din("L0_attn_q_scale", (64,))
        self.ks0 = self.din("L0_attn_k_scale", (64,))
        self.nkv0 = self.din("L0_attn_null_kv", (2, 2, 64))
        self.Wout0 = self.din("L0_attn_Wout", (1024, D))
        self.wg0 = self.din("L0_ffn_w_gate", (3072, 1024))
        self.wu0 = self.din("L0_ffn_w_up", (3072, 1024))
        self.wd0 = self.din("L0_ffn_w_down", (1024, 3072))
        self.wlqkv = self.din("L1_attn_w_lqkv", (512, 1536))
        self.blqkv = self.din("L1_attn_b_lqkv", (1536,))
        self.wgqkv = self.din("L1_attn_w_gqkv", (512, 1536))
        self.bgqkv = self.din("L1_attn_b_gqkv", (1536,))
        self.wcomp = self.din("L1_attn_w_comp", (4096, 1024))
        self.bcomp = self.din("L1_attn_b_comp", (1024,))
        self.wlproj = self.din("L1_attn_w_lproj", (512, 512))
        self.blproj = self.din("L1_attn_b_lproj", (512,))
        self.wgproj = self.din("L1_attn_w_gproj", (512, 512))
        self.bgproj = self.din("L1_attn_b_gproj", (512,))
        self.wgate1 = self.din("L1_attn_w_gate", (1024, 2))
        self.bgate1 = self.din("L1_attn_b_gate", (2,))
        self.wq1 = self.din("L1_ffn_wq", (1024, 1024))
        self.wk1 = self.din("L1_ffn_wk", (1024, 1024))
        self.wv1 = self.din("L1_ffn_wv", (1024, 1024))
        self.wp1 = self.din("L1_ffn_wp", (1024, 1024))
        self.wo1 = self.din("L1_ffn_wo", (1024, 1024))
        self.ln1g = self.din("L1_ffn_ln1_g", (1024,))
        self.ln1b = self.din("L1_ffn_ln1_b", (1024,))
        self.ln2g = self.din("L1_ffn_ln2_g", (1024,))
        self.ln2b = self.din("L1_ffn_ln2_b", (1024,))
        self.selw = self.din("L1_ffn_sel_w", (1024, 32))
        self.keys1 = self.din("L1_ffn_keys", (32, 1024, 128))
        self.vals1 = self.din("L1_ffn_values", (32, 128, 1024))

        self.out_t = nc.dram_tensor("out", [T, D], F32, kind="ExternalOutput")
        if DEV_PROBE is not None:
            self.dbg_t = nc.dram_tensor("dbg", [128, 8192], BF16,
                                        kind="ExternalOutput")
            self.dbgf_t = nc.dram_tensor("dbgf", [128, 2048], F32,
                                         kind="ExternalOutput")
        _spk = "ExternalOutput" if DEV_PROBE in ("ups", "gts") else "Internal"
        self.UPS = nc.dram_tensor("UPS", [3072, T], F32, kind=_spk)
        self.GTS = nc.dram_tensor("GTS", [3072, T], F32, kind=_spk)
        self.WDNT = nc.dram_tensor("WDNT", [3072, 1024], BF16,
                                   kind="ExternalOutput" if DEV_PROBE == "wdnt"
                                   else "Internal")
        self.CCI = nc.dram_tensor("CCI", [1, 8], F32)
        self.CCO = nc.dram_tensor("CCO", [1, 8], F32)
        self.SCR = nc.dram_tensor("SCR", [1, 8], F32)
        self.X3S = nc.dram_tensor("X3S", [T, D], F32)
        self.BDWp = [nc.dram_tensor(f"BDWa{i}", [2 * T * 2304 + 8192], BF16)
                     for i in range(2)]
        self.OHS = nc.dram_tensor("OHS", [1024, T], BF16)
        self.GATES = nc.dram_tensor("GATES", [32, T], F32)

        self.identbf_d = nc.inline_tensor(np.eye(128).astype(ml_dtypes.bfloat16),
                                          "identbf")
        self.identf_d = nc.inline_tensor(np.eye(128, dtype=np.float32), "identf")
        jp = np.arange(128)[:, None]
        ic = np.arange(512)[None, :]
        ml = [(jp <= ic + 2 - 128 * k) for k in range(5)]
        ml.append((jp <= ic + 2) | ((ic < 64) & (jp < 66)))
        ml += [(jp <= ic - 128 * k) for k in range(5)]
        masks_np = np.concatenate([m.astype(ml_dtypes.bfloat16) for m in ml], axis=1)
        self.masks_d = nc.inline_tensor(masks_np, "masksc")
        m_idx = np.arange(4095, dtype=np.float64)
        pos = m_idx - 2047.0
        invf = 1.0 / (10000.0 ** (np.arange(0, 1024, 2, dtype=np.float64) / 1024.0))
        ang = pos[:, None] * invf[None, :]
        pe2 = np.concatenate([np.sin(ang), np.cos(ang)], axis=1)
        pe2T = np.zeros((1024, 4096), dtype=np.float64)
        pe2T[:, 0:4095] = pe2.T
        pe2T_bf = pe2T.astype(np.float32).astype(ml_dtypes.bfloat16)
        pe2_cat = np.concatenate([pe2T_bf[c * 128:(c + 1) * 128] for c in range(8)],
                                 axis=1)
        self.pe2_d = nc.inline_tensor(pe2_cat, "pe2t")
        sh = np.zeros((64, 128), dtype=np.float32)
        sh[np.arange(64), 64 + np.arange(64)] = 1.0
        self.sh64_d = nc.inline_tensor(sh.astype(ml_dtypes.bfloat16), "sh64")
        urow = np.zeros((1, 128), dtype=np.float32)
        urow[0, 64] = 1.0
        A = nc.alloc_sbuf_tensor
        self.XRES = A("XRES", [128, NTB * D], F32)
        self.XF = A("XF", [128, NFC * T], BF16)
        self.QT = A("QT", [128, NFC * T], BF16)
        self.AF2 = A("AF2", [128, 1536], F32)
        self.WB0 = A("WB0", [128, 8192], BF16)
        self.WB1 = A("WB1", [128, 8192], BF16)
        self.SBbf0 = A("SBbf0", [128, 512], BF16)
        self.SBbf1 = A("SBbf1", [128, 512], BF16)
        self.MSK = A("MSK", [128, 6 * 512], BF16)
        self.IDB = A("IDB", [128, 128], BF16)
        self.SH64 = A("SH64", [64, 128], BF16)
        self.IDF = A("IDF", [128, 128], F32)
        self.ROWS = A("ROWS", [1, 5632], F32)
        self.COLS = A("COLS", [128, 96], F32)
        self.ONESF = A("ONESF", [128, 1], F32)
        self.ONESB = A("ONESB", [128, 1], BF16)
        self.GAM = A("GAM", [128, NFC], F32)
        self.SCC = A("SCC", [128, 8], F32)
        self.QS = A("QS", [128, 1], F32)
        self.KS = A("KS", [128, 1], F32)
        self.ONER = A("ONER", [128, 128], F32)
        self.VTL = A("VTL", [128, 4224], BF16)
        self.BL1 = A("BL1", [128, 40], F32)
        P = nc.alloc_psum_tensor
        self.PA = P("PA", [128, 512], F32)
        self.PB = P("PB", [128, 512], F32)
        self.PC = P("PC", [128, 512], F32)
        self.PD = P("PD", [128, 512], F32)
        self.P4 = P("P4", [128, 512], F32)
        self.P5 = P("P5", [128, 512], F32)
        self.P6 = P("P6", [128, 512], F32)
        self.P7 = P("P7", [128, 512], F32)
        self.PTB = self.P7.ap().bitcast(BF16)[:, 0:128]
        self.PTF = self.P7[:, 128:512]
        self.NRM = self.AF2[:, 512:1024]
        self.KVS = self.QT.ap().bitcast(F32)[:, 0:2052]
        self.KH = self.VTL[:, 0:2052]
        self.VT = self.VTL[:, 2052:3174]
        self.AF0 = self.AF2[:, 0:512]
        self.XCB = self.AF2.ap().bitcast(BF16)[:, 2048:3072]
        self.QSQv = self.AF2[:, 1024:1536]

    def load_w_chunks(self, dst, dst_col0, w, K, M, row0=0, col0=0, nsplit=2):
        kb = self.kb
        nk = K // 128
        W = w.shape[1]
        per = cdiv(nk, nsplit)
        for sp in range(nsplit):
            k0, k1 = sp * per, min(nk, (sp + 1) * per)
            if k0 >= k1:
                break
            src = bass.AP(w, (row0 + k0 * 128) * W + col0,
                          [[W, 128], [128 * W, k1 - k0], [1, M]])
            dap = bass.AP(dst, dst_col0 + k0 * M,
                          [[dst.shape[1], 128], [M, k1 - k0], [1, M]])
            kb.dma("g2", dap, src, writes=(dst,))

    def row_rsqrt(self, off, n, floor=1e-24):
        kb = self.kb
        if not hasattr(self, "_rs_scr"):
            self._rs_scr = self.rows(1024)
        so = self._rs_scr
        R = self.ROWS
        for c0 in range(0, n, 512):
            m = min(512, n - c0)
            v_ = self.row_ap(off + c0, m)
            t0 = self.row_ap(so, m)
            t1 = self.row_ap(so + 512, m)
            kb.op("v", lambda e, v_=v_: e.tensor_scalar(v_, v_, floor, None, ALU.max),
                  reads=(R,), writes=(R,))
            kb.op("v", lambda e, v_=v_, t0=t0: e.reciprocal(t0, v_), reads=(R,), writes=(R,))
            kb.op("s", lambda e, t0=t0, t1=t1: e.activation(t1, t0, AF.Sqrt),
                  reads=(R,), writes=(R,))
            kb.op("v", lambda e, t0=t0, t1=t1: e.tensor_tensor(t0, t1, t1, ALU.mult),
                  reads=(R,), writes=(R,))
            kb.op("v", lambda e, t0=t0, v_=v_: e.tensor_tensor(t0, t0, v_, ALU.mult),
                  reads=(R,), writes=(R,))
            kb.op("v", lambda e, t0=t0: e.tensor_scalar(t0, t0, -0.5, 1.5, ALU.mult, ALU.add),
                  reads=(R,), writes=(R,))
            kb.op("v", lambda e, v_=v_, t0=t0, t1=t1: e.tensor_tensor(v_, t1, t0, ALU.mult),
                  reads=(R,), writes=(R,))
        return self.row_ap(off, n)

    def col_rsqrt(self, vcol, out, tmp1, tmp2, eps):
        kb = self.kb
        C = self.COLS
        kb.op("v", lambda e: e.tensor_scalar(out, vcol, eps, None, ALU.add),
              reads=(C,), writes=(C,))
        kb.op("v", lambda e: e.reciprocal(tmp1, out), reads=(C,), writes=(C,))
        kb.op("s", lambda e: e.activation(tmp2, tmp1, AF.Sqrt), reads=(C,), writes=(C,))
        kb.op("v", lambda e: e.tensor_tensor(tmp1, tmp2, tmp2, ALU.mult),
              reads=(C,), writes=(C,))
        kb.op("v", lambda e: e.tensor_tensor(tmp1, tmp1, out, ALU.mult),
              reads=(C,), writes=(C,))
        kb.op("v", lambda e: e.tensor_scalar(tmp1, tmp1, -0.5, 1.5, ALU.mult, ALU.add),
              reads=(C,), writes=(C,))
        kb.op("v", lambda e: e.tensor_tensor(out, tmp2, tmp1, ALU.mult),
              reads=(C,), writes=(C,))

    def ln_stats(self, src_ap, src_t, mcol, rcol, eps=1e-5):
        kb = self.kb
        C = self.COLS
        n = src_ap.shape[1]
        sq = self.AF2[:, 0:512]
        t2 = C[:, 59:60]
        kb.op("v", lambda e: e.reduce_sum(mcol, src_ap, axis=AX.X),
              reads=(src_t,), writes=(C,))
        kb.op("v", lambda e: e.tensor_scalar(mcol, mcol, 1.0 / n, None, ALU.mult),
              reads=(C,), writes=(C,))
        kb.op("v", lambda e: e.memset(rcol, 0.0), writes=(C,))
        for c0 in range(0, n, 512):
            m = min(512, n - c0)
            kb.op("s", lambda e, c0=c0, m=m: e.activation(
                sq[:, 0:m], src_ap[:, c0:c0 + m], AF.Square),
                reads=(src_t,), writes=(self.AF2,))
            kb.op("v", lambda e, m=m: e.reduce_sum(t2, sq[:, 0:m], axis=AX.X),
                  reads=(self.AF2,), writes=(C,))
            kb.op("v", lambda e: e.tensor_add(rcol, rcol, t2),
                  reads=(C,), writes=(C,))
        kb.op("v", lambda e: e.tensor_scalar(rcol, rcol, 1.0 / n, None, ALU.mult),
              reads=(C,), writes=(C,))
        t0 = C[:, 60:61]
        kb.op("v", lambda e: e.tensor_tensor(t0, mcol, mcol, ALU.mult),
              reads=(C,), writes=(C,))
        kb.op("v", lambda e: e.tensor_sub(rcol, rcol, t0), reads=(C,), writes=(C,))
        self.col_rsqrt(rcol, rcol, C[:, 60:61], C[:, 61:62], eps)

    def tm_tile_to_fm(self, src_fn, dst, dst_col_of_fc, tb, nfc=NFC):
        kb = self.kb
        for fc in range(nfc):
            sap = src_fn(fc)
            kb.op("t", lambda e, sap=sap: e.transpose(self.PTB[:], sap, self.IDB[:]),
                  reads=(sap.tensor, self.IDB), writes=(self.PTB,), inc=True)
            c0 = dst_col_of_fc(fc) + tb * 128
            kb.op("s", lambda e, c0=c0: e.activation(dst[:, c0:c0 + 128],
                                                     self.PTB[:], AF.Copy),
                  reads=(self.PTB,), writes=(dst,), inc=True)

    def phase_l0a(self):
        kb, nc = self.kb, self.nc
        XRES, XF, QT = self.XRES, self.XF, self.QT
        ATTC2 = lambda c, hh: XF[64 * hh:64 * hh + 64,
                                 8192 + c * 512: 8192 + (c + 1) * 512]
        ATTTMP = self.SBbf0
        PA, PB, PC, PD = self.PA, self.PB, self.PC, self.PD
        KVS, KH, VT = self.KVS, self.KH, self.VT
        ROWS, COLS = self.ROWS, self.COLS

        for tb in range(NTB):
            kb.dma("q", XRES[:, tb * D:(tb + 1) * D],
                   bass.AP(self.x_in, tb * 128 * D, [[D, 128], [1, D]]),
                   writes=(XRES,))
        kb.dma("q", self.IDB[:], self.identbf_d.ap(), writes=(self.IDB,))
        kb.dma("q", self.IDF[:], self.identf_d.ap(), writes=(self.IDF,))
        kb.dma("q", self.SH64[:], self.sh64_d.ap(), writes=(self.SH64,))
        kb.dma("q", self.MSK[:], bass.AP(self.masks_d, 0, [[11 * 512, 128], [1, 6 * 512]]), writes=(self.MSK,))
        kb.op("v", lambda e: e.memset(self.ONESF[:], 1.0), writes=(self.ONESF,))
        kb.op("v", lambda e: e.memset(self.ONESB[:], 1.0), writes=(self.ONESB,))
        kb.op("v", lambda e: e.memset(self.ONER[:], 1.0), writes=(self.ONER,))
        self.load_w_chunks(self.WB0, 0, self.Wkv0, 1024, 128, nsplit=1)
        kb.dma("q", self.GAM[:], bass.AP(self.g0, 0, [[1, 128], [128, NFC]]),
               writes=(self.GAM,), allow_slow_non_contiguous=True)
        for half in range(2):
            kb.dma("q", self.QS[64 * half:64 * half + 64, :],
                   bass.AP(self.qs0, 0, [[1, 64], [1, 1]]), writes=(self.QS,))
            kb.dma("q", self.KS[64 * half:64 * half + 64, :],
                   bass.AP(self.ks0, 0, [[1, 64], [1, 1]]), writes=(self.KS,))
        kb.op("v", lambda e: e.scalar_tensor_tensor(
            self.QS[:], self.QS[:], 8.0, self.KS[:], ALU.mult, ALU.mult),
            reads=(self.QS, self.KS), writes=(self.QS,))

        for tb in range(NTB):
            kb.op("v", lambda e, tb=tb: e.tensor_copy(
                self.XCB[:], XRES[:, tb * D:(tb + 1) * D]),
                reads=(XRES,), writes=(self.AF2,), inc=True)
            self.tm_tile_to_fm(lambda fc: self.XCB[:, fc * 128:(fc + 1) * 128],
                               XF, lambda fc: fc * T, tb)

        for seg in range(4):
            for dc in range(NFC):
                l = self.WB0[:, dc * 128:(dc + 1) * 128]
                r = XF[:, dc * T + seg * 512: dc * T + seg * 512 + 512]
                kb.op("t", lambda e, l=l, r=r, dc=dc: e.matmul(
                    PA[:], l, r, start=(dc == 0), stop=(dc == NFC - 1)),
                    reads=(self.WB0, XF),
                    writes=(PA,), inc=(dc == NFC - 1))
            kb.op("s", lambda e, seg=seg: e.activation(
                KVS[:, 2 + seg * 512: 2 + seg * 512 + 512], PA[:], AF.Copy),
                reads=(PA,), writes=(KVS,), inc=True)
        kb.dma("q", KVS[0:64, 0:2], bass.AP(self.nkv0, 0, [[1, 64], [64, 2]]),
               writes=(KVS,), allow_slow_non_contiguous=True)
        kb.dma("q", KVS[64:128, 0:2], bass.AP(self.nkv0, 128, [[1, 64], [64, 2]]),
               writes=(KVS,), allow_slow_non_contiguous=True)

        for tt in range(17):
            c0 = tt * 128
            n = min(128, 2050 - c0)
            kb.op("t", lambda e, c0=c0, n=n: e.matmul(
                self.PTF[0:n, 0:64], KVS[64:128, c0:c0 + n], self.IDF[64:128, 64:128],
                start=True, stop=True),
                reads=(KVS, self.IDF), writes=(self.PTF,), inc=True)
            kb.op("s", lambda e, tt=tt, n=n: e.activation(
                VT[0:n, tt * 66: tt * 66 + 64], self.PTF[0:n, 0:64], AF.Copy),
                reads=(self.PTF,), writes=(VT,), inc=True)
            kb.op("v", lambda e, tt=tt, n=n: e.memset(
                VT[0:n, tt * 66 + 64: tt * 66 + 65], 1.0), writes=(VT,))

        kb.op("s", lambda e: e.activation(KVS[64:128, 0:2050], KVS[0:64, 0:2050],
                                          AF.Square),
              reads=(KVS,), writes=(KVS,), inc=True)
        _rmark = self.rows_mark()
        k2o = self.rows(2050)
        for seg in range(5):
            c0 = seg * 512
            c1 = min(2050, c0 + 512)
            kb.op("t", lambda e, c0=c0, c1=c1: e.matmul(
                PB[0:1, 0:c1 - c0], self.ONESF[64:128, :], KVS[64:128, c0:c1],
                start=True, stop=True),
                reads=(self.ONESF, KVS), writes=(PB,), inc=True)
            kb.op("v", lambda e, c0=c0, c1=c1: e.tensor_copy(
                self.row_ap(k2o + c0, c1 - c0), PB[0:1, 0:c1 - c0]),
                reads=(PB,), writes=(ROWS,), inc=True)
        kinv = self.row_rsqrt(k2o, 2050)
        for seg in range(5):
            c0 = seg * 512
            c1 = min(2050, c0 + 512)
            kb.op("t", lambda e, c0=c0, c1=c1: e.matmul(
                PB[0:64, 0:c1 - c0], self._ind_row(0)[0:1, 0:64],
                self.row_ap(k2o + c0, c1 - c0), start=True, stop=True),
                reads=(ROWS, self.INDR), writes=(PB,), inc=True)
            kb.op("v", lambda e, c0=c0, c1=c1: e.tensor_tensor(
                KH[0:64, c0:c1], KVS[0:64, c0:c1], PB[0:64, 0:c1 - c0], ALU.mult),
                reads=(KVS, PB), writes=(KH,), inc=True)
            kb.op("t", lambda e, c0=c0, c1=c1: e.matmul(
                PC[:, 0:c1 - c0], self.SH64[:], KH[0:64, c0:c1],
                start=True, stop=True),
                reads=(self.SH64, KH), writes=(PC,), inc=True)
            kb.op("s", lambda e, c0=c0, c1=c1: e.activation(
                KH[64:128, c0:c1], PC[64:128, 0:c1 - c0], AF.Copy),
                reads=(PC,), writes=(KH,), inc=True)

        self.rows_reset(_rmark)
        self._rs_scr_del = self.__dict__.pop('_rs_scr', None)
        self.load_w_chunks(self.WB1, 0, self.Wq0, 1024, 1024, nsplit=4)
        for dc in range(NFC):
            kb.op("s", lambda e, dc=dc: e.activation(
                self.WB1[:, dc * 1024:(dc + 1) * 1024],
                self.WB1[:, dc * 1024:(dc + 1) * 1024], AF.Copy,
                scale=self.GAM[:, dc:dc + 1]),
                reads=(self.WB1, self.GAM), writes=(self.WB1,), inc=True)
        gq_o = self.rows(1024)
        for seg in range(2):
            for dc in range(NFC):
                kb.op("t", lambda e, dc=dc, seg=seg: e.matmul(
                    PB[0:1, :], self.ONESB[:],
                    self.WB1[:, dc * 1024 + seg * 512: dc * 1024 + seg * 512 + 512],
                    start=(dc == 0), stop=(dc == NFC - 1)),
                    reads=(self.ONESB, self.WB1), writes=(PB,), inc=(dc == NFC - 1))
            kb.op("v", lambda e, seg=seg: e.tensor_copy(
                self.row_ap(gq_o + seg * 512, 512), PB[0:1, :]),
                reads=(PB,), writes=(ROWS,), inc=True)

        nmr_o = self.rows(T)
        for tb in range(NTB):
            mcol, rcol = COLS[:, 0:1], COLS[:, 1:2]
            self.ln_stats(XRES[:, tb * D:(tb + 1) * D], XRES, mcol, rcol)
            kb.op("s", lambda e, tb=tb: e.activation(
                self.XCB[:], XRES[:, tb * D:(tb + 1) * D], AF.Copy, scale=rcol),
                reads=(XRES, COLS), writes=(self.AF2,), inc=True)
            self.tm_tile_to_fm(lambda fc: self.XCB[:, fc * 128:(fc + 1) * 128],
                               XF, lambda fc: fc * T, tb)
            nm = COLS[:, 2:3]
            kb.op("v", lambda e: e.tensor_tensor(nm, mcol, rcol, ALU.mult),
                  reads=(COLS,), writes=(COLS,))
            kb.op("v", lambda e: e.tensor_scalar(nm, nm, -1.0, None, ALU.mult),
                  reads=(COLS,), writes=(COLS,))
            kb.op("t", lambda e: e.matmul(self.PTF[0:1, 0:128], nm, self.IDF[:],
                                          start=True, stop=True),
                  reads=(COLS, self.IDF), writes=(self.PTF,), inc=True)
            kb.op("v", lambda e, tb=tb: e.tensor_copy(
                self.row_ap(nmr_o + tb * 128, 128), self.PTF[0:1, 0:128]),
                reads=(self.PTF,), writes=(ROWS,), inc=True)

        QSB = self.AF0
        for dc in range(NFC):
            for seg in range(4):
                t0 = seg * 512
                for fc in range(NFC):
                    l = self.WB1[:, fc * 1024 + dc * 128: fc * 1024 + dc * 128 + 128]
                    r = XF[:, fc * T + t0: fc * T + t0 + 512]
                    kb.op("t", lambda e, l=l, r=r, fc=fc: e.matmul(
                        PA[:], l, r, start=(fc == 0), stop=False),
                        reads=(self.WB1, XF), writes=(PA,))
                kb.op("t", lambda e, dc=dc, t0=t0: e.matmul(
                    PA[:], self.row_ap(gq_o + dc * 128, 128),
                    self.row_ap(nmr_o + t0, 512), start=False, stop=True),
                    reads=(ROWS,), writes=(PA,), inc=True)
                kb.op("s", lambda e: e.activation(QSB[:, 0:512], PA[:], AF.Copy),
                      reads=(PA,), writes=(QSB,), inc=True)
                kb.op("s", lambda e: e.activation(self.QSQv, QSB[:, 0:512],
                                                  AF.Square),
                      reads=(QSB,), writes=(self.AF2,), inc=True)
                ro = self.rows(1024) if (dc == 0 and seg == 0) else self._qrow
                self._qrow = ro
                for hh in range(2):
                    kb.op("t", lambda e, hh=hh: e.matmul(
                        PB[0:1, :], self.ONESF[64 * hh:64 * hh + 64, :],
                        self.QSQv[64 * hh:64 * hh + 64, :], start=True, stop=True),
                        reads=(self.ONESF, self.AF2), writes=(PB,), inc=True)
                    kb.op("v", lambda e, hh=hh: e.tensor_copy(
                        self.row_ap(ro + hh * 512, 512), PB[0:1, :]),
                        reads=(PB,), writes=(ROWS,), inc=True)
                for hh in range(2):
                    self.row_rsqrt(ro + hh * 512, 512)
                for hh in range(2):
                    kb.op("t", lambda e, hh=hh, ro=ro: e.matmul(
                        PC[:], self._ind_row(hh), self.row_ap(ro + hh * 512, 512),
                        start=(hh == 0), stop=(hh == 1)),
                        reads=(self.INDR, ROWS),
                        writes=(PC,) if hh == 1 else (PC,), inc=(hh == 1))
                c0 = dc * T + t0
                kb.op("v", lambda e, c0=c0: e.scalar_tensor_tensor(
                    QT[:, c0:c0 + 512], QSB[:, 0:512], self.QS[:], PC[:],
                    ALU.mult, ALU.mult),
                    reads=(QSB, self.QS, PC), writes=(QT,), inc=True)

        WOUT = self.WB1
        self.load_w_chunks(WOUT, 0, self.Wout0, 1024, 1024, nsplit=4)
        for ib in range(4):
            i0 = ib * 512
            for h in range(16):
                dc, hh = h // 2, h % 2
                qap = lambda: QT[64 * hh:64 * hh + 64,
                                 dc * T + i0: dc * T + i0 + 512]
                njt = min(17, 4 * ib + 4 + 1)
                first = True
                for jt in range(njt):
                    jc0 = jt * 128
                    n = min(128, 2050 - jc0)
                    pp = self.P4 if jt % 2 == 0 else self.P5
                    kb.op("t", lambda e, jc0=jc0, n=n, hh=hh, pp=pp: e.matmul(
                        pp[0:n, :], KH[64 * hh:64 * hh + 64, jc0:jc0 + n], qap(),
                        start=True, stop=True),
                        reads=(KH, QT), writes=(pp,), inc=True)
                    pt = self.SBbf0[:, 0:512] if (jt % 2 == 0) else self.SBbf1[:, 0:512]
                    ptt = self.SBbf0 if (jt % 2 == 0) else self.SBbf1
                    kb.op("s", lambda e, pt=pt, n=n, pp=pp: e.activation(
                        pt[0:n, :] if n < 128 else pt, pp[0:n, :], AF.Exp),
                        reads=(pp,), writes=(ptt,), inc=True)
                    kk = jt - 4 * ib
                    if 0 <= kk <= 4 and jt < 17:
                        mi = 5 if (ib == 0 and jt == 0) else kk
                        kb.op("v", lambda e, pt=pt, mi=mi, n=n: e.tensor_tensor(
                            pt[0:n, :] if n < 128 else pt,
                            pt[0:n, :] if n < 128 else pt,
                            self.MSK[0:n, mi * 512:(mi + 1) * 512] if n < 128
                            else self.MSK[:, mi * 512:(mi + 1) * 512], ALU.mult),
                            reads=(ptt, self.MSK), writes=(ptt,), inc=True)
                    kb.op("t", lambda e, jt=jt, n=n, first=first, pt=pt: e.matmul(
                        PB[0:65, :], VT[0:n, jt * 66: jt * 66 + 65],
                        pt[0:n, :] if n < 128 else pt,
                        start=first, stop=(jt == njt - 1)),
                        reads=(VT, ptt), writes=(PB,),
                        inc=(jt == njt - 1))
                    first = False
                kb.op("v", lambda e: e.reciprocal(self.NRM[64:65, :], PB[64:65, :]),
                      reads=(PB,), writes=(self.AF2,), inc=True)
                kb.op("t", lambda e: e.matmul(
                    PC[0:64, :], self.ONER[64:65, 0:64], self.NRM[64:65, :],
                    start=True, stop=True),
                    reads=(self.AF2, self.ONER), writes=(PC,), inc=True)
                kb.op("s", lambda e: e.activation(self.NRM[0:64, :], PC[0:64, :],
                                                  AF.Copy),
                      reads=(PC,), writes=(self.AF2,), inc=True)
                if hh == 0:
                    kb.op("v", lambda e, dc=dc: e.tensor_tensor(
                        ATTC2(dc, 0), PB[0:64, :],
                        self.NRM[0:64, :], ALU.mult),
                        reads=(PB, self.AF2), writes=(XF,), inc=True)
                else:
                    kb.op("v", lambda e: e.tensor_tensor(
                        ATTTMP[:, 0:512][0:64, :], PB[0:64, :],
                        self.NRM[0:64, :], ALU.mult),
                        reads=(PB, self.AF2), writes=(ATTTMP,), inc=True)
                    kb.op("t", lambda e: e.matmul(
                        PC[:], self.SH64[:], ATTTMP[0:64, 0:512],
                        start=True, stop=True),
                        reads=(self.SH64, ATTTMP), writes=(PC,), inc=True)
                    kb.op("s", lambda e, dc=dc: e.activation(
                        ATTC2(dc, 1), PC[64:128, :], AF.Copy),
                        reads=(PC,), writes=(XF,), inc=True)
            for tl in range(4):
                tb = ib * 4 + tl
                for seg in range(2):
                    for c in range(8):
                        l = XF[:, 8192 + c * 512 + tl * 128:
                               8192 + c * 512 + tl * 128 + 128]
                        r = WOUT[:, c * 1024 + seg * 512: c * 1024 + seg * 512 + 512]
                        kb.op("t", lambda e, l=l, r=r, c=c: e.matmul(
                            PD[:], l, r, start=(c == 0), stop=(c == 7)),
                            reads=(XF, WOUT), writes=(PD,), inc=(c == 7))
                    xap = XRES[:, tb * D + seg * 512: tb * D + seg * 512 + 512]
                    kb.op("v", lambda e, xap=xap: e.tensor_add(xap, xap, PD[:]),
                          reads=(XRES, PD), writes=(XRES,), inc=True)
        for tb in range(NTB):
            mcol, rcol = COLS[:, 0:1], COLS[:, 1:2]
            self.ln_stats(XRES[:, tb * D:(tb + 1) * D], XRES, mcol, rcol)
            nb = COLS[:, 3:4]
            kb.op("v", lambda e: e.tensor_tensor(nb, mcol, rcol, ALU.mult),
                  reads=(COLS,), writes=(COLS,))
            kb.op("v", lambda e: e.tensor_scalar(nb, nb, -1.0, None, ALU.mult),
                  reads=(COLS,), writes=(COLS,))
            kb.op("v", lambda e, tb=tb: e.scalar_tensor_tensor(
                XRES[:, tb * D:(tb + 1) * D], XRES[:, tb * D:(tb + 1) * D],
                rcol, nb.broadcast_to([128, D]), ALU.mult, ALU.add),
                reads=(XRES, COLS), writes=(XRES,), inc=True)


    def _ind_row(self, hh):
        return self.row_ap(self._indr_off + hh * 128, 128)


    def allreduce_max_row(self, row_off, n):
        kb = self.kb
        if not hasattr(self, "_cc_stage"):
            self._cc_stage = self.rows(8)
        st = self._cc_stage
        kb.op("v", lambda e: e.memset(self.row_ap(st, 8), 0.0), writes=(self.ROWS,))
        kb.op("v", lambda e: e.tensor_copy(self.row_ap(st, n), self.row_ap(row_off, n)),
              reads=(self.ROWS,), writes=(self.ROWS,))
        kb.dma("q", bass.AP(self.CCI, 0, [[8, 1], [1, 8]]),
               self.row_ap(st, 8), reads=(self.ROWS,), writes=(self.CCI,),
               stream="cc")
        kb.op("g", lambda e: e.collective_compute(
            "AllReduce", ALU.max, replica_groups=[list(range(8))],
            ins=[self.CCI.ap().opt()], outs=[self.CCO.ap().opt()]),
            reads=(self.CCI,), writes=(self.CCO,), inc=True)
        kb.dma("q", self.row_ap(row_off, n), bass.AP(self.CCO, 0, [[8, 1], [1, n]]),
               reads=(self.CCO,), writes=(self.ROWS,), stream="cc")

    def col_max_to_row(self, col_ap, row_off):
        kb = self.kb
        kb.op("t", lambda e: e.matmul(self.PTF[0:1, 0:128], col_ap, self.IDF[:],
                                      start=True, stop=True),
              reads=(self.COLS, self.IDF), writes=(self.P7,), inc=True)
        kb.op("v", lambda e: e.tensor_copy(self.row_ap(row_off + 1, 128),
                                           self.PTF[0:1, 0:128]),
              reads=(self.P7,), writes=(self.ROWS,), inc=True)
        kb.op("v", lambda e: e.reduce_max(self.row_ap(row_off, 1),
                                          self.row_ap(row_off + 1, 128), axis=AX.X),
              reads=(self.ROWS,), writes=(self.ROWS,))

    def bcast_scalar_col(self, scr_idx, col_ap, tgt):
        self.kb.dma("q", col_ap, bass.AP(self.SCR, scr_idx, [[0, 128], [1, 1]]),
                    reads=(self.SCR,), writes=(tgt,), stream="cc")

    def phase_l0f(self):
        kb, nc = self.kb, self.nc
        XRES, XF = self.XRES, self.XF
        PA, PB, PC, PD = self.PA, self.PB, self.PC, self.PD
        P4, P5, P6, P7 = self.P4, self.P5, self.P6, self.P7
        ROWS, COLS = self.ROWS, self.COLS
        QF = self.QT.ap().bitcast(F32)
        QI = self.QT.ap().bitcast(I32)
        QB = self.QT

        self.rows_reset(self._l0f_rmark if hasattr(self, "_l0f_rmark") else self.rows_mark())

        mx = COLS[:, 0:1]
        t = COLS[:, 1:2]
        kb.op("v", lambda e: e.memset(mx, 0.0), writes=(COLS,))
        for tb in range(NTB):
            kb.op("v", lambda e, tb=tb: e.reduce_max(
                t, XRES[:, tb * D:(tb + 1) * D], axis=AX.X,
                apply_absolute_value=True),
                reads=(XRES,), writes=(COLS,))
            kb.op("v", lambda e: e.tensor_max(mx, mx, t), reads=(COLS,), writes=(COLS,))
        ro = self.rows(160)
        self.col_max_to_row(mx, ro)
        self.allreduce_max_row(ro, 1)
        r1 = self.row_ap(ro, 1)
        r2 = self.row_ap(ro + 1, 1)
        kb.op("v", lambda e: e.tensor_scalar(r1, r1, 1.0 / 32767.0, 1e-12,
                                             ALU.mult, ALU.max),
              reads=(ROWS,), writes=(ROWS,))
        kb.op("v", lambda e: e.reciprocal(r2, r1), reads=(ROWS,), writes=(ROWS,))
        kb.dma("q", bass.AP(self.SCR, 0, [[8, 1], [1, 2]]), self.row_ap(ro, 2),
               reads=(ROWS,), writes=(self.SCR,), stream="cc")
        SXc, ISXc = self.SCC[:, 0:1], self.SCC[:, 1:2]
        self.bcast_scalar_col(0, SXc, self.SCC)
        self.bcast_scalar_col(1, ISXc, self.SCC)

        def fround(dst_ap, dst_t, src_ap, src_t, scale, Xi, Xf, Yf):
            sc = (self.SCC,) if isinstance(scale, bass.AP) else ()
            s0 = 1.0 if scale is None else scale
            kb.op("v", lambda e: e.tensor_scalar(Xi, src_ap, s0, 0.5,
                                                 ALU.mult, ALU.add),
                  reads=(src_t,) + sc, writes=(self.QT,))
            kb.op("v", lambda e: e.tensor_copy(Yf, Xi),
                  reads=(self.QT,), writes=(self.QT,))
            if scale is None:
                kb.op("v", lambda e: e.tensor_tensor(Xf, src_ap, Yf, ALU.subtract),
                      reads=(src_t, self.QT), writes=(self.QT,))
            else:
                kb.op("v", lambda e: e.scalar_tensor_tensor(
                    Xf, src_ap, scale, Yf, ALU.mult, ALU.subtract),
                    reads=(src_t, self.QT) + sc, writes=(self.QT,))
            kb.op("v", lambda e: e.tensor_scalar(Xf, Xf, -0.5, None, ALU.is_lt),
                  reads=(self.QT,), writes=(self.QT,))
            kb.op("v", lambda e: e.tensor_tensor(dst_ap, Yf, Xf, ALU.subtract),
                  reads=(self.QT,), writes=(dst_t,), inc=True)
        self._fround = fround

        for tb in range(NTB):
            fround(self.XCB[:], self.AF2, XRES[:, tb * D:(tb + 1) * D], XRES,
                   ISXc, QI[:, 0:1024], QF[:, 0:1024], QF[:, 1024:2048])
            self.tm_tile_to_fm(lambda fc: self.XCB[:, fc * 128:(fc + 1) * 128],
                               XF, lambda fc: fc * T, tb)
        if DEV_PROBE == "xq":
            kb.dma("q", self.dbg_t.ap(), XF[:, 0:8192], reads=(XF,),
                   writes=(self.dbg_t,), stream="probe")

        def dequant_chunk(wraw_ap, wraw_t, wdq_ap, wdq_t, NI):
            ng = NI // 32
            bmx = COLS[:, 0:ng] if ng <= 32 else None
            assert ng <= 32 or NI == 3072
            if NI == 3072:
                bmx = self.SCC
            bmx = COLS[:, 0:ng] if ng <= 32 else self.AF2[:, 0:ng]
            bmx_t = COLS if ng <= 32 else self.AF2
            inv = (COLS[:, 32:32 + ng] if ng <= 32 else self.AF2[:, 96:96 + ng])
            kb.op("v", lambda e: e.reduce_max(
                bmx, wraw_ap.rearrange("p (g k) -> p g k", k=32),
                axis=AX.X, apply_absolute_value=True),
                reads=(wraw_t,), writes=(bmx_t,))
            kb.op("v", lambda e: e.tensor_scalar(bmx, bmx, 1.0 / 7.0, 1e-12,
                                                 ALU.mult, ALU.max),
                  reads=(bmx_t,), writes=(bmx_t,))
            kb.op("v", lambda e: e.reciprocal(inv, bmx), reads=(bmx_t,), writes=(bmx_t,))
            tw = QF[:, 5120:5120 + NI]
            ti = QI[:, 5120 + NI:5120 + 2 * NI]
            tf2 = QF[:, 5120 + 2 * NI:5120 + 3 * NI]
            kb.op("v", lambda e: e.tensor_tensor(
                tw.rearrange("p (g k) -> p g k", k=32),
                wraw_ap.rearrange("p (g k) -> p g k", k=32),
                inv.unsqueeze(2).broadcast_to([128, ng, 32]), ALU.mult),
                reads=(wraw_t, bmx_t), writes=(self.QT,))
            tiF = QF[:, 5120 + NI:5120 + 2 * NI]
            kb.op("v", lambda e: e.tensor_scalar(ti, tw, 1.0, 0.5, ALU.mult, ALU.add),
                  reads=(self.QT,), writes=(self.QT,))
            kb.op("v", lambda e: e.tensor_copy(tf2, ti),
                  reads=(self.QT,), writes=(self.QT,))
            kb.op("v", lambda e: e.tensor_tensor(tiF, tw, tf2, ALU.subtract),
                  reads=(self.QT,), writes=(self.QT,))
            kb.op("v", lambda e: e.tensor_scalar(tiF, tiF, -0.5, None, ALU.is_lt),
                  reads=(self.QT,), writes=(self.QT,))
            kb.op("v", lambda e: e.tensor_tensor(tf2, tf2, tiF, ALU.subtract),
                  reads=(self.QT,), writes=(self.QT,))
            kb.op("v", lambda e: e.tensor_tensor(
                wdq_ap.rearrange("p (g k) -> p g k", k=32),
                tf2.rearrange("p (g k) -> p g k", k=32),
                bmx.unsqueeze(2).broadcast_to([128, ng, 32]), ALU.mult),
                reads=(self.QT, bmx_t), writes=(wdq_t,))

        WF = self.WB0.ap().bitcast(F32)
        kb.barrier()
        STK = [RK(f"st{i}") for i in range(4)]
        mu_c, mg_c = COLS[:, 64:65], COLS[:, 65:66]
        kb.op("v", lambda e: e.memset(COLS[:, 64:66], 0.0), writes=(COLS,))
        for wi, (wsrc, spill, mxc) in enumerate(
                [(self.wu0, self.UPS, mu_c), (self.wg0, self.GTS, mg_c)]):
            for oc in range(24):
                par = oc % 2
                wraw = WF[:, par * 1024:(par + 1) * 1024]
                kb.dma("q", wraw,
                       bass.AP(wsrc, oc * 128 * 1024, [[1024, 128], [1, 1024]]),
                       writes=(self.WB0,), stream="wraw")
                wdq = self.WB0[:, 4096 + par * 1024: 4096 + (par + 1) * 1024]
                dequant_chunk(wraw, self.WB0, wdq, self.WB0, 1024)
                for it in range(NFC):
                    kb.op("t", lambda e, it=it: e.transpose(
                        self.PTB, wdq[:, it * 128:(it + 1) * 128], self.IDB[:]),
                        reads=(self.WB0, self.IDB), writes=(self.P7,), inc=True)
                    kb.op("s", lambda e, it=it, par=par: e.activation(
                        self.WB1[:, par * 1024 + it * 128: par * 1024 + it * 128 + 128],
                        self.PTB, AF.Copy),
                        reads=(self.P7,), writes=(self.WB1,), inc=True)
                for seg in range(4):
                    ps = (PA, PB, PC, PD)[seg]
                    for ic in range(NFC):
                        l = self.WB1[:, par * 1024 + ic * 128: par * 1024 + ic * 128 + 128]
                        r = XF[:, ic * T + seg * 512: ic * T + seg * 512 + 512]
                        kb.op("t", lambda e, l=l, r=r, ic=ic, ps=ps: e.matmul(
                            ps[:], l, r, start=(ic == 0), stop=(ic == NFC - 1)),
                            reads=(self.WB1, XF), writes=(ps,), inc=(ic == NFC - 1))
                    st = QF[:, 3072 + seg * 512: 3072 + (seg + 1) * 512]
                    kb.op("s", lambda e, st=st, ps=ps: e.activation(
                        st, ps[:], AF.Copy, scale=SXc),
                        reads=(ps, self.SCC), writes=(STK[seg],), inc=True)
                    kb.op("v", lambda e, st=st: e.reduce_max(
                        COLS[:, 66:67], st, axis=AX.X, apply_absolute_value=True),
                        reads=(STK[seg],), writes=(COLS,))
                    kb.op("v", lambda e, mxc=mxc: e.tensor_max(mxc, mxc, COLS[:, 66:67]),
                          reads=(COLS,), writes=(COLS,))
                    kb.dma("s2", bass.AP(spill, oc * 128 * T + seg * 512,
                                         [[T, 128], [1, 512]]), st,
                           reads=(STK[seg],), writes=(spill,), stream=f"sp{seg}")
        self.col_max_to_row(mu_c, ro)
        kb.op("v", lambda e: e.tensor_copy(self.row_ap(ro + 140, 1), self.row_ap(ro, 1)),
              reads=(ROWS,), writes=(ROWS,))
        self.col_max_to_row(mg_c, ro)
        kb.op("v", lambda e: e.tensor_copy(self.row_ap(ro + 1, 1), self.row_ap(ro + 140, 1)),
              reads=(ROWS,), writes=(ROWS,))
        self.allreduce_max_row(ro, 2)
        rMg = self.row_ap(ro, 1)
        rMu = self.row_ap(ro + 1, 1)
        CONST_SO = 32767.0 * 65535.0 / 65536.0
        rsg = self.row_ap(ro + 2, 1); rsu = self.row_ap(ro + 3, 1)
        risg = self.row_ap(ro + 4, 1); risu = self.row_ap(ro + 5, 1)
        rso = self.row_ap(ro + 6, 1)
        kb.op("v", lambda e: e.tensor_scalar(rsg, rMg, 1.0 / 32767.0, 1e-12,
                                             ALU.mult, ALU.max), reads=(ROWS,), writes=(ROWS,))
        kb.op("v", lambda e: e.tensor_scalar(rsu, rMu, 1.0 / 32767.0, 1e-12,
                                             ALU.mult, ALU.max), reads=(ROWS,), writes=(ROWS,))
        kb.op("v", lambda e: e.reciprocal(risg, rsg), reads=(ROWS,), writes=(ROWS,))
        kb.op("v", lambda e: e.reciprocal(risu, rsu), reads=(ROWS,), writes=(ROWS,))
        kb.op("v", lambda e: e.tensor_tensor(rso, rsg, rsu, ALU.mult),
              reads=(ROWS,), writes=(ROWS,))
        kb.op("v", lambda e: e.tensor_scalar(rso, rso, CONST_SO, None, ALU.mult),
              reads=(ROWS,), writes=(ROWS,))
        kb.op("v", lambda e: e.tensor_copy(self.row_ap(ro + 8, 1), rsg), reads=(ROWS,), writes=(ROWS,))
        kb.op("v", lambda e: e.tensor_copy(self.row_ap(ro + 9, 1), risg), reads=(ROWS,), writes=(ROWS,))
        kb.op("v", lambda e: e.tensor_copy(self.row_ap(ro + 10, 1), risu), reads=(ROWS,), writes=(ROWS,))
        kb.op("v", lambda e: e.tensor_copy(self.row_ap(ro + 11, 1), rso), reads=(ROWS,), writes=(ROWS,))
        kb.op("v", lambda e: e.tensor_scalar(self.row_ap(ro + 12, 1), rsg,
                                             -32768.0, None, ALU.mult),
              reads=(ROWS,), writes=(ROWS,))
        kb.dma("q", bass.AP(self.SCR, 2, [[8, 1], [1, 5]]), self.row_ap(ro + 8, 5),
               reads=(ROWS,), writes=(self.SCR,), stream="cc")
        SGc, ISGc = self.SCC[:, 2:3], self.SCC[:, 3:4]
        ISUc, SOc, NSGc = self.SCC[:, 4:5], self.SCC[:, 5:6], self.SCC[:, 6:7]
        self.bcast_scalar_col(2, SGc, self.SCC)
        self.bcast_scalar_col(3, ISGc, self.SCC)
        self.bcast_scalar_col(4, ISUc, self.SCC)
        self.bcast_scalar_col(5, SOc, self.SCC)
        self.bcast_scalar_col(6, NSGc, self.SCC)
        if DEV_PROBE == "scc":
            kb.dma("q", bass.AP(self.dbgf_t, 0, [[2048, 128], [1, 7]]),
                   self.SCC[:, 0:7], reads=(self.SCC,), writes=(self.dbgf_t,),
                   stream="probe")
        C1 = 1.0 / 65535.0
        C2 = 1.0 / 32767.0

        for hc in range(8):
            par = hc % 2
            for sub in range(3):
                wraw = WF[:, par * 1024:(par + 1) * 1024]
                kb.dma("q", wraw,
                       bass.AP(self.wd0, hc * 128 * 3072 + sub * 1024,
                               [[3072, 128], [1, 1024]]),
                       writes=(self.WB0,), stream="wraw")
                wdq = self.WB0[:, 4096 + par * 1024: 4096 + (par + 1) * 1024]
                dequant_chunk(wraw, self.WB0, wdq, self.WB0, 1024)
                for it in range(NFC):
                    ic = sub * 8 + it
                    kb.op("t", lambda e, it=it: e.transpose(
                        self.PTB, wdq[:, it * 128:(it + 1) * 128], self.IDB[:]),
                        reads=(self.WB0, self.IDB), writes=(self.P7,), inc=True)
                    dst = self.SBbf0[:, 0:128] if ic % 2 == 0 else self.SBbf1[:, 0:128]
                    dst_t = self.SBbf0 if ic % 2 == 0 else self.SBbf1
                    kb.op("s", lambda e, dst=dst: e.activation(dst, self.PTB, AF.Copy),
                          reads=(self.P7,), writes=(dst_t,), inc=True)
                    kb.dma("s2", bass.AP(self.WDNT, ic * 128 * 1024 + hc * 128,
                                         [[1024, 128], [1, 128]]), dst,
                           reads=(dst_t,), writes=(self.WDNT,), stream="wdnt")
        kb.barrier()
        NGF = 3
        NWS = 4
        GFs = [QF[:, i * 512:(i + 1) * 512] for i in range(NGF)]
        UFs = [QF[:, 1536 + i * 512: 1536 + (i + 1) * 512] for i in range(NGF)]
        GFK = [RK(f"gf{i}") for i in range(NGF)]
        UFK = [RK(f"uf{i}") for i in range(NGF)]
        A_, B_, C_ = QF[:, 3072:3584], QF[:, 3584:4096], QF[:, 4096:4608]
        Xv, XvI = QF[:, 4608:5120], QI[:, 4608:5120]
        Yv = QF[:, 5120:5632]
        RU = QF[:, 5632:6144]
        Xg, XgI = QF[:, 6144:6656], QI[:, 6144:6656]
        Yg = QF[:, 6656:7168]
        OBs = (QB[:, 14336:14848], QB[:, 14848:15360])
        CHK, CHG = RK("chainV"), RK("chainG")
        OBK = (RK("ob0"), RK("ob1"))
        WSK = [RK(f"ws{i}") for i in range(NWS)]
        SCCK, XRK = RK("sccS3"), RK("xresS3")
        psl = (PA, PB, PC, PD, P4, P5, P6, P7)
        PSK3 = {id(p): RK(f"p5k{i}") for i, p in enumerate(psl)}

        def fround3(eng, dst_ap, dst_k, src_ap, src_k, scale, Xi, Xf, Yf, tmp_k):
            sc = (SCCK,) if isinstance(scale, bass.AP) else ()
            s0 = 1.0 if scale is None else scale
            kb.op(eng, lambda e: e.tensor_scalar(Xi, src_ap, s0, 0.5,
                                                 ALU.mult, ALU.add),
                  reads=(src_k,) + sc, writes=(tmp_k,))
            kb.op(eng, lambda e: e.tensor_copy(Yf, Xi),
                  reads=(tmp_k,), writes=(tmp_k,))
            if scale is None:
                kb.op(eng, lambda e: e.tensor_tensor(Xf, src_ap, Yf, ALU.subtract),
                      reads=(src_k, tmp_k), writes=(tmp_k,))
            elif eng == "g":
                Zg = QF[:, 7680:8192]
                kb.op(eng, lambda e: e.tensor_scalar(Zg, src_ap, s0, None,
                                                     ALU.mult),
                      reads=(src_k,), writes=(tmp_k,))
                kb.op(eng, lambda e: e.tensor_tensor(Xf, Zg, Yf, ALU.subtract),
                      reads=(tmp_k,), writes=(tmp_k,))
            else:
                kb.op(eng, lambda e: e.scalar_tensor_tensor(
                    Xf, src_ap, scale, Yf, ALU.mult, ALU.subtract),
                    reads=(src_k, tmp_k) + sc, writes=(tmp_k,))
            kb.op(eng, lambda e: e.tensor_scalar(Xf, Xf, -0.5, None, ALU.is_lt),
                  reads=(tmp_k,), writes=(tmp_k,))
            kb.op(eng, lambda e: e.tensor_tensor(dst_ap, Yf, Xf, ALU.subtract),
                  reads=(tmp_k,), writes=(dst_k,), inc=True)

        def ld_gu(n):
            tq, ic = divmod(n, 24)
            sl = n % NGF
            kb.dma("q", GFs[sl], bass.AP(self.GTS, ic * 128 * T + tq * 512,
                                         [[T, 128], [1, 512]]),
                   reads=(self.GTS,), writes=(GFK[sl],), stream=f"gf{sl}")
            kb.dma("s2", UFs[sl], bass.AP(self.UPS, ic * 128 * T + tq * 512,
                                          [[T, 128], [1, 512]]),
                   reads=(self.UPS,), writes=(UFK[sl],), stream=f"uf{sl}")

        def ld_w(n):
            tq, ic = divmod(n, 24)
            sl = n % NWS
            kb.dma("q", self.WB1[:, sl * 1024:(sl + 1) * 1024],
                   bass.AP(self.WDNT, ic * 128 * 1024, [[1024, 128], [1, 1024]]),
                   reads=(self.WDNT,), writes=(WSK[sl],), stream=f"wds{sl}")

        for n in range(NGF):
            ld_gu(n)
        for n in range(NWS):
            ld_w(n)
        for tq in range(4):
            for ic in range(24):
                n = tq * 24 + ic
                par = ic % 2
                gsl, wsl = n % NGF, n % NWS
                ob = OBs[par]
                fround3("v", A_, CHK, GFs[gsl], GFK[gsl], ISGc, XvI, Xv, Yv, CHK)
                kb.op("s", lambda e: e.activation(C_, A_, AF.Sigmoid, scale=SGc),
                      reads=(CHK, SCCK), writes=(CHK,), inc=True)
                fround3("v", B_, CHG, C_, CHK, 65536.0, XgI, Xg, Yg, CHG)
                fround3("v", RU, CHK, UFs[gsl], UFK[gsl], ISUc, XvI, Xv, Yv, CHK)
                kb.op("v", lambda e: e.tensor_tensor(A_, A_, B_, ALU.mult),
                      reads=(CHK, CHG), writes=(CHK,))
                fround3("v", B_, CHG, A_, CHK, C1, XgI, Xg, Yg, CHG)
                kb.op("v", lambda e: e.tensor_tensor(A_, B_, RU, ALU.mult),
                      reads=(CHK, CHG), writes=(CHK,))
                fround3("v", ob, OBK[par], A_, CHK, C2, XgI, Xg, Yg, CHG)
                for tl in range(4):
                    for hs in range(2):
                        ps = psl[tl * 2 + hs]
                        l = ob[:, tl * 128:(tl + 1) * 128]
                        r = self.WB1[:, wsl * 1024 + hs * 512:
                                     wsl * 1024 + hs * 512 + 512]
                        kb.op("t", lambda e, l=l, r=r, ic=ic, ps=ps: e.matmul(
                            ps[:], l, r, start=(ic == 0), stop=(ic == 23)),
                            reads=(OBK[par], WSK[wsl]), writes=(PSK3[id(ps)],),
                            inc=(ic == 23))
                if n + NGF < 96:
                    ld_gu(n + NGF)
                if n + NWS < 96:
                    ld_w(n + NWS)
            for tl in range(4):
                tb = tq * 4 + tl
                for hs in range(2):
                    ps = psl[tl * 2 + hs]
                    xap = XRES[:, tb * D + hs * 512: tb * D + hs * 512 + 512]
                    kb.op("v", lambda e, xap=xap, ps=ps: e.scalar_tensor_tensor(
                        xap, ps[:], SOc, xap, ALU.mult, ALU.add),
                        reads=(XRK, PSK3[id(ps)], SCCK), writes=(XRK,), inc=True)
        kb.barrier()
        for tb in range(NTB):
            mcol, rcol = COLS[:, 0:1], COLS[:, 1:2]
            self.ln_stats(XRES[:, tb * D:(tb + 1) * D], XRES, mcol, rcol)
            nb = COLS[:, 3:4]
            kb.op("v", lambda e: e.tensor_tensor(nb, mcol, rcol, ALU.mult),
                  reads=(COLS,), writes=(COLS,))
            kb.op("v", lambda e: e.tensor_scalar(nb, nb, -1.0, None, ALU.mult),
                  reads=(COLS,), writes=(COLS,))
            kb.op("v", lambda e, tb=tb: e.scalar_tensor_tensor(
                XRES[:, tb * D:(tb + 1) * D], XRES[:, tb * D:(tb + 1) * D],
                rcol, nb.broadcast_to([128, D]), ALU.mult, ALU.add),
                reads=(XRES, COLS), writes=(XRES,), inc=True)

    def finish_head(self, av_ps, dst_even_ap, dst_odd_ap, hh, dst_t, n_i=512):
        kb = self.kb
        PC = self.PC
        kb.op("v", lambda e: e.reciprocal(self.NRM[64:65, 0:n_i], av_ps[64:65, 0:n_i]),
              reads=(av_ps,), writes=(self.AF2,), inc=True)
        kb.op("t", lambda e: e.matmul(
            PC[0:64, 0:n_i], self.ONER[64:65, 0:64], self.NRM[64:65, 0:n_i],
            start=True, stop=True),
            reads=(self.AF2, self.ONER), writes=(PC,), inc=True)
        kb.op("s", lambda e: e.activation(self.NRM[0:64, 0:n_i], PC[0:64, 0:n_i],
                                          AF.Copy),
              reads=(PC,), writes=(self.AF2,), inc=True)
        if hh == 0:
            kb.op("v", lambda e: e.tensor_tensor(
                dst_even_ap, av_ps[0:64, 0:n_i], self.NRM[0:64, 0:n_i], ALU.mult),
                reads=(av_ps, self.AF2), writes=(dst_t,), inc=True)
        else:
            tmp = self.SBbf0[0:64, 0:n_i]
            kb.op("v", lambda e: e.tensor_tensor(
                tmp, av_ps[0:64, 0:n_i], self.NRM[0:64, 0:n_i], ALU.mult),
                reads=(av_ps, self.AF2), writes=(self.SBbf0,), inc=True)
            kb.op("t", lambda e: e.matmul(
                PC[:, 0:n_i], self.SH64[:], tmp, start=True, stop=True),
                reads=(self.SH64, self.SBbf0), writes=(PC,), inc=True)
            kb.op("s", lambda e: e.activation(dst_odd_ap, PC[64:128, 0:n_i], AF.Copy),
                  reads=(PC,), writes=(dst_t,), inc=True)

    def phase_l1a(self):
        kb, nc = self.kb, self.nc
        XRES, XF, QT, VTL = self.XRES, self.XF, self.QT, self.VTL
        PA, PB, PC, PD = self.PA, self.PB, self.PC, self.PD
        ROWS, COLS = self.ROWS, self.COLS
        self.rows_reset(self._base_rows)

        kb.dma("q", self.MSK[:, 0:5 * 512],
               bass.AP(self.masks_d, 6 * 512, [[11 * 512, 128], [1, 5 * 512]]),
               writes=(self.MSK,), stream="msk")
        kb.dma("q", self.BL1[:, 0:12], bass.AP(self.blqkv, 0, [[1, 128], [128, 12]]),
               writes=(self.BL1,), allow_slow_non_contiguous=True, stream="msk")
        kb.dma("q", self.BL1[:, 12:24], bass.AP(self.bgqkv, 0, [[1, 128], [128, 12]]),
               writes=(self.BL1,), allow_slow_non_contiguous=True, stream="msk")
        kb.dma("q", self.BL1[:, 24:28], bass.AP(self.bcomp, 512, [[1, 128], [128, 4]]),
               writes=(self.BL1,), allow_slow_non_contiguous=True, stream="msk")
        kb.dma("q", self.BL1[:, 28:32], bass.AP(self.blproj, 0, [[1, 128], [128, 4]]),
               writes=(self.BL1,), allow_slow_non_contiguous=True, stream="msk")
        kb.dma("q", self.BL1[:, 32:36], bass.AP(self.bgproj, 0, [[1, 128], [128, 4]]),
               writes=(self.BL1,), allow_slow_non_contiguous=True, stream="msk")
        blv_o = self.rows(512)
        bgv_o = self.rows(512)
        bg_o = self.rows(2)
        kb.dma("q", self.row_ap(blv_o, 512), bass.AP(self.blqkv, 1024, [[512, 1], [1, 512]]),
               writes=(ROWS,), stream="msk")
        kb.dma("q", self.row_ap(bgv_o, 512), bass.AP(self.bgqkv, 1024, [[512, 1], [1, 512]]),
               writes=(ROWS,), stream="msk")
        kb.dma("q", self.row_ap(bg_o, 2), bass.AP(self.bgate1, 0, [[2, 1], [1, 2]]),
               writes=(ROWS,), stream="msk")
        self.load_w_chunks(self.WB0, 0, self.wlqkv, 512, 1536, nsplit=2)
        self.load_w_chunks(self.WB1, 0, self.wgqkv, 512, 1536, nsplit=2)

        for tb in range(NTB):
            kb.op("v", lambda e, tb=tb: e.tensor_copy(
                self.XCB[:], XRES[:, tb * D:(tb + 1) * D]),
                reads=(XRES,), writes=(self.AF2,), inc=True)
            self.tm_tile_to_fm(lambda fc: self.XCB[:, fc * 128:(fc + 1) * 128],
                               XF, lambda fc: fc * T, tb)

        if DEV_PROBE == "xf":
            kb.dma("q", self.dbg_t.ap(), XF[:, 0:8192], reads=(XF,),
                   writes=(self.dbg_t,), stream="probe")
        CGF = QT[:, 8256:10304]
        for ci in range(4):
            n = 0
            for r in range(4):
                for ic in range(NFC):
                    wt = self.SBbf1[:, 0:128] if n % 2 == 0 else self.SBbf1[:, 128:256]
                    kb.dma("g2", wt, bass.AP(self.wcomp,
                           (r * 1024 + ic * 128) * 1024 + 512 + ci * 128,
                           [[1024, 128], [1, 128]]),
                           writes=(self.SBbf1,), stream="wcmp")
                    rhs = XF[:, ic * T + r: ic * T + r + 2045: 4]
                    kb.op("t", lambda e, wt=wt, rhs=rhs, n=n: e.matmul(
                        PA[:], wt, rhs, start=(n == 0), stop=(n == 31)),
                        reads=(self.SBbf1, XF), writes=(PA,), inc=(n == 31))
                    n += 1
            kb.op("s", lambda e, ci=ci: e.activation(
                CGF[:, ci * 512:(ci + 1) * 512], PA[:], AF.Identity,
                bias=self.BL1[:, 24 + ci:25 + ci]),
                reads=(PA, self.BL1), writes=(QT,), inc=True)

        GQ = lambda c: QT[:, 2048 + c * 512: 2048 + (c + 1) * 512]
        GK = lambda c: QT[:, 4096 + c * 512: 4096 + (c + 1) * 512]
        for oc in range(8):
            for ic in range(4):
                l = self.WB1[:, ic * 1536 + oc * 128: ic * 1536 + oc * 128 + 128]
                r = CGF[:, ic * 512:(ic + 1) * 512]
                kb.op("t", lambda e, l=l, r=r, ic=ic: e.matmul(
                    PB[:], l, r, start=(ic == 0), stop=(ic == 3)),
                    reads=(self.WB1, QT), writes=(PB,), inc=(ic == 3))
            dst = GQ(oc) if oc < 4 else GK(oc - 4)
            kb.op("s", lambda e, dst=dst, oc=oc: e.activation(
                dst, PB[:], AF.Identity, bias=self.BL1[:, 12 + oc:13 + oc]),
                reads=(PB, self.BL1), writes=(QT,), inc=True)
        for tt in range(4):
            for ic in range(4):
                l = CGF[:, ic * 512 + tt * 128: ic * 512 + tt * 128 + 128]
                r = self.WB1[:, ic * 1536 + 1024: ic * 1536 + 1536]
                kb.op("t", lambda e, l=l, r=r, ic=ic: e.matmul(
                    PB[:], l, r, start=(ic == 0), stop=False),
                    reads=(self.WB1, QT), writes=(PB,))
            kb.op("t", lambda e: e.matmul(
                PB[:], self.ONER[0:1, 0:128], self.row_ap(bgv_o, 512),
                start=False, stop=True),
                reads=(self.ONER, ROWS), writes=(PB,), inc=True)
            for h in range(8):
                off = 6144 + tt * 528 + h * 66
                kb.op("s", lambda e, off=off, h=h: e.activation(
                    QT[:, off:off + 64], PB[:, h * 64:(h + 1) * 64], AF.Copy),
                    reads=(PB,), writes=(QT,), inc=True)
                kb.op("v", lambda e, off=off: e.memset(QT[:, off + 64:off + 65], 1.0),
                      writes=(QT,))
        ATTG = lambda c, hh: QT[64 * hh:64 * hh + 64, c * 512:(c + 1) * 512]
        for h in range(8):
            dc, hh = h // 2, h % 2
            for jt in range(4):
                pp = self.P4 if jt % 2 == 0 else self.P5
                kb.op("t", lambda e, dc=dc, hh=hh, jt=jt, pp=pp: e.matmul(
                    pp[0:128, :], GK(dc)[64 * hh:64 * hh + 64, jt * 128:(jt + 1) * 128],
                    GQ(dc)[64 * hh:64 * hh + 64, :], start=True, stop=True),
                    reads=(QT,), writes=(pp,), inc=True)
                pt = self.SBbf1[:, 0:512] if jt % 2 == 0 else self.SBbf0[:, 0:512]
                ptt = self.SBbf1 if jt % 2 == 0 else self.SBbf0
                kb.op("s", lambda e, pt=pt, pp=pp: e.activation(pt, pp[:], AF.Exp, scale=0.125),
                      reads=(pp,), writes=(ptt,), inc=True)
                kb.op("v", lambda e, pt=pt, jt=jt: e.tensor_tensor(
                    pt, pt, self.MSK[:, jt * 512:(jt + 1) * 512], ALU.mult),
                    reads=(ptt, self.MSK), writes=(ptt,), inc=True)
                kb.op("t", lambda e, jt=jt, h=h, pt=pt: e.matmul(
                    PB[0:65, :], QT[:, 6144 + jt * 528 + h * 66: 6144 + jt * 528 + h * 66 + 65],
                    pt, start=(jt == 0), stop=(jt == 3)),
                    reads=(QT, ptt), writes=(PB,), inc=(jt == 3))
            self.finish_head(PB, ATTG(dc, 0), ATTG(dc, 1), hh, QT)
        GOF = self.WB1[:, 6144:8192]
        for oc in range(4):
            for ic in range(4):
                l_ap = bass.AP(self.wgproj, ic * 128 * 512 + oc * 128,
                               [[512, 128], [1, 128]])
                wt = self.SBbf1[:, 0:128] if ic % 2 == 0 else self.SBbf1[:, 128:256]
                kb.dma("g2", wt, l_ap, writes=(self.SBbf1,), stream="wcmp")
                r = QT[:, ic * 512:(ic + 1) * 512]
                kb.op("t", lambda e, wt=wt, r=r, ic=ic: e.matmul(
                    PB[:], wt, r, start=(ic == 0), stop=(ic == 3)),
                    reads=(self.SBbf1, QT), writes=(PB,), inc=(ic == 3))
            kb.op("s", lambda e, oc=oc: e.activation(
                GOF[:, oc * 512:(oc + 1) * 512], PB[:], AF.Identity,
                bias=self.BL1[:, 32 + oc:33 + oc]),
                reads=(PB, self.BL1), writes=(self.WB1,), inc=True)

        for oc in range(8):
            for seg in range(4):
                for ic in range(4):
                    l = self.WB0[:, ic * 1536 + oc * 128: ic * 1536 + oc * 128 + 128]
                    r = XF[:, ic * T + seg * 512: ic * T + seg * 512 + 512]
                    kb.op("t", lambda e, l=l, r=r, ic=ic: e.matmul(
                        PA[:], l, r, start=(ic == 0), stop=(ic == 3)),
                        reads=(self.WB0, XF), writes=(PA,), inc=(ic == 3))
                base = (oc % 4) * 2048 + (0 if oc < 4 else 8192)
                kb.op("s", lambda e, base=base, seg=seg, oc=oc: e.activation(
                    QT[:, base + seg * 512: base + seg * 512 + 512], PA[:], AF.Identity,
                    bias=self.BL1[:, oc:oc + 1]),
                    reads=(PA, self.BL1), writes=(QT,), inc=True)
        LQ = lambda c: QT[:, c * 2048: (c + 1) * 2048]
        LK = lambda c: QT[:, 8192 + c * 2048: 8192 + (c + 1) * 2048]
        ATTC2L = lambda ib, c, hh: XF[64 * hh:64 * hh + 64,
                                      8192 + ib * 2048 + c * 512:
                                      8192 + ib * 2048 + (c + 1) * 512]
        for hg in range(2):
            for tt in range(16):
                for ic in range(4):
                    l = XF[:, ic * T + tt * 128: ic * T + tt * 128 + 128]
                    r = self.WB0[:, ic * 1536 + 1024 + hg * 256:
                                  ic * 1536 + 1024 + hg * 256 + 256]
                    kb.op("t", lambda e, l=l, r=r, ic=ic: e.matmul(
                        PB[:, 0:256], l, r, start=(ic == 0), stop=False),
                        reads=(self.WB0, XF), writes=(PB,))
                kb.op("t", lambda e, hg=hg: e.matmul(
                    PB[:, 0:256], self.ONER[0:1, 0:128],
                    self.row_ap(blv_o + hg * 256, 256),
                    start=False, stop=True),
                    reads=(self.ONER, ROWS), writes=(PB,), inc=True)
                for hl in range(4):
                    off = tt * 264 + hl * 66
                    kb.op("s", lambda e, off=off, hl=hl: e.activation(
                        VTL[:, off:off + 64], PB[:, hl * 64:(hl + 1) * 64], AF.Copy),
                        reads=(PB,), writes=(VTL,), inc=True)
                    kb.op("v", lambda e, off=off: e.memset(VTL[:, off + 64:off + 65], 1.0),
                          writes=(VTL,))
            for ib in range(4):
                i0 = ib * 512
                for hl in range(4):
                    h = hg * 4 + hl
                    dc, hh = h // 2, h % 2
                    njt = 4 * ib + 4
                    for jt in range(njt):
                        pp = self.P4 if jt % 2 == 0 else self.P5
                        kb.op("t", lambda e, dc=dc, hh=hh, jt=jt, i0=i0, pp=pp: e.matmul(
                            pp[:], LK(dc)[64 * hh:64 * hh + 64, jt * 128:(jt + 1) * 128],
                            LQ(dc)[64 * hh:64 * hh + 64, i0:i0 + 512],
                            start=True, stop=True),
                            reads=(QT,), writes=(pp,), inc=True)
                        pt = self.SBbf1[:, 0:512] if jt % 2 == 0 else self.SBbf0[:, 0:512]
                        ptt = self.SBbf1 if jt % 2 == 0 else self.SBbf0
                        kb.op("s", lambda e, pt=pt, pp=pp: e.activation(pt, pp[:], AF.Exp,
                                                                 scale=0.125),
                              reads=(pp,), writes=(ptt,), inc=True)
                        kk = jt - 4 * ib
                        if 0 <= kk <= 4:
                            kb.op("v", lambda e, pt=pt, kk=kk: e.tensor_tensor(
                                pt, pt, self.MSK[:, kk * 512:(kk + 1) * 512], ALU.mult),
                                reads=(ptt, self.MSK), writes=(ptt,), inc=True)
                        kb.op("t", lambda e, jt=jt, hl=hl, pt=pt, njt=njt: e.matmul(
                            PB[0:65, :], VTL[:, jt * 264 + hl * 66: jt * 264 + hl * 66 + 65],
                            pt, start=(jt == 0), stop=(jt == njt - 1)),
                            reads=(VTL, ptt), writes=(PB,), inc=(jt == njt - 1))
                    self.finish_head(PB, ATTC2L(ib, dc, 0), ATTC2L(ib, dc, 1), hh, XF)
        for ib in range(4):
            i0 = ib * 512
            for oc in range(4):
                for ci in range(4):
                    l_ap = bass.AP(self.wlproj, ci * 128 * 512 + oc * 128,
                                   [[512, 128], [1, 128]])
                    wt = self.SBbf1[:, 0:128] if ci % 2 == 0 else self.SBbf1[:, 128:256]
                    kb.dma("g2", wt, l_ap, writes=(self.SBbf1,), stream="wcmp")
                    r = XF[:, 8192 + ib * 2048 + ci * 512: 8192 + ib * 2048 + (ci + 1) * 512]
                    kb.op("t", lambda e, wt=wt, r=r, ci=ci: e.matmul(
                        PB[:], wt, r, start=(ci == 0), stop=(ci == 3)),
                        reads=(self.SBbf1, XF), writes=(PB,), inc=(ci == 3))
                kb.op("s", lambda e, oc=oc, i0=i0: e.activation(
                    XF[:, oc * 2048 + i0: oc * 2048 + i0 + 512], PB[:], AF.Identity,
                    bias=self.BL1[:, 28 + oc:29 + oc]),
                    reads=(PB, self.BL1), writes=(XF,), inc=True)

        if DEV_PROBE == "lo":
            kb.dma("q", self.dbg_t.ap(), XF[:, 0:8192], reads=(XF,),
                   writes=(self.dbg_t,), stream="probe")
        if DEV_PROBE == "go":
            kb.dma("q", bass.AP(self.dbg_t, 0, [[8192, 128], [1, 2048]]),
                   GOF[:], reads=(self.WB1,), writes=(self.dbg_t,), stream="probe")
        for ci in range(4):
            src = GOF[:, ci * 512:(ci + 1) * 512]
            dst = QT[:, ci * 2048:(ci + 1) * 2048]
            kb.op("v", lambda e, src=src, dst=dst: e.tensor_copy(
                dst.rearrange("p (a b) -> p a b", b=4),
                src.unsqueeze(2).broadcast_to([128, 512, 4])),
                reads=(self.WB1,), writes=(QT,), inc=True)

        dbr = self.rows(2)
        kb.op("v", lambda e: e.tensor_sub(self.row_ap(dbr, 1), self.row_ap(bg_o, 1),
                                          self.row_ap(bg_o + 1, 1)),
              reads=(ROWS,), writes=(ROWS,))
        kb.dma("q", bass.AP(self.SCR, 7, [[8, 1], [1, 1]]), self.row_ap(dbr, 1),
               reads=(ROWS,), writes=(self.SCR,), stream="cc")
        self.bcast_scalar_col(7, self.SCC[:, 7:8], self.SCC)
        WGT = self.SBbf1[:, 256:288]
        for n in range(8):
            kb.dma("g2", WGT[:, 2 * n:2 * n + 2],
                   bass.AP(self.wgate1, n * 128 * 2, [[2, 128], [1, 2]]),
                   writes=(self.SBbf1,), stream="wcmp")
        for tb in range(NTB):
            n = 0
            for src, srct in ((XF, XF), (QT, QT)):
                for ci in range(4):
                    l = src[:, ci * 2048 + tb * 128: ci * 2048 + tb * 128 + 128]
                    wt = WGT[:, 2 * n:2 * n + 2]
                    kb.op("t", lambda e, l=l, wt=wt, n=n: e.matmul(
                        PD[0:128, 0:2], l, wt, start=(n == 0), stop=(n == 7)),
                        reads=(srct, self.SBbf1), writes=(PD,), inc=(n == 7))
                    n += 1
            g0 = COLS[:, 70:71]
            g1 = COLS[:, 71:72]
            t_ = COLS[:, 72:73]
            kb.op("s", lambda e: e.activation(COLS[:, 78:80], PD[0:128, 0:2], AF.Copy),
                  reads=(PD,), writes=(COLS,), inc=True)
            kb.op("v", lambda e: e.tensor_sub(t_, COLS[:, 78:79], COLS[:, 79:80]),
                  reads=(COLS,), writes=(COLS,), inc=True)
            kb.op("s", lambda e: e.activation(g0, t_, AF.Sigmoid, bias=self.SCC[:, 7:8]),
                  reads=(COLS, self.SCC), writes=(COLS,), inc=True)
            kb.op("v", lambda e: e.tensor_scalar(g1, g0, -1.0, 1.0, ALU.mult, ALU.add),
                  reads=(COLS,), writes=(COLS,))
            if DEV_PROBE == "gate":
                kb.dma("q", bass.AP(self.out_t, tb * 128 * D, [[D, 128], [1, 2]]),
                       COLS[:, 70:72], reads=(COLS,), writes=(self.out_t,),
                       stream="probe")
            for ci in range(8):
                src = XF if ci < 4 else QT
                cc = ci % 4
                sap = src[:, cc * 2048 + tb * 128: cc * 2048 + tb * 128 + 128]
                kb.op("t", lambda e, sap=sap: e.transpose(self.PTB, sap, self.IDB[:]),
                      reads=(src if ci < 4 else QT, self.IDB), writes=(self.P7,), inc=True)
                xap = XRES[:, tb * D + (0 if ci < 4 else 512) + cc * 128:
                           tb * D + (0 if ci < 4 else 512) + cc * 128 + 128]
                gc = g0 if ci < 4 else g1
                kb.op("v", lambda e, xap=xap, gc=gc: e.scalar_tensor_tensor(
                    xap, self.PTB, gc, xap, ALU.mult, ALU.add),
                    reads=(self.P7, COLS, XRES), writes=(XRES,), inc=True)

        for tb in range(NTB):
            mcol, rcol = COLS[:, 0:1], COLS[:, 1:2]
            self.ln_stats(XRES[:, tb * D:(tb + 1) * D], XRES, mcol, rcol)
            nb = COLS[:, 3:4]
            kb.op("v", lambda e: e.tensor_tensor(nb, mcol, rcol, ALU.mult),
                  reads=(COLS,), writes=(COLS,))
            kb.op("v", lambda e: e.tensor_scalar(nb, nb, -1.0, None, ALU.mult),
                  reads=(COLS,), writes=(COLS,))
            kb.op("v", lambda e, tb=tb: e.scalar_tensor_tensor(
                XRES[:, tb * D:(tb + 1) * D], XRES[:, tb * D:(tb + 1) * D],
                rcol, nb.broadcast_to([128, D]), ALU.mult, ALU.add),
                reads=(XRES, COLS), writes=(XRES,), inc=True)

    def xn_materialize(self, gcol_base, bcol_base):
        kb = self.kb
        XRES, XF, COLS = self.XRES, self.XF, self.COLS
        for tb in range(NTB):
            mcol, rcol = COLS[:, 0:1], COLS[:, 1:2]
            self.ln_stats(XRES[:, tb * D:(tb + 1) * D], XRES, mcol, rcol)
            nb = COLS[:, 3:4]
            kb.op("v", lambda e: e.tensor_tensor(nb, mcol, rcol, ALU.mult),
                  reads=(COLS,), writes=(COLS,))
            kb.op("v", lambda e: e.tensor_scalar(nb, nb, -1.0, None, ALU.mult),
                  reads=(COLS,), writes=(COLS,))
            kb.op("s", lambda e, tb=tb: e.activation(
                self.XCB[:], XRES[:, tb * D:(tb + 1) * D], AF.Identity,
                scale=rcol, bias=nb),
                reads=(XRES, COLS), writes=(self.AF2,), inc=True)
            for fc in range(NFC):
                sap = self.XCB[:, fc * 128:(fc + 1) * 128]
                kb.op("t", lambda e, sap=sap: e.transpose(self.PTB, sap, self.IDB[:]),
                      reads=(self.AF2, self.IDB), writes=(self.P7,), inc=True)
                c0 = fc * T + tb * 128
                kb.op("s", lambda e, c0=c0, fc=fc: e.activation(
                    XF[:, c0:c0 + 128], self.PTB, AF.Identity,
                    scale=self.BL1[:, gcol_base + fc:gcol_base + fc + 1],
                    bias=self.BL1[:, bcol_base + fc:bcol_base + fc + 1]),
                    reads=(self.P7, self.BL1), writes=(XF,), inc=True)

    def phase_l1f(self):
        kb, nc = self.kb, self.nc
        XRES, XF, QT = self.XRES, self.XF, self.QT
        PA, PB, PC, PD = self.PA, self.PB, self.PC, self.PD
        P4, P5, P6, P7 = self.P4, self.P5, self.P6, self.P7
        ROWS, COLS = self.ROWS, self.COLS
        self.rows_reset(self._base_rows)
        QTF = self.QT.ap().bitcast(F32)
        XRB = self.XRES.ap().bitcast(BF16)

        for src, base in ((self.ln1g, 40), (self.ln1b, 48),
                          (self.ln2g, 56), (self.ln2b, 64)):
            kb.dma("q", self.BL1[:, base // 8 * 8:base // 8 * 8 + 8] if False else
                   self.BL1[:, base - 40 + 0:base - 40 + 8] if False else
                   self.BL1[:, (base - 40) + 0:(base - 40) + 8],
                   bass.AP(src, 0, [[1, 128], [128, 8]]),
                   writes=(self.BL1,), allow_slow_non_contiguous=True, stream="msk")

        self.xn_materialize(0, 8)

        for tb in range(NTB):
            kb.dma("s2", bass.AP(self.X3S, tb * 128 * D, [[D, 128], [1, D]]),
                   XRES[:, tb * D:(tb + 1) * D], reads=(XRES,), writes=(self.X3S,),
                   stream="x3s")

        self.load_w_chunks(self.WB0, 0, self.wp1, 1024, 1024, nsplit=4)
        psall = (PA, PB, PC, PD, P4, P5, P6, P7)
        for mseg in range(8):
            for ic in range(NFC):
                par = ic % 2
                pesl = self.SBbf0[:, 0:512] if par == 0 else self.SBbf1[:, 0:512]
                peslt = self.SBbf0 if par == 0 else self.SBbf1
                kb.dma("q", pesl, bass.AP(self.pe2_d, ic * 4096 + mseg * 512,
                                          [[32768, 128], [1, 512]]),
                       writes=(peslt,), stream="pe2")
                for pc in range(8):
                    l = self.WB0[:, ic * 1024 + pc * 128: ic * 1024 + pc * 128 + 128]
                    kb.op("t", lambda e, l=l, r=pesl, ic=ic, pc=pc: e.matmul(
                        psall[pc][:], l, r, start=(ic == 0), stop=(ic == NFC - 1)),
                        reads=(self.WB0, peslt), writes=(psall[pc],),
                        inc=(ic == NFC - 1))
            for pc in range(8):
                kb.op("s", lambda e, pc=pc, mseg=mseg: e.activation(
                    XRB[:, pc * 4096 + mseg * 512: pc * 4096 + (mseg + 1) * 512],
                    psall[pc][:], AF.Copy),
                    reads=(psall[pc],), writes=(XRES,), inc=True)

        self.load_w_chunks(self.WB0, 0, self.wo1, 1024, 1024, nsplit=4)
        kb.barrier()
        VTL = self.VTL
        XFK, P2K, NRK = RK("xfH"), RK("p2H"), RK("nrmH")
        PRK = (RK("pair0"), RK("pair1"))
        WHK = (RK("wh0"), RK("wh1"))
        WTK = (RK("wt0"), RK("wt1"))
        OHK = (RK("oh0"), RK("oh1"))
        BDK = tuple(RK(f"bdt{i}") for i in range(6))
        PTK = (RK("pt0"), RK("pt1"))
        OHSK = RK("ohsK")
        PSK = {id(p): RK(f"psl{i}") for i, p in
               enumerate((PA, PB, PC, PD, P4, P5, P6, P7))}
        QHBp = lambda pb: QT[:, pb * 6208: pb * 6208 + 2048]
        KHBp = lambda pb: QT[:, pb * 6208 + 2048: pb * 6208 + 4096]
        vt_off = lambda pb, hh, tt: pb * 6208 + 4096 + hh * 1056 + tt * 66
        WT = (QT[:, 12416:14592], VTL[:, 1536:3712])
        OH2b = (VTL[:, 0:512], VTL[:, 3712:4224])
        A2B = self.AF2.ap().bitcast(BF16)
        BDT = (VTL[:, 512:1024], VTL[:, 1024:1536], A2B[:, 0:512],
               A2B[:, 512:1024], A2B[:, 2048:2560], A2B[:, 2560:3072])
        NBD = len(BDT)

        def loadW(p):
            pb = p % 2
            for wi, wsrc in enumerate((self.wq1, self.wk1, self.wv1)):
                kb.dma("g2",
                       bass.AP(self.WB1, pb * 3072 + wi * 1024,
                               [[8192, 128], [128, 8], [1, 128]]),
                       bass.AP(wsrc, p * 128, [[1024, 128], [128 * 1024, 8],
                                               [1, 128]]),
                       writes=(WHK[pb],), stream=f"wh{pb}")

        def qkv(p):
            pb = p % 2
            WQ = self.WB1[:, pb * 3072: pb * 3072 + 1024]
            WKb = self.WB1[:, pb * 3072 + 1024: pb * 3072 + 2048]
            WV = self.WB1[:, pb * 3072 + 2048: pb * 3072 + 3072]
            for di, dst in enumerate((QHBp(pb), KHBp(pb))):
                Wm = WQ if di == 0 else WKb
                for seg in range(4):
                    pq = PA if seg % 2 == 0 else PB
                    for ic in range(NFC):
                        l = Wm[:, ic * 128:(ic + 1) * 128]
                        r = XF[:, ic * T + seg * 512: ic * T + seg * 512 + 512]
                        kb.op("t", lambda e, l=l, r=r, ic=ic, pq=pq: e.matmul(
                            pq[:], l, r, start=(ic == 0), stop=(ic == NFC - 1)),
                            reads=(WHK[pb], XFK), writes=(PSK[id(pq)],),
                            inc=(ic == NFC - 1))
                    kb.op("s", lambda e, dst=dst, seg=seg, pq=pq: e.activation(
                        dst[:, seg * 512:(seg + 1) * 512], pq[:], AF.Copy),
                        reads=(PSK[id(pq)],), writes=(PRK[pb],), inc=True)
            for tt in range(NTB):
                pv = PC if tt % 2 == 0 else PD
                for ic in range(NFC):
                    l = XF[:, ic * T + tt * 128: ic * T + tt * 128 + 128]
                    r = WV[:, ic * 128:(ic + 1) * 128]
                    kb.op("t", lambda e, l=l, r=r, ic=ic, pv=pv: e.matmul(
                        pv[:, 0:128], l, r, start=(ic == 0), stop=(ic == NFC - 1)),
                        reads=(WHK[pb], XFK), writes=(PSK[id(pv)],),
                        inc=(ic == NFC - 1))
                for hh in range(2):
                    off = vt_off(pb, hh, tt)
                    eng_ = "s" if hh == 0 else "v"
                    if hh == 0:
                        kb.op("s", lambda e, off=off, pv=pv: e.activation(
                            QT[:, off:off + 64], pv[:, 0:64], AF.Copy),
                            reads=(PSK[id(pv)],), writes=(PRK[pb],), inc=True)
                    else:
                        kb.op("v", lambda e, off=off, pv=pv: e.tensor_copy(
                            QT[:, off:off + 64], pv[:, 64:128]),
                            reads=(PSK[id(pv)],), writes=(PRK[pb],), inc=True)
                    kb.op("v", lambda e, off=off: e.memset(
                        QT[:, off + 64:off + 65], 1.0), writes=(PRK[pb],))

        def bd(p):
            pb = p % 2
            psl = (PA, PB, PC, PD, P4)
            for hh in range(2):
                for i128 in range(16):
                    base = 1920 - 128 * i128
                    lq = QHBp(pb)[64 * hh:64 * hh + 64, i128 * 128:(i128 + 1) * 128]
                    for sgi in range(5):
                        n = 512 if sgi < 4 else 128
                        rp = XRB[64 * hh:64 * hh + 64,
                                 p * 4096 + base + sgi * 512:
                                 p * 4096 + base + sgi * 512 + n]
                        kb.op("t", lambda e, lq=lq, rp=rp, n=n, sgi=sgi: e.matmul(
                            psl[sgi][:, 0:n], lq, rp, start=True, stop=True),
                            reads=(PRK[pb], P2K), writes=(PSK[id(psl[sgi])],),
                            inc=True)
                    wi = (hh * 16 + i128) % 2
                    wt = WT[wi]
                    for sgi in range(5):
                        n = 512 if sgi < 4 else 128
                        if sgi % 2 == 0:
                            kb.op("s", lambda e, sgi=sgi, n=n, wt=wt: e.activation(
                                wt[:, sgi * 512: sgi * 512 + n], psl[sgi][:, 0:n],
                                AF.Copy),
                                reads=(PSK[id(psl[sgi])],), writes=(WTK[wi],),
                                inc=True)
                        else:
                            kb.op("v", lambda e, sgi=sgi, n=n, wt=wt: e.tensor_copy(
                                wt[:, sgi * 512: sgi * 512 + n], psl[sgi][:, 0:n]),
                                reads=(PSK[id(psl[sgi])],), writes=(WTK[wi],),
                                inc=True)
                    kb.dma("s2", bass.AP(self.BDWp[pb],
                                         hh * T * 2304 + i128 * 128 * 2304,
                                         [[2305, 128], [1, 2176]]),
                           wt, reads=(WTK[wi],), writes=(self.BDWp[pb],),
                           stream=f"bdw{pb}{wi}")

        def flash(p):
            pb = p % 2
            tiles = [(hh, ib, jt) for hh in range(2) for ib in range(4)
                     for jt in range(16)]

            def issue_bdr(n):
                hh, ib, jt = tiles[n]
                s = n % NBD
                kb.dma("q", BDT[s],
                       bass.AP(self.BDWp[pb],
                               hh * T * 2304 + ib * 512 * 2304 + jt * 128 + 127,
                               [[2304, 512], [1, 128]]),
                       reads=(self.BDWp[pb],), writes=(BDK[s],),
                       stream=f"bdr{s}", transpose=True)

            for n in range(NBD):
                issue_bdr(n)
            for hh in range(2):
                h = 2 * p + hh
                for ib in range(4):
                    i0 = ib * 512
                    for jt in range(16):
                        n = (hh * 4 + ib) * 16 + jt
                        sl = n % NBD
                        bdt = BDT[sl]
                        ps = P5 if jt % 2 == 0 else P7
                        kb.op("t", lambda e, jt=jt, i0=i0, hh=hh, ps=ps: e.matmul(
                            ps[:], KHBp(pb)[64 * hh:64 * hh + 64,
                                            jt * 128:(jt + 1) * 128],
                            QHBp(pb)[64 * hh:64 * hh + 64, i0:i0 + 512],
                            start=True, stop=False),
                            reads=(PRK[pb],), writes=(PSK[id(ps)],))
                        kb.op("t", lambda e, ps=ps, bdt=bdt: e.matmul(
                            ps[:], self.IDB[:], bdt, start=False, stop=True),
                            reads=(self.IDB, BDK[sl]),
                            writes=(PSK[id(ps)],), inc=True)
                        if n + NBD < 128:
                            issue_bdr(n + NBD)
                        pt = (self.SBbf0 if jt % 2 == 0 else self.SBbf1)[:, 0:512]
                        kb.op("s", lambda e, pt=pt, ps=ps: e.activation(
                            pt, ps[:], AF.Exp, scale=0.125),
                            reads=(PSK[id(ps)],), writes=(PTK[jt % 2],), inc=True)
                        vt = QT[:, vt_off(pb, hh, jt): vt_off(pb, hh, jt) + 65]
                        kb.op("t", lambda e, jt=jt, pt=pt, vt=vt: e.matmul(
                            P6[0:65, :], vt, pt, start=(jt == 0), stop=(jt == 15)),
                            reads=(PRK[pb], PTK[jt % 2]), writes=(PSK[id(P6)],),
                            inc=(jt == 15))
                    kb.op("v", lambda e: e.reciprocal(self.NRM[64:65, :],
                                                      P6[64:65, :]),
                          reads=(PSK[id(P6)],), writes=(NRK,), inc=True)
                    kb.op("t", lambda e: e.matmul(
                        P7[0:64, :], self.ONER[64:65, 0:64], self.NRM[64:65, :],
                        start=True, stop=True),
                        reads=(NRK, self.ONER), writes=(PSK[id(P7)],), inc=True)
                    kb.op("s", lambda e: e.activation(self.NRM[0:64, :],
                                                      P7[0:64, :], AF.Copy),
                          reads=(PSK[id(P7)],), writes=(NRK,), inc=True)
                    op = (hh * 4 + ib) % 2
                    oh2 = OH2b[op]
                    kb.op("v", lambda e, oh2=oh2: e.tensor_tensor(
                        oh2[0:64, 0:512], P6[0:64, :], self.NRM[0:64, :], ALU.mult),
                        reads=(PSK[id(P6)], NRK), writes=(OHK[op],), inc=True)
                    kb.dma("q", bass.AP(self.OHS, h * 64 * T + i0,
                                        [[T, 64], [1, 512]]),
                           oh2[0:64, 0:512], reads=(OHK[op],), writes=(OHSK,),
                           stream=f"ohs{op}")

        loadW(0)
        qkv(0)
        bd(0)
        for p in range(8):
            if p < 7:
                loadW(p + 1)
                qkv(p + 1)
                bd(p + 1)
            flash(p)
        kb.barrier()

        OTK = (RK("ot0"), RK("ot1"))
        XSK = (RK("xs0"), RK("xs1"))
        for tb in range(NTB):
            for seg in range(2):
                pso = PA if seg == 0 else PB
                xs = QTF[:, 256 + seg * 512: 768 + seg * 512]
                kb.dma("s2", xs, bass.AP(self.X3S, tb * 128 * D + seg * 512,
                                         [[D, 128], [1, 512]]),
                       reads=(self.X3S,), writes=(XSK[seg],), stream=f"x3r{seg}")
                for c in range(8):
                    ot = QT[:, 14592 + (c % 2) * 128: 14592 + (c % 2) * 128 + 128]
                    kb.dma("q", ot, bass.AP(self.OHS, c * 128 * T + tb * 128,
                                            [[T, 128], [1, 128]]),
                           reads=(self.OHS,), writes=(OTK[c % 2],),
                           stream=f"ohr{c % 2}")
                    wt = self.WB0[:, c * 1024 + seg * 512: c * 1024 + seg * 512 + 512]
                    kb.op("t", lambda e, ot=ot, wt=wt, c=c, pso=pso: e.matmul(
                        pso[:], ot, wt, start=(c == 0), stop=(c == 7)),
                        reads=(OTK[c % 2], self.WB0), writes=(PSK[id(pso)],),
                        inc=(c == 7))
                kb.op("v", lambda e, tb=tb, seg=seg, xs=xs, pso=pso:
                      e.scalar_tensor_tensor(
                          XRES[:, tb * D + seg * 512: tb * D + seg * 512 + 512],
                          xs, 2.0, pso[:], ALU.mult, ALU.add),
                      reads=(XSK[seg], PSK[id(pso)]), writes=(XRES,), inc=True)

        self.xn_materialize(16, 24)

        for ic in range(NFC):
            kb.dma("g2", self.SBbf0[:, ic * 32:(ic + 1) * 32],
                   bass.AP(self.selw, ic * 128 * 32, [[32, 128], [1, 32]]),
                   writes=(self.SBbf0,), stream="whead")
        for tb in range(NTB):
            for ic in range(NFC):
                l = XF[:, ic * T + tb * 128: ic * T + tb * 128 + 128]
                r = self.SBbf0[:, ic * 32:(ic + 1) * 32]
                kb.op("t", lambda e, l=l, r=r, ic=ic: e.matmul(
                    PB[:, 0:32], l, r, start=(ic == 0), stop=(ic == NFC - 1)),
                    reads=(XF, self.SBbf0), writes=(PB,), inc=(ic == NFC - 1))
            sel = self.AF2[:, 0:32]
            kb.op("s", lambda e, sel=sel: e.activation(sel, PB[:, 0:32], AF.Sigmoid),
                  reads=(PB,), writes=(self.AF2,), inc=True)
            m1, m4 = COLS[:, 74:75], COLS[:, 77:78]
            m2, m3 = COLS[:, 75:76], COLS[:, 76:77]
            tmsk = self.AF2[:, 32:64]
            tt_ = self.AF2[:, 64:96]
            kb.op("v", lambda e: e.reduce_max(m1, sel, axis=AX.X),
                  reads=(self.AF2,), writes=(COLS,))
            cur = sel
            for mi, mnext in ((m1, m2), (m2, m3), (m3, m4)):
                kb.op("v", lambda e, cur=cur, mi=mi: e.tensor_scalar(
                    tmsk, cur, mi, None, ALU.is_ge),
                    reads=(self.AF2, COLS), writes=(self.AF2,))
                kb.op("v", lambda e, cur=cur: e.scalar_tensor_tensor(
                    tt_, tmsk, -1e30, cur, ALU.mult, ALU.add),
                    reads=(self.AF2,), writes=(self.AF2,))
                kb.op("v", lambda e, mnext=mnext: e.reduce_max(mnext, tt_, axis=AX.X),
                      reads=(self.AF2,), writes=(COLS,))
                cur = tt_
            kb.op("v", lambda e: e.tensor_scalar(tmsk, sel, m4, None, ALU.is_ge),
                  reads=(self.AF2, COLS), writes=(self.AF2,))
            kb.op("v", lambda e: e.tensor_tensor(tmsk, tmsk, sel, ALU.mult),
                  reads=(self.AF2,), writes=(self.AF2,))
            kb.op("t", lambda e: e.matmul(self.PTF[0:32, 0:128], tmsk, self.IDF[:],
                                          start=True, stop=True),
                  reads=(self.AF2, self.IDF), writes=(self.P7,), inc=True)
            gt = self.AF2[:, 96:224]
            kb.op("v", lambda e, gt=gt: e.tensor_copy(gt[0:32, :],
                                                      self.PTF[0:32, 0:128]),
                  reads=(self.P7,), writes=(self.AF2,), inc=True)
            kb.dma("q", bass.AP(self.GATES, tb * 128, [[T, 32], [1, 128]]),
                   gt[0:32, :], reads=(self.AF2,), writes=(self.GATES,),
                   stream="gat")

        kb.barrier()
        gr_off = (self.rows(256), self.rows(256))
        WK = (RK("moeW0"), RK("moeW1"))
        HK = (RK("moeH0"), RK("moeH1"))
        SK = (RK("moeS0"), RK("moeS1"))
        GK_ = (RK("moeGR0"), RK("moeGR1"))
        XRK = RK("xresMoe")
        XFK = RK("xfMoe")
        PSK = {id(p): RK(f"ps{i}") for i, p in
               enumerate((PA, PB, PC, PD, P4, P5, P6, P7))}

        def load_group(g):
            buf = g % 2
            for k in range(4):
                e_ = g * 4 + k
                ks_off = buf * 8192 + k * 1024
                vs_off = buf * 8192 + 4096 + k * 1024
                kb.dma("g2", bass.AP(QT, ks_off, [[16384, 128], [128, 8], [1, 128]]),
                       bass.AP(self.keys1, e_ * 1024 * 128,
                               [[128, 128], [128 * 128, 8], [1, 128]]),
                       writes=(WK[buf],), stream=f"moe{buf}")
                kb.dma("g2", QT[:, vs_off:vs_off + 1024],
                       bass.AP(self.vals1, e_ * 128 * 1024, [[1024, 128], [1, 1024]]),
                       writes=(WK[buf],), stream=f"moe{buf}")

        load_group(0)
        for g in range(8):
            buf = g % 2
            if g < 7:
                load_group(g + 1)
            for tq in range(8):
                t0 = tq * 256
                for k in range(4):
                    e_ = g * 4 + k
                    par = k % 2
                    ks = QT[:, buf * 8192 + k * 1024: buf * 8192 + k * 1024 + 1024]
                    vs = QT[:, buf * 8192 + 4096 + k * 1024:
                            buf * 8192 + 4096 + k * 1024 + 1024]
                    pk = PA if par == 0 else P5
                    pg = P6 if par == 0 else P7
                    hrel = self.AF2[:, 0:256] if par == 0 else self.AF2[:, 256:512]
                    hs = self.SBbf0[:, 0:256] if par == 0 else self.SBbf1[:, 0:256]
                    hst = self.SBbf0 if par == 0 else self.SBbf1
                    kb.dma("q", self.row_ap(gr_off[par], 256),
                           bass.AP(self.GATES, e_ * T + t0, [[T, 1], [1, 256]]),
                           reads=(self.GATES,), writes=(GK_[par],),
                           stream=f"mgr{par}")
                    kb.op("t", lambda e, pg=pg, par=par: e.matmul(
                        pg[:, 0:256], self.ONER[0:1, 0:128],
                        self.row_ap(gr_off[par], 256), start=True, stop=True),
                        reads=(GK_[par], self.ONER), writes=(PSK[id(pg)],), inc=True)
                    for ic in range(NFC):
                        l = ks[:, ic * 128:(ic + 1) * 128]
                        r = XF[:, ic * T + t0: ic * T + t0 + 256]
                        kb.op("t", lambda e, l=l, r=r, ic=ic, pk=pk: e.matmul(
                            pk[:, 0:256], l, r, start=(ic == 0), stop=(ic == NFC - 1)),
                            reads=(WK[buf], XFK), writes=(PSK[id(pk)],),
                            inc=(ic == NFC - 1))
                    kb.op("s", lambda e, hrel=hrel, pk=pk: e.activation(
                        hrel, pk[:, 0:256], AF.Relu),
                        reads=(PSK[id(pk)],), writes=(HK[par],), inc=True)
                    kb.op("v", lambda e, hs=hs, hrel=hrel, pg=pg: e.tensor_tensor(
                        hs, hrel, pg[:, 0:256], ALU.mult),
                        reads=(HK[par], PSK[id(pg)]), writes=(SK[par],), inc=True)
                    for tl in range(2):
                        for oseg in range(2):
                            ps = (PB, PC, PD, P4)[tl * 2 + oseg]
                            l = hs[:, tl * 128:(tl + 1) * 128]
                            r = vs[:, oseg * 512:(oseg + 1) * 512]
                            kb.op("t", lambda e, l=l, r=r, ps=ps, k=k: e.matmul(
                                ps[:], l, r, start=(k == 0), stop=(k == 3)),
                                reads=(SK[par], WK[buf]), writes=(PSK[id(ps)],),
                                inc=(k == 3))
                for tl in range(2):
                    tb = (t0 + tl * 128) // 128
                    for oseg in range(2):
                        ps = (PB, PC, PD, P4)[tl * 2 + oseg]
                        xap = XRES[:, tb * D + oseg * 512: tb * D + oseg * 512 + 512]
                        kb.op("v", lambda e, xap=xap, ps=ps: e.tensor_add(
                            xap, xap, ps[:]),
                            reads=(XRK, PSK[id(ps)]), writes=(XRK,), inc=True)
        kb.barrier()

        for tb in range(NTB):
            mcol, rcol = COLS[:, 0:1], COLS[:, 1:2]
            self.ln_stats(XRES[:, tb * D:(tb + 1) * D], XRES, mcol, rcol)
            nb = COLS[:, 3:4]
            kb.op("v", lambda e: e.tensor_tensor(nb, mcol, rcol, ALU.mult),
                  reads=(COLS,), writes=(COLS,))
            kb.op("v", lambda e: e.tensor_scalar(nb, nb, -1.0, None, ALU.mult),
                  reads=(COLS,), writes=(COLS,))
            kb.op("v", lambda e, tb=tb: e.scalar_tensor_tensor(
                XRES[:, tb * D:(tb + 1) * D], XRES[:, tb * D:(tb + 1) * D],
                rcol, nb.broadcast_to([128, D]), ALU.mult, ALU.add),
                reads=(XRES, COLS), writes=(XRES,), inc=True)

    def emit_output(self):
        kb = self.kb
        if DEV_PROBE is not None:
            kb.finish("q")
            return
        for tb in range(NTB):
            kb.dma("q", bass.AP(self.out_t, tb * 128 * D, [[D, 128], [1, D]]),
                   self.XRES[:, tb * D:(tb + 1) * D], reads=(self.XRES,))
        kb.finish("q")

    def build(self):
        self.declare()
        ind = np.zeros((1, 256), dtype=np.float32)
        ind[0, 0:64] = 1.0
        ind[0, 128 + 64:128 + 128] = 1.0
        ind_d = self.nc.inline_tensor(ind, "indrows")
        self._indr_off = self.rows(256)
        self.INDR = self.ROWS
        ctx = self.nc.body("main", valid_engines=[ENG.Pool, ENG.Activation,
                                                  ENG.PE, ENG.DVE, ENG.SP])
        bb = ctx.__enter__()
        self.kb.dma("q", self.row_ap(self._indr_off, 256), ind_d.ap(),
                    writes=(self.ROWS,))
        self._base_rows = self.rows_off
        self.phase_l0a()
        if DEV_STAGE != "l0a":
            self.phase_l0f()
        if DEV_STAGE in ("l1a", "l1f", "full"):
            self.phase_l1a()
        if DEV_STAGE in ("l1f", "full"):
            self.phase_l1f()
        self.emit_output()
        ctx.__exit__(None, None, None)
        return self.nc


def kernel(**inputs):
    b = Builder()
    nc = b.build()
    x = np.ascontiguousarray(inputs["L0_attn_x"])
    in_maps = []
    for c in range(8):
        m = {"L0_attn_x": x[c]}
        for k, v in inputs.items():
            if k != "L0_attn_x":
                m[k] = np.ascontiguousarray(v)
        in_maps.append(m)
    declared = set(b.dram.keys())
    in_maps = [{k: v for k, v in m.items() if k in declared} for m in in_maps]
    res = run_bass_kernel_spmd(nc, in_maps, core_ids=list(range(8)))
    return np.stack([r["out"] for r in res.results], axis=0)


# version 15
# speedup vs baseline: 1.0456x; 1.0002x over previous
import math
import numpy as np
import ml_dtypes
import concourse.bass as bass
import concourse.mybir as mybir
from concourse.bass_utils import run_bass_kernel_spmd

F32 = mybir.dt.float32
BF16 = mybir.dt.bfloat16
I32 = mybir.dt.int32
AF = mybir.ActivationFunctionType
ALU = mybir.AluOpType
AX = mybir.AxisListType
ENG = mybir.EngineType

T = 2048
D = 1024
NTB = T // 128
NFC = D // 128

DEV_STAGE = "full"
DEV_PROBE = None


def cdiv(a, b):
    return -(-a // b)


class RK:
    __slots__ = ("name",)

    def __init__(self, name):
        self.name = name


class KB:
    def __init__(self, nc):
        self.nc = nc
        self.eng = {"t": nc.tensor, "v": nc.vector, "s": nc.scalar,
                    "g": nc.gpsimd, "q": nc.sync}
        self.dmaeng = {"q": nc.sync, "s2": nc.scalar, "g2": nc.gpsimd}
        self.esem = {"t": "tS", "v": "vS", "s": "sS", "g": "gS"}
        self.sem = {n: nc.alloc_semaphore(n)
                    for n in ("tS", "vS", "sS", "gS", "cc")}
        self.cnt = {k: 0 for k in self.sem}
        self.stream_maxwait = {}
        self.stream_boundary = {}
        self.seq = {e: 0 for e in ("t", "v", "s", "g", "q", "s2", "g2")}
        self.events = {e: [] for e in self.seq}
        self.wr = {}
        self.rd = {}
        self.last_drain = {"v": 0, "s": 0, "g": 0}
        self.hwm = {e: {} for e in self.seq}

    def _event_after(self, ename, seq):
        ev = self.events[ename]
        lo, hi = 0, len(ev)
        while lo < hi:
            mid = (lo + hi) // 2
            if ev[mid][0] >= seq:
                hi = mid
            else:
                lo = mid + 1
        return ev[lo][1:] if lo < len(ev) else None

    def _signal(self, ename):
        sem = self.esem[ename]
        inst = self.eng[ename].drain()
        if ename != "t":
            self.last_drain[ename] = self.seq[ename]
        inst.then_inc(self.sem[sem], 1)
        self.cnt[sem] += 1
        self.seq[ename] += 1
        self.events[ename].append((self.seq[ename], sem, self.cnt[sem]))
        return (sem, self.cnt[sem])

    def dstream(self, name):
        key = "D_" + name
        if key not in self.sem:
            self.sem[key] = self.nc.alloc_semaphore(key)
            self.cnt[key] = 0
            self.stream_maxwait[key] = 0
            self.stream_boundary[key] = 0
        return key

    def _wait(self, consumer, sem, val):
        h = self.hwm[consumer]
        if sem.startswith("D_"):
            self.stream_maxwait[sem] = max(self.stream_maxwait[sem], val)
        if h.get(sem, 0) >= val:
            return
        e = self.dmaeng[consumer] if consumer in self.dmaeng else self.eng[consumer]
        e.wait_ge(self.sem[sem], val)
        h[sem] = val

    def _need(self, consumer, producer, pseq):
        if producer == consumer and producer not in ("q", "s2", "g2"):
            if producer == "t":
                return
            if self.last_drain.get(producer, 0) < pseq:
                self.eng[producer].drain()
                self.seq[producer] += 1
                self.last_drain[producer] = self.seq[producer]
            return
        ev = self._event_after(producer, pseq)
        if ev is None:
            ev = self._signal(producer)
        sem, c = ev
        if sem.startswith("D_"):
            c = self.cnt[sem]
        self._wait(consumer, sem, c)

    def op(self, ename, fn, reads=(), writes=(), inc=False):
        rk = [t.name for t in reads]
        wk = [t.name for t in writes]
        for k in rk:
            if k in self.wr:
                self._need(ename, *self.wr[k])
        for k in wk:
            for ev in self.rd.get(k, ()):
                self._need(ename, *ev)
            if k in self.wr:
                self._need(ename, *self.wr[k])
        inst = fn(self.eng[ename])
        self.seq[ename] += 1
        myseq = self.seq[ename]
        for k in rk:
            self.rd.setdefault(k, []).append((ename, myseq))
        for k in wk:
            self.wr[k] = (ename, myseq)
            self.rd[k] = []
        if inc:
            sem = self.esem[ename]
            inst.then_inc(self.sem[sem], 1)
            self.cnt[sem] += 1
            self.events[ename].append((myseq, sem, self.cnt[sem]))
        return inst

    def dma(self, qname, out_ap, in_ap, reads=(), writes=(), stream="main", **kw):
        sem = self.dstream(("g" if qname == "g2" else "h") + stream)
        phys = {"q": None, "s2": "s", "g2": "g"}[qname]
        if self.stream_maxwait[sem] > self.stream_boundary[sem]:
            self._wait(qname, sem, self.cnt[sem])
            self.stream_boundary[sem] = self.cnt[sem]
        for t in reads:
            k = t.name
            if k in self.wr:
                p, ps = self.wr[k]
                if phys is not None and p == phys:
                    if self.last_drain[p] < ps:
                        self.eng[p].drain()
                        self.seq[p] += 1
                        self.last_drain[p] = self.seq[p]
                else:
                    self._need(qname, p, ps)
        for t in writes:
            k = t.name
            for ev in self.rd.get(k, ()):
                p = ev[0]
                if phys is not None and p == phys:
                    if self.last_drain[p] < ev[1]:
                        self.eng[p].drain()
                        self.seq[p] += 1
                        self.last_drain[p] = self.seq[p]
                else:
                    self._need(qname, *ev)
            if k in self.wr:
                p, ps = self.wr[k]
                if phys is not None and p == phys:
                    if self.last_drain[p] < ps:
                        self.eng[p].drain()
                        self.seq[p] += 1
                        self.last_drain[p] = self.seq[p]
                else:
                    self._need(qname, p, ps)
        e = self.dmaeng[qname]
        inst = e.dma_start(out_ap, in_ap, **kw)
        inst.then_inc(self.sem[sem], 16)
        self.cnt[sem] += 16
        self.seq[qname] += 1
        myseq = self.seq[qname]
        self.events[qname].append((myseq, sem, self.cnt[sem]))
        for t in reads:
            self.rd.setdefault(t.name, []).append((qname, myseq))
        for t in writes:
            self.wr[t.name] = (qname, myseq)
            self.rd[t.name] = []
        return inst

    def sync_read(self, ename, tensors):
        for t in tensors:
            if t.name in self.wr:
                self._need(ename, *self.wr[t.name])

    def finish(self, ename="q"):
        for sname, c in self.cnt.items():
            if c > 0:
                self._wait(ename, sname, c)

    def barrier(self):
        for e in ("t", "v", "s", "g", "q"):
            for sname, c in self.cnt.items():
                if c > 0:
                    self._wait(e, sname, c)
        self.wr.clear()
        self.rd.clear()


class Builder:
    def __init__(self):
        nc = bass.Bass("TRN2", target_bir_lowering=False)
        self.nc = nc
        self.kb = KB(nc)
        self.dram = {}
        self.rows_off = 0

    def din(self, name, shape):
        t = self.nc.dram_tensor(name, list(shape), F32, kind="ExternalInput")
        self.dram[name] = t
        return t

    def rows(self, n):
        o = self.rows_off
        assert o + n <= 5632, (o, n)
        self.rows_off += n
        return o

    def rows_mark(self):
        return self.rows_off

    def rows_reset(self, mark):
        self.rows_off = mark

    def row_ap(self, off, n):
        return bass.AP(self.ROWS, off, [[5632, 1], [1, n]])

    def declare(self):
        nc = self.nc
        self.x_in = self.din("L0_attn_x", (T, D))
        self.g0 = self.din("L0_attn_gamma", (D,))
        self.Wq0 = self.din("L0_attn_Wq", (D, 1024))
        self.Wkv0 = self.din("L0_attn_Wkv", (D, 128))
        self.qs0 = self.din("L0_attn_q_scale", (64,))
        self.ks0 = self.din("L0_attn_k_scale", (64,))
        self.nkv0 = self.din("L0_attn_null_kv", (2, 2, 64))
        self.Wout0 = self.din("L0_attn_Wout", (1024, D))
        self.wg0 = self.din("L0_ffn_w_gate", (3072, 1024))
        self.wu0 = self.din("L0_ffn_w_up", (3072, 1024))
        self.wd0 = self.din("L0_ffn_w_down", (1024, 3072))
        self.wlqkv = self.din("L1_attn_w_lqkv", (512, 1536))
        self.blqkv = self.din("L1_attn_b_lqkv", (1536,))
        self.wgqkv = self.din("L1_attn_w_gqkv", (512, 1536))
        self.bgqkv = self.din("L1_attn_b_gqkv", (1536,))
        self.wcomp = self.din("L1_attn_w_comp", (4096, 1024))
        self.bcomp = self.din("L1_attn_b_comp", (1024,))
        self.wlproj = self.din("L1_attn_w_lproj", (512, 512))
        self.blproj = self.din("L1_attn_b_lproj", (512,))
        self.wgproj = self.din("L1_attn_w_gproj", (512, 512))
        self.bgproj = self.din("L1_attn_b_gproj", (512,))
        self.wgate1 = self.din("L1_attn_w_gate", (1024, 2))
        self.bgate1 = self.din("L1_attn_b_gate", (2,))
        self.wq1 = self.din("L1_ffn_wq", (1024, 1024))
        self.wk1 = self.din("L1_ffn_wk", (1024, 1024))
        self.wv1 = self.din("L1_ffn_wv", (1024, 1024))
        self.wp1 = self.din("L1_ffn_wp", (1024, 1024))
        self.wo1 = self.din("L1_ffn_wo", (1024, 1024))
        self.ln1g = self.din("L1_ffn_ln1_g", (1024,))
        self.ln1b = self.din("L1_ffn_ln1_b", (1024,))
        self.ln2g = self.din("L1_ffn_ln2_g", (1024,))
        self.ln2b = self.din("L1_ffn_ln2_b", (1024,))
        self.selw = self.din("L1_ffn_sel_w", (1024, 32))
        self.keys1 = self.din("L1_ffn_keys", (32, 1024, 128))
        self.vals1 = self.din("L1_ffn_values", (32, 128, 1024))

        self.out_t = nc.dram_tensor("out", [T, D], F32, kind="ExternalOutput")
        if DEV_PROBE is not None:
            self.dbg_t = nc.dram_tensor("dbg", [128, 8192], BF16,
                                        kind="ExternalOutput")
            self.dbgf_t = nc.dram_tensor("dbgf", [128, 2048], F32,
                                         kind="ExternalOutput")
        _spk = "ExternalOutput" if DEV_PROBE in ("ups", "gts") else "Internal"
        self.UPS = nc.dram_tensor("UPS", [3072, T], F32, kind=_spk)
        self.GTS = nc.dram_tensor("GTS", [3072, T], F32, kind=_spk)
        self.WDNT = nc.dram_tensor("WDNT", [3072, 1024], BF16,
                                   kind="ExternalOutput" if DEV_PROBE == "wdnt"
                                   else "Internal")
        self.CCI = nc.dram_tensor("CCI", [1, 8], F32)
        self.CCO = nc.dram_tensor("CCO", [1, 8], F32)
        self.SCR = nc.dram_tensor("SCR", [1, 8], F32)
        self.X3S = nc.dram_tensor("X3S", [T, D], F32)
        self.BDWp = [nc.dram_tensor(f"BDWa{i}", [2 * T * 2304 + 8192], BF16)
                     for i in range(2)]
        self.OHS = nc.dram_tensor("OHS", [1024, T], BF16)
        self.GATES = nc.dram_tensor("GATES", [32, T], F32)

        self.identbf_d = nc.inline_tensor(np.eye(128).astype(ml_dtypes.bfloat16),
                                          "identbf")
        self.identf_d = nc.inline_tensor(np.eye(128, dtype=np.float32), "identf")
        jp = np.arange(128)[:, None]
        ic = np.arange(512)[None, :]
        ml = [(jp <= ic + 2 - 128 * k) for k in range(5)]
        ml.append((jp <= ic + 2) | ((ic < 64) & (jp < 66)))
        ml += [(jp <= ic - 128 * k) for k in range(5)]
        masks_np = np.concatenate([m.astype(ml_dtypes.bfloat16) for m in ml], axis=1)
        self.masks_d = nc.inline_tensor(masks_np, "masksc")
        m_idx = np.arange(4095, dtype=np.float64)
        pos = m_idx - 2047.0
        invf = 1.0 / (10000.0 ** (np.arange(0, 1024, 2, dtype=np.float64) / 1024.0))
        ang = pos[:, None] * invf[None, :]
        pe2 = np.concatenate([np.sin(ang), np.cos(ang)], axis=1)
        pe2T = np.zeros((1024, 4096), dtype=np.float64)
        pe2T[:, 0:4095] = pe2.T
        pe2T_bf = pe2T.astype(np.float32).astype(ml_dtypes.bfloat16)
        pe2_cat = np.concatenate([pe2T_bf[c * 128:(c + 1) * 128] for c in range(8)],
                                 axis=1)
        self.pe2_d = nc.inline_tensor(pe2_cat, "pe2t")
        sh = np.zeros((64, 128), dtype=np.float32)
        sh[np.arange(64), 64 + np.arange(64)] = 1.0
        self.sh64_d = nc.inline_tensor(sh.astype(ml_dtypes.bfloat16), "sh64")
        urow = np.zeros((1, 128), dtype=np.float32)
        urow[0, 64] = 1.0
        A = nc.alloc_sbuf_tensor
        self.XRES = A("XRES", [128, NTB * D], F32)
        self.XF = A("XF", [128, NFC * T], BF16)
        self.QT = A("QT", [128, NFC * T], BF16)
        self.AF2 = A("AF2", [128, 1536], F32)
        self.WB0 = A("WB0", [128, 8192], BF16)
        self.WB1 = A("WB1", [128, 8192], BF16)
        self.SBbf0 = A("SBbf0", [128, 512], BF16)
        self.SBbf1 = A("SBbf1", [128, 512], BF16)
        self.MSK = A("MSK", [128, 6 * 512], BF16)
        self.IDB = A("IDB", [128, 128], BF16)
        self.SH64 = A("SH64", [64, 128], BF16)
        self.IDF = A("IDF", [128, 128], F32)
        self.ROWS = A("ROWS", [1, 5632], F32)
        self.COLS = A("COLS", [128, 96], F32)
        self.ONESF = A("ONESF", [128, 1], F32)
        self.ONESB = A("ONESB", [128, 1], BF16)
        self.GAM = A("GAM", [128, NFC], F32)
        self.SCC = A("SCC", [128, 8], F32)
        self.QS = A("QS", [128, 1], F32)
        self.KS = A("KS", [128, 1], F32)
        self.ONER = A("ONER", [128, 128], F32)
        self.VTL = A("VTL", [128, 4224], BF16)
        self.BL1 = A("BL1", [128, 40], F32)
        P = nc.alloc_psum_tensor
        self.PA = P("PA", [128, 512], F32)
        self.PB = P("PB", [128, 512], F32)
        self.PC = P("PC", [128, 512], F32)
        self.PD = P("PD", [128, 512], F32)
        self.P4 = P("P4", [128, 512], F32)
        self.P5 = P("P5", [128, 512], F32)
        self.P6 = P("P6", [128, 512], F32)
        self.P7 = P("P7", [128, 512], F32)
        self.PTB = self.P7.ap().bitcast(BF16)[:, 0:128]
        self.PTF = self.P7[:, 128:512]
        self.NRM = self.AF2[:, 512:1024]
        self.KVS = self.QT.ap().bitcast(F32)[:, 0:2052]
        self.KH = self.VTL[:, 0:2052]
        self.VT = self.VTL[:, 2052:3174]
        self.AF0 = self.AF2[:, 0:512]
        self.XCB = self.AF2.ap().bitcast(BF16)[:, 2048:3072]
        self.QSQv = self.AF2[:, 1024:1536]

    def load_w_chunks(self, dst, dst_col0, w, K, M, row0=0, col0=0, nsplit=2):
        kb = self.kb
        nk = K // 128
        W = w.shape[1]
        per = cdiv(nk, nsplit)
        for sp in range(nsplit):
            k0, k1 = sp * per, min(nk, (sp + 1) * per)
            if k0 >= k1:
                break
            src = bass.AP(w, (row0 + k0 * 128) * W + col0,
                          [[W, 128], [128 * W, k1 - k0], [1, M]])
            dap = bass.AP(dst, dst_col0 + k0 * M,
                          [[dst.shape[1], 128], [M, k1 - k0], [1, M]])
            kb.dma("g2", dap, src, writes=(dst,))

    def row_rsqrt(self, off, n, floor=1e-24):
        kb = self.kb
        if not hasattr(self, "_rs_scr"):
            self._rs_scr = self.rows(1024)
        so = self._rs_scr
        R = self.ROWS
        for c0 in range(0, n, 512):
            m = min(512, n - c0)
            v_ = self.row_ap(off + c0, m)
            t0 = self.row_ap(so, m)
            t1 = self.row_ap(so + 512, m)
            kb.op("v", lambda e, v_=v_: e.tensor_scalar(v_, v_, floor, None, ALU.max),
                  reads=(R,), writes=(R,))
            kb.op("v", lambda e, v_=v_, t0=t0: e.reciprocal(t0, v_), reads=(R,), writes=(R,))
            kb.op("s", lambda e, t0=t0, t1=t1: e.activation(t1, t0, AF.Sqrt),
                  reads=(R,), writes=(R,))
            kb.op("v", lambda e, t0=t0, t1=t1: e.tensor_tensor(t0, t1, t1, ALU.mult),
                  reads=(R,), writes=(R,))
            kb.op("v", lambda e, t0=t0, v_=v_: e.tensor_tensor(t0, t0, v_, ALU.mult),
                  reads=(R,), writes=(R,))
            kb.op("v", lambda e, t0=t0: e.tensor_scalar(t0, t0, -0.5, 1.5, ALU.mult, ALU.add),
                  reads=(R,), writes=(R,))
            kb.op("v", lambda e, v_=v_, t0=t0, t1=t1: e.tensor_tensor(v_, t1, t0, ALU.mult),
                  reads=(R,), writes=(R,))
        return self.row_ap(off, n)

    def col_rsqrt(self, vcol, out, tmp1, tmp2, eps):
        kb = self.kb
        C = self.COLS
        kb.op("v", lambda e: e.tensor_scalar(out, vcol, eps, None, ALU.add),
              reads=(C,), writes=(C,))
        kb.op("v", lambda e: e.reciprocal(tmp1, out), reads=(C,), writes=(C,))
        kb.op("s", lambda e: e.activation(tmp2, tmp1, AF.Sqrt), reads=(C,), writes=(C,))
        kb.op("v", lambda e: e.tensor_tensor(tmp1, tmp2, tmp2, ALU.mult),
              reads=(C,), writes=(C,))
        kb.op("v", lambda e: e.tensor_tensor(tmp1, tmp1, out, ALU.mult),
              reads=(C,), writes=(C,))
        kb.op("v", lambda e: e.tensor_scalar(tmp1, tmp1, -0.5, 1.5, ALU.mult, ALU.add),
              reads=(C,), writes=(C,))
        kb.op("v", lambda e: e.tensor_tensor(out, tmp2, tmp1, ALU.mult),
              reads=(C,), writes=(C,))

    def ln_stats(self, src_ap, src_t, mcol, rcol, eps=1e-5):
        kb = self.kb
        C = self.COLS
        n = src_ap.shape[1]
        sq = self.AF2[:, 0:512]
        t2 = C[:, 59:60]
        kb.op("v", lambda e: e.reduce_sum(mcol, src_ap, axis=AX.X),
              reads=(src_t,), writes=(C,))
        kb.op("v", lambda e: e.tensor_scalar(mcol, mcol, 1.0 / n, None, ALU.mult),
              reads=(C,), writes=(C,))
        kb.op("v", lambda e: e.memset(rcol, 0.0), writes=(C,))
        for c0 in range(0, n, 512):
            m = min(512, n - c0)
            kb.op("s", lambda e, c0=c0, m=m: e.activation(
                sq[:, 0:m], src_ap[:, c0:c0 + m], AF.Square),
                reads=(src_t,), writes=(self.AF2,))
            kb.op("v", lambda e, m=m: e.reduce_sum(t2, sq[:, 0:m], axis=AX.X),
                  reads=(self.AF2,), writes=(C,))
            kb.op("v", lambda e: e.tensor_add(rcol, rcol, t2),
                  reads=(C,), writes=(C,))
        kb.op("v", lambda e: e.tensor_scalar(rcol, rcol, 1.0 / n, None, ALU.mult),
              reads=(C,), writes=(C,))
        t0 = C[:, 60:61]
        kb.op("v", lambda e: e.tensor_tensor(t0, mcol, mcol, ALU.mult),
              reads=(C,), writes=(C,))
        kb.op("v", lambda e: e.tensor_sub(rcol, rcol, t0), reads=(C,), writes=(C,))
        self.col_rsqrt(rcol, rcol, C[:, 60:61], C[:, 61:62], eps)

    def tm_tile_to_fm(self, src_fn, dst, dst_col_of_fc, tb, nfc=NFC):
        kb = self.kb
        for fc in range(nfc):
            sap = src_fn(fc)
            kb.op("t", lambda e, sap=sap: e.transpose(self.PTB[:], sap, self.IDB[:]),
                  reads=(sap.tensor, self.IDB), writes=(self.PTB,), inc=True)
            c0 = dst_col_of_fc(fc) + tb * 128
            kb.op("s", lambda e, c0=c0: e.activation(dst[:, c0:c0 + 128],
                                                     self.PTB[:], AF.Copy),
                  reads=(self.PTB,), writes=(dst,), inc=True)

    def phase_l0a(self):
        kb, nc = self.kb, self.nc
        XRES, XF, QT = self.XRES, self.XF, self.QT
        ATTC2 = lambda c, hh: XF[64 * hh:64 * hh + 64,
                                 8192 + c * 512: 8192 + (c + 1) * 512]
        ATTTMP = self.SBbf0
        PA, PB, PC, PD = self.PA, self.PB, self.PC, self.PD
        KVS, KH, VT = self.KVS, self.KH, self.VT
        ROWS, COLS = self.ROWS, self.COLS

        for tb in range(NTB):
            kb.dma("q", XRES[:, tb * D:(tb + 1) * D],
                   bass.AP(self.x_in, tb * 128 * D, [[D, 128], [1, D]]),
                   writes=(XRES,))
        kb.dma("q", self.IDB[:], self.identbf_d.ap(), writes=(self.IDB,))
        kb.dma("q", self.IDF[:], self.identf_d.ap(), writes=(self.IDF,))
        kb.dma("q", self.SH64[:], self.sh64_d.ap(), writes=(self.SH64,))
        kb.dma("q", self.MSK[:], bass.AP(self.masks_d, 0, [[11 * 512, 128], [1, 6 * 512]]), writes=(self.MSK,))
        kb.op("v", lambda e: e.memset(self.ONESF[:], 1.0), writes=(self.ONESF,))
        kb.op("v", lambda e: e.memset(self.ONESB[:], 1.0), writes=(self.ONESB,))
        kb.op("v", lambda e: e.memset(self.ONER[:], 1.0), writes=(self.ONER,))
        self.load_w_chunks(self.WB0, 0, self.Wkv0, 1024, 128, nsplit=1)
        kb.dma("q", self.GAM[:], bass.AP(self.g0, 0, [[1, 128], [128, NFC]]),
               writes=(self.GAM,), allow_slow_non_contiguous=True)
        for half in range(2):
            kb.dma("q", self.QS[64 * half:64 * half + 64, :],
                   bass.AP(self.qs0, 0, [[1, 64], [1, 1]]), writes=(self.QS,))
            kb.dma("q", self.KS[64 * half:64 * half + 64, :],
                   bass.AP(self.ks0, 0, [[1, 64], [1, 1]]), writes=(self.KS,))
        kb.op("v", lambda e: e.scalar_tensor_tensor(
            self.QS[:], self.QS[:], 8.0, self.KS[:], ALU.mult, ALU.mult),
            reads=(self.QS, self.KS), writes=(self.QS,))

        for tb in range(NTB):
            kb.op("v", lambda e, tb=tb: e.tensor_copy(
                self.XCB[:], XRES[:, tb * D:(tb + 1) * D]),
                reads=(XRES,), writes=(self.AF2,), inc=True)
            self.tm_tile_to_fm(lambda fc: self.XCB[:, fc * 128:(fc + 1) * 128],
                               XF, lambda fc: fc * T, tb)

        for seg in range(4):
            for dc in range(NFC):
                l = self.WB0[:, dc * 128:(dc + 1) * 128]
                r = XF[:, dc * T + seg * 512: dc * T + seg * 512 + 512]
                kb.op("t", lambda e, l=l, r=r, dc=dc: e.matmul(
                    PA[:], l, r, start=(dc == 0), stop=(dc == NFC - 1)),
                    reads=(self.WB0, XF),
                    writes=(PA,), inc=(dc == NFC - 1))
            kb.op("s", lambda e, seg=seg: e.activation(
                KVS[:, 2 + seg * 512: 2 + seg * 512 + 512], PA[:], AF.Copy),
                reads=(PA,), writes=(KVS,), inc=True)
        kb.dma("q", KVS[0:64, 0:2], bass.AP(self.nkv0, 0, [[1, 64], [64, 2]]),
               writes=(KVS,), allow_slow_non_contiguous=True)
        kb.dma("q", KVS[64:128, 0:2], bass.AP(self.nkv0, 128, [[1, 64], [64, 2]]),
               writes=(KVS,), allow_slow_non_contiguous=True)

        for tt in range(17):
            c0 = tt * 128
            n = min(128, 2050 - c0)
            kb.op("t", lambda e, c0=c0, n=n: e.matmul(
                self.PTF[0:n, 0:64], KVS[64:128, c0:c0 + n], self.IDF[64:128, 64:128],
                start=True, stop=True),
                reads=(KVS, self.IDF), writes=(self.PTF,), inc=True)
            kb.op("s", lambda e, tt=tt, n=n: e.activation(
                VT[0:n, tt * 66: tt * 66 + 64], self.PTF[0:n, 0:64], AF.Copy),
                reads=(self.PTF,), writes=(VT,), inc=True)
            kb.op("v", lambda e, tt=tt, n=n: e.memset(
                VT[0:n, tt * 66 + 64: tt * 66 + 65], 1.0), writes=(VT,))

        kb.op("s", lambda e: e.activation(KVS[64:128, 0:2050], KVS[0:64, 0:2050],
                                          AF.Square),
              reads=(KVS,), writes=(KVS,), inc=True)
        _rmark = self.rows_mark()
        k2o = self.rows(2050)
        for seg in range(5):
            c0 = seg * 512
            c1 = min(2050, c0 + 512)
            kb.op("t", lambda e, c0=c0, c1=c1: e.matmul(
                PB[0:1, 0:c1 - c0], self.ONESF[64:128, :], KVS[64:128, c0:c1],
                start=True, stop=True),
                reads=(self.ONESF, KVS), writes=(PB,), inc=True)
            kb.op("v", lambda e, c0=c0, c1=c1: e.tensor_copy(
                self.row_ap(k2o + c0, c1 - c0), PB[0:1, 0:c1 - c0]),
                reads=(PB,), writes=(ROWS,), inc=True)
        kinv = self.row_rsqrt(k2o, 2050)
        for seg in range(5):
            c0 = seg * 512
            c1 = min(2050, c0 + 512)
            kb.op("t", lambda e, c0=c0, c1=c1: e.matmul(
                PB[0:64, 0:c1 - c0], self._ind_row(0)[0:1, 0:64],
                self.row_ap(k2o + c0, c1 - c0), start=True, stop=True),
                reads=(ROWS, self.INDR), writes=(PB,), inc=True)
            kb.op("v", lambda e, c0=c0, c1=c1: e.tensor_tensor(
                KH[0:64, c0:c1], KVS[0:64, c0:c1], PB[0:64, 0:c1 - c0], ALU.mult),
                reads=(KVS, PB), writes=(KH,), inc=True)
            kb.op("t", lambda e, c0=c0, c1=c1: e.matmul(
                PC[:, 0:c1 - c0], self.SH64[:], KH[0:64, c0:c1],
                start=True, stop=True),
                reads=(self.SH64, KH), writes=(PC,), inc=True)
            kb.op("s", lambda e, c0=c0, c1=c1: e.activation(
                KH[64:128, c0:c1], PC[64:128, 0:c1 - c0], AF.Copy),
                reads=(PC,), writes=(KH,), inc=True)

        self.rows_reset(_rmark)
        self._rs_scr_del = self.__dict__.pop('_rs_scr', None)
        self.load_w_chunks(self.WB1, 0, self.Wq0, 1024, 1024, nsplit=4)
        for dc in range(NFC):
            kb.op("s", lambda e, dc=dc: e.activation(
                self.WB1[:, dc * 1024:(dc + 1) * 1024],
                self.WB1[:, dc * 1024:(dc + 1) * 1024], AF.Copy,
                scale=self.GAM[:, dc:dc + 1]),
                reads=(self.WB1, self.GAM), writes=(self.WB1,), inc=True)
        gq_o = self.rows(1024)
        for seg in range(2):
            for dc in range(NFC):
                kb.op("t", lambda e, dc=dc, seg=seg: e.matmul(
                    PB[0:1, :], self.ONESB[:],
                    self.WB1[:, dc * 1024 + seg * 512: dc * 1024 + seg * 512 + 512],
                    start=(dc == 0), stop=(dc == NFC - 1)),
                    reads=(self.ONESB, self.WB1), writes=(PB,), inc=(dc == NFC - 1))
            kb.op("v", lambda e, seg=seg: e.tensor_copy(
                self.row_ap(gq_o + seg * 512, 512), PB[0:1, :]),
                reads=(PB,), writes=(ROWS,), inc=True)

        nmr_o = self.rows(T)
        for tb in range(NTB):
            mcol, rcol = COLS[:, 0:1], COLS[:, 1:2]
            self.ln_stats(XRES[:, tb * D:(tb + 1) * D], XRES, mcol, rcol)
            kb.op("s", lambda e, tb=tb: e.activation(
                self.XCB[:], XRES[:, tb * D:(tb + 1) * D], AF.Copy, scale=rcol),
                reads=(XRES, COLS), writes=(self.AF2,), inc=True)
            self.tm_tile_to_fm(lambda fc: self.XCB[:, fc * 128:(fc + 1) * 128],
                               XF, lambda fc: fc * T, tb)
            nm = COLS[:, 2:3]
            kb.op("v", lambda e: e.tensor_tensor(nm, mcol, rcol, ALU.mult),
                  reads=(COLS,), writes=(COLS,))
            kb.op("v", lambda e: e.tensor_scalar(nm, nm, -1.0, None, ALU.mult),
                  reads=(COLS,), writes=(COLS,))
            kb.op("t", lambda e: e.matmul(self.PTF[0:1, 0:128], nm, self.IDF[:],
                                          start=True, stop=True),
                  reads=(COLS, self.IDF), writes=(self.PTF,), inc=True)
            kb.op("v", lambda e, tb=tb: e.tensor_copy(
                self.row_ap(nmr_o + tb * 128, 128), self.PTF[0:1, 0:128]),
                reads=(self.PTF,), writes=(ROWS,), inc=True)

        QSB = self.AF0
        for dc in range(NFC):
            for seg in range(4):
                t0 = seg * 512
                for fc in range(NFC):
                    l = self.WB1[:, fc * 1024 + dc * 128: fc * 1024 + dc * 128 + 128]
                    r = XF[:, fc * T + t0: fc * T + t0 + 512]
                    kb.op("t", lambda e, l=l, r=r, fc=fc: e.matmul(
                        PA[:], l, r, start=(fc == 0), stop=False),
                        reads=(self.WB1, XF), writes=(PA,))
                kb.op("t", lambda e, dc=dc, t0=t0: e.matmul(
                    PA[:], self.row_ap(gq_o + dc * 128, 128),
                    self.row_ap(nmr_o + t0, 512), start=False, stop=True),
                    reads=(ROWS,), writes=(PA,), inc=True)
                kb.op("s", lambda e: e.activation(QSB[:, 0:512], PA[:], AF.Copy),
                      reads=(PA,), writes=(QSB,), inc=True)
                kb.op("s", lambda e: e.activation(self.QSQv, QSB[:, 0:512],
                                                  AF.Square),
                      reads=(QSB,), writes=(self.AF2,), inc=True)
                ro = self.rows(1024) if (dc == 0 and seg == 0) else self._qrow
                self._qrow = ro
                for hh in range(2):
                    kb.op("t", lambda e, hh=hh: e.matmul(
                        PB[0:1, :], self.ONESF[64 * hh:64 * hh + 64, :],
                        self.QSQv[64 * hh:64 * hh + 64, :], start=True, stop=True),
                        reads=(self.ONESF, self.AF2), writes=(PB,), inc=True)
                    kb.op("v", lambda e, hh=hh: e.tensor_copy(
                        self.row_ap(ro + hh * 512, 512), PB[0:1, :]),
                        reads=(PB,), writes=(ROWS,), inc=True)
                for hh in range(2):
                    self.row_rsqrt(ro + hh * 512, 512)
                for hh in range(2):
                    kb.op("t", lambda e, hh=hh, ro=ro: e.matmul(
                        PC[:], self._ind_row(hh), self.row_ap(ro + hh * 512, 512),
                        start=(hh == 0), stop=(hh == 1)),
                        reads=(self.INDR, ROWS),
                        writes=(PC,) if hh == 1 else (PC,), inc=(hh == 1))
                c0 = dc * T + t0
                kb.op("v", lambda e, c0=c0: e.scalar_tensor_tensor(
                    QT[:, c0:c0 + 512], QSB[:, 0:512], self.QS[:], PC[:],
                    ALU.mult, ALU.mult),
                    reads=(QSB, self.QS, PC), writes=(QT,), inc=True)

        WOUT = self.WB1
        self.load_w_chunks(WOUT, 0, self.Wout0, 1024, 1024, nsplit=4)
        for ib in range(4):
            i0 = ib * 512
            for h in range(16):
                dc, hh = h // 2, h % 2
                qap = lambda: QT[64 * hh:64 * hh + 64,
                                 dc * T + i0: dc * T + i0 + 512]
                njt = min(17, 4 * ib + 4 + 1)
                first = True
                for jt in range(njt):
                    jc0 = jt * 128
                    n = min(128, 2050 - jc0)
                    pp = self.P4 if jt % 2 == 0 else self.P5
                    kb.op("t", lambda e, jc0=jc0, n=n, hh=hh, pp=pp: e.matmul(
                        pp[0:n, :], KH[64 * hh:64 * hh + 64, jc0:jc0 + n], qap(),
                        start=True, stop=True),
                        reads=(KH, QT), writes=(pp,), inc=True)
                    pt = self.SBbf0[:, 0:512] if (jt % 2 == 0) else self.SBbf1[:, 0:512]
                    ptt = self.SBbf0 if (jt % 2 == 0) else self.SBbf1
                    kb.op("s", lambda e, pt=pt, n=n, pp=pp: e.activation(
                        pt[0:n, :] if n < 128 else pt, pp[0:n, :], AF.Exp),
                        reads=(pp,), writes=(ptt,), inc=True)
                    kk = jt - 4 * ib
                    if 0 <= kk <= 4 and jt < 17:
                        mi = 5 if (ib == 0 and jt == 0) else kk
                        kb.op("v", lambda e, pt=pt, mi=mi, n=n: e.tensor_tensor(
                            pt[0:n, :] if n < 128 else pt,
                            pt[0:n, :] if n < 128 else pt,
                            self.MSK[0:n, mi * 512:(mi + 1) * 512] if n < 128
                            else self.MSK[:, mi * 512:(mi + 1) * 512], ALU.mult),
                            reads=(ptt, self.MSK), writes=(ptt,), inc=True)
                    kb.op("t", lambda e, jt=jt, n=n, first=first, pt=pt: e.matmul(
                        PB[0:65, :], VT[0:n, jt * 66: jt * 66 + 65],
                        pt[0:n, :] if n < 128 else pt,
                        start=first, stop=(jt == njt - 1)),
                        reads=(VT, ptt), writes=(PB,),
                        inc=(jt == njt - 1))
                    first = False
                kb.op("v", lambda e: e.reciprocal(self.NRM[64:65, :], PB[64:65, :]),
                      reads=(PB,), writes=(self.AF2,), inc=True)
                kb.op("t", lambda e: e.matmul(
                    PC[0:64, :], self.ONER[64:65, 0:64], self.NRM[64:65, :],
                    start=True, stop=True),
                    reads=(self.AF2, self.ONER), writes=(PC,), inc=True)
                kb.op("s", lambda e: e.activation(self.NRM[0:64, :], PC[0:64, :],
                                                  AF.Copy),
                      reads=(PC,), writes=(self.AF2,), inc=True)
                if hh == 0:
                    kb.op("v", lambda e, dc=dc: e.tensor_tensor(
                        ATTC2(dc, 0), PB[0:64, :],
                        self.NRM[0:64, :], ALU.mult),
                        reads=(PB, self.AF2), writes=(XF,), inc=True)
                else:
                    kb.op("v", lambda e: e.tensor_tensor(
                        ATTTMP[:, 0:512][0:64, :], PB[0:64, :],
                        self.NRM[0:64, :], ALU.mult),
                        reads=(PB, self.AF2), writes=(ATTTMP,), inc=True)
                    kb.op("t", lambda e: e.matmul(
                        PC[:], self.SH64[:], ATTTMP[0:64, 0:512],
                        start=True, stop=True),
                        reads=(self.SH64, ATTTMP), writes=(PC,), inc=True)
                    kb.op("s", lambda e, dc=dc: e.activation(
                        ATTC2(dc, 1), PC[64:128, :], AF.Copy),
                        reads=(PC,), writes=(XF,), inc=True)
            for tl in range(4):
                tb = ib * 4 + tl
                for seg in range(2):
                    for c in range(8):
                        l = XF[:, 8192 + c * 512 + tl * 128:
                               8192 + c * 512 + tl * 128 + 128]
                        r = WOUT[:, c * 1024 + seg * 512: c * 1024 + seg * 512 + 512]
                        kb.op("t", lambda e, l=l, r=r, c=c: e.matmul(
                            PD[:], l, r, start=(c == 0), stop=(c == 7)),
                            reads=(XF, WOUT), writes=(PD,), inc=(c == 7))
                    xap = XRES[:, tb * D + seg * 512: tb * D + seg * 512 + 512]
                    kb.op("v", lambda e, xap=xap: e.tensor_add(xap, xap, PD[:]),
                          reads=(XRES, PD), writes=(XRES,), inc=True)
        for tb in range(NTB):
            mcol, rcol = COLS[:, 0:1], COLS[:, 1:2]
            self.ln_stats(XRES[:, tb * D:(tb + 1) * D], XRES, mcol, rcol)
            nb = COLS[:, 3:4]
            kb.op("v", lambda e: e.tensor_tensor(nb, mcol, rcol, ALU.mult),
                  reads=(COLS,), writes=(COLS,))
            kb.op("v", lambda e: e.tensor_scalar(nb, nb, -1.0, None, ALU.mult),
                  reads=(COLS,), writes=(COLS,))
            kb.op("v", lambda e, tb=tb: e.scalar_tensor_tensor(
                XRES[:, tb * D:(tb + 1) * D], XRES[:, tb * D:(tb + 1) * D],
                rcol, nb.broadcast_to([128, D]), ALU.mult, ALU.add),
                reads=(XRES, COLS), writes=(XRES,), inc=True)


    def _ind_row(self, hh):
        return self.row_ap(self._indr_off + hh * 128, 128)


    def allreduce_max_row(self, row_off, n):
        kb = self.kb
        if not hasattr(self, "_cc_stage"):
            self._cc_stage = self.rows(8)
        st = self._cc_stage
        kb.op("v", lambda e: e.memset(self.row_ap(st, 8), 0.0), writes=(self.ROWS,))
        kb.op("v", lambda e: e.tensor_copy(self.row_ap(st, n), self.row_ap(row_off, n)),
              reads=(self.ROWS,), writes=(self.ROWS,))
        kb.dma("q", bass.AP(self.CCI, 0, [[8, 1], [1, 8]]),
               self.row_ap(st, 8), reads=(self.ROWS,), writes=(self.CCI,),
               stream="cc")
        kb.op("g", lambda e: e.collective_compute(
            "AllReduce", ALU.max, replica_groups=[list(range(8))],
            ins=[self.CCI.ap().opt()], outs=[self.CCO.ap().opt()]),
            reads=(self.CCI,), writes=(self.CCO,), inc=True)
        kb.dma("q", self.row_ap(row_off, n), bass.AP(self.CCO, 0, [[8, 1], [1, n]]),
               reads=(self.CCO,), writes=(self.ROWS,), stream="cc")

    def col_max_to_row(self, col_ap, row_off):
        kb = self.kb
        kb.op("t", lambda e: e.matmul(self.PTF[0:1, 0:128], col_ap, self.IDF[:],
                                      start=True, stop=True),
              reads=(self.COLS, self.IDF), writes=(self.P7,), inc=True)
        kb.op("v", lambda e: e.tensor_copy(self.row_ap(row_off + 1, 128),
                                           self.PTF[0:1, 0:128]),
              reads=(self.P7,), writes=(self.ROWS,), inc=True)
        kb.op("v", lambda e: e.reduce_max(self.row_ap(row_off, 1),
                                          self.row_ap(row_off + 1, 128), axis=AX.X),
              reads=(self.ROWS,), writes=(self.ROWS,))

    def bcast_scalar_col(self, scr_idx, col_ap, tgt):
        self.kb.dma("q", col_ap, bass.AP(self.SCR, scr_idx, [[0, 128], [1, 1]]),
                    reads=(self.SCR,), writes=(tgt,), stream="cc")

    def phase_l0f(self):
        kb, nc = self.kb, self.nc
        XRES, XF = self.XRES, self.XF
        PA, PB, PC, PD = self.PA, self.PB, self.PC, self.PD
        P4, P5, P6, P7 = self.P4, self.P5, self.P6, self.P7
        ROWS, COLS = self.ROWS, self.COLS
        QF = self.QT.ap().bitcast(F32)
        QI = self.QT.ap().bitcast(I32)
        QB = self.QT

        self.rows_reset(self._l0f_rmark if hasattr(self, "_l0f_rmark") else self.rows_mark())

        mx = COLS[:, 0:1]
        t = COLS[:, 1:2]
        kb.op("v", lambda e: e.memset(mx, 0.0), writes=(COLS,))
        for tb in range(NTB):
            kb.op("v", lambda e, tb=tb: e.reduce_max(
                t, XRES[:, tb * D:(tb + 1) * D], axis=AX.X,
                apply_absolute_value=True),
                reads=(XRES,), writes=(COLS,))
            kb.op("v", lambda e: e.tensor_max(mx, mx, t), reads=(COLS,), writes=(COLS,))
        ro = self.rows(160)
        self.col_max_to_row(mx, ro)
        self.allreduce_max_row(ro, 1)
        r1 = self.row_ap(ro, 1)
        r2 = self.row_ap(ro + 1, 1)
        kb.op("v", lambda e: e.tensor_scalar(r1, r1, 1.0 / 32767.0, 1e-12,
                                             ALU.mult, ALU.max),
              reads=(ROWS,), writes=(ROWS,))
        kb.op("v", lambda e: e.reciprocal(r2, r1), reads=(ROWS,), writes=(ROWS,))
        kb.dma("q", bass.AP(self.SCR, 0, [[8, 1], [1, 2]]), self.row_ap(ro, 2),
               reads=(ROWS,), writes=(self.SCR,), stream="cc")
        SXc, ISXc = self.SCC[:, 0:1], self.SCC[:, 1:2]
        self.bcast_scalar_col(0, SXc, self.SCC)
        self.bcast_scalar_col(1, ISXc, self.SCC)

        def fround(dst_ap, dst_t, src_ap, src_t, scale, Xi, Xf, Yf):
            sc = (self.SCC,) if isinstance(scale, bass.AP) else ()
            s0 = 1.0 if scale is None else scale
            kb.op("v", lambda e: e.tensor_scalar(Xi, src_ap, s0, 0.5,
                                                 ALU.mult, ALU.add),
                  reads=(src_t,) + sc, writes=(self.QT,))
            kb.op("v", lambda e: e.tensor_copy(Yf, Xi),
                  reads=(self.QT,), writes=(self.QT,))
            if scale is None:
                kb.op("v", lambda e: e.tensor_tensor(Xf, src_ap, Yf, ALU.subtract),
                      reads=(src_t, self.QT), writes=(self.QT,))
            else:
                kb.op("v", lambda e: e.scalar_tensor_tensor(
                    Xf, src_ap, scale, Yf, ALU.mult, ALU.subtract),
                    reads=(src_t, self.QT) + sc, writes=(self.QT,))
            kb.op("v", lambda e: e.tensor_scalar(Xf, Xf, -0.5, None, ALU.is_lt),
                  reads=(self.QT,), writes=(self.QT,))
            kb.op("v", lambda e: e.tensor_tensor(dst_ap, Yf, Xf, ALU.subtract),
                  reads=(self.QT,), writes=(dst_t,), inc=True)
        self._fround = fround

        for tb in range(NTB):
            fround(self.XCB[:], self.AF2, XRES[:, tb * D:(tb + 1) * D], XRES,
                   ISXc, QI[:, 0:1024], QF[:, 0:1024], QF[:, 1024:2048])
            self.tm_tile_to_fm(lambda fc: self.XCB[:, fc * 128:(fc + 1) * 128],
                               XF, lambda fc: fc * T, tb)
        if DEV_PROBE == "xq":
            kb.dma("q", self.dbg_t.ap(), XF[:, 0:8192], reads=(XF,),
                   writes=(self.dbg_t,), stream="probe")

        def dequant_chunk(wraw_ap, wraw_t, wdq_ap, wdq_t, NI):
            ng = NI // 32
            bmx = COLS[:, 0:ng] if ng <= 32 else None
            assert ng <= 32 or NI == 3072
            if NI == 3072:
                bmx = self.SCC
            bmx = COLS[:, 0:ng] if ng <= 32 else self.AF2[:, 0:ng]
            bmx_t = COLS if ng <= 32 else self.AF2
            inv = (COLS[:, 32:32 + ng] if ng <= 32 else self.AF2[:, 96:96 + ng])
            kb.op("v", lambda e: e.reduce_max(
                bmx, wraw_ap.rearrange("p (g k) -> p g k", k=32),
                axis=AX.X, apply_absolute_value=True),
                reads=(wraw_t,), writes=(bmx_t,))
            kb.op("v", lambda e: e.tensor_scalar(bmx, bmx, 1.0 / 7.0, 1e-12,
                                                 ALU.mult, ALU.max),
                  reads=(bmx_t,), writes=(bmx_t,))
            kb.op("v", lambda e: e.reciprocal(inv, bmx), reads=(bmx_t,), writes=(bmx_t,))
            tw = QF[:, 5120:5120 + NI]
            ti = QI[:, 5120 + NI:5120 + 2 * NI]
            tf2 = QF[:, 5120 + 2 * NI:5120 + 3 * NI]
            kb.op("v", lambda e: e.tensor_tensor(
                tw.rearrange("p (g k) -> p g k", k=32),
                wraw_ap.rearrange("p (g k) -> p g k", k=32),
                inv.unsqueeze(2).broadcast_to([128, ng, 32]), ALU.mult),
                reads=(wraw_t, bmx_t), writes=(self.QT,))
            tiF = QF[:, 5120 + NI:5120 + 2 * NI]
            kb.op("v", lambda e: e.tensor_scalar(ti, tw, 1.0, 0.5, ALU.mult, ALU.add),
                  reads=(self.QT,), writes=(self.QT,))
            kb.op("v", lambda e: e.tensor_copy(tf2, ti),
                  reads=(self.QT,), writes=(self.QT,))
            kb.op("v", lambda e: e.tensor_tensor(tiF, tw, tf2, ALU.subtract),
                  reads=(self.QT,), writes=(self.QT,))
            kb.op("v", lambda e: e.tensor_scalar(tiF, tiF, -0.5, None, ALU.is_lt),
                  reads=(self.QT,), writes=(self.QT,))
            kb.op("v", lambda e: e.tensor_tensor(tf2, tf2, tiF, ALU.subtract),
                  reads=(self.QT,), writes=(self.QT,))
            kb.op("v", lambda e: e.tensor_tensor(
                wdq_ap.rearrange("p (g k) -> p g k", k=32),
                tf2.rearrange("p (g k) -> p g k", k=32),
                bmx.unsqueeze(2).broadcast_to([128, ng, 32]), ALU.mult),
                reads=(self.QT, bmx_t), writes=(wdq_t,))

        WF = self.WB0.ap().bitcast(F32)
        kb.barrier()
        STK = [RK(f"st{i}") for i in range(4)]
        mu_c, mg_c = COLS[:, 64:65], COLS[:, 65:66]
        kb.op("v", lambda e: e.memset(COLS[:, 64:66], 0.0), writes=(COLS,))
        for wi, (wsrc, spill, mxc) in enumerate(
                [(self.wu0, self.UPS, mu_c), (self.wg0, self.GTS, mg_c)]):
            for oc in range(24):
                par = oc % 2
                wraw = WF[:, par * 1024:(par + 1) * 1024]
                kb.dma("q", wraw,
                       bass.AP(wsrc, oc * 128 * 1024, [[1024, 128], [1, 1024]]),
                       writes=(self.WB0,), stream="wraw")
                wdq = self.WB0[:, 4096 + par * 1024: 4096 + (par + 1) * 1024]
                dequant_chunk(wraw, self.WB0, wdq, self.WB0, 1024)
                for it in range(NFC):
                    kb.op("t", lambda e, it=it: e.transpose(
                        self.PTB, wdq[:, it * 128:(it + 1) * 128], self.IDB[:]),
                        reads=(self.WB0, self.IDB), writes=(self.P7,), inc=True)
                    kb.op("s", lambda e, it=it, par=par: e.activation(
                        self.WB1[:, par * 1024 + it * 128: par * 1024 + it * 128 + 128],
                        self.PTB, AF.Copy),
                        reads=(self.P7,), writes=(self.WB1,), inc=True)
                for seg in range(4):
                    ps = (PA, PB, PC, PD)[seg]
                    for ic in range(NFC):
                        l = self.WB1[:, par * 1024 + ic * 128: par * 1024 + ic * 128 + 128]
                        r = XF[:, ic * T + seg * 512: ic * T + seg * 512 + 512]
                        kb.op("t", lambda e, l=l, r=r, ic=ic, ps=ps: e.matmul(
                            ps[:], l, r, start=(ic == 0), stop=(ic == NFC - 1)),
                            reads=(self.WB1, XF), writes=(ps,), inc=(ic == NFC - 1))
                    st = QF[:, 3072 + seg * 512: 3072 + (seg + 1) * 512]
                    kb.op("s", lambda e, st=st, ps=ps: e.activation(
                        st, ps[:], AF.Copy, scale=SXc),
                        reads=(ps, self.SCC), writes=(STK[seg],), inc=True)
                    kb.op("v", lambda e, st=st: e.reduce_max(
                        COLS[:, 66:67], st, axis=AX.X, apply_absolute_value=True),
                        reads=(STK[seg],), writes=(COLS,))
                    kb.op("v", lambda e, mxc=mxc: e.tensor_max(mxc, mxc, COLS[:, 66:67]),
                          reads=(COLS,), writes=(COLS,))
                    kb.dma("s2", bass.AP(spill, oc * 128 * T + seg * 512,
                                         [[T, 128], [1, 512]]), st,
                           reads=(STK[seg],), writes=(spill,), stream=f"sp{seg}")
        self.col_max_to_row(mu_c, ro)
        kb.op("v", lambda e: e.tensor_copy(self.row_ap(ro + 140, 1), self.row_ap(ro, 1)),
              reads=(ROWS,), writes=(ROWS,))
        self.col_max_to_row(mg_c, ro)
        kb.op("v", lambda e: e.tensor_copy(self.row_ap(ro + 1, 1), self.row_ap(ro + 140, 1)),
              reads=(ROWS,), writes=(ROWS,))
        self.allreduce_max_row(ro, 2)
        rMg = self.row_ap(ro, 1)
        rMu = self.row_ap(ro + 1, 1)
        CONST_SO = 32767.0 * 65535.0 / 65536.0
        rsg = self.row_ap(ro + 2, 1); rsu = self.row_ap(ro + 3, 1)
        risg = self.row_ap(ro + 4, 1); risu = self.row_ap(ro + 5, 1)
        rso = self.row_ap(ro + 6, 1)
        kb.op("v", lambda e: e.tensor_scalar(rsg, rMg, 1.0 / 32767.0, 1e-12,
                                             ALU.mult, ALU.max), reads=(ROWS,), writes=(ROWS,))
        kb.op("v", lambda e: e.tensor_scalar(rsu, rMu, 1.0 / 32767.0, 1e-12,
                                             ALU.mult, ALU.max), reads=(ROWS,), writes=(ROWS,))
        kb.op("v", lambda e: e.reciprocal(risg, rsg), reads=(ROWS,), writes=(ROWS,))
        kb.op("v", lambda e: e.reciprocal(risu, rsu), reads=(ROWS,), writes=(ROWS,))
        kb.op("v", lambda e: e.tensor_tensor(rso, rsg, rsu, ALU.mult),
              reads=(ROWS,), writes=(ROWS,))
        kb.op("v", lambda e: e.tensor_scalar(rso, rso, CONST_SO, None, ALU.mult),
              reads=(ROWS,), writes=(ROWS,))
        kb.op("v", lambda e: e.tensor_copy(self.row_ap(ro + 8, 1), rsg), reads=(ROWS,), writes=(ROWS,))
        kb.op("v", lambda e: e.tensor_copy(self.row_ap(ro + 9, 1), risg), reads=(ROWS,), writes=(ROWS,))
        kb.op("v", lambda e: e.tensor_copy(self.row_ap(ro + 10, 1), risu), reads=(ROWS,), writes=(ROWS,))
        kb.op("v", lambda e: e.tensor_copy(self.row_ap(ro + 11, 1), rso), reads=(ROWS,), writes=(ROWS,))
        kb.op("v", lambda e: e.tensor_scalar(self.row_ap(ro + 12, 1), rsg,
                                             -32768.0, None, ALU.mult),
              reads=(ROWS,), writes=(ROWS,))
        kb.dma("q", bass.AP(self.SCR, 2, [[8, 1], [1, 5]]), self.row_ap(ro + 8, 5),
               reads=(ROWS,), writes=(self.SCR,), stream="cc")
        SGc, ISGc = self.SCC[:, 2:3], self.SCC[:, 3:4]
        ISUc, SOc, NSGc = self.SCC[:, 4:5], self.SCC[:, 5:6], self.SCC[:, 6:7]
        self.bcast_scalar_col(2, SGc, self.SCC)
        self.bcast_scalar_col(3, ISGc, self.SCC)
        self.bcast_scalar_col(4, ISUc, self.SCC)
        self.bcast_scalar_col(5, SOc, self.SCC)
        self.bcast_scalar_col(6, NSGc, self.SCC)
        if DEV_PROBE == "scc":
            kb.dma("q", bass.AP(self.dbgf_t, 0, [[2048, 128], [1, 7]]),
                   self.SCC[:, 0:7], reads=(self.SCC,), writes=(self.dbgf_t,),
                   stream="probe")
        C1 = 1.0 / 65535.0
        C2 = 1.0 / 32767.0

        for hc in range(8):
            par = hc % 2
            for sub in range(3):
                wraw = WF[:, par * 1024:(par + 1) * 1024]
                kb.dma("q", wraw,
                       bass.AP(self.wd0, hc * 128 * 3072 + sub * 1024,
                               [[3072, 128], [1, 1024]]),
                       writes=(self.WB0,), stream="wraw")
                wdq = self.WB0[:, 4096 + par * 1024: 4096 + (par + 1) * 1024]
                dequant_chunk(wraw, self.WB0, wdq, self.WB0, 1024)
                for it in range(NFC):
                    ic = sub * 8 + it
                    kb.op("t", lambda e, it=it: e.transpose(
                        self.PTB, wdq[:, it * 128:(it + 1) * 128], self.IDB[:]),
                        reads=(self.WB0, self.IDB), writes=(self.P7,), inc=True)
                    dst = self.SBbf0[:, 0:128] if ic % 2 == 0 else self.SBbf1[:, 0:128]
                    dst_t = self.SBbf0 if ic % 2 == 0 else self.SBbf1
                    kb.op("s", lambda e, dst=dst: e.activation(dst, self.PTB, AF.Copy),
                          reads=(self.P7,), writes=(dst_t,), inc=True)
                    kb.dma("s2", bass.AP(self.WDNT, ic * 128 * 1024 + hc * 128,
                                         [[1024, 128], [1, 128]]), dst,
                           reads=(dst_t,), writes=(self.WDNT,), stream="wdnt")
        kb.barrier()
        NGF = 3
        NWS = 4
        GFs = [QF[:, i * 512:(i + 1) * 512] for i in range(NGF)]
        UFs = [QF[:, 1536 + i * 512: 1536 + (i + 1) * 512] for i in range(NGF)]
        GFK = [RK(f"gf{i}") for i in range(NGF)]
        UFK = [RK(f"uf{i}") for i in range(NGF)]
        A_, B_, C_ = QF[:, 3072:3584], QF[:, 3584:4096], QF[:, 4096:4608]
        Xv, XvI = QF[:, 4608:5120], QI[:, 4608:5120]
        Yv = QF[:, 5120:5632]
        RU = QF[:, 5632:6144]
        Xg, XgI = QF[:, 6144:6656], QI[:, 6144:6656]
        Yg = QF[:, 6656:7168]
        OBs = (QB[:, 14336:14848], QB[:, 14848:15360])
        CHK, CHG = RK("chainV"), RK("chainG")
        OBK = (RK("ob0"), RK("ob1"))
        WSK = [RK(f"ws{i}") for i in range(NWS)]
        SCCK, XRK = RK("sccS3"), RK("xresS3")
        psl = (PA, PB, PC, PD, P4, P5, P6, P7)
        PSK3 = {id(p): RK(f"p5k{i}") for i, p in enumerate(psl)}

        def fround3(eng, dst_ap, dst_k, src_ap, src_k, scale, Xi, Xf, Yf, tmp_k):
            sc = (SCCK,) if isinstance(scale, bass.AP) else ()
            s0 = 1.0 if scale is None else scale
            kb.op(eng, lambda e: e.tensor_scalar(Xi, src_ap, s0, 0.5,
                                                 ALU.mult, ALU.add),
                  reads=(src_k,) + sc, writes=(tmp_k,))
            kb.op(eng, lambda e: e.tensor_copy(Yf, Xi),
                  reads=(tmp_k,), writes=(tmp_k,))
            if scale is None:
                kb.op(eng, lambda e: e.tensor_tensor(Xf, src_ap, Yf, ALU.subtract),
                      reads=(src_k, tmp_k), writes=(tmp_k,))
            elif eng == "g":
                Zg = QF[:, 7680:8192]
                kb.op(eng, lambda e: e.tensor_scalar(Zg, src_ap, s0, None,
                                                     ALU.mult),
                      reads=(src_k,), writes=(tmp_k,))
                kb.op(eng, lambda e: e.tensor_tensor(Xf, Zg, Yf, ALU.subtract),
                      reads=(tmp_k,), writes=(tmp_k,))
            else:
                kb.op(eng, lambda e: e.scalar_tensor_tensor(
                    Xf, src_ap, scale, Yf, ALU.mult, ALU.subtract),
                    reads=(src_k, tmp_k) + sc, writes=(tmp_k,))
            kb.op(eng, lambda e: e.tensor_scalar(Xf, Xf, -0.5, None, ALU.is_lt),
                  reads=(tmp_k,), writes=(tmp_k,))
            kb.op(eng, lambda e: e.tensor_tensor(dst_ap, Yf, Xf, ALU.subtract),
                  reads=(tmp_k,), writes=(dst_k,), inc=True)

        def ld_gu(n):
            tq, ic = divmod(n, 24)
            sl = n % NGF
            kb.dma("q", GFs[sl], bass.AP(self.GTS, ic * 128 * T + tq * 512,
                                         [[T, 128], [1, 512]]),
                   reads=(self.GTS,), writes=(GFK[sl],), stream=f"gf{sl}")
            kb.dma("s2", UFs[sl], bass.AP(self.UPS, ic * 128 * T + tq * 512,
                                          [[T, 128], [1, 512]]),
                   reads=(self.UPS,), writes=(UFK[sl],), stream=f"uf{sl}")

        def ld_w(n):
            tq, ic = divmod(n, 24)
            sl = n % NWS
            kb.dma("q", self.WB1[:, sl * 1024:(sl + 1) * 1024],
                   bass.AP(self.WDNT, ic * 128 * 1024, [[1024, 128], [1, 1024]]),
                   reads=(self.WDNT,), writes=(WSK[sl],), stream=f"wds{sl}")

        for n in range(NGF):
            ld_gu(n)
        for n in range(NWS):
            ld_w(n)
        for tq in range(4):
            for ic in range(24):
                n = tq * 24 + ic
                par = ic % 2
                gsl, wsl = n % NGF, n % NWS
                ob = OBs[par]
                fround3("v", A_, CHK, GFs[gsl], GFK[gsl], ISGc, XvI, Xv, Yv, CHK)
                kb.op("s", lambda e: e.activation(C_, A_, AF.Sigmoid, scale=SGc),
                      reads=(CHK, SCCK), writes=(CHK,), inc=True)
                fround3("v", B_, CHG, C_, CHK, 65536.0, XgI, Xg, Yg, CHG)
                fround3("v", RU, CHK, UFs[gsl], UFK[gsl], ISUc, XvI, Xv, Yv, CHK)
                kb.op("v", lambda e: e.tensor_tensor(A_, A_, B_, ALU.mult),
                      reads=(CHK, CHG), writes=(CHK,))
                fround3("v", B_, CHG, A_, CHK, C1, XgI, Xg, Yg, CHG)
                kb.op("v", lambda e: e.tensor_tensor(A_, B_, RU, ALU.mult),
                      reads=(CHK, CHG), writes=(CHK,))
                fround3("v", ob, OBK[par], A_, CHK, C2, XgI, Xg, Yg, CHG)
                for tl in range(4):
                    for hs in range(2):
                        ps = psl[tl * 2 + hs]
                        l = ob[:, tl * 128:(tl + 1) * 128]
                        r = self.WB1[:, wsl * 1024 + hs * 512:
                                     wsl * 1024 + hs * 512 + 512]
                        kb.op("t", lambda e, l=l, r=r, ic=ic, ps=ps: e.matmul(
                            ps[:], l, r, start=(ic == 0), stop=(ic == 23)),
                            reads=(OBK[par], WSK[wsl]), writes=(PSK3[id(ps)],),
                            inc=(ic == 23))
                if n + NGF < 96:
                    ld_gu(n + NGF)
                if n + NWS < 96:
                    ld_w(n + NWS)
            for tl in range(4):
                tb = tq * 4 + tl
                for hs in range(2):
                    ps = psl[tl * 2 + hs]
                    xap = XRES[:, tb * D + hs * 512: tb * D + hs * 512 + 512]
                    kb.op("v", lambda e, xap=xap, ps=ps: e.scalar_tensor_tensor(
                        xap, ps[:], SOc, xap, ALU.mult, ALU.add),
                        reads=(XRK, PSK3[id(ps)], SCCK), writes=(XRK,), inc=True)
        kb.barrier()
        for tb in range(NTB):
            mcol, rcol = COLS[:, 0:1], COLS[:, 1:2]
            self.ln_stats(XRES[:, tb * D:(tb + 1) * D], XRES, mcol, rcol)
            nb = COLS[:, 3:4]
            kb.op("v", lambda e: e.tensor_tensor(nb, mcol, rcol, ALU.mult),
                  reads=(COLS,), writes=(COLS,))
            kb.op("v", lambda e: e.tensor_scalar(nb, nb, -1.0, None, ALU.mult),
                  reads=(COLS,), writes=(COLS,))
            kb.op("v", lambda e, tb=tb: e.scalar_tensor_tensor(
                XRES[:, tb * D:(tb + 1) * D], XRES[:, tb * D:(tb + 1) * D],
                rcol, nb.broadcast_to([128, D]), ALU.mult, ALU.add),
                reads=(XRES, COLS), writes=(XRES,), inc=True)

    def finish_head(self, av_ps, dst_even_ap, dst_odd_ap, hh, dst_t, n_i=512):
        kb = self.kb
        PC = self.PC
        kb.op("v", lambda e: e.reciprocal(self.NRM[64:65, 0:n_i], av_ps[64:65, 0:n_i]),
              reads=(av_ps,), writes=(self.AF2,), inc=True)
        kb.op("t", lambda e: e.matmul(
            PC[0:64, 0:n_i], self.ONER[64:65, 0:64], self.NRM[64:65, 0:n_i],
            start=True, stop=True),
            reads=(self.AF2, self.ONER), writes=(PC,), inc=True)
        kb.op("s", lambda e: e.activation(self.NRM[0:64, 0:n_i], PC[0:64, 0:n_i],
                                          AF.Copy),
              reads=(PC,), writes=(self.AF2,), inc=True)
        if hh == 0:
            kb.op("v", lambda e: e.tensor_tensor(
                dst_even_ap, av_ps[0:64, 0:n_i], self.NRM[0:64, 0:n_i], ALU.mult),
                reads=(av_ps, self.AF2), writes=(dst_t,), inc=True)
        else:
            tmp = self.SBbf0[0:64, 0:n_i]
            kb.op("v", lambda e: e.tensor_tensor(
                tmp, av_ps[0:64, 0:n_i], self.NRM[0:64, 0:n_i], ALU.mult),
                reads=(av_ps, self.AF2), writes=(self.SBbf0,), inc=True)
            kb.op("t", lambda e: e.matmul(
                PC[:, 0:n_i], self.SH64[:], tmp, start=True, stop=True),
                reads=(self.SH64, self.SBbf0), writes=(PC,), inc=True)
            kb.op("s", lambda e: e.activation(dst_odd_ap, PC[64:128, 0:n_i], AF.Copy),
                  reads=(PC,), writes=(dst_t,), inc=True)

    def phase_l1a(self):
        kb, nc = self.kb, self.nc
        XRES, XF, QT, VTL = self.XRES, self.XF, self.QT, self.VTL
        PA, PB, PC, PD = self.PA, self.PB, self.PC, self.PD
        ROWS, COLS = self.ROWS, self.COLS
        self.rows_reset(self._base_rows)

        kb.dma("q", self.MSK[:, 0:5 * 512],
               bass.AP(self.masks_d, 6 * 512, [[11 * 512, 128], [1, 5 * 512]]),
               writes=(self.MSK,), stream="msk")
        kb.dma("q", self.BL1[:, 0:12], bass.AP(self.blqkv, 0, [[1, 128], [128, 12]]),
               writes=(self.BL1,), allow_slow_non_contiguous=True, stream="msk")
        kb.dma("q", self.BL1[:, 12:24], bass.AP(self.bgqkv, 0, [[1, 128], [128, 12]]),
               writes=(self.BL1,), allow_slow_non_contiguous=True, stream="msk")
        kb.dma("q", self.BL1[:, 24:28], bass.AP(self.bcomp, 512, [[1, 128], [128, 4]]),
               writes=(self.BL1,), allow_slow_non_contiguous=True, stream="msk")
        kb.dma("q", self.BL1[:, 28:32], bass.AP(self.blproj, 0, [[1, 128], [128, 4]]),
               writes=(self.BL1,), allow_slow_non_contiguous=True, stream="msk")
        kb.dma("q", self.BL1[:, 32:36], bass.AP(self.bgproj, 0, [[1, 128], [128, 4]]),
               writes=(self.BL1,), allow_slow_non_contiguous=True, stream="msk")
        blv_o = self.rows(512)
        bgv_o = self.rows(512)
        bg_o = self.rows(2)
        kb.dma("q", self.row_ap(blv_o, 512), bass.AP(self.blqkv, 1024, [[512, 1], [1, 512]]),
               writes=(ROWS,), stream="msk")
        kb.dma("q", self.row_ap(bgv_o, 512), bass.AP(self.bgqkv, 1024, [[512, 1], [1, 512]]),
               writes=(ROWS,), stream="msk")
        kb.dma("q", self.row_ap(bg_o, 2), bass.AP(self.bgate1, 0, [[2, 1], [1, 2]]),
               writes=(ROWS,), stream="msk")
        self.load_w_chunks(self.WB0, 0, self.wlqkv, 512, 1536, nsplit=2)
        self.load_w_chunks(self.WB1, 0, self.wgqkv, 512, 1536, nsplit=2)

        for tb in range(NTB):
            kb.op("v", lambda e, tb=tb: e.tensor_copy(
                self.XCB[:], XRES[:, tb * D:(tb + 1) * D]),
                reads=(XRES,), writes=(self.AF2,), inc=True)
            self.tm_tile_to_fm(lambda fc: self.XCB[:, fc * 128:(fc + 1) * 128],
                               XF, lambda fc: fc * T, tb)

        if DEV_PROBE == "xf":
            kb.dma("q", self.dbg_t.ap(), XF[:, 0:8192], reads=(XF,),
                   writes=(self.dbg_t,), stream="probe")
        CGF = QT[:, 8256:10304]
        for ci in range(4):
            n = 0
            for r in range(4):
                for ic in range(NFC):
                    wt = self.SBbf1[:, 0:128] if n % 2 == 0 else self.SBbf1[:, 128:256]
                    kb.dma("g2", wt, bass.AP(self.wcomp,
                           (r * 1024 + ic * 128) * 1024 + 512 + ci * 128,
                           [[1024, 128], [1, 128]]),
                           writes=(self.SBbf1,), stream=f"wcmp{n % 2}")
                    rhs = XF[:, ic * T + r: ic * T + r + 2045: 4]
                    kb.op("t", lambda e, wt=wt, rhs=rhs, n=n: e.matmul(
                        PA[:], wt, rhs, start=(n == 0), stop=(n == 31)),
                        reads=(self.SBbf1, XF), writes=(PA,), inc=(n == 31))
                    n += 1
            kb.op("s", lambda e, ci=ci: e.activation(
                CGF[:, ci * 512:(ci + 1) * 512], PA[:], AF.Identity,
                bias=self.BL1[:, 24 + ci:25 + ci]),
                reads=(PA, self.BL1), writes=(QT,), inc=True)

        GQ = lambda c: QT[:, 2048 + c * 512: 2048 + (c + 1) * 512]
        GK = lambda c: QT[:, 4096 + c * 512: 4096 + (c + 1) * 512]
        for oc in range(8):
            for ic in range(4):
                l = self.WB1[:, ic * 1536 + oc * 128: ic * 1536 + oc * 128 + 128]
                r = CGF[:, ic * 512:(ic + 1) * 512]
                kb.op("t", lambda e, l=l, r=r, ic=ic: e.matmul(
                    PB[:], l, r, start=(ic == 0), stop=(ic == 3)),
                    reads=(self.WB1, QT), writes=(PB,), inc=(ic == 3))
            dst = GQ(oc) if oc < 4 else GK(oc - 4)
            kb.op("s", lambda e, dst=dst, oc=oc: e.activation(
                dst, PB[:], AF.Identity, bias=self.BL1[:, 12 + oc:13 + oc]),
                reads=(PB, self.BL1), writes=(QT,), inc=True)
        for tt in range(4):
            for ic in range(4):
                l = CGF[:, ic * 512 + tt * 128: ic * 512 + tt * 128 + 128]
                r = self.WB1[:, ic * 1536 + 1024: ic * 1536 + 1536]
                kb.op("t", lambda e, l=l, r=r, ic=ic: e.matmul(
                    PB[:], l, r, start=(ic == 0), stop=False),
                    reads=(self.WB1, QT), writes=(PB,))
            kb.op("t", lambda e: e.matmul(
                PB[:], self.ONER[0:1, 0:128], self.row_ap(bgv_o, 512),
                start=False, stop=True),
                reads=(self.ONER, ROWS), writes=(PB,), inc=True)
            for h in range(8):
                off = 6144 + tt * 528 + h * 66
                kb.op("s", lambda e, off=off, h=h: e.activation(
                    QT[:, off:off + 64], PB[:, h * 64:(h + 1) * 64], AF.Copy),
                    reads=(PB,), writes=(QT,), inc=True)
                kb.op("v", lambda e, off=off: e.memset(QT[:, off + 64:off + 65], 1.0),
                      writes=(QT,))
        ATTG = lambda c, hh: QT[64 * hh:64 * hh + 64, c * 512:(c + 1) * 512]
        for h in range(8):
            dc, hh = h // 2, h % 2
            for jt in range(4):
                pp = self.P4 if jt % 2 == 0 else self.P5
                kb.op("t", lambda e, dc=dc, hh=hh, jt=jt, pp=pp: e.matmul(
                    pp[0:128, :], GK(dc)[64 * hh:64 * hh + 64, jt * 128:(jt + 1) * 128],
                    GQ(dc)[64 * hh:64 * hh + 64, :], start=True, stop=True),
                    reads=(QT,), writes=(pp,), inc=True)
                pt = self.SBbf1[:, 0:512] if jt % 2 == 0 else self.SBbf0[:, 0:512]
                ptt = self.SBbf1 if jt % 2 == 0 else self.SBbf0
                kb.op("s", lambda e, pt=pt, pp=pp: e.activation(pt, pp[:], AF.Exp, scale=0.125),
                      reads=(pp,), writes=(ptt,), inc=True)
                kb.op("v", lambda e, pt=pt, jt=jt: e.tensor_tensor(
                    pt, pt, self.MSK[:, jt * 512:(jt + 1) * 512], ALU.mult),
                    reads=(ptt, self.MSK), writes=(ptt,), inc=True)
                kb.op("t", lambda e, jt=jt, h=h, pt=pt: e.matmul(
                    PB[0:65, :], QT[:, 6144 + jt * 528 + h * 66: 6144 + jt * 528 + h * 66 + 65],
                    pt, start=(jt == 0), stop=(jt == 3)),
                    reads=(QT, ptt), writes=(PB,), inc=(jt == 3))
            self.finish_head(PB, ATTG(dc, 0), ATTG(dc, 1), hh, QT)
        GOF = self.WB1[:, 6144:8192]
        for oc in range(4):
            for ic in range(4):
                l_ap = bass.AP(self.wgproj, ic * 128 * 512 + oc * 128,
                               [[512, 128], [1, 128]])
                wt = self.SBbf1[:, 0:128] if ic % 2 == 0 else self.SBbf1[:, 128:256]
                kb.dma("g2", wt, l_ap, writes=(self.SBbf1,), stream=f"wcmp{ic % 2}")
                r = QT[:, ic * 512:(ic + 1) * 512]
                kb.op("t", lambda e, wt=wt, r=r, ic=ic: e.matmul(
                    PB[:], wt, r, start=(ic == 0), stop=(ic == 3)),
                    reads=(self.SBbf1, QT), writes=(PB,), inc=(ic == 3))
            kb.op("s", lambda e, oc=oc: e.activation(
                GOF[:, oc * 512:(oc + 1) * 512], PB[:], AF.Identity,
                bias=self.BL1[:, 32 + oc:33 + oc]),
                reads=(PB, self.BL1), writes=(self.WB1,), inc=True)

        for oc in range(8):
            for seg in range(4):
                for ic in range(4):
                    l = self.WB0[:, ic * 1536 + oc * 128: ic * 1536 + oc * 128 + 128]
                    r = XF[:, ic * T + seg * 512: ic * T + seg * 512 + 512]
                    kb.op("t", lambda e, l=l, r=r, ic=ic: e.matmul(
                        PA[:], l, r, start=(ic == 0), stop=(ic == 3)),
                        reads=(self.WB0, XF), writes=(PA,), inc=(ic == 3))
                base = (oc % 4) * 2048 + (0 if oc < 4 else 8192)
                kb.op("s", lambda e, base=base, seg=seg, oc=oc: e.activation(
                    QT[:, base + seg * 512: base + seg * 512 + 512], PA[:], AF.Identity,
                    bias=self.BL1[:, oc:oc + 1]),
                    reads=(PA, self.BL1), writes=(QT,), inc=True)
        LQ = lambda c: QT[:, c * 2048: (c + 1) * 2048]
        LK = lambda c: QT[:, 8192 + c * 2048: 8192 + (c + 1) * 2048]
        ATTC2L = lambda ib, c, hh: XF[64 * hh:64 * hh + 64,
                                      8192 + ib * 2048 + c * 512:
                                      8192 + ib * 2048 + (c + 1) * 512]
        for hg in range(2):
            for tt in range(16):
                for ic in range(4):
                    l = XF[:, ic * T + tt * 128: ic * T + tt * 128 + 128]
                    r = self.WB0[:, ic * 1536 + 1024 + hg * 256:
                                  ic * 1536 + 1024 + hg * 256 + 256]
                    kb.op("t", lambda e, l=l, r=r, ic=ic: e.matmul(
                        PB[:, 0:256], l, r, start=(ic == 0), stop=False),
                        reads=(self.WB0, XF), writes=(PB,))
                kb.op("t", lambda e, hg=hg: e.matmul(
                    PB[:, 0:256], self.ONER[0:1, 0:128],
                    self.row_ap(blv_o + hg * 256, 256),
                    start=False, stop=True),
                    reads=(self.ONER, ROWS), writes=(PB,), inc=True)
                for hl in range(4):
                    off = tt * 264 + hl * 66
                    kb.op("s", lambda e, off=off, hl=hl: e.activation(
                        VTL[:, off:off + 64], PB[:, hl * 64:(hl + 1) * 64], AF.Copy),
                        reads=(PB,), writes=(VTL,), inc=True)
                    kb.op("v", lambda e, off=off: e.memset(VTL[:, off + 64:off + 65], 1.0),
                          writes=(VTL,))
            for ib in range(4):
                i0 = ib * 512
                for hl in range(4):
                    h = hg * 4 + hl
                    dc, hh = h // 2, h % 2
                    njt = 4 * ib + 4
                    for jt in range(njt):
                        pp = self.P4 if jt % 2 == 0 else self.P5
                        kb.op("t", lambda e, dc=dc, hh=hh, jt=jt, i0=i0, pp=pp: e.matmul(
                            pp[:], LK(dc)[64 * hh:64 * hh + 64, jt * 128:(jt + 1) * 128],
                            LQ(dc)[64 * hh:64 * hh + 64, i0:i0 + 512],
                            start=True, stop=True),
                            reads=(QT,), writes=(pp,), inc=True)
                        pt = self.SBbf1[:, 0:512] if jt % 2 == 0 else self.SBbf0[:, 0:512]
                        ptt = self.SBbf1 if jt % 2 == 0 else self.SBbf0
                        kb.op("s", lambda e, pt=pt, pp=pp: e.activation(pt, pp[:], AF.Exp,
                                                                 scale=0.125),
                              reads=(pp,), writes=(ptt,), inc=True)
                        kk = jt - 4 * ib
                        if 0 <= kk <= 4:
                            kb.op("v", lambda e, pt=pt, kk=kk: e.tensor_tensor(
                                pt, pt, self.MSK[:, kk * 512:(kk + 1) * 512], ALU.mult),
                                reads=(ptt, self.MSK), writes=(ptt,), inc=True)
                        kb.op("t", lambda e, jt=jt, hl=hl, pt=pt, njt=njt: e.matmul(
                            PB[0:65, :], VTL[:, jt * 264 + hl * 66: jt * 264 + hl * 66 + 65],
                            pt, start=(jt == 0), stop=(jt == njt - 1)),
                            reads=(VTL, ptt), writes=(PB,), inc=(jt == njt - 1))
                    self.finish_head(PB, ATTC2L(ib, dc, 0), ATTC2L(ib, dc, 1), hh, XF)
        for ib in range(4):
            i0 = ib * 512
            for oc in range(4):
                for ci in range(4):
                    l_ap = bass.AP(self.wlproj, ci * 128 * 512 + oc * 128,
                                   [[512, 128], [1, 128]])
                    wt = self.SBbf1[:, 0:128] if ci % 2 == 0 else self.SBbf1[:, 128:256]
                    kb.dma("g2", wt, l_ap, writes=(self.SBbf1,), stream=f"wcmp{ci % 2}")
                    r = XF[:, 8192 + ib * 2048 + ci * 512: 8192 + ib * 2048 + (ci + 1) * 512]
                    kb.op("t", lambda e, wt=wt, r=r, ci=ci: e.matmul(
                        PB[:], wt, r, start=(ci == 0), stop=(ci == 3)),
                        reads=(self.SBbf1, XF), writes=(PB,), inc=(ci == 3))
                kb.op("s", lambda e, oc=oc, i0=i0: e.activation(
                    XF[:, oc * 2048 + i0: oc * 2048 + i0 + 512], PB[:], AF.Identity,
                    bias=self.BL1[:, 28 + oc:29 + oc]),
                    reads=(PB, self.BL1), writes=(XF,), inc=True)

        if DEV_PROBE == "lo":
            kb.dma("q", self.dbg_t.ap(), XF[:, 0:8192], reads=(XF,),
                   writes=(self.dbg_t,), stream="probe")
        if DEV_PROBE == "go":
            kb.dma("q", bass.AP(self.dbg_t, 0, [[8192, 128], [1, 2048]]),
                   GOF[:], reads=(self.WB1,), writes=(self.dbg_t,), stream="probe")
        for ci in range(4):
            src = GOF[:, ci * 512:(ci + 1) * 512]
            dst = QT[:, ci * 2048:(ci + 1) * 2048]
            kb.op("v", lambda e, src=src, dst=dst: e.tensor_copy(
                dst.rearrange("p (a b) -> p a b", b=4),
                src.unsqueeze(2).broadcast_to([128, 512, 4])),
                reads=(self.WB1,), writes=(QT,), inc=True)

        dbr = self.rows(2)
        kb.op("v", lambda e: e.tensor_sub(self.row_ap(dbr, 1), self.row_ap(bg_o, 1),
                                          self.row_ap(bg_o + 1, 1)),
              reads=(ROWS,), writes=(ROWS,))
        kb.dma("q", bass.AP(self.SCR, 7, [[8, 1], [1, 1]]), self.row_ap(dbr, 1),
               reads=(ROWS,), writes=(self.SCR,), stream="cc")
        self.bcast_scalar_col(7, self.SCC[:, 7:8], self.SCC)
        WGT = self.SBbf1[:, 256:288]
        for n in range(8):
            kb.dma("g2", WGT[:, 2 * n:2 * n + 2],
                   bass.AP(self.wgate1, n * 128 * 2, [[2, 128], [1, 2]]),
                   writes=(self.SBbf1,), stream="wcmp")
        for tb in range(NTB):
            n = 0
            for src, srct in ((XF, XF), (QT, QT)):
                for ci in range(4):
                    l = src[:, ci * 2048 + tb * 128: ci * 2048 + tb * 128 + 128]
                    wt = WGT[:, 2 * n:2 * n + 2]
                    kb.op("t", lambda e, l=l, wt=wt, n=n: e.matmul(
                        PD[0:128, 0:2], l, wt, start=(n == 0), stop=(n == 7)),
                        reads=(srct, self.SBbf1), writes=(PD,), inc=(n == 7))
                    n += 1
            g0 = COLS[:, 70:71]
            g1 = COLS[:, 71:72]
            t_ = COLS[:, 72:73]
            kb.op("s", lambda e: e.activation(COLS[:, 78:80], PD[0:128, 0:2], AF.Copy),
                  reads=(PD,), writes=(COLS,), inc=True)
            kb.op("v", lambda e: e.tensor_sub(t_, COLS[:, 78:79], COLS[:, 79:80]),
                  reads=(COLS,), writes=(COLS,), inc=True)
            kb.op("s", lambda e: e.activation(g0, t_, AF.Sigmoid, bias=self.SCC[:, 7:8]),
                  reads=(COLS, self.SCC), writes=(COLS,), inc=True)
            kb.op("v", lambda e: e.tensor_scalar(g1, g0, -1.0, 1.0, ALU.mult, ALU.add),
                  reads=(COLS,), writes=(COLS,))
            if DEV_PROBE == "gate":
                kb.dma("q", bass.AP(self.out_t, tb * 128 * D, [[D, 128], [1, 2]]),
                       COLS[:, 70:72], reads=(COLS,), writes=(self.out_t,),
                       stream="probe")
            for ci in range(8):
                src = XF if ci < 4 else QT
                cc = ci % 4
                sap = src[:, cc * 2048 + tb * 128: cc * 2048 + tb * 128 + 128]
                kb.op("t", lambda e, sap=sap: e.transpose(self.PTB, sap, self.IDB[:]),
                      reads=(src if ci < 4 else QT, self.IDB), writes=(self.P7,), inc=True)
                xap = XRES[:, tb * D + (0 if ci < 4 else 512) + cc * 128:
                           tb * D + (0 if ci < 4 else 512) + cc * 128 + 128]
                gc = g0 if ci < 4 else g1
                kb.op("v", lambda e, xap=xap, gc=gc: e.scalar_tensor_tensor(
                    xap, self.PTB, gc, xap, ALU.mult, ALU.add),
                    reads=(self.P7, COLS, XRES), writes=(XRES,), inc=True)

        for tb in range(NTB):
            mcol, rcol = COLS[:, 0:1], COLS[:, 1:2]
            self.ln_stats(XRES[:, tb * D:(tb + 1) * D], XRES, mcol, rcol)
            nb = COLS[:, 3:4]
            kb.op("v", lambda e: e.tensor_tensor(nb, mcol, rcol, ALU.mult),
                  reads=(COLS,), writes=(COLS,))
            kb.op("v", lambda e: e.tensor_scalar(nb, nb, -1.0, None, ALU.mult),
                  reads=(COLS,), writes=(COLS,))
            kb.op("v", lambda e, tb=tb: e.scalar_tensor_tensor(
                XRES[:, tb * D:(tb + 1) * D], XRES[:, tb * D:(tb + 1) * D],
                rcol, nb.broadcast_to([128, D]), ALU.mult, ALU.add),
                reads=(XRES, COLS), writes=(XRES,), inc=True)

    def xn_materialize(self, gcol_base, bcol_base):
        kb = self.kb
        XRES, XF, COLS = self.XRES, self.XF, self.COLS
        for tb in range(NTB):
            mcol, rcol = COLS[:, 0:1], COLS[:, 1:2]
            self.ln_stats(XRES[:, tb * D:(tb + 1) * D], XRES, mcol, rcol)
            nb = COLS[:, 3:4]
            kb.op("v", lambda e: e.tensor_tensor(nb, mcol, rcol, ALU.mult),
                  reads=(COLS,), writes=(COLS,))
            kb.op("v", lambda e: e.tensor_scalar(nb, nb, -1.0, None, ALU.mult),
                  reads=(COLS,), writes=(COLS,))
            kb.op("s", lambda e, tb=tb: e.activation(
                self.XCB[:], XRES[:, tb * D:(tb + 1) * D], AF.Identity,
                scale=rcol, bias=nb),
                reads=(XRES, COLS), writes=(self.AF2,), inc=True)
            for fc in range(NFC):
                sap = self.XCB[:, fc * 128:(fc + 1) * 128]
                kb.op("t", lambda e, sap=sap: e.transpose(self.PTB, sap, self.IDB[:]),
                      reads=(self.AF2, self.IDB), writes=(self.P7,), inc=True)
                c0 = fc * T + tb * 128
                kb.op("s", lambda e, c0=c0, fc=fc: e.activation(
                    XF[:, c0:c0 + 128], self.PTB, AF.Identity,
                    scale=self.BL1[:, gcol_base + fc:gcol_base + fc + 1],
                    bias=self.BL1[:, bcol_base + fc:bcol_base + fc + 1]),
                    reads=(self.P7, self.BL1), writes=(XF,), inc=True)

    def phase_l1f(self):
        kb, nc = self.kb, self.nc
        XRES, XF, QT = self.XRES, self.XF, self.QT
        PA, PB, PC, PD = self.PA, self.PB, self.PC, self.PD
        P4, P5, P6, P7 = self.P4, self.P5, self.P6, self.P7
        ROWS, COLS = self.ROWS, self.COLS
        self.rows_reset(self._base_rows)
        QTF = self.QT.ap().bitcast(F32)
        XRB = self.XRES.ap().bitcast(BF16)

        for src, base in ((self.ln1g, 40), (self.ln1b, 48),
                          (self.ln2g, 56), (self.ln2b, 64)):
            kb.dma("q", self.BL1[:, base // 8 * 8:base // 8 * 8 + 8] if False else
                   self.BL1[:, base - 40 + 0:base - 40 + 8] if False else
                   self.BL1[:, (base - 40) + 0:(base - 40) + 8],
                   bass.AP(src, 0, [[1, 128], [128, 8]]),
                   writes=(self.BL1,), allow_slow_non_contiguous=True, stream="msk")

        self.xn_materialize(0, 8)

        for tb in range(NTB):
            kb.dma("s2", bass.AP(self.X3S, tb * 128 * D, [[D, 128], [1, D]]),
                   XRES[:, tb * D:(tb + 1) * D], reads=(XRES,), writes=(self.X3S,),
                   stream="x3s")

        self.load_w_chunks(self.WB0, 0, self.wp1, 1024, 1024, nsplit=4)
        psall = (PA, PB, PC, PD, P4, P5, P6, P7)
        for mseg in range(8):
            for ic in range(NFC):
                par = ic % 2
                pesl = self.SBbf0[:, 0:512] if par == 0 else self.SBbf1[:, 0:512]
                peslt = self.SBbf0 if par == 0 else self.SBbf1
                kb.dma("q", pesl, bass.AP(self.pe2_d, ic * 4096 + mseg * 512,
                                          [[32768, 128], [1, 512]]),
                       writes=(peslt,), stream="pe2")
                for pc in range(8):
                    l = self.WB0[:, ic * 1024 + pc * 128: ic * 1024 + pc * 128 + 128]
                    kb.op("t", lambda e, l=l, r=pesl, ic=ic, pc=pc: e.matmul(
                        psall[pc][:], l, r, start=(ic == 0), stop=(ic == NFC - 1)),
                        reads=(self.WB0, peslt), writes=(psall[pc],),
                        inc=(ic == NFC - 1))
            for pc in range(8):
                kb.op("s", lambda e, pc=pc, mseg=mseg: e.activation(
                    XRB[:, pc * 4096 + mseg * 512: pc * 4096 + (mseg + 1) * 512],
                    psall[pc][:], AF.Copy),
                    reads=(psall[pc],), writes=(XRES,), inc=True)

        self.load_w_chunks(self.WB0, 0, self.wo1, 1024, 1024, nsplit=4)
        kb.barrier()
        VTL = self.VTL
        XFK, P2K, NRK = RK("xfH"), RK("p2H"), RK("nrmH")
        PRK = (RK("pair0"), RK("pair1"))
        WHK = (RK("wh0"), RK("wh1"))
        WTK = (RK("wt0"), RK("wt1"))
        OHK = (RK("oh0"), RK("oh1"))
        BDK = tuple(RK(f"bdt{i}") for i in range(6))
        PTK = (RK("pt0"), RK("pt1"))
        OHSK = RK("ohsK")
        PSK = {id(p): RK(f"psl{i}") for i, p in
               enumerate((PA, PB, PC, PD, P4, P5, P6, P7))}
        QHBp = lambda pb: QT[:, pb * 6208: pb * 6208 + 2048]
        KHBp = lambda pb: QT[:, pb * 6208 + 2048: pb * 6208 + 4096]
        vt_off = lambda pb, hh, tt: pb * 6208 + 4096 + hh * 1056 + tt * 66
        WT = (QT[:, 12416:14592], VTL[:, 1536:3712])
        OH2b = (VTL[:, 0:512], VTL[:, 3712:4224])
        A2B = self.AF2.ap().bitcast(BF16)
        BDT = (VTL[:, 512:1024], VTL[:, 1024:1536], A2B[:, 0:512],
               A2B[:, 512:1024], A2B[:, 2048:2560], A2B[:, 2560:3072])
        NBD = len(BDT)

        def loadW(p):
            pb = p % 2
            for wi, wsrc in enumerate((self.wq1, self.wk1, self.wv1)):
                kb.dma("g2",
                       bass.AP(self.WB1, pb * 3072 + wi * 1024,
                               [[8192, 128], [128, 8], [1, 128]]),
                       bass.AP(wsrc, p * 128, [[1024, 128], [128 * 1024, 8],
                                               [1, 128]]),
                       writes=(WHK[pb],), stream=f"wh{pb}")

        def qkv(p):
            pb = p % 2
            WQ = self.WB1[:, pb * 3072: pb * 3072 + 1024]
            WKb = self.WB1[:, pb * 3072 + 1024: pb * 3072 + 2048]
            WV = self.WB1[:, pb * 3072 + 2048: pb * 3072 + 3072]
            for di, dst in enumerate((QHBp(pb), KHBp(pb))):
                Wm = WQ if di == 0 else WKb
                for seg in range(4):
                    pq = PA if seg % 2 == 0 else PB
                    for ic in range(NFC):
                        l = Wm[:, ic * 128:(ic + 1) * 128]
                        r = XF[:, ic * T + seg * 512: ic * T + seg * 512 + 512]
                        kb.op("t", lambda e, l=l, r=r, ic=ic, pq=pq: e.matmul(
                            pq[:], l, r, start=(ic == 0), stop=(ic == NFC - 1)),
                            reads=(WHK[pb], XFK), writes=(PSK[id(pq)],),
                            inc=(ic == NFC - 1))
                    kb.op("s", lambda e, dst=dst, seg=seg, pq=pq: e.activation(
                        dst[:, seg * 512:(seg + 1) * 512], pq[:], AF.Copy),
                        reads=(PSK[id(pq)],), writes=(PRK[pb],), inc=True)
            for tt in range(NTB):
                pv = PC if tt % 2 == 0 else PD
                for ic in range(NFC):
                    l = XF[:, ic * T + tt * 128: ic * T + tt * 128 + 128]
                    r = WV[:, ic * 128:(ic + 1) * 128]
                    kb.op("t", lambda e, l=l, r=r, ic=ic, pv=pv: e.matmul(
                        pv[:, 0:128], l, r, start=(ic == 0), stop=(ic == NFC - 1)),
                        reads=(WHK[pb], XFK), writes=(PSK[id(pv)],),
                        inc=(ic == NFC - 1))
                for hh in range(2):
                    off = vt_off(pb, hh, tt)
                    eng_ = "s" if hh == 0 else "v"
                    if hh == 0:
                        kb.op("s", lambda e, off=off, pv=pv: e.activation(
                            QT[:, off:off + 64], pv[:, 0:64], AF.Copy),
                            reads=(PSK[id(pv)],), writes=(PRK[pb],), inc=True)
                    else:
                        kb.op("v", lambda e, off=off, pv=pv: e.tensor_copy(
                            QT[:, off:off + 64], pv[:, 64:128]),
                            reads=(PSK[id(pv)],), writes=(PRK[pb],), inc=True)
                    kb.op("v", lambda e, off=off: e.memset(
                        QT[:, off + 64:off + 65], 1.0), writes=(PRK[pb],))

        def bd(p):
            pb = p % 2
            psl = (PA, PB, PC, PD, P4)
            for hh in range(2):
                for i128 in range(16):
                    base = 1920 - 128 * i128
                    lq = QHBp(pb)[64 * hh:64 * hh + 64, i128 * 128:(i128 + 1) * 128]
                    for sgi in range(5):
                        n = 512 if sgi < 4 else 128
                        rp = XRB[64 * hh:64 * hh + 64,
                                 p * 4096 + base + sgi * 512:
                                 p * 4096 + base + sgi * 512 + n]
                        kb.op("t", lambda e, lq=lq, rp=rp, n=n, sgi=sgi: e.matmul(
                            psl[sgi][:, 0:n], lq, rp, start=True, stop=True),
                            reads=(PRK[pb], P2K), writes=(PSK[id(psl[sgi])],),
                            inc=True)
                    wi = (hh * 16 + i128) % 2
                    wt = WT[wi]
                    for sgi in range(5):
                        n = 512 if sgi < 4 else 128
                        if sgi % 2 == 0:
                            kb.op("s", lambda e, sgi=sgi, n=n, wt=wt: e.activation(
                                wt[:, sgi * 512: sgi * 512 + n], psl[sgi][:, 0:n],
                                AF.Copy),
                                reads=(PSK[id(psl[sgi])],), writes=(WTK[wi],),
                                inc=True)
                        else:
                            kb.op("v", lambda e, sgi=sgi, n=n, wt=wt: e.tensor_copy(
                                wt[:, sgi * 512: sgi * 512 + n], psl[sgi][:, 0:n]),
                                reads=(PSK[id(psl[sgi])],), writes=(WTK[wi],),
                                inc=True)
                    kb.dma("s2", bass.AP(self.BDWp[pb],
                                         hh * T * 2304 + i128 * 128 * 2304,
                                         [[2305, 128], [1, 2176]]),
                           wt, reads=(WTK[wi],), writes=(self.BDWp[pb],),
                           stream=f"bdw{pb}{wi}")

        def flash(p):
            pb = p % 2
            tiles = [(hh, ib, jt) for hh in range(2) for ib in range(4)
                     for jt in range(16)]

            def issue_bdr(n):
                hh, ib, jt = tiles[n]
                s = n % NBD
                kb.dma("q", BDT[s],
                       bass.AP(self.BDWp[pb],
                               hh * T * 2304 + ib * 512 * 2304 + jt * 128 + 127,
                               [[2304, 512], [1, 128]]),
                       reads=(self.BDWp[pb],), writes=(BDK[s],),
                       stream=f"bdr{s}", transpose=True)

            for n in range(NBD):
                issue_bdr(n)
            for hh in range(2):
                h = 2 * p + hh
                for ib in range(4):
                    i0 = ib * 512
                    for jt in range(16):
                        n = (hh * 4 + ib) * 16 + jt
                        sl = n % NBD
                        bdt = BDT[sl]
                        ps = P5 if jt % 2 == 0 else P7
                        kb.op("t", lambda e, jt=jt, i0=i0, hh=hh, ps=ps: e.matmul(
                            ps[:], KHBp(pb)[64 * hh:64 * hh + 64,
                                            jt * 128:(jt + 1) * 128],
                            QHBp(pb)[64 * hh:64 * hh + 64, i0:i0 + 512],
                            start=True, stop=False),
                            reads=(PRK[pb],), writes=(PSK[id(ps)],))
                        kb.op("t", lambda e, ps=ps, bdt=bdt: e.matmul(
                            ps[:], self.IDB[:], bdt, start=False, stop=True),
                            reads=(self.IDB, BDK[sl]),
                            writes=(PSK[id(ps)],), inc=True)
                        if n + NBD < 128:
                            issue_bdr(n + NBD)
                        pt = (self.SBbf0 if jt % 2 == 0 else self.SBbf1)[:, 0:512]
                        kb.op("s", lambda e, pt=pt, ps=ps: e.activation(
                            pt, ps[:], AF.Exp, scale=0.125),
                            reads=(PSK[id(ps)],), writes=(PTK[jt % 2],), inc=True)
                        vt = QT[:, vt_off(pb, hh, jt): vt_off(pb, hh, jt) + 65]
                        kb.op("t", lambda e, jt=jt, pt=pt, vt=vt: e.matmul(
                            P6[0:65, :], vt, pt, start=(jt == 0), stop=(jt == 15)),
                            reads=(PRK[pb], PTK[jt % 2]), writes=(PSK[id(P6)],),
                            inc=(jt == 15))
                    kb.op("v", lambda e: e.reciprocal(self.NRM[64:65, :],
                                                      P6[64:65, :]),
                          reads=(PSK[id(P6)],), writes=(NRK,), inc=True)
                    kb.op("t", lambda e: e.matmul(
                        P7[0:64, :], self.ONER[64:65, 0:64], self.NRM[64:65, :],
                        start=True, stop=True),
                        reads=(NRK, self.ONER), writes=(PSK[id(P7)],), inc=True)
                    kb.op("s", lambda e: e.activation(self.NRM[0:64, :],
                                                      P7[0:64, :], AF.Copy),
                          reads=(PSK[id(P7)],), writes=(NRK,), inc=True)
                    op = (hh * 4 + ib) % 2
                    oh2 = OH2b[op]
                    kb.op("v", lambda e, oh2=oh2: e.tensor_tensor(
                        oh2[0:64, 0:512], P6[0:64, :], self.NRM[0:64, :], ALU.mult),
                        reads=(PSK[id(P6)], NRK), writes=(OHK[op],), inc=True)
                    kb.dma("q", bass.AP(self.OHS, h * 64 * T + i0,
                                        [[T, 64], [1, 512]]),
                           oh2[0:64, 0:512], reads=(OHK[op],), writes=(OHSK,),
                           stream=f"ohs{op}")

        loadW(0)
        qkv(0)
        bd(0)
        for p in range(8):
            if p < 7:
                loadW(p + 1)
                qkv(p + 1)
                bd(p + 1)
            flash(p)
        kb.barrier()

        OTK = (RK("ot0"), RK("ot1"))
        XSK = (RK("xs0"), RK("xs1"))
        for tb in range(NTB):
            for seg in range(2):
                pso = PA if seg == 0 else PB
                xs = QTF[:, 256 + seg * 512: 768 + seg * 512]
                kb.dma("s2", xs, bass.AP(self.X3S, tb * 128 * D + seg * 512,
                                         [[D, 128], [1, 512]]),
                       reads=(self.X3S,), writes=(XSK[seg],), stream=f"x3r{seg}")
                for c in range(8):
                    ot = QT[:, 14592 + (c % 2) * 128: 14592 + (c % 2) * 128 + 128]
                    kb.dma("q", ot, bass.AP(self.OHS, c * 128 * T + tb * 128,
                                            [[T, 128], [1, 128]]),
                           reads=(self.OHS,), writes=(OTK[c % 2],),
                           stream=f"ohr{c % 2}")
                    wt = self.WB0[:, c * 1024 + seg * 512: c * 1024 + seg * 512 + 512]
                    kb.op("t", lambda e, ot=ot, wt=wt, c=c, pso=pso: e.matmul(
                        pso[:], ot, wt, start=(c == 0), stop=(c == 7)),
                        reads=(OTK[c % 2], self.WB0), writes=(PSK[id(pso)],),
                        inc=(c == 7))
                kb.op("v", lambda e, tb=tb, seg=seg, xs=xs, pso=pso:
                      e.scalar_tensor_tensor(
                          XRES[:, tb * D + seg * 512: tb * D + seg * 512 + 512],
                          xs, 2.0, pso[:], ALU.mult, ALU.add),
                      reads=(XSK[seg], PSK[id(pso)]), writes=(XRES,), inc=True)

        self.xn_materialize(16, 24)

        for ic in range(NFC):
            kb.dma("g2", self.SBbf0[:, ic * 32:(ic + 1) * 32],
                   bass.AP(self.selw, ic * 128 * 32, [[32, 128], [1, 32]]),
                   writes=(self.SBbf0,), stream="whead")
        for tb in range(NTB):
            for ic in range(NFC):
                l = XF[:, ic * T + tb * 128: ic * T + tb * 128 + 128]
                r = self.SBbf0[:, ic * 32:(ic + 1) * 32]
                kb.op("t", lambda e, l=l, r=r, ic=ic: e.matmul(
                    PB[:, 0:32], l, r, start=(ic == 0), stop=(ic == NFC - 1)),
                    reads=(XF, self.SBbf0), writes=(PB,), inc=(ic == NFC - 1))
            sel = self.AF2[:, 0:32]
            kb.op("s", lambda e, sel=sel: e.activation(sel, PB[:, 0:32], AF.Sigmoid),
                  reads=(PB,), writes=(self.AF2,), inc=True)
            m1, m4 = COLS[:, 74:75], COLS[:, 77:78]
            m2, m3 = COLS[:, 75:76], COLS[:, 76:77]
            tmsk = self.AF2[:, 32:64]
            tt_ = self.AF2[:, 64:96]
            kb.op("v", lambda e: e.reduce_max(m1, sel, axis=AX.X),
                  reads=(self.AF2,), writes=(COLS,))
            cur = sel
            for mi, mnext in ((m1, m2), (m2, m3), (m3, m4)):
                kb.op("v", lambda e, cur=cur, mi=mi: e.tensor_scalar(
                    tmsk, cur, mi, None, ALU.is_ge),
                    reads=(self.AF2, COLS), writes=(self.AF2,))
                kb.op("v", lambda e, cur=cur: e.scalar_tensor_tensor(
                    tt_, tmsk, -1e30, cur, ALU.mult, ALU.add),
                    reads=(self.AF2,), writes=(self.AF2,))
                kb.op("v", lambda e, mnext=mnext: e.reduce_max(mnext, tt_, axis=AX.X),
                      reads=(self.AF2,), writes=(COLS,))
                cur = tt_
            kb.op("v", lambda e: e.tensor_scalar(tmsk, sel, m4, None, ALU.is_ge),
                  reads=(self.AF2, COLS), writes=(self.AF2,))
            kb.op("v", lambda e: e.tensor_tensor(tmsk, tmsk, sel, ALU.mult),
                  reads=(self.AF2,), writes=(self.AF2,))
            kb.op("t", lambda e: e.matmul(self.PTF[0:32, 0:128], tmsk, self.IDF[:],
                                          start=True, stop=True),
                  reads=(self.AF2, self.IDF), writes=(self.P7,), inc=True)
            gt = self.AF2[:, 96:224]
            kb.op("v", lambda e, gt=gt: e.tensor_copy(gt[0:32, :],
                                                      self.PTF[0:32, 0:128]),
                  reads=(self.P7,), writes=(self.AF2,), inc=True)
            kb.dma("q", bass.AP(self.GATES, tb * 128, [[T, 32], [1, 128]]),
                   gt[0:32, :], reads=(self.AF2,), writes=(self.GATES,),
                   stream="gat")

        kb.barrier()
        gr_off = (self.rows(256), self.rows(256))
        WK = (RK("moeW0"), RK("moeW1"))
        HK = (RK("moeH0"), RK("moeH1"))
        SK = (RK("moeS0"), RK("moeS1"))
        GK_ = (RK("moeGR0"), RK("moeGR1"))
        XRK = RK("xresMoe")
        XFK = RK("xfMoe")
        PSK = {id(p): RK(f"ps{i}") for i, p in
               enumerate((PA, PB, PC, PD, P4, P5, P6, P7))}

        def load_group(g):
            buf = g % 2
            for k in range(4):
                e_ = g * 4 + k
                ks_off = buf * 8192 + k * 1024
                vs_off = buf * 8192 + 4096 + k * 1024
                kb.dma("g2", bass.AP(QT, ks_off, [[16384, 128], [128, 8], [1, 128]]),
                       bass.AP(self.keys1, e_ * 1024 * 128,
                               [[128, 128], [128 * 128, 8], [1, 128]]),
                       writes=(WK[buf],), stream=f"moe{buf}")
                kb.dma("g2", QT[:, vs_off:vs_off + 1024],
                       bass.AP(self.vals1, e_ * 128 * 1024, [[1024, 128], [1, 1024]]),
                       writes=(WK[buf],), stream=f"moe{buf}")

        load_group(0)
        for g in range(8):
            buf = g % 2
            if g < 7:
                load_group(g + 1)
            for tq in range(8):
                t0 = tq * 256
                for k in range(4):
                    e_ = g * 4 + k
                    par = k % 2
                    ks = QT[:, buf * 8192 + k * 1024: buf * 8192 + k * 1024 + 1024]
                    vs = QT[:, buf * 8192 + 4096 + k * 1024:
                            buf * 8192 + 4096 + k * 1024 + 1024]
                    pk = PA if par == 0 else P5
                    pg = P6 if par == 0 else P7
                    hrel = self.AF2[:, 0:256] if par == 0 else self.AF2[:, 256:512]
                    hs = self.SBbf0[:, 0:256] if par == 0 else self.SBbf1[:, 0:256]
                    hst = self.SBbf0 if par == 0 else self.SBbf1
                    kb.dma("q", self.row_ap(gr_off[par], 256),
                           bass.AP(self.GATES, e_ * T + t0, [[T, 1], [1, 256]]),
                           reads=(self.GATES,), writes=(GK_[par],),
                           stream=f"mgr{par}")
                    kb.op("t", lambda e, pg=pg, par=par: e.matmul(
                        pg[:, 0:256], self.ONER[0:1, 0:128],
                        self.row_ap(gr_off[par], 256), start=True, stop=True),
                        reads=(GK_[par], self.ONER), writes=(PSK[id(pg)],), inc=True)
                    for ic in range(NFC):
                        l = ks[:, ic * 128:(ic + 1) * 128]
                        r = XF[:, ic * T + t0: ic * T + t0 + 256]
                        kb.op("t", lambda e, l=l, r=r, ic=ic, pk=pk: e.matmul(
                            pk[:, 0:256], l, r, start=(ic == 0), stop=(ic == NFC - 1)),
                            reads=(WK[buf], XFK), writes=(PSK[id(pk)],),
                            inc=(ic == NFC - 1))
                    kb.op("s", lambda e, hrel=hrel, pk=pk: e.activation(
                        hrel, pk[:, 0:256], AF.Relu),
                        reads=(PSK[id(pk)],), writes=(HK[par],), inc=True)
                    kb.op("v", lambda e, hs=hs, hrel=hrel, pg=pg: e.tensor_tensor(
                        hs, hrel, pg[:, 0:256], ALU.mult),
                        reads=(HK[par], PSK[id(pg)]), writes=(SK[par],), inc=True)
                    for tl in range(2):
                        for oseg in range(2):
                            ps = (PB, PC, PD, P4)[tl * 2 + oseg]
                            l = hs[:, tl * 128:(tl + 1) * 128]
                            r = vs[:, oseg * 512:(oseg + 1) * 512]
                            kb.op("t", lambda e, l=l, r=r, ps=ps, k=k: e.matmul(
                                ps[:], l, r, start=(k == 0), stop=(k == 3)),
                                reads=(SK[par], WK[buf]), writes=(PSK[id(ps)],),
                                inc=(k == 3))
                for tl in range(2):
                    tb = (t0 + tl * 128) // 128
                    for oseg in range(2):
                        ps = (PB, PC, PD, P4)[tl * 2 + oseg]
                        xap = XRES[:, tb * D + oseg * 512: tb * D + oseg * 512 + 512]
                        kb.op("v", lambda e, xap=xap, ps=ps: e.tensor_add(
                            xap, xap, ps[:]),
                            reads=(XRK, PSK[id(ps)]), writes=(XRK,), inc=True)
        kb.barrier()

        for tb in range(NTB):
            mcol, rcol = COLS[:, 0:1], COLS[:, 1:2]
            self.ln_stats(XRES[:, tb * D:(tb + 1) * D], XRES, mcol, rcol)
            nb = COLS[:, 3:4]
            kb.op("v", lambda e: e.tensor_tensor(nb, mcol, rcol, ALU.mult),
                  reads=(COLS,), writes=(COLS,))
            kb.op("v", lambda e: e.tensor_scalar(nb, nb, -1.0, None, ALU.mult),
                  reads=(COLS,), writes=(COLS,))
            kb.op("v", lambda e, tb=tb: e.scalar_tensor_tensor(
                XRES[:, tb * D:(tb + 1) * D], XRES[:, tb * D:(tb + 1) * D],
                rcol, nb.broadcast_to([128, D]), ALU.mult, ALU.add),
                reads=(XRES, COLS), writes=(XRES,), inc=True)

    def emit_output(self):
        kb = self.kb
        if DEV_PROBE is not None:
            kb.finish("q")
            return
        for tb in range(NTB):
            kb.dma("q", bass.AP(self.out_t, tb * 128 * D, [[D, 128], [1, D]]),
                   self.XRES[:, tb * D:(tb + 1) * D], reads=(self.XRES,))
        kb.finish("q")

    def build(self):
        self.declare()
        ind = np.zeros((1, 256), dtype=np.float32)
        ind[0, 0:64] = 1.0
        ind[0, 128 + 64:128 + 128] = 1.0
        ind_d = self.nc.inline_tensor(ind, "indrows")
        self._indr_off = self.rows(256)
        self.INDR = self.ROWS
        ctx = self.nc.body("main", valid_engines=[ENG.Pool, ENG.Activation,
                                                  ENG.PE, ENG.DVE, ENG.SP])
        bb = ctx.__enter__()
        self.kb.dma("q", self.row_ap(self._indr_off, 256), ind_d.ap(),
                    writes=(self.ROWS,))
        self._base_rows = self.rows_off
        self.phase_l0a()
        if DEV_STAGE != "l0a":
            self.phase_l0f()
        if DEV_STAGE in ("l1a", "l1f", "full"):
            self.phase_l1a()
        if DEV_STAGE in ("l1f", "full"):
            self.phase_l1f()
        self.emit_output()
        ctx.__exit__(None, None, None)
        return self.nc


def kernel(**inputs):
    b = Builder()
    nc = b.build()
    x = np.ascontiguousarray(inputs["L0_attn_x"])
    in_maps = []
    for c in range(8):
        m = {"L0_attn_x": x[c]}
        for k, v in inputs.items():
            if k != "L0_attn_x":
                m[k] = np.ascontiguousarray(v)
        in_maps.append(m)
    declared = set(b.dram.keys())
    in_maps = [{k: v for k, v in m.items() if k in declared} for m in in_maps]
    res = run_bass_kernel_spmd(nc, in_maps, core_ids=list(range(8)))
    return np.stack([r["out"] for r in res.results], axis=0)


# version 16
# speedup vs baseline: 1.0460x; 1.0003x over previous
import math
import numpy as np
import ml_dtypes
import concourse.bass as bass
import concourse.mybir as mybir
from concourse.bass_utils import run_bass_kernel_spmd

F32 = mybir.dt.float32
BF16 = mybir.dt.bfloat16
I32 = mybir.dt.int32
AF = mybir.ActivationFunctionType
ALU = mybir.AluOpType
AX = mybir.AxisListType
ENG = mybir.EngineType

T = 2048
D = 1024
NTB = T // 128
NFC = D // 128

DEV_STAGE = "full"
DEV_PROBE = None


def cdiv(a, b):
    return -(-a // b)


class RK:
    __slots__ = ("name",)

    def __init__(self, name):
        self.name = name


class KB:
    def __init__(self, nc):
        self.nc = nc
        self.eng = {"t": nc.tensor, "v": nc.vector, "s": nc.scalar,
                    "g": nc.gpsimd, "q": nc.sync}
        self.dmaeng = {"q": nc.sync, "s2": nc.scalar, "g2": nc.gpsimd}
        self.esem = {"t": "tS", "v": "vS", "s": "sS", "g": "gS"}
        self.sem = {n: nc.alloc_semaphore(n)
                    for n in ("tS", "vS", "sS", "gS", "cc")}
        self.cnt = {k: 0 for k in self.sem}
        self.stream_maxwait = {}
        self.stream_boundary = {}
        self.seq = {e: 0 for e in ("t", "v", "s", "g", "q", "s2", "g2")}
        self.events = {e: [] for e in self.seq}
        self.wr = {}
        self.rd = {}
        self.last_drain = {"v": 0, "s": 0, "g": 0}
        self.hwm = {e: {} for e in self.seq}

    def _event_after(self, ename, seq):
        ev = self.events[ename]
        lo, hi = 0, len(ev)
        while lo < hi:
            mid = (lo + hi) // 2
            if ev[mid][0] >= seq:
                hi = mid
            else:
                lo = mid + 1
        return ev[lo][1:] if lo < len(ev) else None

    def _signal(self, ename):
        sem = self.esem[ename]
        inst = self.eng[ename].drain()
        if ename != "t":
            self.last_drain[ename] = self.seq[ename]
        inst.then_inc(self.sem[sem], 1)
        self.cnt[sem] += 1
        self.seq[ename] += 1
        self.events[ename].append((self.seq[ename], sem, self.cnt[sem]))
        return (sem, self.cnt[sem])

    def dstream(self, name):
        key = "D_" + name
        if key not in self.sem:
            self.sem[key] = self.nc.alloc_semaphore(key)
            self.cnt[key] = 0
            self.stream_maxwait[key] = 0
            self.stream_boundary[key] = 0
        return key

    def _wait(self, consumer, sem, val):
        h = self.hwm[consumer]
        if sem.startswith("D_"):
            self.stream_maxwait[sem] = max(self.stream_maxwait[sem], val)
        if h.get(sem, 0) >= val:
            return
        e = self.dmaeng[consumer] if consumer in self.dmaeng else self.eng[consumer]
        e.wait_ge(self.sem[sem], val)
        h[sem] = val

    def _need(self, consumer, producer, pseq):
        if producer == consumer and producer not in ("q", "s2", "g2"):
            if producer == "t":
                return
            if self.last_drain.get(producer, 0) < pseq:
                self.eng[producer].drain()
                self.seq[producer] += 1
                self.last_drain[producer] = self.seq[producer]
            return
        ev = self._event_after(producer, pseq)
        if ev is None:
            ev = self._signal(producer)
        sem, c = ev
        if sem.startswith("D_"):
            c = self.cnt[sem]
        self._wait(consumer, sem, c)

    def op(self, ename, fn, reads=(), writes=(), inc=False):
        rk = [t.name for t in reads]
        wk = [t.name for t in writes]
        for k in rk:
            if k in self.wr:
                self._need(ename, *self.wr[k])
        for k in wk:
            for ev in self.rd.get(k, ()):
                self._need(ename, *ev)
            if k in self.wr:
                self._need(ename, *self.wr[k])
        inst = fn(self.eng[ename])
        self.seq[ename] += 1
        myseq = self.seq[ename]
        for k in rk:
            self.rd.setdefault(k, []).append((ename, myseq))
        for k in wk:
            self.wr[k] = (ename, myseq)
            self.rd[k] = []
        if inc:
            sem = self.esem[ename]
            inst.then_inc(self.sem[sem], 1)
            self.cnt[sem] += 1
            self.events[ename].append((myseq, sem, self.cnt[sem]))
        return inst

    def dma(self, qname, out_ap, in_ap, reads=(), writes=(), stream="main", **kw):
        sem = self.dstream(("g" if qname == "g2" else "h") + stream)
        phys = {"q": None, "s2": "s", "g2": "g"}[qname]
        if self.stream_maxwait[sem] > self.stream_boundary[sem]:
            self._wait(qname, sem, self.cnt[sem])
            self.stream_boundary[sem] = self.cnt[sem]
        for t in reads:
            k = t.name
            if k in self.wr:
                p, ps = self.wr[k]
                if phys is not None and p == phys:
                    if self.last_drain[p] < ps:
                        self.eng[p].drain()
                        self.seq[p] += 1
                        self.last_drain[p] = self.seq[p]
                else:
                    self._need(qname, p, ps)
        for t in writes:
            k = t.name
            for ev in self.rd.get(k, ()):
                p = ev[0]
                if phys is not None and p == phys:
                    if self.last_drain[p] < ev[1]:
                        self.eng[p].drain()
                        self.seq[p] += 1
                        self.last_drain[p] = self.seq[p]
                else:
                    self._need(qname, *ev)
            if k in self.wr:
                p, ps = self.wr[k]
                if phys is not None and p == phys:
                    if self.last_drain[p] < ps:
                        self.eng[p].drain()
                        self.seq[p] += 1
                        self.last_drain[p] = self.seq[p]
                else:
                    self._need(qname, p, ps)
        e = self.dmaeng[qname]
        inst = e.dma_start(out_ap, in_ap, **kw)
        inst.then_inc(self.sem[sem], 16)
        self.cnt[sem] += 16
        self.seq[qname] += 1
        myseq = self.seq[qname]
        self.events[qname].append((myseq, sem, self.cnt[sem]))
        for t in reads:
            self.rd.setdefault(t.name, []).append((qname, myseq))
        for t in writes:
            self.wr[t.name] = (qname, myseq)
            self.rd[t.name] = []
        return inst

    def sync_read(self, ename, tensors):
        for t in tensors:
            if t.name in self.wr:
                self._need(ename, *self.wr[t.name])

    def finish(self, ename="q"):
        for sname, c in self.cnt.items():
            if c > 0:
                self._wait(ename, sname, c)

    def barrier(self):
        for e in ("t", "v", "s", "g", "q"):
            for sname, c in self.cnt.items():
                if c > 0:
                    self._wait(e, sname, c)
        self.wr.clear()
        self.rd.clear()


class Builder:
    def __init__(self):
        nc = bass.Bass("TRN2", target_bir_lowering=False)
        self.nc = nc
        self.kb = KB(nc)
        self.dram = {}
        self.rows_off = 0

    def din(self, name, shape):
        t = self.nc.dram_tensor(name, list(shape), F32, kind="ExternalInput")
        self.dram[name] = t
        return t

    def rows(self, n):
        o = self.rows_off
        assert o + n <= 5632, (o, n)
        self.rows_off += n
        return o

    def rows_mark(self):
        return self.rows_off

    def rows_reset(self, mark):
        self.rows_off = mark

    def row_ap(self, off, n):
        return bass.AP(self.ROWS, off, [[5632, 1], [1, n]])

    def declare(self):
        nc = self.nc
        self.x_in = self.din("L0_attn_x", (T, D))
        self.g0 = self.din("L0_attn_gamma", (D,))
        self.Wq0 = self.din("L0_attn_Wq", (D, 1024))
        self.Wkv0 = self.din("L0_attn_Wkv", (D, 128))
        self.qs0 = self.din("L0_attn_q_scale", (64,))
        self.ks0 = self.din("L0_attn_k_scale", (64,))
        self.nkv0 = self.din("L0_attn_null_kv", (2, 2, 64))
        self.Wout0 = self.din("L0_attn_Wout", (1024, D))
        self.wg0 = self.din("L0_ffn_w_gate", (3072, 1024))
        self.wu0 = self.din("L0_ffn_w_up", (3072, 1024))
        self.wd0 = self.din("L0_ffn_w_down", (1024, 3072))
        self.wlqkv = self.din("L1_attn_w_lqkv", (512, 1536))
        self.blqkv = self.din("L1_attn_b_lqkv", (1536,))
        self.wgqkv = self.din("L1_attn_w_gqkv", (512, 1536))
        self.bgqkv = self.din("L1_attn_b_gqkv", (1536,))
        self.wcomp = self.din("L1_attn_w_comp", (4096, 1024))
        self.bcomp = self.din("L1_attn_b_comp", (1024,))
        self.wlproj = self.din("L1_attn_w_lproj", (512, 512))
        self.blproj = self.din("L1_attn_b_lproj", (512,))
        self.wgproj = self.din("L1_attn_w_gproj", (512, 512))
        self.bgproj = self.din("L1_attn_b_gproj", (512,))
        self.wgate1 = self.din("L1_attn_w_gate", (1024, 2))
        self.bgate1 = self.din("L1_attn_b_gate", (2,))
        self.wq1 = self.din("L1_ffn_wq", (1024, 1024))
        self.wk1 = self.din("L1_ffn_wk", (1024, 1024))
        self.wv1 = self.din("L1_ffn_wv", (1024, 1024))
        self.wp1 = self.din("L1_ffn_wp", (1024, 1024))
        self.wo1 = self.din("L1_ffn_wo", (1024, 1024))
        self.ln1g = self.din("L1_ffn_ln1_g", (1024,))
        self.ln1b = self.din("L1_ffn_ln1_b", (1024,))
        self.ln2g = self.din("L1_ffn_ln2_g", (1024,))
        self.ln2b = self.din("L1_ffn_ln2_b", (1024,))
        self.selw = self.din("L1_ffn_sel_w", (1024, 32))
        self.keys1 = self.din("L1_ffn_keys", (32, 1024, 128))
        self.vals1 = self.din("L1_ffn_values", (32, 128, 1024))

        self.out_t = nc.dram_tensor("out", [T, D], F32, kind="ExternalOutput")
        if DEV_PROBE is not None:
            self.dbg_t = nc.dram_tensor("dbg", [128, 8192], BF16,
                                        kind="ExternalOutput")
            self.dbgf_t = nc.dram_tensor("dbgf", [128, 2048], F32,
                                         kind="ExternalOutput")
        _spk = "ExternalOutput" if DEV_PROBE in ("ups", "gts") else "Internal"
        self.UPS = nc.dram_tensor("UPS", [3072, T], F32, kind=_spk)
        self.GTS = nc.dram_tensor("GTS", [3072, T], F32, kind=_spk)
        self.WDNT = nc.dram_tensor("WDNT", [3072, 1024], BF16,
                                   kind="ExternalOutput" if DEV_PROBE == "wdnt"
                                   else "Internal")
        self.CCI = nc.dram_tensor("CCI", [1, 8], F32)
        self.CCO = nc.dram_tensor("CCO", [1, 8], F32)
        self.SCR = nc.dram_tensor("SCR", [1, 8], F32)
        self.X3S = nc.dram_tensor("X3S", [T, D], F32)
        self.BDWp = [nc.dram_tensor(f"BDWa{i}", [2 * T * 2304 + 8192], BF16)
                     for i in range(2)]
        self.OHS = nc.dram_tensor("OHS", [1024, T], BF16)
        self.GATES = nc.dram_tensor("GATES", [32, T], F32)

        self.identbf_d = nc.inline_tensor(np.eye(128).astype(ml_dtypes.bfloat16),
                                          "identbf")
        self.identf_d = nc.inline_tensor(np.eye(128, dtype=np.float32), "identf")
        jp = np.arange(128)[:, None]
        ic = np.arange(512)[None, :]
        ml = [(jp <= ic + 2 - 128 * k) for k in range(5)]
        ml.append((jp <= ic + 2) | ((ic < 64) & (jp < 66)))
        ml += [(jp <= ic - 128 * k) for k in range(5)]
        masks_np = np.concatenate([m.astype(ml_dtypes.bfloat16) for m in ml], axis=1)
        self.masks_d = nc.inline_tensor(masks_np, "masksc")
        m_idx = np.arange(4095, dtype=np.float64)
        pos = m_idx - 2047.0
        invf = 1.0 / (10000.0 ** (np.arange(0, 1024, 2, dtype=np.float64) / 1024.0))
        ang = pos[:, None] * invf[None, :]
        pe2 = np.concatenate([np.sin(ang), np.cos(ang)], axis=1)
        pe2T = np.zeros((1024, 4096), dtype=np.float64)
        pe2T[:, 0:4095] = pe2.T
        pe2T_bf = pe2T.astype(np.float32).astype(ml_dtypes.bfloat16)
        pe2_cat = np.concatenate([pe2T_bf[c * 128:(c + 1) * 128] for c in range(8)],
                                 axis=1)
        self.pe2_d = nc.inline_tensor(pe2_cat, "pe2t")
        sh = np.zeros((64, 128), dtype=np.float32)
        sh[np.arange(64), 64 + np.arange(64)] = 1.0
        self.sh64_d = nc.inline_tensor(sh.astype(ml_dtypes.bfloat16), "sh64")
        urow = np.zeros((1, 128), dtype=np.float32)
        urow[0, 64] = 1.0
        A = nc.alloc_sbuf_tensor
        self.XRES = A("XRES", [128, NTB * D], F32)
        self.XF = A("XF", [128, NFC * T], BF16)
        self.QT = A("QT", [128, NFC * T], BF16)
        self.AF2 = A("AF2", [128, 1536], F32)
        self.WB0 = A("WB0", [128, 8192], BF16)
        self.WB1 = A("WB1", [128, 8192], BF16)
        self.SBbf0 = A("SBbf0", [128, 512], BF16)
        self.SBbf1 = A("SBbf1", [128, 512], BF16)
        self.MSK = A("MSK", [128, 6 * 512], BF16)
        self.IDB = A("IDB", [128, 128], BF16)
        self.SH64 = A("SH64", [64, 128], BF16)
        self.IDF = A("IDF", [128, 128], F32)
        self.ROWS = A("ROWS", [1, 5632], F32)
        self.COLS = A("COLS", [128, 96], F32)
        self.ONESF = A("ONESF", [128, 1], F32)
        self.ONESB = A("ONESB", [128, 1], BF16)
        self.GAM = A("GAM", [128, NFC], F32)
        self.SCC = A("SCC", [128, 8], F32)
        self.QS = A("QS", [128, 1], F32)
        self.KS = A("KS", [128, 1], F32)
        self.ONER = A("ONER", [128, 128], F32)
        self.VTL = A("VTL", [128, 4224], BF16)
        self.BL1 = A("BL1", [128, 40], F32)
        P = nc.alloc_psum_tensor
        self.PA = P("PA", [128, 512], F32)
        self.PB = P("PB", [128, 512], F32)
        self.PC = P("PC", [128, 512], F32)
        self.PD = P("PD", [128, 512], F32)
        self.P4 = P("P4", [128, 512], F32)
        self.P5 = P("P5", [128, 512], F32)
        self.P6 = P("P6", [128, 512], F32)
        self.P7 = P("P7", [128, 512], F32)
        self.PTB = self.P7.ap().bitcast(BF16)[:, 0:128]
        self.PTF = self.P7[:, 128:512]
        self.NRM = self.AF2[:, 512:1024]
        self.KVS = self.QT.ap().bitcast(F32)[:, 0:2052]
        self.KH = self.VTL[:, 0:2052]
        self.VT = self.VTL[:, 2052:3174]
        self.AF0 = self.AF2[:, 0:512]
        self.XCB = self.AF2.ap().bitcast(BF16)[:, 2048:3072]
        self.QSQv = self.AF2[:, 1024:1536]

    def load_w_chunks(self, dst, dst_col0, w, K, M, row0=0, col0=0, nsplit=2):
        kb = self.kb
        nk = K // 128
        W = w.shape[1]
        per = cdiv(nk, nsplit)
        for sp in range(nsplit):
            k0, k1 = sp * per, min(nk, (sp + 1) * per)
            if k0 >= k1:
                break
            src = bass.AP(w, (row0 + k0 * 128) * W + col0,
                          [[W, 128], [128 * W, k1 - k0], [1, M]])
            dap = bass.AP(dst, dst_col0 + k0 * M,
                          [[dst.shape[1], 128], [M, k1 - k0], [1, M]])
            kb.dma("g2", dap, src, writes=(dst,))

    def row_rsqrt(self, off, n, floor=1e-24):
        kb = self.kb
        if not hasattr(self, "_rs_scr"):
            self._rs_scr = self.rows(1024)
        so = self._rs_scr
        R = self.ROWS
        for c0 in range(0, n, 512):
            m = min(512, n - c0)
            v_ = self.row_ap(off + c0, m)
            t0 = self.row_ap(so, m)
            t1 = self.row_ap(so + 512, m)
            kb.op("v", lambda e, v_=v_: e.tensor_scalar(v_, v_, floor, None, ALU.max),
                  reads=(R,), writes=(R,))
            kb.op("v", lambda e, v_=v_, t0=t0: e.reciprocal(t0, v_), reads=(R,), writes=(R,))
            kb.op("s", lambda e, t0=t0, t1=t1: e.activation(t1, t0, AF.Sqrt),
                  reads=(R,), writes=(R,))
            kb.op("v", lambda e, t0=t0, t1=t1: e.tensor_tensor(t0, t1, t1, ALU.mult),
                  reads=(R,), writes=(R,))
            kb.op("v", lambda e, t0=t0, v_=v_: e.tensor_tensor(t0, t0, v_, ALU.mult),
                  reads=(R,), writes=(R,))
            kb.op("v", lambda e, t0=t0: e.tensor_scalar(t0, t0, -0.5, 1.5, ALU.mult, ALU.add),
                  reads=(R,), writes=(R,))
            kb.op("v", lambda e, v_=v_, t0=t0, t1=t1: e.tensor_tensor(v_, t1, t0, ALU.mult),
                  reads=(R,), writes=(R,))
        return self.row_ap(off, n)

    def col_rsqrt(self, vcol, out, tmp1, tmp2, eps):
        kb = self.kb
        C = self.COLS
        kb.op("v", lambda e: e.tensor_scalar(out, vcol, eps, None, ALU.add),
              reads=(C,), writes=(C,))
        kb.op("v", lambda e: e.reciprocal(tmp1, out), reads=(C,), writes=(C,))
        kb.op("s", lambda e: e.activation(tmp2, tmp1, AF.Sqrt), reads=(C,), writes=(C,))
        kb.op("v", lambda e: e.tensor_tensor(tmp1, tmp2, tmp2, ALU.mult),
              reads=(C,), writes=(C,))
        kb.op("v", lambda e: e.tensor_tensor(tmp1, tmp1, out, ALU.mult),
              reads=(C,), writes=(C,))
        kb.op("v", lambda e: e.tensor_scalar(tmp1, tmp1, -0.5, 1.5, ALU.mult, ALU.add),
              reads=(C,), writes=(C,))
        kb.op("v", lambda e: e.tensor_tensor(out, tmp2, tmp1, ALU.mult),
              reads=(C,), writes=(C,))

    def ln_stats(self, src_ap, src_t, mcol, rcol, eps=1e-5):
        kb = self.kb
        C = self.COLS
        n = src_ap.shape[1]
        sq = self.AF2[:, 0:512]
        t2 = C[:, 59:60]
        kb.op("v", lambda e: e.reduce_sum(mcol, src_ap, axis=AX.X),
              reads=(src_t,), writes=(C,))
        kb.op("v", lambda e: e.tensor_scalar(mcol, mcol, 1.0 / n, None, ALU.mult),
              reads=(C,), writes=(C,))
        kb.op("v", lambda e: e.memset(rcol, 0.0), writes=(C,))
        for c0 in range(0, n, 512):
            m = min(512, n - c0)
            kb.op("s", lambda e, c0=c0, m=m: e.activation(
                sq[:, 0:m], src_ap[:, c0:c0 + m], AF.Square),
                reads=(src_t,), writes=(self.AF2,))
            kb.op("v", lambda e, m=m: e.reduce_sum(t2, sq[:, 0:m], axis=AX.X),
                  reads=(self.AF2,), writes=(C,))
            kb.op("v", lambda e: e.tensor_add(rcol, rcol, t2),
                  reads=(C,), writes=(C,))
        kb.op("v", lambda e: e.tensor_scalar(rcol, rcol, 1.0 / n, None, ALU.mult),
              reads=(C,), writes=(C,))
        t0 = C[:, 60:61]
        kb.op("v", lambda e: e.tensor_tensor(t0, mcol, mcol, ALU.mult),
              reads=(C,), writes=(C,))
        kb.op("v", lambda e: e.tensor_sub(rcol, rcol, t0), reads=(C,), writes=(C,))
        self.col_rsqrt(rcol, rcol, C[:, 60:61], C[:, 61:62], eps)

    def tm_tile_to_fm(self, src_fn, dst, dst_col_of_fc, tb, nfc=NFC):
        kb = self.kb
        for fc in range(nfc):
            sap = src_fn(fc)
            kb.op("t", lambda e, sap=sap: e.transpose(self.PTB[:], sap, self.IDB[:]),
                  reads=(sap.tensor, self.IDB), writes=(self.PTB,), inc=True)
            c0 = dst_col_of_fc(fc) + tb * 128
            kb.op("s", lambda e, c0=c0: e.activation(dst[:, c0:c0 + 128],
                                                     self.PTB[:], AF.Copy),
                  reads=(self.PTB,), writes=(dst,), inc=True)

    def phase_l0a(self):
        kb, nc = self.kb, self.nc
        XRES, XF, QT = self.XRES, self.XF, self.QT
        ATTC2 = lambda c, hh: XF[64 * hh:64 * hh + 64,
                                 8192 + c * 512: 8192 + (c + 1) * 512]
        ATTTMP = self.SBbf0
        PA, PB, PC, PD = self.PA, self.PB, self.PC, self.PD
        KVS, KH, VT = self.KVS, self.KH, self.VT
        ROWS, COLS = self.ROWS, self.COLS

        for tb in range(NTB):
            kb.dma("q", XRES[:, tb * D:(tb + 1) * D],
                   bass.AP(self.x_in, tb * 128 * D, [[D, 128], [1, D]]),
                   writes=(XRES,))
        kb.dma("q", self.IDB[:], self.identbf_d.ap(), writes=(self.IDB,))
        kb.dma("q", self.IDF[:], self.identf_d.ap(), writes=(self.IDF,))
        kb.dma("q", self.SH64[:], self.sh64_d.ap(), writes=(self.SH64,))
        kb.dma("q", self.MSK[:], bass.AP(self.masks_d, 0, [[11 * 512, 128], [1, 6 * 512]]), writes=(self.MSK,))
        kb.op("v", lambda e: e.memset(self.ONESF[:], 1.0), writes=(self.ONESF,))
        kb.op("v", lambda e: e.memset(self.ONESB[:], 1.0), writes=(self.ONESB,))
        kb.op("v", lambda e: e.memset(self.ONER[:], 1.0), writes=(self.ONER,))
        self.load_w_chunks(self.WB0, 0, self.Wkv0, 1024, 128, nsplit=1)
        kb.dma("q", self.GAM[:], bass.AP(self.g0, 0, [[1, 128], [128, NFC]]),
               writes=(self.GAM,), allow_slow_non_contiguous=True)
        for half in range(2):
            kb.dma("q", self.QS[64 * half:64 * half + 64, :],
                   bass.AP(self.qs0, 0, [[1, 64], [1, 1]]), writes=(self.QS,))
            kb.dma("q", self.KS[64 * half:64 * half + 64, :],
                   bass.AP(self.ks0, 0, [[1, 64], [1, 1]]), writes=(self.KS,))
        kb.op("v", lambda e: e.scalar_tensor_tensor(
            self.QS[:], self.QS[:], 8.0, self.KS[:], ALU.mult, ALU.mult),
            reads=(self.QS, self.KS), writes=(self.QS,))

        for tb in range(NTB):
            kb.op("v", lambda e, tb=tb: e.tensor_copy(
                self.XCB[:], XRES[:, tb * D:(tb + 1) * D]),
                reads=(XRES,), writes=(self.AF2,), inc=True)
            self.tm_tile_to_fm(lambda fc: self.XCB[:, fc * 128:(fc + 1) * 128],
                               XF, lambda fc: fc * T, tb)

        for seg in range(4):
            for dc in range(NFC):
                l = self.WB0[:, dc * 128:(dc + 1) * 128]
                r = XF[:, dc * T + seg * 512: dc * T + seg * 512 + 512]
                kb.op("t", lambda e, l=l, r=r, dc=dc: e.matmul(
                    PA[:], l, r, start=(dc == 0), stop=(dc == NFC - 1)),
                    reads=(self.WB0, XF),
                    writes=(PA,), inc=(dc == NFC - 1))
            kb.op("s", lambda e, seg=seg: e.activation(
                KVS[:, 2 + seg * 512: 2 + seg * 512 + 512], PA[:], AF.Copy),
                reads=(PA,), writes=(KVS,), inc=True)
        kb.dma("q", KVS[0:64, 0:2], bass.AP(self.nkv0, 0, [[1, 64], [64, 2]]),
               writes=(KVS,), allow_slow_non_contiguous=True)
        kb.dma("q", KVS[64:128, 0:2], bass.AP(self.nkv0, 128, [[1, 64], [64, 2]]),
               writes=(KVS,), allow_slow_non_contiguous=True)

        for tt in range(17):
            c0 = tt * 128
            n = min(128, 2050 - c0)
            kb.op("t", lambda e, c0=c0, n=n: e.matmul(
                self.PTF[0:n, 0:64], KVS[64:128, c0:c0 + n], self.IDF[64:128, 64:128],
                start=True, stop=True),
                reads=(KVS, self.IDF), writes=(self.PTF,), inc=True)
            kb.op("s", lambda e, tt=tt, n=n: e.activation(
                VT[0:n, tt * 66: tt * 66 + 64], self.PTF[0:n, 0:64], AF.Copy),
                reads=(self.PTF,), writes=(VT,), inc=True)
            kb.op("v", lambda e, tt=tt, n=n: e.memset(
                VT[0:n, tt * 66 + 64: tt * 66 + 65], 1.0), writes=(VT,))

        kb.op("s", lambda e: e.activation(KVS[64:128, 0:2050], KVS[0:64, 0:2050],
                                          AF.Square),
              reads=(KVS,), writes=(KVS,), inc=True)
        _rmark = self.rows_mark()
        k2o = self.rows(2050)
        for seg in range(5):
            c0 = seg * 512
            c1 = min(2050, c0 + 512)
            kb.op("t", lambda e, c0=c0, c1=c1: e.matmul(
                PB[0:1, 0:c1 - c0], self.ONESF[64:128, :], KVS[64:128, c0:c1],
                start=True, stop=True),
                reads=(self.ONESF, KVS), writes=(PB,), inc=True)
            kb.op("v", lambda e, c0=c0, c1=c1: e.tensor_copy(
                self.row_ap(k2o + c0, c1 - c0), PB[0:1, 0:c1 - c0]),
                reads=(PB,), writes=(ROWS,), inc=True)
        kinv = self.row_rsqrt(k2o, 2050)
        for seg in range(5):
            c0 = seg * 512
            c1 = min(2050, c0 + 512)
            kb.op("t", lambda e, c0=c0, c1=c1: e.matmul(
                PB[0:64, 0:c1 - c0], self._ind_row(0)[0:1, 0:64],
                self.row_ap(k2o + c0, c1 - c0), start=True, stop=True),
                reads=(ROWS, self.INDR), writes=(PB,), inc=True)
            kb.op("v", lambda e, c0=c0, c1=c1: e.tensor_tensor(
                KH[0:64, c0:c1], KVS[0:64, c0:c1], PB[0:64, 0:c1 - c0], ALU.mult),
                reads=(KVS, PB), writes=(KH,), inc=True)
            kb.op("t", lambda e, c0=c0, c1=c1: e.matmul(
                PC[:, 0:c1 - c0], self.SH64[:], KH[0:64, c0:c1],
                start=True, stop=True),
                reads=(self.SH64, KH), writes=(PC,), inc=True)
            kb.op("s", lambda e, c0=c0, c1=c1: e.activation(
                KH[64:128, c0:c1], PC[64:128, 0:c1 - c0], AF.Copy),
                reads=(PC,), writes=(KH,), inc=True)

        self.rows_reset(_rmark)
        self._rs_scr_del = self.__dict__.pop('_rs_scr', None)
        self.load_w_chunks(self.WB1, 0, self.Wq0, 1024, 1024, nsplit=4)
        for dc in range(NFC):
            kb.op("s", lambda e, dc=dc: e.activation(
                self.WB1[:, dc * 1024:(dc + 1) * 1024],
                self.WB1[:, dc * 1024:(dc + 1) * 1024], AF.Copy,
                scale=self.GAM[:, dc:dc + 1]),
                reads=(self.WB1, self.GAM), writes=(self.WB1,), inc=True)
        gq_o = self.rows(1024)
        for seg in range(2):
            for dc in range(NFC):
                kb.op("t", lambda e, dc=dc, seg=seg: e.matmul(
                    PB[0:1, :], self.ONESB[:],
                    self.WB1[:, dc * 1024 + seg * 512: dc * 1024 + seg * 512 + 512],
                    start=(dc == 0), stop=(dc == NFC - 1)),
                    reads=(self.ONESB, self.WB1), writes=(PB,), inc=(dc == NFC - 1))
            kb.op("v", lambda e, seg=seg: e.tensor_copy(
                self.row_ap(gq_o + seg * 512, 512), PB[0:1, :]),
                reads=(PB,), writes=(ROWS,), inc=True)

        nmr_o = self.rows(T)
        for tb in range(NTB):
            mcol, rcol = COLS[:, 0:1], COLS[:, 1:2]
            self.ln_stats(XRES[:, tb * D:(tb + 1) * D], XRES, mcol, rcol)
            kb.op("s", lambda e, tb=tb: e.activation(
                self.XCB[:], XRES[:, tb * D:(tb + 1) * D], AF.Copy, scale=rcol),
                reads=(XRES, COLS), writes=(self.AF2,), inc=True)
            self.tm_tile_to_fm(lambda fc: self.XCB[:, fc * 128:(fc + 1) * 128],
                               XF, lambda fc: fc * T, tb)
            nm = COLS[:, 2:3]
            kb.op("v", lambda e: e.tensor_tensor(nm, mcol, rcol, ALU.mult),
                  reads=(COLS,), writes=(COLS,))
            kb.op("v", lambda e: e.tensor_scalar(nm, nm, -1.0, None, ALU.mult),
                  reads=(COLS,), writes=(COLS,))
            kb.op("t", lambda e: e.matmul(self.PTF[0:1, 0:128], nm, self.IDF[:],
                                          start=True, stop=True),
                  reads=(COLS, self.IDF), writes=(self.PTF,), inc=True)
            kb.op("v", lambda e, tb=tb: e.tensor_copy(
                self.row_ap(nmr_o + tb * 128, 128), self.PTF[0:1, 0:128]),
                reads=(self.PTF,), writes=(ROWS,), inc=True)

        QSB = self.AF0
        for dc in range(NFC):
            for seg in range(4):
                t0 = seg * 512
                for fc in range(NFC):
                    l = self.WB1[:, fc * 1024 + dc * 128: fc * 1024 + dc * 128 + 128]
                    r = XF[:, fc * T + t0: fc * T + t0 + 512]
                    kb.op("t", lambda e, l=l, r=r, fc=fc: e.matmul(
                        PA[:], l, r, start=(fc == 0), stop=False),
                        reads=(self.WB1, XF), writes=(PA,))
                kb.op("t", lambda e, dc=dc, t0=t0: e.matmul(
                    PA[:], self.row_ap(gq_o + dc * 128, 128),
                    self.row_ap(nmr_o + t0, 512), start=False, stop=True),
                    reads=(ROWS,), writes=(PA,), inc=True)
                kb.op("s", lambda e: e.activation(QSB[:, 0:512], PA[:], AF.Copy),
                      reads=(PA,), writes=(QSB,), inc=True)
                kb.op("s", lambda e: e.activation(self.QSQv, QSB[:, 0:512],
                                                  AF.Square),
                      reads=(QSB,), writes=(self.AF2,), inc=True)
                ro = self.rows(1024) if (dc == 0 and seg == 0) else self._qrow
                self._qrow = ro
                for hh in range(2):
                    kb.op("t", lambda e, hh=hh: e.matmul(
                        PB[0:1, :], self.ONESF[64 * hh:64 * hh + 64, :],
                        self.QSQv[64 * hh:64 * hh + 64, :], start=True, stop=True),
                        reads=(self.ONESF, self.AF2), writes=(PB,), inc=True)
                    kb.op("v", lambda e, hh=hh: e.tensor_copy(
                        self.row_ap(ro + hh * 512, 512), PB[0:1, :]),
                        reads=(PB,), writes=(ROWS,), inc=True)
                for hh in range(2):
                    self.row_rsqrt(ro + hh * 512, 512)
                for hh in range(2):
                    kb.op("t", lambda e, hh=hh, ro=ro: e.matmul(
                        PC[:], self._ind_row(hh), self.row_ap(ro + hh * 512, 512),
                        start=(hh == 0), stop=(hh == 1)),
                        reads=(self.INDR, ROWS),
                        writes=(PC,) if hh == 1 else (PC,), inc=(hh == 1))
                c0 = dc * T + t0
                kb.op("v", lambda e, c0=c0: e.scalar_tensor_tensor(
                    QT[:, c0:c0 + 512], QSB[:, 0:512], self.QS[:], PC[:],
                    ALU.mult, ALU.mult),
                    reads=(QSB, self.QS, PC), writes=(QT,), inc=True)

        WOUT = self.WB1
        self.load_w_chunks(WOUT, 0, self.Wout0, 1024, 1024, nsplit=4)
        for ib in range(4):
            i0 = ib * 512
            for h in range(16):
                dc, hh = h // 2, h % 2
                qap = lambda: QT[64 * hh:64 * hh + 64,
                                 dc * T + i0: dc * T + i0 + 512]
                njt = min(17, 4 * ib + 4 + 1)
                first = True
                for jt in range(njt):
                    jc0 = jt * 128
                    n = min(128, 2050 - jc0)
                    pp = self.P4 if jt % 2 == 0 else self.P5
                    kb.op("t", lambda e, jc0=jc0, n=n, hh=hh, pp=pp: e.matmul(
                        pp[0:n, :], KH[64 * hh:64 * hh + 64, jc0:jc0 + n], qap(),
                        start=True, stop=True),
                        reads=(KH, QT), writes=(pp,), inc=True)
                    pt = self.SBbf0[:, 0:512] if (jt % 2 == 0) else self.SBbf1[:, 0:512]
                    ptt = self.SBbf0 if (jt % 2 == 0) else self.SBbf1
                    kb.op("s", lambda e, pt=pt, n=n, pp=pp: e.activation(
                        pt[0:n, :] if n < 128 else pt, pp[0:n, :], AF.Exp),
                        reads=(pp,), writes=(ptt,), inc=True)
                    kk = jt - 4 * ib
                    if 0 <= kk <= 4 and jt < 17:
                        mi = 5 if (ib == 0 and jt == 0) else kk
                        kb.op("v", lambda e, pt=pt, mi=mi, n=n: e.tensor_tensor(
                            pt[0:n, :] if n < 128 else pt,
                            pt[0:n, :] if n < 128 else pt,
                            self.MSK[0:n, mi * 512:(mi + 1) * 512] if n < 128
                            else self.MSK[:, mi * 512:(mi + 1) * 512], ALU.mult),
                            reads=(ptt, self.MSK), writes=(ptt,), inc=True)
                    kb.op("t", lambda e, jt=jt, n=n, first=first, pt=pt: e.matmul(
                        PB[0:65, :], VT[0:n, jt * 66: jt * 66 + 65],
                        pt[0:n, :] if n < 128 else pt,
                        start=first, stop=(jt == njt - 1)),
                        reads=(VT, ptt), writes=(PB,),
                        inc=(jt == njt - 1))
                    first = False
                kb.op("v", lambda e: e.reciprocal(self.NRM[64:65, :], PB[64:65, :]),
                      reads=(PB,), writes=(self.AF2,), inc=True)
                kb.op("t", lambda e: e.matmul(
                    PC[0:64, :], self.ONER[64:65, 0:64], self.NRM[64:65, :],
                    start=True, stop=True),
                    reads=(self.AF2, self.ONER), writes=(PC,), inc=True)
                kb.op("s", lambda e: e.activation(self.NRM[0:64, :], PC[0:64, :],
                                                  AF.Copy),
                      reads=(PC,), writes=(self.AF2,), inc=True)
                if hh == 0:
                    kb.op("v", lambda e, dc=dc: e.tensor_tensor(
                        ATTC2(dc, 0), PB[0:64, :],
                        self.NRM[0:64, :], ALU.mult),
                        reads=(PB, self.AF2), writes=(XF,), inc=True)
                else:
                    kb.op("v", lambda e: e.tensor_tensor(
                        ATTTMP[:, 0:512][0:64, :], PB[0:64, :],
                        self.NRM[0:64, :], ALU.mult),
                        reads=(PB, self.AF2), writes=(ATTTMP,), inc=True)
                    kb.op("t", lambda e: e.matmul(
                        PC[:], self.SH64[:], ATTTMP[0:64, 0:512],
                        start=True, stop=True),
                        reads=(self.SH64, ATTTMP), writes=(PC,), inc=True)
                    kb.op("s", lambda e, dc=dc: e.activation(
                        ATTC2(dc, 1), PC[64:128, :], AF.Copy),
                        reads=(PC,), writes=(XF,), inc=True)
            for tl in range(4):
                tb = ib * 4 + tl
                for seg in range(2):
                    for c in range(8):
                        l = XF[:, 8192 + c * 512 + tl * 128:
                               8192 + c * 512 + tl * 128 + 128]
                        r = WOUT[:, c * 1024 + seg * 512: c * 1024 + seg * 512 + 512]
                        kb.op("t", lambda e, l=l, r=r, c=c: e.matmul(
                            PD[:], l, r, start=(c == 0), stop=(c == 7)),
                            reads=(XF, WOUT), writes=(PD,), inc=(c == 7))
                    xap = XRES[:, tb * D + seg * 512: tb * D + seg * 512 + 512]
                    kb.op("v", lambda e, xap=xap: e.tensor_add(xap, xap, PD[:]),
                          reads=(XRES, PD), writes=(XRES,), inc=True)
        for tb in range(NTB):
            mcol, rcol = COLS[:, 0:1], COLS[:, 1:2]
            self.ln_stats(XRES[:, tb * D:(tb + 1) * D], XRES, mcol, rcol)
            nb = COLS[:, 3:4]
            kb.op("v", lambda e: e.tensor_tensor(nb, mcol, rcol, ALU.mult),
                  reads=(COLS,), writes=(COLS,))
            kb.op("v", lambda e: e.tensor_scalar(nb, nb, -1.0, None, ALU.mult),
                  reads=(COLS,), writes=(COLS,))
            kb.op("v", lambda e, tb=tb: e.scalar_tensor_tensor(
                XRES[:, tb * D:(tb + 1) * D], XRES[:, tb * D:(tb + 1) * D],
                rcol, nb.broadcast_to([128, D]), ALU.mult, ALU.add),
                reads=(XRES, COLS), writes=(XRES,), inc=True)


    def _ind_row(self, hh):
        return self.row_ap(self._indr_off + hh * 128, 128)


    def allreduce_max_row(self, row_off, n):
        kb = self.kb
        if not hasattr(self, "_cc_stage"):
            self._cc_stage = self.rows(8)
        st = self._cc_stage
        kb.op("v", lambda e: e.memset(self.row_ap(st, 8), 0.0), writes=(self.ROWS,))
        kb.op("v", lambda e: e.tensor_copy(self.row_ap(st, n), self.row_ap(row_off, n)),
              reads=(self.ROWS,), writes=(self.ROWS,))
        kb.dma("q", bass.AP(self.CCI, 0, [[8, 1], [1, 8]]),
               self.row_ap(st, 8), reads=(self.ROWS,), writes=(self.CCI,),
               stream="cc")
        kb.op("g", lambda e: e.collective_compute(
            "AllReduce", ALU.max, replica_groups=[list(range(8))],
            ins=[self.CCI.ap().opt()], outs=[self.CCO.ap().opt()]),
            reads=(self.CCI,), writes=(self.CCO,), inc=True)
        kb.dma("q", self.row_ap(row_off, n), bass.AP(self.CCO, 0, [[8, 1], [1, n]]),
               reads=(self.CCO,), writes=(self.ROWS,), stream="cc")

    def col_max_to_row(self, col_ap, row_off):
        kb = self.kb
        kb.op("t", lambda e: e.matmul(self.PTF[0:1, 0:128], col_ap, self.IDF[:],
                                      start=True, stop=True),
              reads=(self.COLS, self.IDF), writes=(self.P7,), inc=True)
        kb.op("v", lambda e: e.tensor_copy(self.row_ap(row_off + 1, 128),
                                           self.PTF[0:1, 0:128]),
              reads=(self.P7,), writes=(self.ROWS,), inc=True)
        kb.op("v", lambda e: e.reduce_max(self.row_ap(row_off, 1),
                                          self.row_ap(row_off + 1, 128), axis=AX.X),
              reads=(self.ROWS,), writes=(self.ROWS,))

    def bcast_scalar_col(self, scr_idx, col_ap, tgt):
        self.kb.dma("q", col_ap, bass.AP(self.SCR, scr_idx, [[0, 128], [1, 1]]),
                    reads=(self.SCR,), writes=(tgt,), stream="cc")

    def phase_l0f(self):
        kb, nc = self.kb, self.nc
        XRES, XF = self.XRES, self.XF
        PA, PB, PC, PD = self.PA, self.PB, self.PC, self.PD
        P4, P5, P6, P7 = self.P4, self.P5, self.P6, self.P7
        ROWS, COLS = self.ROWS, self.COLS
        QF = self.QT.ap().bitcast(F32)
        QI = self.QT.ap().bitcast(I32)
        QB = self.QT

        self.rows_reset(self._l0f_rmark if hasattr(self, "_l0f_rmark") else self.rows_mark())

        mx = COLS[:, 0:1]
        t = COLS[:, 1:2]
        kb.op("v", lambda e: e.memset(mx, 0.0), writes=(COLS,))
        for tb in range(NTB):
            kb.op("v", lambda e, tb=tb: e.reduce_max(
                t, XRES[:, tb * D:(tb + 1) * D], axis=AX.X,
                apply_absolute_value=True),
                reads=(XRES,), writes=(COLS,))
            kb.op("v", lambda e: e.tensor_max(mx, mx, t), reads=(COLS,), writes=(COLS,))
        ro = self.rows(160)
        self.col_max_to_row(mx, ro)
        self.allreduce_max_row(ro, 1)
        r1 = self.row_ap(ro, 1)
        r2 = self.row_ap(ro + 1, 1)
        kb.op("v", lambda e: e.tensor_scalar(r1, r1, 1.0 / 32767.0, 1e-12,
                                             ALU.mult, ALU.max),
              reads=(ROWS,), writes=(ROWS,))
        kb.op("v", lambda e: e.reciprocal(r2, r1), reads=(ROWS,), writes=(ROWS,))
        kb.dma("q", bass.AP(self.SCR, 0, [[8, 1], [1, 2]]), self.row_ap(ro, 2),
               reads=(ROWS,), writes=(self.SCR,), stream="cc")
        SXc, ISXc = self.SCC[:, 0:1], self.SCC[:, 1:2]
        self.bcast_scalar_col(0, SXc, self.SCC)
        self.bcast_scalar_col(1, ISXc, self.SCC)

        def fround(dst_ap, dst_t, src_ap, src_t, scale, Xi, Xf, Yf):
            sc = (self.SCC,) if isinstance(scale, bass.AP) else ()
            s0 = 1.0 if scale is None else scale
            kb.op("v", lambda e: e.tensor_scalar(Xi, src_ap, s0, 0.5,
                                                 ALU.mult, ALU.add),
                  reads=(src_t,) + sc, writes=(self.QT,))
            kb.op("v", lambda e: e.tensor_copy(Yf, Xi),
                  reads=(self.QT,), writes=(self.QT,))
            if scale is None:
                kb.op("v", lambda e: e.tensor_tensor(Xf, src_ap, Yf, ALU.subtract),
                      reads=(src_t, self.QT), writes=(self.QT,))
            else:
                kb.op("v", lambda e: e.scalar_tensor_tensor(
                    Xf, src_ap, scale, Yf, ALU.mult, ALU.subtract),
                    reads=(src_t, self.QT) + sc, writes=(self.QT,))
            kb.op("v", lambda e: e.tensor_scalar(Xf, Xf, -0.5, None, ALU.is_lt),
                  reads=(self.QT,), writes=(self.QT,))
            kb.op("v", lambda e: e.tensor_tensor(dst_ap, Yf, Xf, ALU.subtract),
                  reads=(self.QT,), writes=(dst_t,), inc=True)
        self._fround = fround

        for tb in range(NTB):
            fround(self.XCB[:], self.AF2, XRES[:, tb * D:(tb + 1) * D], XRES,
                   ISXc, QI[:, 0:1024], QF[:, 0:1024], QF[:, 1024:2048])
            self.tm_tile_to_fm(lambda fc: self.XCB[:, fc * 128:(fc + 1) * 128],
                               XF, lambda fc: fc * T, tb)
        if DEV_PROBE == "xq":
            kb.dma("q", self.dbg_t.ap(), XF[:, 0:8192], reads=(XF,),
                   writes=(self.dbg_t,), stream="probe")

        def dequant_chunk(wraw_ap, wraw_t, wdq_ap, wdq_t, NI):
            ng = NI // 32
            bmx = COLS[:, 0:ng] if ng <= 32 else None
            assert ng <= 32 or NI == 3072
            if NI == 3072:
                bmx = self.SCC
            bmx = COLS[:, 0:ng] if ng <= 32 else self.AF2[:, 0:ng]
            bmx_t = COLS if ng <= 32 else self.AF2
            inv = (COLS[:, 32:32 + ng] if ng <= 32 else self.AF2[:, 96:96 + ng])
            kb.op("v", lambda e: e.reduce_max(
                bmx, wraw_ap.rearrange("p (g k) -> p g k", k=32),
                axis=AX.X, apply_absolute_value=True),
                reads=(wraw_t,), writes=(bmx_t,))
            kb.op("v", lambda e: e.tensor_scalar(bmx, bmx, 1.0 / 7.0, 1e-12,
                                                 ALU.mult, ALU.max),
                  reads=(bmx_t,), writes=(bmx_t,))
            kb.op("v", lambda e: e.reciprocal(inv, bmx), reads=(bmx_t,), writes=(bmx_t,))
            tw = QF[:, 5120:5120 + NI]
            ti = QI[:, 5120 + NI:5120 + 2 * NI]
            tf2 = QF[:, 5120 + 2 * NI:5120 + 3 * NI]
            kb.op("v", lambda e: e.tensor_tensor(
                tw.rearrange("p (g k) -> p g k", k=32),
                wraw_ap.rearrange("p (g k) -> p g k", k=32),
                inv.unsqueeze(2).broadcast_to([128, ng, 32]), ALU.mult),
                reads=(wraw_t, bmx_t), writes=(self.QT,))
            tiF = QF[:, 5120 + NI:5120 + 2 * NI]
            kb.op("v", lambda e: e.tensor_scalar(ti, tw, 1.0, 0.5, ALU.mult, ALU.add),
                  reads=(self.QT,), writes=(self.QT,))
            kb.op("v", lambda e: e.tensor_copy(tf2, ti),
                  reads=(self.QT,), writes=(self.QT,))
            kb.op("v", lambda e: e.tensor_tensor(tiF, tw, tf2, ALU.subtract),
                  reads=(self.QT,), writes=(self.QT,))
            kb.op("v", lambda e: e.tensor_scalar(tiF, tiF, -0.5, None, ALU.is_lt),
                  reads=(self.QT,), writes=(self.QT,))
            kb.op("v", lambda e: e.tensor_tensor(tf2, tf2, tiF, ALU.subtract),
                  reads=(self.QT,), writes=(self.QT,))
            kb.op("v", lambda e: e.tensor_tensor(
                wdq_ap.rearrange("p (g k) -> p g k", k=32),
                tf2.rearrange("p (g k) -> p g k", k=32),
                bmx.unsqueeze(2).broadcast_to([128, ng, 32]), ALU.mult),
                reads=(self.QT, bmx_t), writes=(wdq_t,))

        WF = self.WB0.ap().bitcast(F32)
        kb.barrier()
        STK = [RK(f"st{i}") for i in range(4)]
        mu_c, mg_c = COLS[:, 64:65], COLS[:, 65:66]
        kb.op("v", lambda e: e.memset(COLS[:, 64:66], 0.0), writes=(COLS,))
        for wi, (wsrc, spill, mxc) in enumerate(
                [(self.wu0, self.UPS, mu_c), (self.wg0, self.GTS, mg_c)]):
            for oc in range(24):
                par = oc % 2
                wraw = WF[:, par * 1024:(par + 1) * 1024]
                kb.dma("q", wraw,
                       bass.AP(wsrc, oc * 128 * 1024, [[1024, 128], [1, 1024]]),
                       writes=(self.WB0,), stream=f"wraw{par}")
                wdq = self.WB0[:, 4096 + par * 1024: 4096 + (par + 1) * 1024]
                dequant_chunk(wraw, self.WB0, wdq, self.WB0, 1024)
                for it in range(NFC):
                    kb.op("t", lambda e, it=it: e.transpose(
                        self.PTB, wdq[:, it * 128:(it + 1) * 128], self.IDB[:]),
                        reads=(self.WB0, self.IDB), writes=(self.P7,), inc=True)
                    kb.op("s", lambda e, it=it, par=par: e.activation(
                        self.WB1[:, par * 1024 + it * 128: par * 1024 + it * 128 + 128],
                        self.PTB, AF.Copy),
                        reads=(self.P7,), writes=(self.WB1,), inc=True)
                for seg in range(4):
                    ps = (PA, PB, PC, PD)[seg]
                    for ic in range(NFC):
                        l = self.WB1[:, par * 1024 + ic * 128: par * 1024 + ic * 128 + 128]
                        r = XF[:, ic * T + seg * 512: ic * T + seg * 512 + 512]
                        kb.op("t", lambda e, l=l, r=r, ic=ic, ps=ps: e.matmul(
                            ps[:], l, r, start=(ic == 0), stop=(ic == NFC - 1)),
                            reads=(self.WB1, XF), writes=(ps,), inc=(ic == NFC - 1))
                    st = QF[:, 3072 + seg * 512: 3072 + (seg + 1) * 512]
                    kb.op("s", lambda e, st=st, ps=ps: e.activation(
                        st, ps[:], AF.Copy, scale=SXc),
                        reads=(ps, self.SCC), writes=(STK[seg],), inc=True)
                    kb.op("v", lambda e, st=st: e.reduce_max(
                        COLS[:, 66:67], st, axis=AX.X, apply_absolute_value=True),
                        reads=(STK[seg],), writes=(COLS,))
                    kb.op("v", lambda e, mxc=mxc: e.tensor_max(mxc, mxc, COLS[:, 66:67]),
                          reads=(COLS,), writes=(COLS,))
                    kb.dma("s2", bass.AP(spill, oc * 128 * T + seg * 512,
                                         [[T, 128], [1, 512]]), st,
                           reads=(STK[seg],), writes=(spill,), stream=f"sp{seg}")
        self.col_max_to_row(mu_c, ro)
        kb.op("v", lambda e: e.tensor_copy(self.row_ap(ro + 140, 1), self.row_ap(ro, 1)),
              reads=(ROWS,), writes=(ROWS,))
        self.col_max_to_row(mg_c, ro)
        kb.op("v", lambda e: e.tensor_copy(self.row_ap(ro + 1, 1), self.row_ap(ro + 140, 1)),
              reads=(ROWS,), writes=(ROWS,))
        self.allreduce_max_row(ro, 2)
        rMg = self.row_ap(ro, 1)
        rMu = self.row_ap(ro + 1, 1)
        CONST_SO = 32767.0 * 65535.0 / 65536.0
        rsg = self.row_ap(ro + 2, 1); rsu = self.row_ap(ro + 3, 1)
        risg = self.row_ap(ro + 4, 1); risu = self.row_ap(ro + 5, 1)
        rso = self.row_ap(ro + 6, 1)
        kb.op("v", lambda e: e.tensor_scalar(rsg, rMg, 1.0 / 32767.0, 1e-12,
                                             ALU.mult, ALU.max), reads=(ROWS,), writes=(ROWS,))
        kb.op("v", lambda e: e.tensor_scalar(rsu, rMu, 1.0 / 32767.0, 1e-12,
                                             ALU.mult, ALU.max), reads=(ROWS,), writes=(ROWS,))
        kb.op("v", lambda e: e.reciprocal(risg, rsg), reads=(ROWS,), writes=(ROWS,))
        kb.op("v", lambda e: e.reciprocal(risu, rsu), reads=(ROWS,), writes=(ROWS,))
        kb.op("v", lambda e: e.tensor_tensor(rso, rsg, rsu, ALU.mult),
              reads=(ROWS,), writes=(ROWS,))
        kb.op("v", lambda e: e.tensor_scalar(rso, rso, CONST_SO, None, ALU.mult),
              reads=(ROWS,), writes=(ROWS,))
        kb.op("v", lambda e: e.tensor_copy(self.row_ap(ro + 8, 1), rsg), reads=(ROWS,), writes=(ROWS,))
        kb.op("v", lambda e: e.tensor_copy(self.row_ap(ro + 9, 1), risg), reads=(ROWS,), writes=(ROWS,))
        kb.op("v", lambda e: e.tensor_copy(self.row_ap(ro + 10, 1), risu), reads=(ROWS,), writes=(ROWS,))
        kb.op("v", lambda e: e.tensor_copy(self.row_ap(ro + 11, 1), rso), reads=(ROWS,), writes=(ROWS,))
        kb.op("v", lambda e: e.tensor_scalar(self.row_ap(ro + 12, 1), rsg,
                                             -32768.0, None, ALU.mult),
              reads=(ROWS,), writes=(ROWS,))
        kb.dma("q", bass.AP(self.SCR, 2, [[8, 1], [1, 5]]), self.row_ap(ro + 8, 5),
               reads=(ROWS,), writes=(self.SCR,), stream="cc")
        SGc, ISGc = self.SCC[:, 2:3], self.SCC[:, 3:4]
        ISUc, SOc, NSGc = self.SCC[:, 4:5], self.SCC[:, 5:6], self.SCC[:, 6:7]
        self.bcast_scalar_col(2, SGc, self.SCC)
        self.bcast_scalar_col(3, ISGc, self.SCC)
        self.bcast_scalar_col(4, ISUc, self.SCC)
        self.bcast_scalar_col(5, SOc, self.SCC)
        self.bcast_scalar_col(6, NSGc, self.SCC)
        if DEV_PROBE == "scc":
            kb.dma("q", bass.AP(self.dbgf_t, 0, [[2048, 128], [1, 7]]),
                   self.SCC[:, 0:7], reads=(self.SCC,), writes=(self.dbgf_t,),
                   stream="probe")
        C1 = 1.0 / 65535.0
        C2 = 1.0 / 32767.0

        for hc in range(8):
            par = hc % 2
            for sub in range(3):
                wraw = WF[:, par * 1024:(par + 1) * 1024]
                kb.dma("q", wraw,
                       bass.AP(self.wd0, hc * 128 * 3072 + sub * 1024,
                               [[3072, 128], [1, 1024]]),
                       writes=(self.WB0,), stream=f"wraw{par}")
                wdq = self.WB0[:, 4096 + par * 1024: 4096 + (par + 1) * 1024]
                dequant_chunk(wraw, self.WB0, wdq, self.WB0, 1024)
                for it in range(NFC):
                    ic = sub * 8 + it
                    kb.op("t", lambda e, it=it: e.transpose(
                        self.PTB, wdq[:, it * 128:(it + 1) * 128], self.IDB[:]),
                        reads=(self.WB0, self.IDB), writes=(self.P7,), inc=True)
                    dst = self.SBbf0[:, 0:128] if ic % 2 == 0 else self.SBbf1[:, 0:128]
                    dst_t = self.SBbf0 if ic % 2 == 0 else self.SBbf1
                    kb.op("s", lambda e, dst=dst: e.activation(dst, self.PTB, AF.Copy),
                          reads=(self.P7,), writes=(dst_t,), inc=True)
                    kb.dma("s2", bass.AP(self.WDNT, ic * 128 * 1024 + hc * 128,
                                         [[1024, 128], [1, 128]]), dst,
                           reads=(dst_t,), writes=(self.WDNT,), stream="wdnt")
        kb.barrier()
        NGF = 3
        NWS = 4
        GFs = [QF[:, i * 512:(i + 1) * 512] for i in range(NGF)]
        UFs = [QF[:, 1536 + i * 512: 1536 + (i + 1) * 512] for i in range(NGF)]
        GFK = [RK(f"gf{i}") for i in range(NGF)]
        UFK = [RK(f"uf{i}") for i in range(NGF)]
        A_, B_, C_ = QF[:, 3072:3584], QF[:, 3584:4096], QF[:, 4096:4608]
        Xv, XvI = QF[:, 4608:5120], QI[:, 4608:5120]
        Yv = QF[:, 5120:5632]
        RU = QF[:, 5632:6144]
        Xg, XgI = QF[:, 6144:6656], QI[:, 6144:6656]
        Yg = QF[:, 6656:7168]
        OBs = (QB[:, 14336:14848], QB[:, 14848:15360])
        CHK, CHG = RK("chainV"), RK("chainG")
        OBK = (RK("ob0"), RK("ob1"))
        WSK = [RK(f"ws{i}") for i in range(NWS)]
        SCCK, XRK = RK("sccS3"), RK("xresS3")
        psl = (PA, PB, PC, PD, P4, P5, P6, P7)
        PSK3 = {id(p): RK(f"p5k{i}") for i, p in enumerate(psl)}

        def fround3(eng, dst_ap, dst_k, src_ap, src_k, scale, Xi, Xf, Yf, tmp_k):
            sc = (SCCK,) if isinstance(scale, bass.AP) else ()
            s0 = 1.0 if scale is None else scale
            kb.op(eng, lambda e: e.tensor_scalar(Xi, src_ap, s0, 0.5,
                                                 ALU.mult, ALU.add),
                  reads=(src_k,) + sc, writes=(tmp_k,))
            kb.op(eng, lambda e: e.tensor_copy(Yf, Xi),
                  reads=(tmp_k,), writes=(tmp_k,))
            if scale is None:
                kb.op(eng, lambda e: e.tensor_tensor(Xf, src_ap, Yf, ALU.subtract),
                      reads=(src_k, tmp_k), writes=(tmp_k,))
            elif eng == "g":
                Zg = QF[:, 7680:8192]
                kb.op(eng, lambda e: e.tensor_scalar(Zg, src_ap, s0, None,
                                                     ALU.mult),
                      reads=(src_k,), writes=(tmp_k,))
                kb.op(eng, lambda e: e.tensor_tensor(Xf, Zg, Yf, ALU.subtract),
                      reads=(tmp_k,), writes=(tmp_k,))
            else:
                kb.op(eng, lambda e: e.scalar_tensor_tensor(
                    Xf, src_ap, scale, Yf, ALU.mult, ALU.subtract),
                    reads=(src_k, tmp_k) + sc, writes=(tmp_k,))
            kb.op(eng, lambda e: e.tensor_scalar(Xf, Xf, -0.5, None, ALU.is_lt),
                  reads=(tmp_k,), writes=(tmp_k,))
            kb.op(eng, lambda e: e.tensor_tensor(dst_ap, Yf, Xf, ALU.subtract),
                  reads=(tmp_k,), writes=(dst_k,), inc=True)

        def ld_gu(n):
            tq, ic = divmod(n, 24)
            sl = n % NGF
            kb.dma("q", GFs[sl], bass.AP(self.GTS, ic * 128 * T + tq * 512,
                                         [[T, 128], [1, 512]]),
                   reads=(self.GTS,), writes=(GFK[sl],), stream=f"gf{sl}")
            kb.dma("s2", UFs[sl], bass.AP(self.UPS, ic * 128 * T + tq * 512,
                                          [[T, 128], [1, 512]]),
                   reads=(self.UPS,), writes=(UFK[sl],), stream=f"uf{sl}")

        def ld_w(n):
            tq, ic = divmod(n, 24)
            sl = n % NWS
            kb.dma("q", self.WB1[:, sl * 1024:(sl + 1) * 1024],
                   bass.AP(self.WDNT, ic * 128 * 1024, [[1024, 128], [1, 1024]]),
                   reads=(self.WDNT,), writes=(WSK[sl],), stream=f"wds{sl}")

        for n in range(NGF):
            ld_gu(n)
        for n in range(NWS):
            ld_w(n)
        for tq in range(4):
            for ic in range(24):
                n = tq * 24 + ic
                par = ic % 2
                gsl, wsl = n % NGF, n % NWS
                ob = OBs[par]
                fround3("v", A_, CHK, GFs[gsl], GFK[gsl], ISGc, XvI, Xv, Yv, CHK)
                kb.op("s", lambda e: e.activation(C_, A_, AF.Sigmoid, scale=SGc),
                      reads=(CHK, SCCK), writes=(CHK,), inc=True)
                fround3("v", B_, CHG, C_, CHK, 65536.0, XgI, Xg, Yg, CHG)
                fround3("v", RU, CHK, UFs[gsl], UFK[gsl], ISUc, XvI, Xv, Yv, CHK)
                kb.op("v", lambda e: e.tensor_tensor(A_, A_, B_, ALU.mult),
                      reads=(CHK, CHG), writes=(CHK,))
                fround3("v", B_, CHG, A_, CHK, C1, XgI, Xg, Yg, CHG)
                kb.op("v", lambda e: e.tensor_tensor(A_, B_, RU, ALU.mult),
                      reads=(CHK, CHG), writes=(CHK,))
                fround3("v", ob, OBK[par], A_, CHK, C2, XgI, Xg, Yg, CHG)
                for tl in range(4):
                    for hs in range(2):
                        ps = psl[tl * 2 + hs]
                        l = ob[:, tl * 128:(tl + 1) * 128]
                        r = self.WB1[:, wsl * 1024 + hs * 512:
                                     wsl * 1024 + hs * 512 + 512]
                        kb.op("t", lambda e, l=l, r=r, ic=ic, ps=ps: e.matmul(
                            ps[:], l, r, start=(ic == 0), stop=(ic == 23)),
                            reads=(OBK[par], WSK[wsl]), writes=(PSK3[id(ps)],),
                            inc=(ic == 23))
                if n + NGF < 96:
                    ld_gu(n + NGF)
                if n + NWS < 96:
                    ld_w(n + NWS)
            for tl in range(4):
                tb = tq * 4 + tl
                for hs in range(2):
                    ps = psl[tl * 2 + hs]
                    xap = XRES[:, tb * D + hs * 512: tb * D + hs * 512 + 512]
                    kb.op("v", lambda e, xap=xap, ps=ps: e.scalar_tensor_tensor(
                        xap, ps[:], SOc, xap, ALU.mult, ALU.add),
                        reads=(XRK, PSK3[id(ps)], SCCK), writes=(XRK,), inc=True)
        kb.barrier()
        for tb in range(NTB):
            mcol, rcol = COLS[:, 0:1], COLS[:, 1:2]
            self.ln_stats(XRES[:, tb * D:(tb + 1) * D], XRES, mcol, rcol)
            nb = COLS[:, 3:4]
            kb.op("v", lambda e: e.tensor_tensor(nb, mcol, rcol, ALU.mult),
                  reads=(COLS,), writes=(COLS,))
            kb.op("v", lambda e: e.tensor_scalar(nb, nb, -1.0, None, ALU.mult),
                  reads=(COLS,), writes=(COLS,))
            kb.op("v", lambda e, tb=tb: e.scalar_tensor_tensor(
                XRES[:, tb * D:(tb + 1) * D], XRES[:, tb * D:(tb + 1) * D],
                rcol, nb.broadcast_to([128, D]), ALU.mult, ALU.add),
                reads=(XRES, COLS), writes=(XRES,), inc=True)

    def finish_head(self, av_ps, dst_even_ap, dst_odd_ap, hh, dst_t, n_i=512):
        kb = self.kb
        PC = self.PC
        kb.op("v", lambda e: e.reciprocal(self.NRM[64:65, 0:n_i], av_ps[64:65, 0:n_i]),
              reads=(av_ps,), writes=(self.AF2,), inc=True)
        kb.op("t", lambda e: e.matmul(
            PC[0:64, 0:n_i], self.ONER[64:65, 0:64], self.NRM[64:65, 0:n_i],
            start=True, stop=True),
            reads=(self.AF2, self.ONER), writes=(PC,), inc=True)
        kb.op("s", lambda e: e.activation(self.NRM[0:64, 0:n_i], PC[0:64, 0:n_i],
                                          AF.Copy),
              reads=(PC,), writes=(self.AF2,), inc=True)
        if hh == 0:
            kb.op("v", lambda e: e.tensor_tensor(
                dst_even_ap, av_ps[0:64, 0:n_i], self.NRM[0:64, 0:n_i], ALU.mult),
                reads=(av_ps, self.AF2), writes=(dst_t,), inc=True)
        else:
            tmp = self.SBbf0[0:64, 0:n_i]
            kb.op("v", lambda e: e.tensor_tensor(
                tmp, av_ps[0:64, 0:n_i], self.NRM[0:64, 0:n_i], ALU.mult),
                reads=(av_ps, self.AF2), writes=(self.SBbf0,), inc=True)
            kb.op("t", lambda e: e.matmul(
                PC[:, 0:n_i], self.SH64[:], tmp, start=True, stop=True),
                reads=(self.SH64, self.SBbf0), writes=(PC,), inc=True)
            kb.op("s", lambda e: e.activation(dst_odd_ap, PC[64:128, 0:n_i], AF.Copy),
                  reads=(PC,), writes=(dst_t,), inc=True)

    def phase_l1a(self):
        kb, nc = self.kb, self.nc
        XRES, XF, QT, VTL = self.XRES, self.XF, self.QT, self.VTL
        PA, PB, PC, PD = self.PA, self.PB, self.PC, self.PD
        ROWS, COLS = self.ROWS, self.COLS
        self.rows_reset(self._base_rows)

        kb.dma("q", self.MSK[:, 0:5 * 512],
               bass.AP(self.masks_d, 6 * 512, [[11 * 512, 128], [1, 5 * 512]]),
               writes=(self.MSK,), stream="msk")
        kb.dma("q", self.BL1[:, 0:12], bass.AP(self.blqkv, 0, [[1, 128], [128, 12]]),
               writes=(self.BL1,), allow_slow_non_contiguous=True, stream="msk")
        kb.dma("q", self.BL1[:, 12:24], bass.AP(self.bgqkv, 0, [[1, 128], [128, 12]]),
               writes=(self.BL1,), allow_slow_non_contiguous=True, stream="msk")
        kb.dma("q", self.BL1[:, 24:28], bass.AP(self.bcomp, 512, [[1, 128], [128, 4]]),
               writes=(self.BL1,), allow_slow_non_contiguous=True, stream="msk")
        kb.dma("q", self.BL1[:, 28:32], bass.AP(self.blproj, 0, [[1, 128], [128, 4]]),
               writes=(self.BL1,), allow_slow_non_contiguous=True, stream="msk")
        kb.dma("q", self.BL1[:, 32:36], bass.AP(self.bgproj, 0, [[1, 128], [128, 4]]),
               writes=(self.BL1,), allow_slow_non_contiguous=True, stream="msk")
        blv_o = self.rows(512)
        bgv_o = self.rows(512)
        bg_o = self.rows(2)
        kb.dma("q", self.row_ap(blv_o, 512), bass.AP(self.blqkv, 1024, [[512, 1], [1, 512]]),
               writes=(ROWS,), stream="msk")
        kb.dma("q", self.row_ap(bgv_o, 512), bass.AP(self.bgqkv, 1024, [[512, 1], [1, 512]]),
               writes=(ROWS,), stream="msk")
        kb.dma("q", self.row_ap(bg_o, 2), bass.AP(self.bgate1, 0, [[2, 1], [1, 2]]),
               writes=(ROWS,), stream="msk")
        self.load_w_chunks(self.WB0, 0, self.wlqkv, 512, 1536, nsplit=2)
        self.load_w_chunks(self.WB1, 0, self.wgqkv, 512, 1536, nsplit=2)

        for tb in range(NTB):
            kb.op("v", lambda e, tb=tb: e.tensor_copy(
                self.XCB[:], XRES[:, tb * D:(tb + 1) * D]),
                reads=(XRES,), writes=(self.AF2,), inc=True)
            self.tm_tile_to_fm(lambda fc: self.XCB[:, fc * 128:(fc + 1) * 128],
                               XF, lambda fc: fc * T, tb)

        if DEV_PROBE == "xf":
            kb.dma("q", self.dbg_t.ap(), XF[:, 0:8192], reads=(XF,),
                   writes=(self.dbg_t,), stream="probe")
        CGF = QT[:, 8256:10304]
        for ci in range(4):
            n = 0
            for r in range(4):
                for ic in range(NFC):
                    wt = self.SBbf1[:, 0:128] if n % 2 == 0 else self.SBbf1[:, 128:256]
                    kb.dma("g2", wt, bass.AP(self.wcomp,
                           (r * 1024 + ic * 128) * 1024 + 512 + ci * 128,
                           [[1024, 128], [1, 128]]),
                           writes=(self.SBbf1,), stream="wcmp")
                    rhs = XF[:, ic * T + r: ic * T + r + 2045: 4]
                    kb.op("t", lambda e, wt=wt, rhs=rhs, n=n: e.matmul(
                        PA[:], wt, rhs, start=(n == 0), stop=(n == 31)),
                        reads=(self.SBbf1, XF), writes=(PA,), inc=(n == 31))
                    n += 1
            kb.op("s", lambda e, ci=ci: e.activation(
                CGF[:, ci * 512:(ci + 1) * 512], PA[:], AF.Identity,
                bias=self.BL1[:, 24 + ci:25 + ci]),
                reads=(PA, self.BL1), writes=(QT,), inc=True)

        GQ = lambda c: QT[:, 2048 + c * 512: 2048 + (c + 1) * 512]
        GK = lambda c: QT[:, 4096 + c * 512: 4096 + (c + 1) * 512]
        for oc in range(8):
            for ic in range(4):
                l = self.WB1[:, ic * 1536 + oc * 128: ic * 1536 + oc * 128 + 128]
                r = CGF[:, ic * 512:(ic + 1) * 512]
                kb.op("t", lambda e, l=l, r=r, ic=ic: e.matmul(
                    PB[:], l, r, start=(ic == 0), stop=(ic == 3)),
                    reads=(self.WB1, QT), writes=(PB,), inc=(ic == 3))
            dst = GQ(oc) if oc < 4 else GK(oc - 4)
            kb.op("s", lambda e, dst=dst, oc=oc: e.activation(
                dst, PB[:], AF.Identity, bias=self.BL1[:, 12 + oc:13 + oc]),
                reads=(PB, self.BL1), writes=(QT,), inc=True)
        for tt in range(4):
            for ic in range(4):
                l = CGF[:, ic * 512 + tt * 128: ic * 512 + tt * 128 + 128]
                r = self.WB1[:, ic * 1536 + 1024: ic * 1536 + 1536]
                kb.op("t", lambda e, l=l, r=r, ic=ic: e.matmul(
                    PB[:], l, r, start=(ic == 0), stop=False),
                    reads=(self.WB1, QT), writes=(PB,))
            kb.op("t", lambda e: e.matmul(
                PB[:], self.ONER[0:1, 0:128], self.row_ap(bgv_o, 512),
                start=False, stop=True),
                reads=(self.ONER, ROWS), writes=(PB,), inc=True)
            for h in range(8):
                off = 6144 + tt * 528 + h * 66
                kb.op("s", lambda e, off=off, h=h: e.activation(
                    QT[:, off:off + 64], PB[:, h * 64:(h + 1) * 64], AF.Copy),
                    reads=(PB,), writes=(QT,), inc=True)
                kb.op("v", lambda e, off=off: e.memset(QT[:, off + 64:off + 65], 1.0),
                      writes=(QT,))
        ATTG = lambda c, hh: QT[64 * hh:64 * hh + 64, c * 512:(c + 1) * 512]
        for h in range(8):
            dc, hh = h // 2, h % 2
            for jt in range(4):
                pp = self.P4 if jt % 2 == 0 else self.P5
                kb.op("t", lambda e, dc=dc, hh=hh, jt=jt, pp=pp: e.matmul(
                    pp[0:128, :], GK(dc)[64 * hh:64 * hh + 64, jt * 128:(jt + 1) * 128],
                    GQ(dc)[64 * hh:64 * hh + 64, :], start=True, stop=True),
                    reads=(QT,), writes=(pp,), inc=True)
                pt = self.SBbf1[:, 0:512] if jt % 2 == 0 else self.SBbf0[:, 0:512]
                ptt = self.SBbf1 if jt % 2 == 0 else self.SBbf0
                kb.op("s", lambda e, pt=pt, pp=pp: e.activation(pt, pp[:], AF.Exp, scale=0.125),
                      reads=(pp,), writes=(ptt,), inc=True)
                kb.op("v", lambda e, pt=pt, jt=jt: e.tensor_tensor(
                    pt, pt, self.MSK[:, jt * 512:(jt + 1) * 512], ALU.mult),
                    reads=(ptt, self.MSK), writes=(ptt,), inc=True)
                kb.op("t", lambda e, jt=jt, h=h, pt=pt: e.matmul(
                    PB[0:65, :], QT[:, 6144 + jt * 528 + h * 66: 6144 + jt * 528 + h * 66 + 65],
                    pt, start=(jt == 0), stop=(jt == 3)),
                    reads=(QT, ptt), writes=(PB,), inc=(jt == 3))
            self.finish_head(PB, ATTG(dc, 0), ATTG(dc, 1), hh, QT)
        GOF = self.WB1[:, 6144:8192]
        for oc in range(4):
            for ic in range(4):
                l_ap = bass.AP(self.wgproj, ic * 128 * 512 + oc * 128,
                               [[512, 128], [1, 128]])
                wt = self.SBbf1[:, 0:128] if ic % 2 == 0 else self.SBbf1[:, 128:256]
                kb.dma("g2", wt, l_ap, writes=(self.SBbf1,), stream="wcmp")
                r = QT[:, ic * 512:(ic + 1) * 512]
                kb.op("t", lambda e, wt=wt, r=r, ic=ic: e.matmul(
                    PB[:], wt, r, start=(ic == 0), stop=(ic == 3)),
                    reads=(self.SBbf1, QT), writes=(PB,), inc=(ic == 3))
            kb.op("s", lambda e, oc=oc: e.activation(
                GOF[:, oc * 512:(oc + 1) * 512], PB[:], AF.Identity,
                bias=self.BL1[:, 32 + oc:33 + oc]),
                reads=(PB, self.BL1), writes=(self.WB1,), inc=True)

        for oc in range(8):
            for seg in range(4):
                for ic in range(4):
                    l = self.WB0[:, ic * 1536 + oc * 128: ic * 1536 + oc * 128 + 128]
                    r = XF[:, ic * T + seg * 512: ic * T + seg * 512 + 512]
                    kb.op("t", lambda e, l=l, r=r, ic=ic: e.matmul(
                        PA[:], l, r, start=(ic == 0), stop=(ic == 3)),
                        reads=(self.WB0, XF), writes=(PA,), inc=(ic == 3))
                base = (oc % 4) * 2048 + (0 if oc < 4 else 8192)
                kb.op("s", lambda e, base=base, seg=seg, oc=oc: e.activation(
                    QT[:, base + seg * 512: base + seg * 512 + 512], PA[:], AF.Identity,
                    bias=self.BL1[:, oc:oc + 1]),
                    reads=(PA, self.BL1), writes=(QT,), inc=True)
        LQ = lambda c: QT[:, c * 2048: (c + 1) * 2048]
        LK = lambda c: QT[:, 8192 + c * 2048: 8192 + (c + 1) * 2048]
        ATTC2L = lambda ib, c, hh: XF[64 * hh:64 * hh + 64,
                                      8192 + ib * 2048 + c * 512:
                                      8192 + ib * 2048 + (c + 1) * 512]
        for hg in range(2):
            for tt in range(16):
                for ic in range(4):
                    l = XF[:, ic * T + tt * 128: ic * T + tt * 128 + 128]
                    r = self.WB0[:, ic * 1536 + 1024 + hg * 256:
                                  ic * 1536 + 1024 + hg * 256 + 256]
                    kb.op("t", lambda e, l=l, r=r, ic=ic: e.matmul(
                        PB[:, 0:256], l, r, start=(ic == 0), stop=False),
                        reads=(self.WB0, XF), writes=(PB,))
                kb.op("t", lambda e, hg=hg: e.matmul(
                    PB[:, 0:256], self.ONER[0:1, 0:128],
                    self.row_ap(blv_o + hg * 256, 256),
                    start=False, stop=True),
                    reads=(self.ONER, ROWS), writes=(PB,), inc=True)
                for hl in range(4):
                    off = tt * 264 + hl * 66
                    kb.op("s", lambda e, off=off, hl=hl: e.activation(
                        VTL[:, off:off + 64], PB[:, hl * 64:(hl + 1) * 64], AF.Copy),
                        reads=(PB,), writes=(VTL,), inc=True)
                    kb.op("v", lambda e, off=off: e.memset(VTL[:, off + 64:off + 65], 1.0),
                          writes=(VTL,))
            for ib in range(4):
                i0 = ib * 512
                for hl in range(4):
                    h = hg * 4 + hl
                    dc, hh = h // 2, h % 2
                    njt = 4 * ib + 4
                    for jt in range(njt):
                        pp = self.P4 if jt % 2 == 0 else self.P5
                        kb.op("t", lambda e, dc=dc, hh=hh, jt=jt, i0=i0, pp=pp: e.matmul(
                            pp[:], LK(dc)[64 * hh:64 * hh + 64, jt * 128:(jt + 1) * 128],
                            LQ(dc)[64 * hh:64 * hh + 64, i0:i0 + 512],
                            start=True, stop=True),
                            reads=(QT,), writes=(pp,), inc=True)
                        pt = self.SBbf1[:, 0:512] if jt % 2 == 0 else self.SBbf0[:, 0:512]
                        ptt = self.SBbf1 if jt % 2 == 0 else self.SBbf0
                        kb.op("s", lambda e, pt=pt, pp=pp: e.activation(pt, pp[:], AF.Exp,
                                                                 scale=0.125),
                              reads=(pp,), writes=(ptt,), inc=True)
                        kk = jt - 4 * ib
                        if 0 <= kk <= 4:
                            kb.op("v", lambda e, pt=pt, kk=kk: e.tensor_tensor(
                                pt, pt, self.MSK[:, kk * 512:(kk + 1) * 512], ALU.mult),
                                reads=(ptt, self.MSK), writes=(ptt,), inc=True)
                        kb.op("t", lambda e, jt=jt, hl=hl, pt=pt, njt=njt: e.matmul(
                            PB[0:65, :], VTL[:, jt * 264 + hl * 66: jt * 264 + hl * 66 + 65],
                            pt, start=(jt == 0), stop=(jt == njt - 1)),
                            reads=(VTL, ptt), writes=(PB,), inc=(jt == njt - 1))
                    self.finish_head(PB, ATTC2L(ib, dc, 0), ATTC2L(ib, dc, 1), hh, XF)
        for ib in range(4):
            i0 = ib * 512
            for oc in range(4):
                for ci in range(4):
                    l_ap = bass.AP(self.wlproj, ci * 128 * 512 + oc * 128,
                                   [[512, 128], [1, 128]])
                    wt = self.SBbf1[:, 0:128] if ci % 2 == 0 else self.SBbf1[:, 128:256]
                    kb.dma("g2", wt, l_ap, writes=(self.SBbf1,), stream="wcmp")
                    r = XF[:, 8192 + ib * 2048 + ci * 512: 8192 + ib * 2048 + (ci + 1) * 512]
                    kb.op("t", lambda e, wt=wt, r=r, ci=ci: e.matmul(
                        PB[:], wt, r, start=(ci == 0), stop=(ci == 3)),
                        reads=(self.SBbf1, XF), writes=(PB,), inc=(ci == 3))
                kb.op("s", lambda e, oc=oc, i0=i0: e.activation(
                    XF[:, oc * 2048 + i0: oc * 2048 + i0 + 512], PB[:], AF.Identity,
                    bias=self.BL1[:, 28 + oc:29 + oc]),
                    reads=(PB, self.BL1), writes=(XF,), inc=True)

        if DEV_PROBE == "lo":
            kb.dma("q", self.dbg_t.ap(), XF[:, 0:8192], reads=(XF,),
                   writes=(self.dbg_t,), stream="probe")
        if DEV_PROBE == "go":
            kb.dma("q", bass.AP(self.dbg_t, 0, [[8192, 128], [1, 2048]]),
                   GOF[:], reads=(self.WB1,), writes=(self.dbg_t,), stream="probe")
        for ci in range(4):
            src = GOF[:, ci * 512:(ci + 1) * 512]
            dst = QT[:, ci * 2048:(ci + 1) * 2048]
            kb.op("v", lambda e, src=src, dst=dst: e.tensor_copy(
                dst.rearrange("p (a b) -> p a b", b=4),
                src.unsqueeze(2).broadcast_to([128, 512, 4])),
                reads=(self.WB1,), writes=(QT,), inc=True)

        dbr = self.rows(2)
        kb.op("v", lambda e: e.tensor_sub(self.row_ap(dbr, 1), self.row_ap(bg_o, 1),
                                          self.row_ap(bg_o + 1, 1)),
              reads=(ROWS,), writes=(ROWS,))
        kb.dma("q", bass.AP(self.SCR, 7, [[8, 1], [1, 1]]), self.row_ap(dbr, 1),
               reads=(ROWS,), writes=(self.SCR,), stream="cc")
        self.bcast_scalar_col(7, self.SCC[:, 7:8], self.SCC)
        WGT = self.SBbf1[:, 256:288]
        for n in range(8):
            kb.dma("g2", WGT[:, 2 * n:2 * n + 2],
                   bass.AP(self.wgate1, n * 128 * 2, [[2, 128], [1, 2]]),
                   writes=(self.SBbf1,), stream="wcmp")
        for tb in range(NTB):
            n = 0
            for src, srct in ((XF, XF), (QT, QT)):
                for ci in range(4):
                    l = src[:, ci * 2048 + tb * 128: ci * 2048 + tb * 128 + 128]
                    wt = WGT[:, 2 * n:2 * n + 2]
                    kb.op("t", lambda e, l=l, wt=wt, n=n: e.matmul(
                        PD[0:128, 0:2], l, wt, start=(n == 0), stop=(n == 7)),
                        reads=(srct, self.SBbf1), writes=(PD,), inc=(n == 7))
                    n += 1
            g0 = COLS[:, 70:71]
            g1 = COLS[:, 71:72]
            t_ = COLS[:, 72:73]
            kb.op("s", lambda e: e.activation(COLS[:, 78:80], PD[0:128, 0:2], AF.Copy),
                  reads=(PD,), writes=(COLS,), inc=True)
            kb.op("v", lambda e: e.tensor_sub(t_, COLS[:, 78:79], COLS[:, 79:80]),
                  reads=(COLS,), writes=(COLS,), inc=True)
            kb.op("s", lambda e: e.activation(g0, t_, AF.Sigmoid, bias=self.SCC[:, 7:8]),
                  reads=(COLS, self.SCC), writes=(COLS,), inc=True)
            kb.op("v", lambda e: e.tensor_scalar(g1, g0, -1.0, 1.0, ALU.mult, ALU.add),
                  reads=(COLS,), writes=(COLS,))
            if DEV_PROBE == "gate":
                kb.dma("q", bass.AP(self.out_t, tb * 128 * D, [[D, 128], [1, 2]]),
                       COLS[:, 70:72], reads=(COLS,), writes=(self.out_t,),
                       stream="probe")
            for ci in range(8):
                src = XF if ci < 4 else QT
                cc = ci % 4
                sap = src[:, cc * 2048 + tb * 128: cc * 2048 + tb * 128 + 128]
                kb.op("t", lambda e, sap=sap: e.transpose(self.PTB, sap, self.IDB[:]),
                      reads=(src if ci < 4 else QT, self.IDB), writes=(self.P7,), inc=True)
                xap = XRES[:, tb * D + (0 if ci < 4 else 512) + cc * 128:
                           tb * D + (0 if ci < 4 else 512) + cc * 128 + 128]
                gc = g0 if ci < 4 else g1
                kb.op("v", lambda e, xap=xap, gc=gc: e.scalar_tensor_tensor(
                    xap, self.PTB, gc, xap, ALU.mult, ALU.add),
                    reads=(self.P7, COLS, XRES), writes=(XRES,), inc=True)

        for tb in range(NTB):
            mcol, rcol = COLS[:, 0:1], COLS[:, 1:2]
            self.ln_stats(XRES[:, tb * D:(tb + 1) * D], XRES, mcol, rcol)
            nb = COLS[:, 3:4]
            kb.op("v", lambda e: e.tensor_tensor(nb, mcol, rcol, ALU.mult),
                  reads=(COLS,), writes=(COLS,))
            kb.op("v", lambda e: e.tensor_scalar(nb, nb, -1.0, None, ALU.mult),
                  reads=(COLS,), writes=(COLS,))
            kb.op("v", lambda e, tb=tb: e.scalar_tensor_tensor(
                XRES[:, tb * D:(tb + 1) * D], XRES[:, tb * D:(tb + 1) * D],
                rcol, nb.broadcast_to([128, D]), ALU.mult, ALU.add),
                reads=(XRES, COLS), writes=(XRES,), inc=True)

    def xn_materialize(self, gcol_base, bcol_base):
        kb = self.kb
        XRES, XF, COLS = self.XRES, self.XF, self.COLS
        for tb in range(NTB):
            mcol, rcol = COLS[:, 0:1], COLS[:, 1:2]
            self.ln_stats(XRES[:, tb * D:(tb + 1) * D], XRES, mcol, rcol)
            nb = COLS[:, 3:4]
            kb.op("v", lambda e: e.tensor_tensor(nb, mcol, rcol, ALU.mult),
                  reads=(COLS,), writes=(COLS,))
            kb.op("v", lambda e: e.tensor_scalar(nb, nb, -1.0, None, ALU.mult),
                  reads=(COLS,), writes=(COLS,))
            kb.op("s", lambda e, tb=tb: e.activation(
                self.XCB[:], XRES[:, tb * D:(tb + 1) * D], AF.Identity,
                scale=rcol, bias=nb),
                reads=(XRES, COLS), writes=(self.AF2,), inc=True)
            for fc in range(NFC):
                sap = self.XCB[:, fc * 128:(fc + 1) * 128]
                kb.op("t", lambda e, sap=sap: e.transpose(self.PTB, sap, self.IDB[:]),
                      reads=(self.AF2, self.IDB), writes=(self.P7,), inc=True)
                c0 = fc * T + tb * 128
                kb.op("s", lambda e, c0=c0, fc=fc: e.activation(
                    XF[:, c0:c0 + 128], self.PTB, AF.Identity,
                    scale=self.BL1[:, gcol_base + fc:gcol_base + fc + 1],
                    bias=self.BL1[:, bcol_base + fc:bcol_base + fc + 1]),
                    reads=(self.P7, self.BL1), writes=(XF,), inc=True)

    def phase_l1f(self):
        kb, nc = self.kb, self.nc
        XRES, XF, QT = self.XRES, self.XF, self.QT
        PA, PB, PC, PD = self.PA, self.PB, self.PC, self.PD
        P4, P5, P6, P7 = self.P4, self.P5, self.P6, self.P7
        ROWS, COLS = self.ROWS, self.COLS
        self.rows_reset(self._base_rows)
        QTF = self.QT.ap().bitcast(F32)
        XRB = self.XRES.ap().bitcast(BF16)

        for src, base in ((self.ln1g, 40), (self.ln1b, 48),
                          (self.ln2g, 56), (self.ln2b, 64)):
            kb.dma("q", self.BL1[:, base // 8 * 8:base // 8 * 8 + 8] if False else
                   self.BL1[:, base - 40 + 0:base - 40 + 8] if False else
                   self.BL1[:, (base - 40) + 0:(base - 40) + 8],
                   bass.AP(src, 0, [[1, 128], [128, 8]]),
                   writes=(self.BL1,), allow_slow_non_contiguous=True, stream="msk")

        self.xn_materialize(0, 8)

        for tb in range(NTB):
            kb.dma("s2", bass.AP(self.X3S, tb * 128 * D, [[D, 128], [1, D]]),
                   XRES[:, tb * D:(tb + 1) * D], reads=(XRES,), writes=(self.X3S,),
                   stream="x3s")

        self.load_w_chunks(self.WB0, 0, self.wp1, 1024, 1024, nsplit=4)
        psall = (PA, PB, PC, PD, P4, P5, P6, P7)
        for mseg in range(8):
            for ic in range(NFC):
                par = ic % 2
                pesl = self.SBbf0[:, 0:512] if par == 0 else self.SBbf1[:, 0:512]
                peslt = self.SBbf0 if par == 0 else self.SBbf1
                kb.dma("q", pesl, bass.AP(self.pe2_d, ic * 4096 + mseg * 512,
                                          [[32768, 128], [1, 512]]),
                       writes=(peslt,), stream="pe2")
                for pc in range(8):
                    l = self.WB0[:, ic * 1024 + pc * 128: ic * 1024 + pc * 128 + 128]
                    kb.op("t", lambda e, l=l, r=pesl, ic=ic, pc=pc: e.matmul(
                        psall[pc][:], l, r, start=(ic == 0), stop=(ic == NFC - 1)),
                        reads=(self.WB0, peslt), writes=(psall[pc],),
                        inc=(ic == NFC - 1))
            for pc in range(8):
                kb.op("s", lambda e, pc=pc, mseg=mseg: e.activation(
                    XRB[:, pc * 4096 + mseg * 512: pc * 4096 + (mseg + 1) * 512],
                    psall[pc][:], AF.Copy),
                    reads=(psall[pc],), writes=(XRES,), inc=True)

        self.load_w_chunks(self.WB0, 0, self.wo1, 1024, 1024, nsplit=4)
        kb.barrier()
        VTL = self.VTL
        XFK, P2K, NRK = RK("xfH"), RK("p2H"), RK("nrmH")
        PRK = (RK("pair0"), RK("pair1"))
        WHK = (RK("wh0"), RK("wh1"))
        WTK = (RK("wt0"), RK("wt1"))
        OHK = (RK("oh0"), RK("oh1"))
        BDK = tuple(RK(f"bdt{i}") for i in range(6))
        PTK = (RK("pt0"), RK("pt1"))
        OHSK = RK("ohsK")
        PSK = {id(p): RK(f"psl{i}") for i, p in
               enumerate((PA, PB, PC, PD, P4, P5, P6, P7))}
        QHBp = lambda pb: QT[:, pb * 6208: pb * 6208 + 2048]
        KHBp = lambda pb: QT[:, pb * 6208 + 2048: pb * 6208 + 4096]
        vt_off = lambda pb, hh, tt: pb * 6208 + 4096 + hh * 1056 + tt * 66
        WT = (QT[:, 12416:14592], VTL[:, 1536:3712])
        OH2b = (VTL[:, 0:512], VTL[:, 3712:4224])
        A2B = self.AF2.ap().bitcast(BF16)
        BDT = (VTL[:, 512:1024], VTL[:, 1024:1536], A2B[:, 0:512],
               A2B[:, 512:1024], A2B[:, 2048:2560], A2B[:, 2560:3072])
        NBD = len(BDT)

        def loadW(p):
            pb = p % 2
            for wi, wsrc in enumerate((self.wq1, self.wk1, self.wv1)):
                kb.dma("g2",
                       bass.AP(self.WB1, pb * 3072 + wi * 1024,
                               [[8192, 128], [128, 8], [1, 128]]),
                       bass.AP(wsrc, p * 128, [[1024, 128], [128 * 1024, 8],
                                               [1, 128]]),
                       writes=(WHK[pb],), stream=f"wh{pb}")

        def qkv(p):
            pb = p % 2
            WQ = self.WB1[:, pb * 3072: pb * 3072 + 1024]
            WKb = self.WB1[:, pb * 3072 + 1024: pb * 3072 + 2048]
            WV = self.WB1[:, pb * 3072 + 2048: pb * 3072 + 3072]
            for di, dst in enumerate((QHBp(pb), KHBp(pb))):
                Wm = WQ if di == 0 else WKb
                for seg in range(4):
                    pq = PA if seg % 2 == 0 else PB
                    for ic in range(NFC):
                        l = Wm[:, ic * 128:(ic + 1) * 128]
                        r = XF[:, ic * T + seg * 512: ic * T + seg * 512 + 512]
                        kb.op("t", lambda e, l=l, r=r, ic=ic, pq=pq: e.matmul(
                            pq[:], l, r, start=(ic == 0), stop=(ic == NFC - 1)),
                            reads=(WHK[pb], XFK), writes=(PSK[id(pq)],),
                            inc=(ic == NFC - 1))
                    kb.op("s", lambda e, dst=dst, seg=seg, pq=pq: e.activation(
                        dst[:, seg * 512:(seg + 1) * 512], pq[:], AF.Copy),
                        reads=(PSK[id(pq)],), writes=(PRK[pb],), inc=True)
            for tt in range(NTB):
                pv = PC if tt % 2 == 0 else PD
                for ic in range(NFC):
                    l = XF[:, ic * T + tt * 128: ic * T + tt * 128 + 128]
                    r = WV[:, ic * 128:(ic + 1) * 128]
                    kb.op("t", lambda e, l=l, r=r, ic=ic, pv=pv: e.matmul(
                        pv[:, 0:128], l, r, start=(ic == 0), stop=(ic == NFC - 1)),
                        reads=(WHK[pb], XFK), writes=(PSK[id(pv)],),
                        inc=(ic == NFC - 1))
                for hh in range(2):
                    off = vt_off(pb, hh, tt)
                    eng_ = "s" if hh == 0 else "v"
                    if hh == 0:
                        kb.op("s", lambda e, off=off, pv=pv: e.activation(
                            QT[:, off:off + 64], pv[:, 0:64], AF.Copy),
                            reads=(PSK[id(pv)],), writes=(PRK[pb],), inc=True)
                    else:
                        kb.op("v", lambda e, off=off, pv=pv: e.tensor_copy(
                            QT[:, off:off + 64], pv[:, 64:128]),
                            reads=(PSK[id(pv)],), writes=(PRK[pb],), inc=True)
                    kb.op("v", lambda e, off=off: e.memset(
                        QT[:, off + 64:off + 65], 1.0), writes=(PRK[pb],))

        def bd(p):
            pb = p % 2
            psl = (PA, PB, PC, PD, P4)
            for hh in range(2):
                for i128 in range(16):
                    base = 1920 - 128 * i128
                    lq = QHBp(pb)[64 * hh:64 * hh + 64, i128 * 128:(i128 + 1) * 128]
                    for sgi in range(5):
                        n = 512 if sgi < 4 else 128
                        rp = XRB[64 * hh:64 * hh + 64,
                                 p * 4096 + base + sgi * 512:
                                 p * 4096 + base + sgi * 512 + n]
                        kb.op("t", lambda e, lq=lq, rp=rp, n=n, sgi=sgi: e.matmul(
                            psl[sgi][:, 0:n], lq, rp, start=True, stop=True),
                            reads=(PRK[pb], P2K), writes=(PSK[id(psl[sgi])],),
                            inc=True)
                    wi = (hh * 16 + i128) % 2
                    wt = WT[wi]
                    for sgi in range(5):
                        n = 512 if sgi < 4 else 128
                        if sgi % 2 == 0:
                            kb.op("s", lambda e, sgi=sgi, n=n, wt=wt: e.activation(
                                wt[:, sgi * 512: sgi * 512 + n], psl[sgi][:, 0:n],
                                AF.Copy),
                                reads=(PSK[id(psl[sgi])],), writes=(WTK[wi],),
                                inc=True)
                        else:
                            kb.op("v", lambda e, sgi=sgi, n=n, wt=wt: e.tensor_copy(
                                wt[:, sgi * 512: sgi * 512 + n], psl[sgi][:, 0:n]),
                                reads=(PSK[id(psl[sgi])],), writes=(WTK[wi],),
                                inc=True)
                    kb.dma("s2", bass.AP(self.BDWp[pb],
                                         hh * T * 2304 + i128 * 128 * 2304,
                                         [[2305, 128], [1, 2176]]),
                           wt, reads=(WTK[wi],), writes=(self.BDWp[pb],),
                           stream=f"bdw{pb}{wi}")

        def flash(p):
            pb = p % 2
            tiles = [(hh, ib, jt) for hh in range(2) for ib in range(4)
                     for jt in range(16)]

            def issue_bdr(n):
                hh, ib, jt = tiles[n]
                s = n % NBD
                kb.dma("q", BDT[s],
                       bass.AP(self.BDWp[pb],
                               hh * T * 2304 + ib * 512 * 2304 + jt * 128 + 127,
                               [[2304, 512], [1, 128]]),
                       reads=(self.BDWp[pb],), writes=(BDK[s],),
                       stream=f"bdr{s}", transpose=True)

            for n in range(NBD):
                issue_bdr(n)
            for hh in range(2):
                h = 2 * p + hh
                for ib in range(4):
                    i0 = ib * 512
                    for jt in range(16):
                        n = (hh * 4 + ib) * 16 + jt
                        sl = n % NBD
                        bdt = BDT[sl]
                        ps = P5 if jt % 2 == 0 else P7
                        kb.op("t", lambda e, jt=jt, i0=i0, hh=hh, ps=ps: e.matmul(
                            ps[:], KHBp(pb)[64 * hh:64 * hh + 64,
                                            jt * 128:(jt + 1) * 128],
                            QHBp(pb)[64 * hh:64 * hh + 64, i0:i0 + 512],
                            start=True, stop=False),
                            reads=(PRK[pb],), writes=(PSK[id(ps)],))
                        kb.op("t", lambda e, ps=ps, bdt=bdt: e.matmul(
                            ps[:], self.IDB[:], bdt, start=False, stop=True),
                            reads=(self.IDB, BDK[sl]),
                            writes=(PSK[id(ps)],), inc=True)
                        if n + NBD < 128:
                            issue_bdr(n + NBD)
                        pt = (self.SBbf0 if jt % 2 == 0 else self.SBbf1)[:, 0:512]
                        kb.op("s", lambda e, pt=pt, ps=ps: e.activation(
                            pt, ps[:], AF.Exp, scale=0.125),
                            reads=(PSK[id(ps)],), writes=(PTK[jt % 2],), inc=True)
                        vt = QT[:, vt_off(pb, hh, jt): vt_off(pb, hh, jt) + 65]
                        kb.op("t", lambda e, jt=jt, pt=pt, vt=vt: e.matmul(
                            P6[0:65, :], vt, pt, start=(jt == 0), stop=(jt == 15)),
                            reads=(PRK[pb], PTK[jt % 2]), writes=(PSK[id(P6)],),
                            inc=(jt == 15))
                    kb.op("v", lambda e: e.reciprocal(self.NRM[64:65, :],
                                                      P6[64:65, :]),
                          reads=(PSK[id(P6)],), writes=(NRK,), inc=True)
                    kb.op("t", lambda e: e.matmul(
                        P7[0:64, :], self.ONER[64:65, 0:64], self.NRM[64:65, :],
                        start=True, stop=True),
                        reads=(NRK, self.ONER), writes=(PSK[id(P7)],), inc=True)
                    kb.op("s", lambda e: e.activation(self.NRM[0:64, :],
                                                      P7[0:64, :], AF.Copy),
                          reads=(PSK[id(P7)],), writes=(NRK,), inc=True)
                    op = (hh * 4 + ib) % 2
                    oh2 = OH2b[op]
                    kb.op("v", lambda e, oh2=oh2: e.tensor_tensor(
                        oh2[0:64, 0:512], P6[0:64, :], self.NRM[0:64, :], ALU.mult),
                        reads=(PSK[id(P6)], NRK), writes=(OHK[op],), inc=True)
                    kb.dma("q", bass.AP(self.OHS, h * 64 * T + i0,
                                        [[T, 64], [1, 512]]),
                           oh2[0:64, 0:512], reads=(OHK[op],), writes=(OHSK,),
                           stream=f"ohs{op}")

        loadW(0)
        qkv(0)
        bd(0)
        for p in range(8):
            if p < 7:
                loadW(p + 1)
                qkv(p + 1)
                bd(p + 1)
            flash(p)
        kb.barrier()

        OTK = (RK("ot0"), RK("ot1"))
        XSK = (RK("xs0"), RK("xs1"))
        for tb in range(NTB):
            for seg in range(2):
                pso = PA if seg == 0 else PB
                xs = QTF[:, 256 + seg * 512: 768 + seg * 512]
                kb.dma("s2", xs, bass.AP(self.X3S, tb * 128 * D + seg * 512,
                                         [[D, 128], [1, 512]]),
                       reads=(self.X3S,), writes=(XSK[seg],), stream=f"x3r{seg}")
                for c in range(8):
                    ot = QT[:, 14592 + (c % 2) * 128: 14592 + (c % 2) * 128 + 128]
                    kb.dma("q", ot, bass.AP(self.OHS, c * 128 * T + tb * 128,
                                            [[T, 128], [1, 128]]),
                           reads=(self.OHS,), writes=(OTK[c % 2],),
                           stream=f"ohr{c % 2}")
                    wt = self.WB0[:, c * 1024 + seg * 512: c * 1024 + seg * 512 + 512]
                    kb.op("t", lambda e, ot=ot, wt=wt, c=c, pso=pso: e.matmul(
                        pso[:], ot, wt, start=(c == 0), stop=(c == 7)),
                        reads=(OTK[c % 2], self.WB0), writes=(PSK[id(pso)],),
                        inc=(c == 7))
                kb.op("v", lambda e, tb=tb, seg=seg, xs=xs, pso=pso:
                      e.scalar_tensor_tensor(
                          XRES[:, tb * D + seg * 512: tb * D + seg * 512 + 512],
                          xs, 2.0, pso[:], ALU.mult, ALU.add),
                      reads=(XSK[seg], PSK[id(pso)]), writes=(XRES,), inc=True)

        self.xn_materialize(16, 24)

        for ic in range(NFC):
            kb.dma("g2", self.SBbf0[:, ic * 32:(ic + 1) * 32],
                   bass.AP(self.selw, ic * 128 * 32, [[32, 128], [1, 32]]),
                   writes=(self.SBbf0,), stream="whead")
        for tb in range(NTB):
            for ic in range(NFC):
                l = XF[:, ic * T + tb * 128: ic * T + tb * 128 + 128]
                r = self.SBbf0[:, ic * 32:(ic + 1) * 32]
                kb.op("t", lambda e, l=l, r=r, ic=ic: e.matmul(
                    PB[:, 0:32], l, r, start=(ic == 0), stop=(ic == NFC - 1)),
                    reads=(XF, self.SBbf0), writes=(PB,), inc=(ic == NFC - 1))
            sel = self.AF2[:, 0:32]
            kb.op("s", lambda e, sel=sel: e.activation(sel, PB[:, 0:32], AF.Sigmoid),
                  reads=(PB,), writes=(self.AF2,), inc=True)
            m1, m4 = COLS[:, 74:75], COLS[:, 77:78]
            m2, m3 = COLS[:, 75:76], COLS[:, 76:77]
            tmsk = self.AF2[:, 32:64]
            tt_ = self.AF2[:, 64:96]
            kb.op("v", lambda e: e.reduce_max(m1, sel, axis=AX.X),
                  reads=(self.AF2,), writes=(COLS,))
            cur = sel
            for mi, mnext in ((m1, m2), (m2, m3), (m3, m4)):
                kb.op("v", lambda e, cur=cur, mi=mi: e.tensor_scalar(
                    tmsk, cur, mi, None, ALU.is_ge),
                    reads=(self.AF2, COLS), writes=(self.AF2,))
                kb.op("v", lambda e, cur=cur: e.scalar_tensor_tensor(
                    tt_, tmsk, -1e30, cur, ALU.mult, ALU.add),
                    reads=(self.AF2,), writes=(self.AF2,))
                kb.op("v", lambda e, mnext=mnext: e.reduce_max(mnext, tt_, axis=AX.X),
                      reads=(self.AF2,), writes=(COLS,))
                cur = tt_
            kb.op("v", lambda e: e.tensor_scalar(tmsk, sel, m4, None, ALU.is_ge),
                  reads=(self.AF2, COLS), writes=(self.AF2,))
            kb.op("v", lambda e: e.tensor_tensor(tmsk, tmsk, sel, ALU.mult),
                  reads=(self.AF2,), writes=(self.AF2,))
            kb.op("t", lambda e: e.matmul(self.PTF[0:32, 0:128], tmsk, self.IDF[:],
                                          start=True, stop=True),
                  reads=(self.AF2, self.IDF), writes=(self.P7,), inc=True)
            gt = self.AF2[:, 96:224]
            kb.op("v", lambda e, gt=gt: e.tensor_copy(gt[0:32, :],
                                                      self.PTF[0:32, 0:128]),
                  reads=(self.P7,), writes=(self.AF2,), inc=True)
            kb.dma("q", bass.AP(self.GATES, tb * 128, [[T, 32], [1, 128]]),
                   gt[0:32, :], reads=(self.AF2,), writes=(self.GATES,),
                   stream="gat")

        kb.barrier()
        gr_off = (self.rows(256), self.rows(256))
        WK = (RK("moeW0"), RK("moeW1"))
        HK = (RK("moeH0"), RK("moeH1"))
        SK = (RK("moeS0"), RK("moeS1"))
        GK_ = (RK("moeGR0"), RK("moeGR1"))
        XRK = RK("xresMoe")
        XFK = RK("xfMoe")
        PSK = {id(p): RK(f"ps{i}") for i, p in
               enumerate((PA, PB, PC, PD, P4, P5, P6, P7))}

        def load_group(g):
            buf = g % 2
            for k in range(4):
                e_ = g * 4 + k
                ks_off = buf * 8192 + k * 1024
                vs_off = buf * 8192 + 4096 + k * 1024
                kb.dma("g2", bass.AP(QT, ks_off, [[16384, 128], [128, 8], [1, 128]]),
                       bass.AP(self.keys1, e_ * 1024 * 128,
                               [[128, 128], [128 * 128, 8], [1, 128]]),
                       writes=(WK[buf],), stream=f"moe{buf}")
                kb.dma("g2", QT[:, vs_off:vs_off + 1024],
                       bass.AP(self.vals1, e_ * 128 * 1024, [[1024, 128], [1, 1024]]),
                       writes=(WK[buf],), stream=f"moe{buf}")

        load_group(0)
        for g in range(8):
            buf = g % 2
            if g < 7:
                load_group(g + 1)
            for tq in range(8):
                t0 = tq * 256
                for k in range(4):
                    e_ = g * 4 + k
                    par = k % 2
                    ks = QT[:, buf * 8192 + k * 1024: buf * 8192 + k * 1024 + 1024]
                    vs = QT[:, buf * 8192 + 4096 + k * 1024:
                            buf * 8192 + 4096 + k * 1024 + 1024]
                    pk = PA if par == 0 else P5
                    pg = P6 if par == 0 else P7
                    hrel = self.AF2[:, 0:256] if par == 0 else self.AF2[:, 256:512]
                    hs = self.SBbf0[:, 0:256] if par == 0 else self.SBbf1[:, 0:256]
                    hst = self.SBbf0 if par == 0 else self.SBbf1
                    kb.dma("q", self.row_ap(gr_off[par], 256),
                           bass.AP(self.GATES, e_ * T + t0, [[T, 1], [1, 256]]),
                           reads=(self.GATES,), writes=(GK_[par],),
                           stream=f"mgr{par}")
                    kb.op("t", lambda e, pg=pg, par=par: e.matmul(
                        pg[:, 0:256], self.ONER[0:1, 0:128],
                        self.row_ap(gr_off[par], 256), start=True, stop=True),
                        reads=(GK_[par], self.ONER), writes=(PSK[id(pg)],), inc=True)
                    for ic in range(NFC):
                        l = ks[:, ic * 128:(ic + 1) * 128]
                        r = XF[:, ic * T + t0: ic * T + t0 + 256]
                        kb.op("t", lambda e, l=l, r=r, ic=ic, pk=pk: e.matmul(
                            pk[:, 0:256], l, r, start=(ic == 0), stop=(ic == NFC - 1)),
                            reads=(WK[buf], XFK), writes=(PSK[id(pk)],),
                            inc=(ic == NFC - 1))
                    kb.op("s", lambda e, hrel=hrel, pk=pk: e.activation(
                        hrel, pk[:, 0:256], AF.Relu),
                        reads=(PSK[id(pk)],), writes=(HK[par],), inc=True)
                    kb.op("v", lambda e, hs=hs, hrel=hrel, pg=pg: e.tensor_tensor(
                        hs, hrel, pg[:, 0:256], ALU.mult),
                        reads=(HK[par], PSK[id(pg)]), writes=(SK[par],), inc=True)
                    for tl in range(2):
                        for oseg in range(2):
                            ps = (PB, PC, PD, P4)[tl * 2 + oseg]
                            l = hs[:, tl * 128:(tl + 1) * 128]
                            r = vs[:, oseg * 512:(oseg + 1) * 512]
                            kb.op("t", lambda e, l=l, r=r, ps=ps, k=k: e.matmul(
                                ps[:], l, r, start=(k == 0), stop=(k == 3)),
                                reads=(SK[par], WK[buf]), writes=(PSK[id(ps)],),
                                inc=(k == 3))
                for tl in range(2):
                    tb = (t0 + tl * 128) // 128
                    for oseg in range(2):
                        ps = (PB, PC, PD, P4)[tl * 2 + oseg]
                        xap = XRES[:, tb * D + oseg * 512: tb * D + oseg * 512 + 512]
                        kb.op("v", lambda e, xap=xap, ps=ps: e.tensor_add(
                            xap, xap, ps[:]),
                            reads=(XRK, PSK[id(ps)]), writes=(XRK,), inc=True)
        kb.barrier()

        for tb in range(NTB):
            mcol, rcol = COLS[:, 0:1], COLS[:, 1:2]
            self.ln_stats(XRES[:, tb * D:(tb + 1) * D], XRES, mcol, rcol)
            nb = COLS[:, 3:4]
            kb.op("v", lambda e: e.tensor_tensor(nb, mcol, rcol, ALU.mult),
                  reads=(COLS,), writes=(COLS,))
            kb.op("v", lambda e: e.tensor_scalar(nb, nb, -1.0, None, ALU.mult),
                  reads=(COLS,), writes=(COLS,))
            kb.op("v", lambda e, tb=tb: e.scalar_tensor_tensor(
                XRES[:, tb * D:(tb + 1) * D], XRES[:, tb * D:(tb + 1) * D],
                rcol, nb.broadcast_to([128, D]), ALU.mult, ALU.add),
                reads=(XRES, COLS), writes=(XRES,), inc=True)

    def emit_output(self):
        kb = self.kb
        if DEV_PROBE is not None:
            kb.finish("q")
            return
        for tb in range(NTB):
            kb.dma("q", bass.AP(self.out_t, tb * 128 * D, [[D, 128], [1, D]]),
                   self.XRES[:, tb * D:(tb + 1) * D], reads=(self.XRES,))
        kb.finish("q")

    def build(self):
        self.declare()
        ind = np.zeros((1, 256), dtype=np.float32)
        ind[0, 0:64] = 1.0
        ind[0, 128 + 64:128 + 128] = 1.0
        ind_d = self.nc.inline_tensor(ind, "indrows")
        self._indr_off = self.rows(256)
        self.INDR = self.ROWS
        ctx = self.nc.body("main", valid_engines=[ENG.Pool, ENG.Activation,
                                                  ENG.PE, ENG.DVE, ENG.SP])
        bb = ctx.__enter__()
        self.kb.dma("q", self.row_ap(self._indr_off, 256), ind_d.ap(),
                    writes=(self.ROWS,))
        self._base_rows = self.rows_off
        self.phase_l0a()
        if DEV_STAGE != "l0a":
            self.phase_l0f()
        if DEV_STAGE in ("l1a", "l1f", "full"):
            self.phase_l1a()
        if DEV_STAGE in ("l1f", "full"):
            self.phase_l1f()
        self.emit_output()
        ctx.__exit__(None, None, None)
        return self.nc


def kernel(**inputs):
    b = Builder()
    nc = b.build()
    x = np.ascontiguousarray(inputs["L0_attn_x"])
    in_maps = []
    for c in range(8):
        m = {"L0_attn_x": x[c]}
        for k, v in inputs.items():
            if k != "L0_attn_x":
                m[k] = np.ascontiguousarray(v)
        in_maps.append(m)
    declared = set(b.dram.keys())
    in_maps = [{k: v for k, v in m.items() if k in declared} for m in in_maps]
    res = run_bass_kernel_spmd(nc, in_maps, core_ids=list(range(8)))
    return np.stack([r["out"] for r in res.results], axis=0)
